# version 03
# baseline (speedup 1.0000x reference)
import numpy as np
from contextlib import ExitStack

import concourse.bass as bass
import concourse.mybir as mybir
from concourse.bass_utils import run_bass_kernel_spmd
from concourse.alu_op_type import AluOpType as OP

F32 = mybir.dt.float32
BF16 = mybir.dt.bfloat16
AF = mybir.ActivationFunctionType
AX = mybir.AxisListType

T = 4096
D = 1024
NCD = 8
N_CORES = 8

C_ONE, C_NEG1, C_NEGHALF, C_EPS_LN, C_EPS_F32, C_ZERO, C_HALF, C_NEG1E30 = range(8)
CPOOL_VALS = [1.0, -1.0, -0.5, 1e-5, float(np.finfo(np.float32).eps), 0.0, 0.5, -1e30]
DBG_TILE = 1


class Prog:
    ENGINES = ("sp", "pool", "act", "dve", "pe")

    def __init__(self, nc, es):
        self.nc = nc
        self.es = es
        self.ops = {e: [] for e in self.ENGINES}
        self.sems = {}
        self.cnt = {}

    def sem(self, name):
        if name not in self.sems:
            self.sems[name] = self.es.enter_context(self.nc.semaphore(name))
            self.cnt[name] = 0
        return self.sems[name]

    def emit(self, eng, fn, waits=(), inc=None, inc_by=1):
        s = None
        if inc is not None:
            s = self.sem(inc)
            self.cnt[inc] += inc_by
        wl = [(self.sem(n), v) for (n, v) in waits]

        def run(e, wl=wl, s=s, inc_by=inc_by, fn=fn):
            for sem, v in wl:
                e.wait_ge(sem, v)
            if fn is not None:
                ins = fn(e)
                if s is not None:
                    ins.then_inc(s, inc_by)
        self.ops[eng].append(run)

    def tok(self, name):
        return (name, self.cnt[name])

    def stage(self, eng, fns, waits=()):
        fns = list(fns)
        assert fns
        for i, fn in enumerate(fns):
            self.emit(eng, fn,
                      waits=waits if i == 0 else (),
                      inc=eng if i == len(fns) - 1 else None)
        return self.tok(eng)

    def finalize(self):
        with self.nc.Block() as block:
            for attr, e in (("sync", "sp"), ("gpsimd", "pool"), ("scalar", "act"),
                            ("vector", "dve"), ("tensor", "pe")):
                ops = self.ops[e]

                def body(eng, ops=ops):
                    for f in ops:
                        f(eng)
                getattr(block, attr)(body)


def _chunked_dma(P, queue, sem_name, dst_tile, src_ap, n_chunks, cols, waits=(), cast=False):
    for c in range(n_chunks):
        def fn(e, c=c):
            return e.dma_start(dst_tile[:, c * cols:(c + 1) * cols],
                               src_ap[c * 128:(c + 1) * 128, :])
        P.emit(queue, fn, waits=waits if c == 0 else (), inc=sem_name, inc_by=16)


def emit_ln(P, t, prev, src_tile, src_off, src_stride, n_chunks, inv_n,
            sq, pstat_x, pstat_x2, stat_mu, stat_rsig, stat_tmp, pbc_mu, pbc_rsig,
            cpool, ones_col, ones_row, out_f32=None, out_bf=None, out_cols=None,
            eps_col=C_EPS_LN, tmp_f32=None, add_to_src=False, ones_sq=None,
            ones_x=None, src_cast=None):
    oc = out_cols if out_cols is not None else t
    if ones_sq is None:
        ones_sq = ones_col
    if ones_x is None:
        ones_x = ones_col
    src = lambda c: src_tile[:, src_off + c * src_stride: src_off + c * src_stride + t]
    prev = P.stage("act", [
        (lambda e, c=c: e.activation(sq[:, c * t:(c + 1) * t], src(c), AF.Square))
        for c in range(n_chunks)], waits=[prev])
    mm = []
    for c in range(n_chunks):
        mm.append(lambda e, c=c: e.matmul(pstat_x[:, :t], ones_x[:], src(c),
                                          start=(c == 0), stop=(c == n_chunks - 1)))
    for c in range(n_chunks):
        mm.append(lambda e, c=c: e.matmul(pstat_x2[:, :t], ones_sq[:], sq[:, c * t:(c + 1) * t],
                                          start=(c == 0), stop=(c == n_chunks - 1)))
    prev = P.stage("pe", mm, waits=[prev])
    prev = P.stage("dve", [
        lambda e: e.tensor_scalar(stat_mu[:, :t], pstat_x[:, :t], inv_n, None, OP.mult),
        lambda e: e.tensor_scalar(stat_tmp[:, :t], pstat_x2[:, :t], inv_n, None, OP.mult),
        lambda e: e.tensor_tensor(stat_rsig[:, :t], stat_mu[:, :t], stat_mu[:, :t], OP.mult),
        lambda e: e.tensor_tensor(stat_tmp[:, :t], stat_tmp[:, :t], stat_rsig[:, :t], OP.subtract),
    ], waits=[prev])
    prev = P.stage("act", [
        lambda e: e.activation(stat_tmp[:, :t], stat_tmp[:, :t], AF.Ln, bias=cpool[0:1, eps_col:eps_col + 1]),
        lambda e: e.activation(stat_rsig[:, :t], stat_tmp[:, :t], AF.Exp, scale=cpool[0:1, C_NEGHALF:C_NEGHALF + 1]),
    ], waits=[prev])
    prev = P.stage("pe", [
        lambda e: e.matmul(pbc_mu[:, :t], ones_row[:], stat_mu[:, :t], start=True, stop=True),
        lambda e: e.matmul(pbc_rsig[:, :t], ones_row[:], stat_rsig[:, :t], start=True, stop=True),
    ], waits=[prev])
    fns = []
    for c in range(n_chunks):
        if tmp_f32 is not None:
            fns.append(lambda e, c=c: e.tensor_tensor(tmp_f32[:, :t], src(c), pbc_mu[:, :t], OP.subtract))
            fns.append(lambda e, c=c: e.tensor_tensor(tmp_f32[:, :t], tmp_f32[:, :t], pbc_rsig[:, :t], OP.mult))
            if out_bf is not None:
                fns.append(lambda e, c=c: e.tensor_copy(out_bf[:, c * oc:c * oc + t], tmp_f32[:, :t]))
            if add_to_src:
                fns.append(lambda e, c=c: e.tensor_tensor(src(c), src(c), tmp_f32[:, :t], OP.add))
        elif out_f32 is not None:
            fns.append(lambda e, c=c: e.tensor_tensor(out_f32[:, c * oc:c * oc + t], src(c), pbc_mu[:, :t], OP.subtract))
            fns.append(lambda e, c=c: e.tensor_tensor(out_f32[:, c * oc:c * oc + t], out_f32[:, c * oc:c * oc + t], pbc_rsig[:, :t], OP.mult))
            if out_bf is not None:
                fns.append(lambda e, c=c: e.tensor_copy(out_bf[:, c * oc:c * oc + t], out_f32[:, c * oc:c * oc + t]))
        elif src_cast is not None:
            fns.append(lambda e, c=c: e.tensor_copy(src_cast[:, :t], src(c)))
            fns.append(lambda e, c=c: e.tensor_tensor(src_cast[:, :t], src_cast[:, :t], pbc_mu[:, :t], OP.subtract))
            fns.append(lambda e, c=c: e.tensor_tensor(out_bf[:, c * oc:c * oc + t], src_cast[:, :t], pbc_rsig[:, :t], OP.mult))
        else:
            fns.append(lambda e, c=c: e.tensor_tensor(out_bf[:, c * oc:c * oc + t], src(c), pbc_mu[:, :t], OP.subtract))
            fns.append(lambda e, c=c: e.tensor_tensor(out_bf[:, c * oc:c * oc + t], out_bf[:, c * oc:c * oc + t], pbc_rsig[:, :t], OP.mult))
    prev = P.stage("dve", fns, waits=[prev])
    return prev


def emit_common_setup(P, es, nc, extra_waits=(), pref=""):
    S = lambda *a, **k: es.enter_context(nc.sbuf_tensor(pref + a[0], *a[1:], **k))
    cpool = S("cpool", [128, len(CPOOL_VALS)], F32)
    ones_col = S("ones_col", [128, 1], F32)
    ones_row = S("ones_row", [1, 128], F32)
    fns = []
    for i, v in enumerate(CPOOL_VALS):
        fns.append(lambda e, i=i, v=v: e.memset(cpool[:, i:i + 1], v))
    ones_bf = S("ones_bf", [128, 1], BF16)
    fns.append(lambda e: e.memset(ones_col[:], 1.0))
    fns.append(lambda e: e.memset(ones_row[:], 1.0))
    fns.append(lambda e: e.memset(ones_bf[:], 1.0))
    tok = P.stage("dve", fns, waits=list(extra_waits))
    return cpool, ones_col, ones_row, ones_bf, tok


def build_l1_attn(debug=False, ctx=None):
    if ctx is None:
        nc = bass.Bass("TRN2", target_bir_lowering=False)
        _pref = ""
    else:
        nc = ctx["nc"]
        _pref = ctx["pref"]
    xin = ctx["xin"] if ctx is not None else nc.dram_tensor("xin", [D, T], F32, kind="ExternalInput").ap()
    w_inT = nc.dram_tensor(_pref + "w_inT", [D, 2 * D], F32, kind="ExternalInput").ap()
    vecs = nc.dram_tensor(_pref + "vecs", [128, 5 * NCD], F32, kind="ExternalInput").ap()
    xout = ctx["xout"] if ctx is not None else nc.dram_tensor("xout", [D, T], F32, kind="ExternalOutput").ap()
    if debug:
        dbg = {k: nc.dram_tensor(_pref + "dbg_" + k, shp, F32, kind="ExternalOutput").ap()
               for k, shp in [("xinT", [128, 8 * 256]), ("zT", [128, 8 * 256]),
                              ("g8", [128, 8 * 256]), ("dtraw", [1, 256]),
                              ("mu", [1, 256]), ("rsig", [1, 256]),
                              ("carry", [128, 8])]}

    TT = 256
    NTILE = T // TT

    es = ExitStack()
    S = lambda *a, **k: es.enter_context(nc.sbuf_tensor(_pref + a[0], *a[1:], **k))
    PS = lambda *a, **k: es.enter_context(nc.psum_tensor(_pref + a[0], *a[1:], **k))
    with es:
        RX = S("RX", [128, NCD * T], F32)
        WIN = S("WIN", [128, NCD * 2 * D], BF16)
        VEC = S("VEC", [128, 5 * NCD], F32)
        xhat_b = S("xhat_b", [128, NCD * TT], BF16)
        sq = S("sq", [128, NCD * TT], F32)
        x_inT = S("x_inT", [128, NCD * TT], F32)
        zT = S("zT", [128, NCD * TT], F32)
        g8 = S("g8", [128, NCD * TT], F32)
        w0 = S("w0", [128, TT], F32)
        w1 = S("w1", [128, TT], F32)
        w2 = S("w2", [128, TT], F32)
        carry = S("carry", [128, NCD], F32)
        stat_mu = S("stat_mu", [1, TT], F32)
        stat_rsig = S("stat_rsig", [1, TT], F32)
        stat_tmp = S("stat_tmp", [1, TT], F32)
        dtraw = S("dtraw", [1, TT], F32)

        pbig0 = PS("pbig0", [128, TT], F32)
        pbig1 = PS("pbig1", [128, TT], F32)
        pstat_x = PS("pstat_x", [1, TT], F32)
        pstat_x2 = PS("pstat_x2", [1, TT], F32)
        pbc_mu = PS("pbc_mu", [128, TT], F32)
        pbc_rsig = PS("pbc_rsig", [128, TT], F32)
        pdtb = PS("pdtb", [128, TT], F32)

        P = ctx["P"] if ctx is not None else Prog(nc, es)
        _pw = [ctx["prev"]] if ctx is not None and ctx.get("prev") else []
        cpool, ones_col, ones_row, ones_bf, tok0 = emit_common_setup(P, es, nc, extra_waits=_pw, pref=_pref)
        P.stage("dve", [lambda e: e.memset(carry[:], 0.0)])

        _chunked_dma(P, "sp", "dqx", RX, xin, NCD, T)
        for c in range(NCD):
            P.emit("pool", lambda e, c=c: e.dma_start(WIN[:, c * 2 * D:(c + 1) * 2 * D],
                                                      w_inT[c * 128:(c + 1) * 128, :]),
                   waits=_pw if c == 0 else (), inc="dqw", inc_by=16)
        P.emit("pool", lambda e: e.dma_start(VEC[:], vecs), inc="dqw", inc_by=16)
        wx0 = lambda c: VEC[:, 0 * NCD + c:0 * NCD + c + 1]
        wdt = lambda c: VEC[:, 1 * NCD + c:1 * NCD + c + 1]
        bdt = lambda c: VEC[:, 2 * NCD + c:2 * NCD + c + 1]
        gam = lambda c: VEC[:, 3 * NCD + c:3 * NCD + c + 1]
        bet = lambda c: VEC[:, 4 * NCD + c:4 * NCD + c + 1]

        loads_tok = [("dqx", P.cnt["dqx"]), ("dqw", P.cnt["dqw"]), P.tok("dve")]

        prev = P.stage("dve", [lambda e: e.engine_nop()], waits=loads_tok)

        for it in range(NTILE):
            t0 = it * TT
            prev = emit_ln(P, TT, prev, RX, t0, T, NCD, 1.0 / D,
                           sq, pstat_x, pstat_x2, stat_mu, stat_rsig, stat_tmp,
                           pbc_mu, pbc_rsig, cpool, ones_col, ones_row,
                           out_bf=xhat_b, tmp_f32=w0, add_to_src=True)
            if debug and it == DBG_TILE:
                P.emit("sp", lambda e: e.dma_start(dbg["mu"], stat_mu[:]), waits=[prev], inc="dqo", inc_by=16)
                P.emit("sp", lambda e: e.dma_start(dbg["rsig"], stat_rsig[:]), inc="dqo", inc_by=16)
                prev = P.stage("dve", [lambda e: e.engine_nop()], waits=[P.tok("dqo")])
            for m in range(16):
                pb = pbig0 if m % 2 == 0 else pbig1
                mm = [(lambda e, c=c, m=m, pb=pb: e.matmul(
                    pb[:], WIN[:, c * 2 * D + m * 128: c * 2 * D + (m + 1) * 128],
                    xhat_b[:, c * TT:(c + 1) * TT], start=(c == 0), stop=(c == NCD - 1)))
                    for c in range(NCD)]
                prev = P.stage("pe", mm, waits=[prev])
                if m < 8:
                    prev = P.stage("act", [lambda e, m=m, pb=pb: e.activation(
                        x_inT[:, m * TT:(m + 1) * TT], pb[:], AF.Tanh)], waits=[prev])
                else:
                    prev = P.stage("act", [lambda e, m=m, pb=pb: e.activation(
                        zT[:, (m - 8) * TT:(m - 7) * TT], pb[:], AF.Sigmoid)], waits=[prev])
            mm = [(lambda e, c=c: e.matmul(pstat_x[:], wx0(c), x_inT[:, c * TT:(c + 1) * TT],
                                           start=(c == 0), stop=(c == NCD - 1)))
                  for c in range(NCD)]
            prev = P.stage("pe", mm, waits=[prev])
            prev = P.stage("dve", [lambda e: e.tensor_copy(dtraw[:], pstat_x[:])], waits=[prev])
            prev = P.stage("pe", [lambda e: e.matmul(pdtb[:], ones_row[:], dtraw[:],
                                                     start=True, stop=True)], waits=[prev])
            for c in range(NCD):
                prev = P.stage("act", [
                    lambda e, c=c: e.activation(w0[:], pdtb[:], AF.Sigmoid, scale=wdt(c), bias=bdt(c)),
                    lambda e, c=c: e.activation(w0[:], w0[:], AF.Sigmoid, scale=cpool[:, C_NEG1:C_NEG1 + 1]),
                ], waits=[prev])
                prev = P.stage("dve", [
                    lambda e, c=c: e.tensor_scalar(w1[:], w0[:], -1.0, 1.0, OP.mult, OP.add),
                    lambda e, c=c: e.tensor_tensor(w1[:], w1[:], x_inT[:, c * TT:(c + 1) * TT], OP.mult),
                    lambda e, c=c: e.tensor_tensor_scan(w2[:], w0[:], w1[:], carry[:, c:c + 1], OP.mult, OP.add),
                ], waits=[prev])
                prev = P.stage("dve", [
                    lambda e, c=c: e.tensor_copy(carry[:, c:c + 1], w2[:, TT - 1:TT]),
                    lambda e, c=c: e.tensor_tensor(g8[:, c * TT:(c + 1) * TT], w2[:], zT[:, c * TT:(c + 1) * TT], OP.mult),
                ], waits=[prev])
            if debug and it == DBG_TILE:
                P.emit("sp", lambda e: e.dma_start(dbg["xinT"], x_inT[:]), waits=[prev], inc="dqo", inc_by=16)
                P.emit("sp", lambda e: e.dma_start(dbg["zT"], zT[:]), inc="dqo", inc_by=16)
                P.emit("sp", lambda e: e.dma_start(dbg["g8"], g8[:]), inc="dqo", inc_by=16)
                P.emit("sp", lambda e: e.dma_start(dbg["dtraw"], dtraw[:]), inc="dqo", inc_by=16)
                P.emit("sp", lambda e: e.dma_start(dbg["carry"], carry[:]), inc="dqo", inc_by=16)
                prev = P.stage("dve", [lambda e: e.engine_nop()], waits=[P.tok("dqo")])
            prev = emit_ln(P, TT, prev, g8, 0, TT, NCD, 1.0 / D,
                           sq, pstat_x, pstat_x2, stat_mu, stat_rsig, stat_tmp,
                           pbc_mu, pbc_rsig, cpool, ones_col, ones_row,
                           out_f32=None, out_bf=sq)
            fns = []
            for c in range(NCD):
                fns.append(lambda e, c=c: e.tensor_scalar(
                    sq[:, c * TT:(c + 1) * TT], sq[:, c * TT:(c + 1) * TT],
                    gam(c), bet(c), OP.mult, OP.add))
                fns.append(lambda e, c=c, t0=t0: e.tensor_tensor(
                    RX[:, c * T + t0: c * T + t0 + TT], RX[:, c * T + t0: c * T + t0 + TT],
                    sq[:, c * TT:(c + 1) * TT], OP.add))
            prev = P.stage("dve", fns, waits=[prev])

        for c in range(NCD):
            P.emit("sp", lambda e, c=c: e.dma_start(xout[c * 128:(c + 1) * 128, :],
                                                    RX[:, c * T:(c + 1) * T]),
                   waits=[prev] if c == 0 else (), inc="dqo", inc_by=16)
        P.emit("sp", None, waits=[P.tok("dqo")])
        if ctx is not None:
            ctx["prev"] = prev
            ctx["out_tok"] = P.tok("dqo")
        else:
            P.finalize()
    return nc


def run_l1_attn(x, W_in, W_x, W_dt, b_dt, gamma, beta, debug=False):
    nc = build_l1_attn(debug=debug)
    w_inT = np.ascontiguousarray(W_in.T).astype(np.float32)
    vecs = np.concatenate([
        W_x[0].reshape(NCD, 128).T,
        W_dt[:, 0].reshape(NCD, 128).T,
        b_dt.reshape(NCD, 128).T,
        gamma.reshape(NCD, 128).T,
        beta.reshape(NCD, 128).T,
    ], axis=1).astype(np.float32)
    in_maps = []
    for b in range(N_CORES):
        in_maps.append(dict(xin=np.ascontiguousarray(x[b].T).astype(np.float32),
                            w_inT=w_inT, vecs=vecs))
    res = run_bass_kernel_spmd(nc, in_maps, core_ids=list(range(N_CORES)))
    out = np.stack([np.ascontiguousarray(r["xout"].T) for r in res.results])
    if debug:
        return out, res.results[0]
    return out


def build_l0_ffn(debug=False, ctx=None):
    if ctx is None:
        nc = bass.Bass("TRN2", target_bir_lowering=False)
        _pref = ""
    else:
        nc = ctx["nc"]
        _pref = ctx["pref"]
    H = 4096
    xin = ctx["xin"] if ctx is not None else nc.dram_tensor("xin", [D, T], F32, kind="ExternalInput").ap()
    w1T = nc.dram_tensor(_pref + "w1T", [D, H], F32, kind="ExternalInput").ap()
    w2T = nc.dram_tensor(_pref + "w2T", [H, D], F32, kind="ExternalInput").ap()
    b1c = nc.dram_tensor(_pref + "b1c", [128, 32], F32, kind="ExternalInput").ap()
    b2c = nc.dram_tensor(_pref + "b2c", [128, 8], F32, kind="ExternalInput").ap()
    sclr = nc.dram_tensor(_pref + "sclr", [128, 1], F32, kind="ExternalInput").ap()
    xout = ctx["xout"] if ctx is not None else nc.dram_tensor("xout", [D, T], F32, kind="ExternalOutput").ap()
    e2scr = nc.dram_tensor(_pref + "e2scr", [16, D], BF16, kind="Internal").ap()
    if debug:
        dbg = {k: nc.dram_tensor(_pref + "dbg_" + k, shp, BF16, kind="ExternalOutput").ap()
               for k, shp in [("xhat", [128, 8 * 256]), ("urow", [1, 4 * 256]),
                              ("hact", [128, 32 * 256]), ("t4", [4, 4 * 256]),
                              ("e2", [4, 4 * 256]), ("evn", [1, 4 * 1024]),
                              ("o2", [128, 32 * 16]), ("lt", [128, 8 * 16])]}

    TT = 256
    NTILE = T // TT
    NH = 32

    es = ExitStack()
    S = lambda *a, **k: es.enter_context(nc.sbuf_tensor(_pref + a[0], *a[1:], **k))
    PS = lambda *a, **k: es.enter_context(nc.psum_tensor(_pref + a[0], *a[1:], **k))
    with es:
        RX = S("RX", [128, NCD * T], F32)
        xhat_b = S("xhat_b", [128, NCD * TT], BF16)
        h_act = S("h_act", [128, NH * TT], BF16)
        sq = S("sq", [128, NCD * TT], BF16)
        w0 = S("w0", [128, TT], F32)
        B1 = S("B1", [128, 32], F32)
        B2 = S("B2", [128, 8], F32)
        SCL = S("SCL", [128, 1], F32)
        LT = S("LT", [128, NCD * 16], BF16)
        LTOK = S("LTOK", [16, D], BF16)
        O2 = S("O2", [128, NH * 16], BF16)
        E2 = S("E2", [4, 4 * 256], BF16)
        EVN = S("EVN", [1, 4 * D], BF16)
        IDN = S("IDN", [128, 128], BF16)
        UROW = S("UROW", [1, 4 * TT], BF16)
        T4 = S("T4", [4, 4 * TT], BF16)
        wb1a = S("wb1a", [128, NCD * 512], BF16)
        wb1b = S("wb1b", [128, NCD * 512], BF16)
        wb2a = wb1a.rearrange("p (c n) -> p (c n)", c=1) if False else wb1a
        wb2b = wb1b
        stat_mu = S("stat_mu", [1, TT], F32)
        stat_rsig = S("stat_rsig", [1, TT], F32)
        stat_tmp = S("stat_tmp", [1, TT], F32)

        PP = [PS("P%d" % i, [128, 512], F32) for i in range(7)]
        PTR = PS("PTR", [16, 1024], BF16)
        pb = lambda i: PP[i]

        P = ctx["P"] if ctx is not None else Prog(nc, es)
        _pw = [ctx["prev"]] if ctx is not None and ctx.get("prev") else []
        cpool, ones_col, ones_row, ones_bf, tok0 = emit_common_setup(P, es, nc, extra_waits=_pw, pref=_pref)

        _chunked_dma(P, "sp", "dqx", RX, xin, NCD, T)
        P.emit("pool", lambda e: e.dma_start(B1[:], b1c), waits=_pw, inc="dqw", inc_by=16)
        P.emit("pool", lambda e: e.dma_start(B2[:], b2c), inc="dqw", inc_by=16)
        P.emit("pool", lambda e: e.dma_start(SCL[:], sclr), inc="dqw", inc_by=16)
        loads_tok = [("dqx", P.cnt["dqx"]), ("dqw", P.cnt["dqw"]), P.tok("dve")]
        idn = nc.dram_tensor(_pref + "idn", [128, 128], F32, kind="ExternalInput").ap()
        P.emit("pool", lambda e: e.dma_start(IDN[:], idn), inc="dqw", inc_by=16)
        loads_tok = [("dqx", P.cnt["dqx"]), ("dqw", P.cnt["dqw"]), P.tok("dve")]

        prev = P.stage("dve", [lambda e: e.engine_nop()], waits=loads_tok)

        for it in range(NTILE):
            t0 = it * TT
            prev = emit_ln(P, TT, prev, RX, t0, T, NCD, 1.0 / D,
                           sq, pb(4)[0:1], pb(5)[0:1], stat_mu, stat_rsig, stat_tmp,
                           pb(2), pb(3), cpool, ones_col, ones_row, out_bf=xhat_b,
                           ones_sq=ones_bf)
            if it == 0:
                fns = [(lambda e, c=c: e.tensor_copy(LT[:, c * 16:(c + 1) * 16],
                                                     xhat_b[:, c * TT + 1:c * TT + 17]))
                       for c in range(NCD)]
                prev = P.stage("dve", fns, waits=[prev])
                mm = [(lambda e, c=c: e.transpose(PTR[:, c * 128:(c + 1) * 128],
                                                  LT[:, c * 16:(c + 1) * 16], IDN[:]))
                      for c in range(NCD)]
                prev = P.stage("pe", mm, waits=[prev])
                prev = P.stage("dve", [lambda e: e.tensor_copy(LTOK[:], PTR[:])], waits=[prev])
                fns = [lambda e: e.memset(O2[:], 0.0)]
                for c2 in range(4):
                    for r in range(4):
                        for j in range(NCD):
                            k = c2 * 8 + j
                            fns.append(lambda e, k=k, r=r, c2=c2, j=j: e.tensor_copy(
                                O2[:, k * 16 + r * 4 + c2: k * 16 + r * 4 + c2 + 1],
                                LT[:, j * 16 + 9 + 2 * r: j * 16 + 10 + 2 * r]))
                prev = P.stage("dve", fns, waits=[prev])
                P.emit("pool", lambda e: e.dma_start(e2scr, LTOK[:]), waits=[prev], inc="dqe", inc_by=16)
                for r in range(4):
                    P.emit("pool", lambda e, r=r: e.dma_start(
                        E2[:, r * 256:(r + 1) * 256],
                        e2scr[8 + 2 * r:9 + 2 * r, :].rearrange("a (u c) -> (a c) u", c=4),
                        allow_slow_non_contiguous=True), waits=[("dqe", 16)], inc="dqe", inc_by=16)
                    P.emit("pool", lambda e, r=r: e.dma_start(
                        EVN[:, r * D:(r + 1) * D], e2scr[2 * r:2 * r + 1, :]),
                        inc="dqe", inc_by=16)
                prev = P.stage("dve", [lambda e: e.engine_nop()], waits=[("dqe", P.cnt["dqe"])])

            for r in range(4):
                mm = [(lambda e, c=c, r=r: e.matmul(pb(6)[0:1, :TT],
                                                    LT[:, c * 16 + 1 + 2 * r: c * 16 + 2 + 2 * r],
                                                    xhat_b[:, c * TT:(c + 1) * TT],
                                                    start=(c == 0), stop=(c == NCD - 1)))
                      for c in range(NCD)]
                prev = P.stage("pe", mm, waits=[prev])
                prev = P.stage("dve", [lambda e, r=r: e.tensor_scalar(
                    UROW[:, r * TT:(r + 1) * TT], pb(6)[0:1, :TT], SCL[0:1, 0:1], None, OP.mult)],
                    waits=[prev])

            for mg in range(8):
                wb = wb1a if mg % 2 == 0 else wb1b
                P.emit("pool", lambda e, mg=mg, wb=wb: e.dma_start(
                    wb[:].rearrange("p (c n) -> p c n", c=NCD),
                    w1T.rearrange("(c p) n -> p c n", p=128)[:, :, mg * 512:(mg + 1) * 512]),
                    waits=[prev], inc="dqw1", inc_by=16)
                wtok = ("dqw1", P.cnt["dqw1"])
                for mi in range(4):
                    m = mg * 4 + mi
                    r = m // 8
                    pm = pb(mi % 2)
                    mm = [(lambda e, c=c, mi=mi, wb=wb, pm=pm: e.matmul(
                        pm[:, :TT], wb[:, c * 512 + mi * 128: c * 512 + (mi + 1) * 128],
                        xhat_b[:, c * TT:(c + 1) * TT], start=(c == 0), stop=False))
                          for c in range(NCD)]
                    mm.append(lambda e, r=r, m=m, pm=pm: e.matmul(
                        pm[:, :TT], EVN[:, r * D + (m % 8) * 128: r * D + (m % 8 + 1) * 128],
                        UROW[:, r * TT:(r + 1) * TT], start=False, stop=True))
                    prev = P.stage("pe", mm, waits=[prev, wtok] if mi == 0 else [prev])
                    prev = P.stage("act", [lambda e, m=m, pm=pm: e.activation(
                        h_act[:, m * TT:(m + 1) * TT], pm[:, :TT], AF.Gelu,
                        bias=B1[:, m:m + 1])], waits=[prev])

            for r in range(4):
                mm = [(lambda e, k=k, r=r: e.matmul(pb(6)[0:4, :TT],
                                                    O2[:, k * 16 + r * 4: k * 16 + r * 4 + 4],
                                                    h_act[:, k * TT:(k + 1) * TT],
                                                    start=(k == 0), stop=(k == NH - 1)))
                      for k in range(NH)]
                prev = P.stage("pe", mm, waits=[prev])
                prev = P.stage("dve", [lambda e, r=r: e.tensor_scalar(
                    T4[:, r * TT:(r + 1) * TT], pb(6)[0:4, :TT], SCL[0:4, 0:1], None, OP.mult)],
                    waits=[prev])

            if debug and it == 0:
                for k, tile in [("xhat", xhat_b), ("urow", UROW), ("hact", h_act),
                                ("t4", T4), ("e2", E2), ("evn", EVN), ("o2", O2), ("lt", LT)]:
                    P.emit("sp", lambda e, k=k, tile=tile: e.dma_start(dbg[k], tile[:]),
                           waits=[prev], inc="dqo", inc_by=16)
                prev = P.stage("dve", [lambda e: e.engine_nop()], waits=[P.tok("dqo")])
            for mo in range(8):
                wb = wb2a if mo % 2 == 0 else wb2b
                P.emit("pool", lambda e, mo=mo, wb=wb: e.dma_start(
                    wb[:].rearrange("p (c n) -> p c n", c=NH),
                    w2T.rearrange("(c p) n -> p c n", p=128)[:, :, mo * 128:(mo + 1) * 128]),
                    waits=[prev], inc="dqw2", inc_by=16)
                wtok = ("dqw2", P.cnt["dqw2"])
                pm = pb(mo % 2)
                r = mo // 2
                mm = [(lambda e, k=k, wb=wb, pm=pm: e.matmul(
                    pm[:, :TT], wb[:, k * 128:(k + 1) * 128],
                    h_act[:, k * TT:(k + 1) * TT], start=(k == 0), stop=False))
                      for k in range(NH)]
                mm.append(lambda e, r=r, mo=mo, pm=pm: e.matmul(
                    pm[:, :TT], E2[:, r * 256 + (mo % 2) * 128: r * 256 + (mo % 2 + 1) * 128],
                    T4[:, r * TT:(r + 1) * TT], start=False, stop=True))
                prev = P.stage("pe", mm, waits=[prev, wtok])
                prev = P.stage("dve", [
                    lambda e, mo=mo, pm=pm: e.tensor_scalar(w0[:], pm[:, :TT], B2[:, mo:mo + 1], None, OP.add),
                    lambda e, mo=mo, t0=t0: e.tensor_tensor(
                        RX[:, mo * T + t0: mo * T + t0 + TT], RX[:, mo * T + t0: mo * T + t0 + TT],
                        w0[:], OP.add),
                ], waits=[prev])

        for c in range(NCD):
            P.emit("sp", lambda e, c=c: e.dma_start(xout[c * 128:(c + 1) * 128, :],
                                                    RX[:, c * T:(c + 1) * T]),
                   waits=[prev] if c == 0 else (), inc="dqo", inc_by=16)
        P.emit("sp", None, waits=[P.tok("dqo")])
        if ctx is not None:
            ctx["prev"] = prev
            ctx["out_tok"] = P.tok("dqo")
        else:
            P.finalize()
    return nc


def run_l0_ffn(x, W1, b1, W2, b2, scale, debug=False):
    nc = build_l0_ffn(debug=debug)
    w1T = np.ascontiguousarray(W1.T).astype(np.float32)
    w2T = np.ascontiguousarray(W2.T).astype(np.float32)
    b1c = np.ascontiguousarray(b1.reshape(32, 128).T).astype(np.float32)
    b2c = np.ascontiguousarray(b2.reshape(8, 128).T).astype(np.float32)
    sclr = np.full((128, 1), np.float32(scale), np.float32)
    idn = np.eye(128, dtype=np.float32)
    in_maps = []
    for b in range(N_CORES):
        in_maps.append(dict(xin=np.ascontiguousarray(x[b].T).astype(np.float32),
                            w1T=w1T, w2T=w2T, b1c=b1c, b2c=b2c, sclr=sclr, idn=idn))
    res = run_bass_kernel_spmd(nc, in_maps, core_ids=list(range(N_CORES)))
    out = np.stack([np.ascontiguousarray(r["xout"].T) for r in res.results])
    if debug:
        return out, res.results[0]
    return out



def build_l0_attn(debug=False, ctx=None):
    if ctx is None:
        nc = bass.Bass("TRN2", target_bir_lowering=False)
        _pref = ""
    else:
        nc = ctx["nc"]
        _pref = ctx["pref"]
    DI = 2048
    NDI = 16
    Q = 64
    xin = ctx["xin"] if ctx is not None else nc.dram_tensor("xin", [D, T], F32, kind="ExternalInput").ap()
    w_in = nc.dram_tensor(_pref + "w_in", [D, 2 * DI], F32, kind="ExternalInput").ap()
    w_abc = nc.dram_tensor(_pref + "w_abc", [DI, 48], F32, kind="ExternalInput").ap()
    w_out = nc.dram_tensor(_pref + "w_out", [DI, D], F32, kind="ExternalInput").ap()
    nwrep = nc.dram_tensor(_pref + "nwrep", [128, DI], F32, kind="ExternalInput").ap()
    trim = nc.dram_tensor(_pref + "trim", [Q, Q], F32, kind="ExternalInput").ap()
    idn = nc.dram_tensor(_pref + "idn", [128, 128], F32, kind="ExternalInput").ap()
    xout = ctx["xout"] if ctx is not None else nc.dram_tensor("xout", [D, T], F32, kind="ExternalOutput").ap()
    if debug:
        dbg = {k: nc.dram_tensor(_pref + "dbg_" + k, shp, F32, kind="ExternalOutput").ap()
               for k, shp in [("abc", [64, 3 * 32]), ("cum", [64, 16]), ("y", [64, 2048]),
                              ("yn", [64, 2048]), ("sst", [16, 2048]), ("xtok", [64, 2 * 2048])]}

    TT = 128
    NTILE = T // TT
    NQ = TT // Q

    es = ExitStack()
    S = lambda *a, **k: es.enter_context(nc.sbuf_tensor(_pref + a[0], *a[1:], **k))
    PS = lambda *a, **k: es.enter_context(nc.psum_tensor(_pref + a[0], *a[1:], **k))
    with es:
        RX = S("RX", [128, NCD * T], F32)
        xhat_b = S("xhat_b", [128, NCD * TT], BF16)
        x_inT = S("x_inT", [128, NDI * TT], BF16)
        XT64 = S("XT64", [64, 2 * DI], BF16)
        SCRB = S("SCRB", [128, 2048], BF16)
        WABC = S("WABC", [128, NDI * 48], BF16)
        NW = S("NW", [128, DI], BF16)
        TRI = S("TRI", [Q, Q], F32)
        IDN = S("IDN", [128, 128], BF16)
        AT = S("AT", [64, 2 * 16], F32)
        BT = S("BT", [64, 2 * 16], F32)
        CTl = S("CTl", [64, 2 * 16], F32)
        E1 = S("E1", [Q, 16], F32)
        E2t = S("E2t", [Q, 16], F32)
        BTq = S("BTq", [Q, 16], BF16)
        CTq = S("CTq", [Q, 16], BF16)
        BTt = S("BTt", [16, Q], BF16)
        CTt = S("CTt", [16, Q], BF16)
        MM = S("MM", [Q, Q], BF16)
        SST = S("SST", [16, DI], BF16)
        EAE = S("EAE", [16, 1], F32)
        GBF = S("GBF", [16, 512], BF16)
        YN = S("YN", [Q, DI], BF16)
        YF = S("YF", [128, NDI * TT], BF16)
        RMS = S("RMS", [Q, 1], F32)
        wba = S("wba", [128, NCD * 512], BF16)
        wbb = S("wbb", [128, NCD * 512], BF16)
        w0 = S("w0", [128, TT], F32)
        w1 = S("w1", [128, TT], F32)
        stat_mu = S("stat_mu", [1, TT], F32)
        stat_rsig = S("stat_rsig", [1, TT], F32)
        stat_tmp = S("stat_tmp", [1, TT], F32)

        PSY = PS("PSY", [Q, DI], F32)
        pb0 = PS("pb0", [128, 512], F32)
        pb1 = PS("pb1", [128, 512], F32)
        pb2 = PS("pb2", [128, 512], F32)
        PTB = PS("PTB", [128, 1024], BF16)

        P = ctx["P"] if ctx is not None else Prog(nc, es)
        _pw = [ctx["prev"]] if ctx is not None and ctx.get("prev") else []
        cpool, ones_col, ones_row, ones_bf, tok0 = emit_common_setup(P, es, nc, extra_waits=_pw, pref=_pref)
        P.stage("dve", [lambda e: e.memset(SST[:], 0.0)])

        _chunked_dma(P, "sp", "dqx", RX, xin, NCD, T)
        for c in range(NDI):
            P.emit("pool", lambda e, c=c: e.dma_start(WABC[:, c * 48:(c + 1) * 48],
                                                      w_abc[c * 128:(c + 1) * 128, :]),
                   waits=_pw if c == 0 else (), inc="dqw", inc_by=16)
        P.emit("pool", lambda e: e.dma_start(NW[:], nwrep), inc="dqw", inc_by=16)
        P.emit("pool", lambda e: e.dma_start(TRI[:], trim), inc="dqw", inc_by=16)
        P.emit("pool", lambda e: e.dma_start(IDN[:], idn), inc="dqw", inc_by=16)
        wa = lambda c: WABC[:, c * 48 + 0: c * 48 + 16]
        wb_ = lambda c: WABC[:, c * 48 + 16: c * 48 + 32]
        wc_ = lambda c: WABC[:, c * 48 + 32: c * 48 + 48]
        loads_tok = [("dqx", P.cnt["dqx"]), ("dqw", P.cnt["dqw"]), P.tok("dve")]

        prev = P.stage("dve", [lambda e: e.engine_nop()], waits=loads_tok)

        for it in range(NTILE):
            t0 = it * TT
            prev = emit_ln(P, TT, prev, RX, t0, T, NCD, 1.0 / D,
                           SCRB, pb2[0:1, 0:TT], pb2[0:1, TT:2 * TT], stat_mu, stat_rsig,
                           stat_tmp, pb0, pb1, cpool, ones_col, ones_row,
                           out_bf=xhat_b, ones_sq=ones_bf)
            for g in range(8):
                wbuf = wba if g % 2 == 0 else wbb
                P.emit("pool", lambda e, g=g, wbuf=wbuf: e.dma_start(
                    wbuf[:].rearrange("p (c n) -> p c n", c=NCD),
                    w_in.rearrange("(c p) n -> p c n", p=128)[:, :, g * 512:(g + 1) * 512]),
                    waits=[prev], inc="dqwi", inc_by=16)
                wtok = ("dqwi", P.cnt["dqwi"])
                for pi in range(2):
                    mp = g * 2 + pi
                    mm = [(lambda e, c=c, pi=pi, wbuf=wbuf: e.matmul(
                        pb0[:, :TT], wbuf[:, c * 512 + pi * 256: c * 512 + pi * 256 + 128],
                        xhat_b[:, c * TT:(c + 1) * TT], start=(c == 0), stop=(c == NCD - 1)))
                          for c in range(NCD)]
                    mm += [(lambda e, c=c, pi=pi, wbuf=wbuf: e.matmul(
                        pb1[:, :TT], wbuf[:, c * 512 + pi * 256 + 128: c * 512 + pi * 256 + 256],
                        xhat_b[:, c * TT:(c + 1) * TT], start=(c == 0), stop=(c == NCD - 1)))
                           for c in range(NCD)]
                    prev = P.stage("pe", mm, waits=[prev, wtok] if pi == 0 else [prev])
                    prev = P.stage("act", [
                        lambda e: e.activation(w0[:, :TT], pb0[:, :TT], AF.Silu),
                        lambda e: e.activation(w1[:, :TT], pb1[:, :TT], AF.Sigmoid),
                    ], waits=[prev])
                    prev = P.stage("dve", [lambda e, mp=mp: e.tensor_tensor(
                        x_inT[:, mp * TT:(mp + 1) * TT], w0[:, :TT], w1[:, :TT], OP.mult)],
                        waits=[prev])
                    mm = [(lambda e, q=q, mp=mp: e.transpose(
                        PTB[0:64, q * 128:(q + 1) * 128],
                        x_inT[:, mp * TT + q * 64: mp * TT + (q + 1) * 64], IDN[:]))
                          for q in range(NQ)]
                    prev = P.stage("pe", mm, waits=[prev])
                    prev = P.stage("dve", [(lambda e, q=q, mp=mp: e.tensor_copy(
                        XT64[:, q * DI + mp * 128: q * DI + (mp + 1) * 128],
                        PTB[0:64, q * 128:(q + 1) * 128])) for q in range(NQ)], waits=[prev])
            for q in range(NQ):
                mm = []
                for proj, pbx in ((wa, pb0), (wb_, pb1), (wc_, pb2)):
                    for c in range(NDI):
                        mm.append(lambda e, c=c, q=q, proj=proj, pbx=pbx: e.matmul(
                            pbx[0:64, 0:16], x_inT[:, c * TT + q * 64: c * TT + (q + 1) * 64],
                            proj(c), start=(c == 0), stop=(c == NDI - 1)))
                prev = P.stage("pe", mm, waits=[prev])
                prev = P.stage("dve", [
                    lambda e, q=q: e.tensor_scalar(AT[:, q * 16:(q + 1) * 16], pb0[0:64, 0:16],
                                                   0.0, -5.0, OP.min, OP.max),
                    lambda e, q=q: e.tensor_copy(BT[:, q * 16:(q + 1) * 16], pb1[0:64, 0:16]),
                    lambda e, q=q: e.tensor_copy(CTl[:, q * 16:(q + 1) * 16], pb2[0:64, 0:16]),
                ], waits=[prev])
                prev = P.stage("act", [lambda e, q=q: e.activation(
                    AT[:, q * 16:(q + 1) * 16], AT[:, q * 16:(q + 1) * 16], AF.Exp)],
                    waits=[prev])
                prev = P.stage("dve", [lambda e, q=q: e.tensor_scalar(
                    AT[:, q * 16:(q + 1) * 16], AT[:, q * 16:(q + 1) * 16], -1.0, None, OP.mult)],
                    waits=[prev])
            if debug and it == 0:
                P.emit("sp", lambda e: e.dma_start(dbg["abc"][:, 0:32], AT[:]), waits=[prev], inc="dqo", inc_by=16)
                P.emit("sp", lambda e: e.dma_start(dbg["abc"][:, 32:64], BT[:]), inc="dqo", inc_by=16)
                P.emit("sp", lambda e: e.dma_start(dbg["abc"][:, 64:96], CTl[:]), inc="dqo", inc_by=16)
                P.emit("pool", lambda e: e.dma_start(dbg["xtok"], XT64[:]), waits=[prev], inc="dqo", inc_by=16)
                prev = P.stage("dve", [lambda e: e.engine_nop()], waits=[P.tok("dqo")])
            for q in range(NQ):
                atq = AT[:, q * 16:(q + 1) * 16]
                btq = BT[:, q * 16:(q + 1) * 16]
                ctq = CTl[:, q * 16:(q + 1) * 16]
                xq = lambda ns, q=q: XT64[:, q * DI + ns * 512: q * DI + (ns + 1) * 512]
                prev = P.stage("pe", [
                    lambda e, atq=atq: e.matmul(pb0[0:64, 0:16], TRI[:], atq, start=True, stop=True),
                    lambda e, atq=atq: e.matmul(pb1[0:16, 0:1], atq, ones_col[0:64, :], start=True, stop=True),
                ], waits=[prev])
                prev = P.stage("act", [
                    lambda e: e.activation(E1[:], pb0[0:64, 0:16], AF.Exp),
                    lambda e: e.activation(E2t[:], pb0[0:64, 0:16], AF.Exp, scale=cpool[0:64, C_NEG1:C_NEG1 + 1]),
                    lambda e: e.activation(EAE[:], pb1[0:16, 0:1], AF.Exp),
                ], waits=[prev])
                prev = P.stage("dve", [
                    lambda e, btq=btq: e.tensor_tensor(BTq[:], btq, E2t[:], OP.mult),
                    lambda e, ctq=ctq: e.tensor_tensor(CTq[:], ctq, E1[:], OP.mult),
                ], waits=[prev])
                prev = P.stage("pe", [
                    lambda e: e.transpose(PTB[0:16, 0:64], BTq[:], IDN[0:64, 0:64]),
                    lambda e: e.transpose(PTB[0:16, 64:128], CTq[:], IDN[0:64, 0:64]),
                ], waits=[prev])
                prev = P.stage("dve", [
                    lambda e: e.tensor_copy(BTt[:], PTB[0:16, 0:64]),
                    lambda e: e.tensor_copy(CTt[:], PTB[0:16, 64:128]),
                ], waits=[prev])
                prev = P.stage("pe", [lambda e: e.matmul(pb0[0:64, 0:64], BTt[:], CTt[:],
                                                         start=True, stop=True)], waits=[prev])
                prev = P.stage("dve", [lambda e: e.tensor_tensor(MM[:], pb0[0:64, 0:64], TRI[:], OP.mult)],
                               waits=[prev])
                mm = []
                for ns in range(4):
                    mm.append(lambda e, ns=ns, xq=xq: e.matmul(
                        PSY[:, ns * 512:(ns + 1) * 512], MM[:], xq(ns), start=True, stop=False))
                    mm.append(lambda e, ns=ns: e.matmul(
                        PSY[:, ns * 512:(ns + 1) * 512], CTt[:], SST[:, ns * 512:(ns + 1) * 512],
                        start=False, stop=True))
                prev = P.stage("pe", mm, waits=[prev])
                for ns in range(4):
                    prev = P.stage("pe", [lambda e, ns=ns, xq=xq: e.matmul(
                        pb2[0:16, 0:512], BTq[:], xq(ns), start=True, stop=True)], waits=[prev])
                    prev = P.stage("dve", [
                        lambda e: e.tensor_scalar(GBF[:], pb2[0:16, 0:512], EAE[0:16, 0:1], None, OP.mult),
                        lambda e, ns=ns: e.tensor_scalar(SST[:, ns * 512:(ns + 1) * 512],
                                                         SST[:, ns * 512:(ns + 1) * 512],
                                                         EAE[0:16, 0:1], None, OP.mult),
                        lambda e, ns=ns: e.tensor_tensor(SST[:, ns * 512:(ns + 1) * 512],
                                                         SST[:, ns * 512:(ns + 1) * 512],
                                                         GBF[:], OP.add),
                    ], waits=[prev])
                if debug and it == 0 and q == 0:
                    P.emit("sp", lambda e: e.dma_start(dbg["cum"], E1[:]), waits=[prev], inc="dqo", inc_by=16)
                    P.emit("pool", lambda e: e.dma_start(dbg["sst"], SST[:]), waits=[prev], inc="dqo", inc_by=16)
                    prev = P.stage("dve", [lambda e: e.engine_nop()], waits=[P.tok("dqo")])
                prev = P.stage("act", [lambda e: e.activation(
                    SCRB[0:64, 0:2048], PSY[:], AF.Square, accum_out=RMS[:])], waits=[prev])
                prev = P.stage("dve", [lambda e: e.tensor_scalar(RMS[:], RMS[:], 1.0 / DI, None, OP.mult)],
                               waits=[prev])
                prev = P.stage("act", [
                    lambda e: e.activation(RMS[:], RMS[:], AF.Ln, bias=cpool[0:64, C_EPS_F32:C_EPS_F32 + 1]),
                    lambda e: e.activation(RMS[:], RMS[:], AF.Exp, scale=cpool[0:64, C_NEGHALF:C_NEGHALF + 1]),
                ], waits=[prev])
                prev = P.stage("dve", [
                    lambda e: e.tensor_scalar(YN[:], PSY[:], RMS[0:64, 0:1], None, OP.mult),
                    lambda e: e.tensor_tensor(YN[:], YN[:], NW[0:64, :], OP.mult),
                ], waits=[prev])
                if debug and it == 0 and q == 0:
                    P.emit("pool", lambda e: e.dma_start(dbg["yn"], YN[:]), waits=[prev], inc="dqo", inc_by=16)
                    prev = P.stage("dve", [lambda e: e.engine_nop()], waits=[P.tok("dqo")])
                for half in range(2):
                    mm = [(lambda e, k=k, half=half: e.transpose(
                        PTB[:, k * 64:(k + 1) * 64],
                        YN[:, (half * 8 + k) * 128:(half * 8 + k + 1) * 128], IDN[0:64, 0:64]))
                          for k in range(8)]
                    prev = P.stage("pe", mm, waits=[prev])
                    prev = P.stage("dve", [(lambda e, k=k, half=half, q=q: e.tensor_copy(
                        YF[:, (half * 8 + k) * TT + q * 64: (half * 8 + k) * TT + (q + 1) * 64],
                        PTB[:, k * 64:(k + 1) * 64])) for k in range(8)], waits=[prev])
            for og in range(4):
                wbuf = wba if og % 2 == 0 else wbb
                P.emit("pool", lambda e, og=og, wbuf=wbuf: e.dma_start(
                    wbuf[:, 0:NDI * 256].rearrange("p (c n) -> p c n", c=NDI),
                    w_out.rearrange("(c p) n -> p c n", p=128)[:, :, og * 256:(og + 1) * 256]),
                    waits=[prev], inc="dqwo", inc_by=16)
                wtok = ("dqwo", P.cnt["dqwo"])
                for oi in range(2):
                    mo = og * 2 + oi
                    pm = pb0 if oi == 0 else pb1
                    mm = [(lambda e, k=k, oi=oi, wbuf=wbuf, pm=pm: e.matmul(
                        pm[:, :TT], wbuf[:, k * 256 + oi * 128: k * 256 + (oi + 1) * 128],
                        YF[:, k * TT:(k + 1) * TT], start=(k == 0), stop=(k == NDI - 1)))
                          for k in range(NDI)]
                    prev = P.stage("pe", mm, waits=[prev, wtok] if oi == 0 else [prev])
                    prev = P.stage("dve", [lambda e, mo=mo, t0=t0, pm=pm: e.tensor_tensor(
                        RX[:, mo * T + t0: mo * T + t0 + TT], RX[:, mo * T + t0: mo * T + t0 + TT],
                        pm[:, :TT], OP.add)], waits=[prev])

        for c in range(NCD):
            P.emit("sp", lambda e, c=c: e.dma_start(xout[c * 128:(c + 1) * 128, :],
                                                    RX[:, c * T:(c + 1) * T]),
                   waits=[prev] if c == 0 else (), inc="dqo", inc_by=16)
        P.emit("sp", None, waits=[P.tok("dqo")])
        if ctx is not None:
            ctx["prev"] = prev
            ctx["out_tok"] = P.tok("dqo")
        else:
            P.finalize()
    return nc


def run_l0_attn(x, W_in, W_A, W_B, W_C, W_out, norm_w, debug=False):
    nc = build_l0_attn(debug=debug)
    DI = 2048
    wi = W_in.reshape(1024, 32, 128)
    order = []
    for mp in range(16):
        order += [mp, mp + 16]
    w_in = np.ascontiguousarray(wi[:, order, :].reshape(1024, 4096)).astype(np.float32)
    w_abc = np.ascontiguousarray(np.concatenate([W_A, W_B, W_C], axis=1)).astype(np.float32)
    w_outc = np.ascontiguousarray(W_out).astype(np.float32)
    nwrep = np.ascontiguousarray(np.broadcast_to(norm_w[None, :], (128, DI))).astype(np.float32)
    tau = np.arange(64)[:, None]; t = np.arange(64)[None, :]
    trim = (tau <= t).astype(np.float32)
    idn = np.eye(128, dtype=np.float32)
    in_maps = []
    for b in range(N_CORES):
        in_maps.append(dict(xin=np.ascontiguousarray(x[b].T).astype(np.float32),
                            w_in=w_in, w_abc=w_abc, w_out=w_outc, nwrep=nwrep,
                            trim=trim, idn=idn))
    res = run_bass_kernel_spmd(nc, in_maps, core_ids=list(range(N_CORES)))
    out = np.stack([np.ascontiguousarray(r["xout"].T) for r in res.results])
    if debug:
        return out, res.results[0]
    return out



def build_l1_ffn(debug=False, ctx=None, DEPTH_RUN=6, NW_RUN=16, FT_RUN=8):
    if ctx is None:
        nc = bass.Bass("TRN2", target_bir_lowering=False)
        _pref = ""
    else:
        nc = ctx["nc"]
        _pref = ctx["pref"]
    DEPTH = 6
    W = 256
    NW = T // W
    HC = 22
    FST = 512
    xin = ctx["xin"] if ctx is not None else nc.dram_tensor("xin", [D, T], F32, kind="ExternalInput").ap()
    wqkv5 = nc.dram_tensor(_pref + "wqkv5", [DEPTH, D, 5 * D], F32, kind="ExternalInput").ap()
    wo = nc.dram_tensor(_pref + "wo", [DEPTH, D, D], F32, kind="ExternalInput").ap()
    w1p = nc.dram_tensor(_pref + "w1p", [DEPTH, D, 2 * 2816], F32, kind="ExternalInput").ap()
    w2p = nc.dram_tensor(_pref + "w2p", [DEPTH, 2816, D], F32, kind="ExternalInput").ap()
    biasc = nc.dram_tensor(_pref + "biasc", [128, 52 * DEPTH], F32, kind="ExternalInput").ap()
    lngb = nc.dram_tensor(_pref + "lngb", [128, 16 * DEPTH], F32, kind="ExternalInput").ap()
    qscols = nc.dram_tensor(_pref + "qscols", [128, 4 * DEPTH], F32, kind="ExternalInput").ap()
    tabs = nc.dram_tensor(_pref + "tabs", [4 * 128, T], F32, kind="ExternalInput").ap()
    masks = nc.dram_tensor(_pref + "masks", [4 * 128, 2 * W], F32, kind="ExternalInput").ap()
    onbm = nc.dram_tensor(_pref + "onbm", [128, 2], F32, kind="ExternalInput").ap()
    onbtm = nc.dram_tensor(_pref + "onbtm", [2, 128], F32, kind="ExternalInput").ap()
    idn = nc.dram_tensor(_pref + "idn", [128, 128], F32, kind="ExternalInput").ap()
    xout = ctx["xout"] if ctx is not None else nc.dram_tensor("xout", [D, T], F32, kind="ExternalOutput").ap()
    if debug:
        dbg = {k: nc.dram_tensor(_pref + "dbg_" + k, shp, F32, kind="ExternalOutput").ap()
               for k, shp in [("qr", [128, 8 * W]), ("kr", [128, 8 * W]),
                              ("att", [128, 2 * W]), ("outf", [128, 8 * W]),
                              ("xw1", [128, W]), ("nsa", [128, 1]), ("nsb", [128, 1]), ("nmx", [128, 1]), ("sim", [128, 2 * W]), ("attp", [128, 2 * W]), ("xn", [128, 8 * FST]),
                              ("act", [128, HC * FST])]}

    es = ExitStack()
    S = lambda *a, **k: es.enter_context(nc.sbuf_tensor(_pref + a[0], *a[1:], **k))
    PS = lambda *a, **k: es.enter_context(nc.psum_tensor(_pref + a[0], *a[1:], **k))
    with es:
        XW = S("XW", [128, NCD * T], F32)
        ARN = S("ARN", [128, 23104], BF16)
        XS = lambda: ARN[:, 0:8 * 264]
        QT = lambda: ARN[:, 2112:2112 + 8 * 256]
        QST = lambda: ARN[:, 4160:4160 + 8 * 256]
        KT = lambda: ARN[:, 6208:6208 + 8 * 256]
        KST = lambda: ARN[:, 8256:8256 + 8 * 256]
        KR = lambda: ARN[:, 10304:10304 + 8 * 512]
        VR = lambda: ARN[:, 14400:14400 + 2 * 2 * 1024]
        OUTF = lambda: ARN[:, 18496:18496 + 8 * 256]
        ATT = lambda: ARN[:, 20544:20544 + 4 * 128]
        SQB = lambda: ARN[:, 21056:21056 + 512]
        INVB = lambda: ARN[:, 21568:21568 + 512]
        TAB = lambda: ARN[:, 22080:22080 + 4 * 256]
        ATL = lambda: ARN[:, 0:512]
        XSF = lambda: ARN[:, 0:8 * 520]
        XN = lambda: ARN[:, 4160:4160 + 8 * 512]
        ACT = lambda: ARN[:, 8256:8256 + HC * 512]
        GEL = lambda: ARN[:, 19584:19584 + 512]
        MASKS = S("MASKS", [128, 2 * 2 * W], F32)
        IDN_BF = S("IDN_BF", [128, 128], BF16)
        INV2B = S("INV2B", [2, 256], BF16)
        ONB = S("ONB", [128, 2], BF16)
        ONBT = S("ONBT", [2, 128], BF16)
        BIAS = S("BIAS", [128, 52 * DEPTH], F32)
        LNGB = S("LNGB", [128, 16 * DEPTH], F32)
        QSC = S("QSC", [128, 4 * DEPTH], F32)
        wba = S("wba", [128, NCD * 512], BF16)
        wbb = S("wbb", [128, NCD * 512], BF16)
        wbc = wba
        wbd = wbb
        w0 = S("w0", [128, 512], F32)
        w1 = S("w1", [128, 512], F32)
        nmax = S("nmax", [128, 1], F32)
        nsum = S("nsum", [128, 1], F32)
        inv2 = S("inv2", [2, 256], F32)
        stat_mu = S("stat_mu", [1, FST], F32)
        stat_rsig = S("stat_rsig", [1, FST], F32)
        stat_tmp = S("stat_tmp", [1, FST], F32)

        pb0 = PS("pb0", [128, 512], F32)
        pb1 = PS("pb1", [128, 512], F32)
        pb2 = PS("pb2", [128, 512], F32)
        pb3 = PS("pb3", [128, 512], F32)
        PTB = PS("PTB", [128, 1024], BF16)

        P = ctx["P"] if ctx is not None else Prog(nc, es)
        _pw = [ctx["prev"]] if ctx is not None and ctx.get("prev") else []
        cpool, ones_col, ones_row, ones_bf, tok0 = emit_common_setup(P, es, nc, extra_waits=_pw, pref=_pref)

        P.emit("pool", lambda e: e.dma_start(ONB[:], onbm), waits=_pw, inc="dqw", inc_by=16)
        P.emit("pool", lambda e: e.dma_start(ONBT[:], onbtm), inc="dqw", inc_by=16)
        P.emit("pool", lambda e: e.dma_start(BIAS[:], biasc), inc="dqw", inc_by=16)
        P.emit("pool", lambda e: e.dma_start(LNGB[:], lngb), inc="dqw", inc_by=16)
        P.emit("pool", lambda e: e.dma_start(QSC[:], qscols), inc="dqw", inc_by=16)
        P.emit("pool", lambda e: e.dma_start(IDN_BF[:], idn), inc="dqw", inc_by=16)
        loads_tok = [("dqw", P.cnt["dqw"]), P.tok("dve")]
        prev = P.stage("dve", [lambda e: e.engine_nop()], waits=loads_tok)

        import os as _os0
        if _os0.environ.get("SKIP_INITLN"):
            _chunked_dma(P, "sp", "dqx", XW, xin, NCD, T, waits=[prev])
            prev = P.stage("dve", [lambda e: e.engine_nop()], waits=[("dqx", P.cnt["dqx"])])
        for it in range(0 if _os0.environ.get("SKIP_INITLN") else T // 256):
            t0 = it * 256
            mmq = []
            for c in range(NCD):
                P.emit("sp", lambda e, c=c, t0=t0: e.dma_start(
                    w0[:, 0:256] if c % 2 == 0 else w1[:, 0:256],
                    xin[c * 128:(c + 1) * 128, t0:t0 + 256]),
                    waits=[prev], inc="dqx", inc_by=16)
                xt = ("dqx", P.cnt["dqx"])
                src = (w0 if c % 2 == 0 else w1)
                prev = P.stage("act", [lambda e, src=src: e.activation(
                    SQB()[:, 0:256], src[:, 0:256], AF.Square)], waits=[xt, prev])
                prev = P.stage("pe", [
                    lambda e, c=c, src=src: e.matmul(pb2[0:1, 0:256], ones_col[:], src[:, 0:256],
                                                     start=(c == 0), stop=(c == NCD - 1)),
                    lambda e, c=c: e.matmul(pb2[0:1, 256:512], ones_bf[:], SQB()[:, 0:256],
                                            start=(c == 0), stop=(c == NCD - 1)),
                ], waits=[prev])
            prev = P.stage("dve", [
                lambda e: e.tensor_scalar(stat_mu[:, 0:256], pb2[0:1, 0:256], 1.0 / D, None, OP.mult),
                lambda e: e.tensor_scalar(stat_tmp[:, 0:256], pb2[0:1, 256:512], 1.0 / D, None, OP.mult),
                lambda e: e.tensor_tensor(stat_rsig[:, 0:256], stat_mu[:, 0:256], stat_mu[:, 0:256], OP.mult),
                lambda e: e.tensor_tensor(stat_tmp[:, 0:256], stat_tmp[:, 0:256], stat_rsig[:, 0:256], OP.subtract),
            ], waits=[prev])
            prev = P.stage("act", [
                lambda e: e.activation(stat_tmp[:, 0:256], stat_tmp[:, 0:256], AF.Ln, bias=cpool[0:1, C_EPS_LN:C_EPS_LN + 1]),
                lambda e: e.activation(stat_rsig[:, 0:256], stat_tmp[:, 0:256], AF.Exp, scale=cpool[0:1, C_NEGHALF:C_NEGHALF + 1]),
            ], waits=[prev])
            prev = P.stage("pe", [
                lambda e: e.matmul(pb0[:, 0:256], ones_row[:], stat_mu[:, 0:256], start=True, stop=True),
                lambda e: e.matmul(pb1[:, 0:256], ones_row[:], stat_rsig[:, 0:256], start=True, stop=True),
            ], waits=[prev])
            for c in range(NCD):
                P.emit("sp", lambda e, c=c, t0=t0: e.dma_start(
                    w0[:, 0:256] if c % 2 == 0 else w1[:, 0:256],
                    xin[c * 128:(c + 1) * 128, t0:t0 + 256]),
                    waits=[prev], inc="dqx", inc_by=16)
                xt = ("dqx", P.cnt["dqx"])
                src = (w0 if c % 2 == 0 else w1)
                prev = P.stage("dve", [
                    lambda e, c=c, t0=t0, src=src: e.tensor_tensor(
                        XW[:, c * T + t0: c * T + t0 + 256], src[:, 0:256], pb0[:, 0:256], OP.subtract),
                    lambda e, c=c, t0=t0: e.tensor_tensor(
                        XW[:, c * T + t0: c * T + t0 + 256], XW[:, c * T + t0: c * T + t0 + 256],
                        pb1[:, 0:256], OP.mult),
                ], waits=[xt, prev])

        def stream_w(buf, src3, lay, col0, ncols, wsem):
            def fn(e, buf=buf, src3=src3, lay=lay, col0=col0, ncols=ncols):
                return e.dma_start(
                    buf[:, 0:NCD * ncols].rearrange("p (c n) -> p c n", c=NCD),
                    src3[lay].rearrange("(c p) n -> p c n", p=128)[:, :, col0:col0 + ncols])
            P.emit("pool", fn, waits=[prev], inc=wsem, inc_by=16)
            return (wsem, P.cnt[wsem])

        for lay in range(DEPTH_RUN):
            qs_c = QSC[:, lay * 4 + 0: lay * 4 + 1]
            qsw_c = QSC[:, lay * 4 + 1: lay * 4 + 2]
            ks_c = QSC[:, lay * 4 + 2: lay * 4 + 3]
            ksw_c = QSC[:, lay * 4 + 3: lay * 4 + 4]
            prev = P.stage("dve", [lambda e: e.memset(KR()[:], 0.0),
                                   lambda e: e.memset(VR()[:], 0.0)], waits=[prev])
            for w in range(NW_RUN):
                t0 = w * W
                sl = w % 2
                fns = []
                for c in range(NCD):
                    off = 0 if c < 4 else 1
                    if t0 - off < 0:
                        fns.append(lambda e, c=c: e.memset(XS()[:, c * 264:c * 264 + 1], 0.0))
                        fns.append(lambda e, c=c, t0=t0: e.tensor_copy(
                            XS()[:, c * 264 + 1:c * 264 + 256], XW[:, c * T: c * T + 255]))
                    else:
                        fns.append(lambda e, c=c, t0=t0, off=off: e.tensor_copy(
                            XS()[:, c * 264:c * 264 + 256], XW[:, c * T + t0 - off: c * T + t0 + 256 - off]))
                prev = P.stage("dve", fns, waits=[prev])
                import os as _os
                parts = [("q", 0, QT), ("qs", 1024, QST), ("k", 2048, KT), ("ks", 3072, KST)]
                if _os.environ.get("SKIP_QKV"):
                    parts = []
                for pname, col0, dstf in parts:
                    buf = wba if pname in ("q", "k") else wbb
                    for hf in range(2):
                        wt = stream_w(buf, wqkv5, lay, col0 + hf * 512, 512, "dqq")
                        for m4 in range(4):
                            m = hf * 4 + m4
                            pm = pb0 if m % 2 == 0 else pb1
                            mm = [(lambda e, c=c, m4=m4, pm=pm, buf=buf: e.matmul(
                                pm[:, 0:W], buf[:, c * 512 + m4 * 128: c * 512 + (m4 + 1) * 128],
                                XS()[:, c * 264:c * 264 + 256], start=(c == 0), stop=(c == NCD - 1)))
                                  for c in range(NCD)]
                            prev = P.stage("pe", mm, waits=[prev, wt] if m4 == 0 else [prev])
                            prev = P.stage("dve", [lambda e, m=m, pm=pm, dstf=dstf: e.tensor_copy(
                                dstf()[:, m * W:(m + 1) * W], pm[:, 0:W])], waits=[prev])
                for nsl in range(2 if not _os.environ.get("SKIP_QKV") else 0):
                    wt = stream_w(wba, wqkv5, lay, 4096 + nsl * 512, 512, "dqq")
                    for tc in range(2):
                        pm = pb0 if tc == 0 else pb1
                        mm = [(lambda e, c=c, tc=tc, pm=pm: e.matmul(
                            pm[:, 0:512], XS()[:, c * 264 + tc * 128: c * 264 + (tc + 1) * 128],
                            wba[:, c * 512:(c + 1) * 512],
                            start=(c == 0), stop=(c == NCD - 1)))
                              for c in range(NCD)]
                        prev = P.stage("pe", mm, waits=[prev, wt] if tc == 0 else [prev])
                        prev = P.stage("dve", [lambda e, tc=tc, nsl=nsl, sl=sl, pm=pm: e.tensor_copy(
                            VR()[:, (sl * 2 + tc) * 1024 + nsl * 512: (sl * 2 + tc) * 1024 + (nsl + 1) * 512],
                            pm[:, 0:512])], waits=[prev])
                if not _os.environ.get("SKIP_TABS"):
                    P.emit("pool", lambda e, t0=t0: e.dma_start(
                        TAB()[:, 0:256], tabs[0:128, t0:t0 + 256]), waits=[prev], inc="dqt", inc_by=16)
                    P.emit("pool", lambda e, t0=t0: e.dma_start(
                        TAB()[:, 256:512], tabs[128:256, t0:t0 + 256]), inc="dqt", inc_by=16)
                    P.emit("pool", lambda e, t0=t0: e.dma_start(
                        TAB()[:, 512:768], tabs[256:384, t0:t0 + 256]), inc="dqt", inc_by=16)
                    P.emit("pool", lambda e, t0=t0: e.dma_start(
                        TAB()[:, 768:1024], tabs[384:512, t0:t0 + 256]), inc="dqt", inc_by=16)
                    moff = 0 if w > 0 else 2
                    P.emit("pool", lambda e, moff=moff: e.dma_start(
                        MASKS[:, 0:512], masks[moff * 128:(moff + 1) * 128, :]), inc="dqt", inc_by=16)
                    P.emit("pool", lambda e, moff=moff: e.dma_start(
                        MASKS[:, 512:1024], masks[(moff + 1) * 128:(moff + 2) * 128, :]), inc="dqt", inc_by=16)
                    prev = P.stage("dve", [
                        lambda e, qs_c=qs_c: e.tensor_scalar(TAB()[:, 0:256], TAB()[:, 0:256], qs_c, None, OP.mult),
                        lambda e, qsw_c=qsw_c: e.tensor_scalar(TAB()[:, 256:512], TAB()[:, 256:512], qsw_c, None, OP.mult),
                        lambda e, ks_c=ks_c: e.tensor_scalar(TAB()[:, 512:768], TAB()[:, 512:768], ks_c, None, OP.mult),
                        lambda e, ksw_c=ksw_c: e.tensor_scalar(TAB()[:, 768:1024], TAB()[:, 768:1024], ksw_c, None, OP.mult),
                    ], waits=[prev, ("dqt", P.cnt["dqt"])])
                import os as _os
                _skip_rot = _os.environ.get("SKIP_ROT")
                for side, base_t, swap_t, taba, tabb in (() if _skip_rot else (
                        ("q", QT, QST, 0, 256), ("k", KT, KST, 512, 768))):
                    for c in range(NCD):
                        prev = P.stage("act", [lambda e, c=c, base_t=base_t: e.activation(
                            SQB()[:, 0:W], base_t()[:, c * W:(c + 1) * W], AF.Square)], waits=[prev])
                        prev = P.stage("pe", [lambda e: e.matmul(
                            pb2[0:2, 0:W], ONB[:], SQB()[:, 0:W], start=True, stop=True)], waits=[prev])
                        prev = P.stage("act", [lambda e: e.activation(
                            inv2[:, 0:W], pb2[0:2, 0:W], AF.Ln)], waits=[prev])
                        prev = P.stage("act", [lambda e: e.activation(
                            inv2[:, 0:W], inv2[:, 0:W], AF.Exp, scale=cpool[0:2, C_HALF:C_HALF + 1])], waits=[prev])
                        prev = P.stage("dve", [
                            lambda e: e.tensor_scalar(inv2[:, 0:W], inv2[:, 0:W], 1e-12, None, OP.max),
                            lambda e: e.reciprocal(inv2[:, 0:W], inv2[:, 0:W]),
                        ], waits=[prev])
                        prev = P.stage("dve", [
                            lambda e: e.tensor_copy(INV2B[:, 0:W], inv2[:, 0:W]),
                        ], waits=[prev])
                        prev = P.stage("pe", [lambda e: e.matmul(
                            pb2[:, 0:W], ONBT[:], INV2B[:, 0:W], start=True, stop=True)], waits=[prev])
                        prev = P.stage("dve", [lambda e: e.tensor_copy(
                            INVB()[:, 0:W], pb2[:, 0:W])], waits=[prev])
                        prev = P.stage("dve", [
                            lambda e, c=c, base_t=base_t, taba=taba: e.tensor_tensor(
                                base_t()[:, c * W:(c + 1) * W], base_t()[:, c * W:(c + 1) * W],
                                TAB()[:, taba:taba + W], OP.mult),
                            lambda e, c=c, swap_t=swap_t, tabb=tabb: e.tensor_tensor(
                                swap_t()[:, c * W:(c + 1) * W], swap_t()[:, c * W:(c + 1) * W],
                                TAB()[:, tabb:tabb + W], OP.mult),
                            lambda e, c=c, base_t=base_t, swap_t=swap_t: e.tensor_tensor(
                                base_t()[:, c * W:(c + 1) * W], base_t()[:, c * W:(c + 1) * W],
                                swap_t()[:, c * W:(c + 1) * W], OP.add),
                            lambda e, c=c, base_t=base_t: e.tensor_tensor(
                                base_t()[:, c * W:(c + 1) * W], base_t()[:, c * W:(c + 1) * W],
                                INVB()[:, 0:W], OP.mult),
                        ], waits=[prev])
                prev = P.stage("dve", [(lambda e, c=c, sl=sl: e.tensor_copy(
                    KR()[:, c * 512 + sl * 256: c * 512 + (sl + 1) * 256], KT()[:, c * W:(c + 1) * W]))
                    for c in range(NCD)], waits=[prev])
                if debug and lay == 0 and w == 0:
                    P.emit("pool", lambda e: e.dma_start(dbg["qr"], QT()[:]), waits=[prev], inc="dqo", inc_by=16)
                    P.emit("pool", lambda e: e.dma_start(dbg["kr"], KT()[:]), inc="dqo", inc_by=16)
                    prev = P.stage("dve", [lambda e: e.engine_nop()], waits=[P.tok("dqo")])
                import os as _os
                _skip_heads = _os.environ.get("SKIP_HEADS")
                for c in range(NCD if not _skip_heads else 0):
                    for hh in range(2):
                        h = 2 * c + hh
                        rb = hh * 64
                        for mi in range(2):
                            prev = P.stage("pe", [
                                (lambda e, c=c, rb=rb, mi=mi, sl=sl: e.matmul(
                                    pb3[0:128, 0:256],
                                    QT()[rb:rb + 64, c * W + mi * 128: c * W + (mi + 1) * 128],
                                    KR()[rb:rb + 64, c * 512 + (1 - sl) * 256: c * 512 + (2 - sl) * 256],
                                    start=True, stop=True)),
                                (lambda e, c=c, rb=rb, mi=mi, sl=sl: e.matmul(
                                    pb3[0:128, 256:512],
                                    QT()[rb:rb + 64, c * W + mi * 128: c * W + (mi + 1) * 128],
                                    KR()[rb:rb + 64, c * 512 + sl * 256: c * 512 + (sl + 1) * 256],
                                    start=True, stop=True)),
                            ], waits=[prev])
                            prev = P.stage("dve", [
                                lambda e, mi=mi: e.scalar_tensor_tensor(
                                    w0[:, 0:512], pb3[0:128, 0:512], 8.0,
                                    MASKS[:, mi * 512:(mi + 1) * 512], OP.mult, OP.add),
                            ], waits=[prev])
                            prev = P.stage("dve", [
                                lambda e: e.tensor_reduce(nmax[:, 0:1], w0[:, 0:512],
                                                          mybir.AxisListType.X, OP.max, negate=True),
                            ], waits=[prev])
                            if debug and lay == 0 and w == 0 and c == 0 and hh == 0 and mi == 0:
                                P.emit("sp", lambda e: e.dma_start(dbg["sim"], w0[:, 0:512]), waits=[prev], inc="dqo", inc_by=16)
                                prev = P.stage("dve", [lambda e: e.engine_nop()], waits=[P.tok("dqo")])
                            prev = P.stage("act", [lambda e: e.activation(
                                ATL()[:, 0:512], w0[:, 0:512], AF.Exp, bias=nmax[:, 0:1])],
                                waits=[prev])
                            prev = P.stage("dve", [
                                lambda e: e.tensor_reduce(nsum[:, 0:1], ATL()[:, 0:512],
                                                          mybir.AxisListType.X, OP.add),
                            ], waits=[prev])
                            prev = P.stage("dve", [
                                lambda e: e.reciprocal(nsum[:, 0:1], nsum[:, 0:1]),
                            ], waits=[prev])
                            if debug and lay == 0 and w == 0 and c == 0 and hh == 0 and mi == 0:
                                P.emit("sp", lambda e: e.dma_start(dbg["nsa"], nsum[:, 0:1]), waits=[prev], inc="dqo", inc_by=16)
                                P.emit("sp", lambda e: e.dma_start(dbg["nmx"], nmax[:, 0:1]), inc="dqo", inc_by=16)
                                prev = P.stage("dve", [lambda e: e.engine_nop()], waits=[P.tok("dqo")])
                            prev = P.stage("dve", [
                                lambda e: e.tensor_scalar(ATL()[:, 0:512], ATL()[:, 0:512],
                                                          nsum[:, 0:1], None, OP.mult),
                            ], waits=[prev])
                            if debug and lay == 0 and w == 0 and c == 0 and hh == 0 and mi == 0:
                                P.emit("pool", lambda e: e.dma_start(dbg["attp"], ATL()[:, 0:512]), waits=[prev], inc="dqo", inc_by=16)
                                prev = P.stage("dve", [lambda e: e.engine_nop()], waits=[P.tok("dqo")])
                            mm = [(lambda e, kc=kc: e.transpose(
                                PTB[:, kc * 128:(kc + 1) * 128],
                                ATL()[:, kc * 128:(kc + 1) * 128], IDN_BF[:]))
                                  for kc in range(4)]
                            prev = P.stage("pe", mm, waits=[prev])
                            prev = P.stage("dve", [(lambda e, kc=kc: e.tensor_copy(
                                ATT()[:, kc * 128:(kc + 1) * 128], PTB[:, kc * 128:(kc + 1) * 128]))
                                for kc in range(4)], waits=[prev])
                            mm = []
                            for kc in range(4):
                                vsl = (1 - sl) if kc < 2 else sl
                                vtc = kc % 2
                                mm.append(lambda e, kc=kc, vsl=vsl, vtc=vtc, h=h, rb=rb: e.matmul(
                                    pb3[rb:rb + 64, 0:128],
                                    VR()[:, (vsl * 2 + vtc) * 1024 + h * 64: (vsl * 2 + vtc) * 1024 + (h + 1) * 64],
                                    ATT()[:, kc * 128:(kc + 1) * 128],
                                    start=(kc == 0), stop=(kc == 3),
                                    tile_position=(0, 64) if rb == 64 else None))
                            prev = P.stage("pe", mm, waits=[prev])
                            prev = P.stage("dve", [lambda e, c=c, mi=mi, rb=rb: e.tensor_copy(
                                OUTF()[rb:rb + 64, c * W + mi * 128: c * W + (mi + 1) * 128],
                                pb3[rb:rb + 64, 0:128])], waits=[prev])
                if debug and lay == 0 and w == 0:
                    P.emit("pool", lambda e: e.dma_start(dbg["outf"], OUTF()[:]), waits=[prev], inc="dqo", inc_by=16)
                    prev = P.stage("dve", [lambda e: e.engine_nop()], waits=[P.tok("dqo")])
                for og in range(2 if not _os.environ.get("SKIP_WO") else 0):
                    wt = stream_w(wbb, wo, lay, og * 512, 512, "dqq")
                    for oi in range(4):
                        mo = og * 4 + oi
                        pm = pb0 if oi % 2 == 0 else pb1
                        mm = [(lambda e, c=c, oi=oi, pm=pm: e.matmul(
                            pm[:, 0:W], wbb[:, c * 512 + oi * 128: c * 512 + (oi + 1) * 128],
                            OUTF()[:, c * W:(c + 1) * W], start=(c == 0), stop=(c == NCD - 1)))
                              for c in range(NCD)]
                        prev = P.stage("pe", mm, waits=[prev, wt] if oi == 0 else [prev])
                        prev = P.stage("dve", [lambda e, mo=mo, t0=t0, pm=pm: e.tensor_tensor(
                            XW[:, mo * T + t0: mo * T + t0 + W], XW[:, mo * T + t0: mo * T + t0 + W],
                            pm[:, 0:W], OP.add)], waits=[prev])
                if debug and lay == 0 and w == 0:
                    P.emit("sp", lambda e: e.dma_start(dbg["xw1"], XW[:, 0:256]), waits=[prev], inc="dqo", inc_by=16)
                    prev = P.stage("dve", [lambda e: e.engine_nop()], waits=[P.tok("dqo")])
            for ft in range(FT_RUN):
                t0 = ft * FST
                fns = []
                for c in range(NCD):
                    off = 0 if c < 4 else 1
                    if t0 - off < 0:
                        fns.append(lambda e, c=c: e.memset(XSF()[:, c * 520:c * 520 + 1], 0.0))
                        fns.append(lambda e, c=c: e.tensor_copy(
                            XSF()[:, c * 520 + 1:c * 520 + FST], XW[:, c * T: c * T + FST - 1]))
                    else:
                        fns.append(lambda e, c=c, t0=t0, off=off: e.tensor_copy(
                            XSF()[:, c * 520:c * 520 + FST], XW[:, c * T + t0 - off: c * T + t0 + FST - off]))
                prev = P.stage("dve", fns, waits=[prev])
                prev = emit_ln(P, FST, prev, ARN, 0, 520, NCD, 1.0 / D,
                               ARN[:, 8256:8256 + 8 * 512], pb2[0:1, 0:FST], pb3[0:1, 0:FST],
                               stat_mu, stat_rsig, stat_tmp, pb0, pb1, cpool, ones_col, ones_row,
                               out_bf=XN(), out_cols=512, ones_sq=ones_bf, ones_x=ones_bf,
                               src_cast=w0)
                fns = []
                for c in range(NCD):
                    g_c = LNGB[:, lay * 16 + c: lay * 16 + c + 1]
                    b_c = LNGB[:, lay * 16 + 8 + c: lay * 16 + 8 + c + 1]
                    fns.append(lambda e, c=c, g_c=g_c, b_c=b_c: e.tensor_scalar(
                        XN()[:, c * 512:(c + 1) * 512], XN()[:, c * 512:(c + 1) * 512],
                        g_c, b_c, OP.mult, OP.add))
                prev = P.stage("dve", fns, waits=[prev])
                for half in range(2):
                    for mg in range(11):
                        buf = wbc if mg % 2 == 0 else wbd
                        wt = None
                        def wfn(e, lay=lay, half=half, mg=mg, buf=buf):
                            return e.dma_start(
                                buf[:, 0:NCD * 256].rearrange("p (c n) -> p c n", c=NCD),
                                w1p[lay].rearrange("(c p) n -> p c n", p=128)[:, :, half * 2816 + mg * 256: half * 2816 + (mg + 1) * 256])
                        P.emit("pool", wfn, waits=[prev], inc="dqf", inc_by=16)
                        wt = ("dqf", P.cnt["dqf"])
                        for mi in range(2):
                            m = mg * 2 + mi
                            pm = pb0 if mi == 0 else pb1
                            mm = [(lambda e, c=c, mi=mi, pm=pm, buf=buf: e.matmul(
                                pm[:, 0:512], buf[:, c * 256 + mi * 128: c * 256 + (mi + 1) * 128],
                                XN()[:, c * 512:(c + 1) * 512], start=(c == 0), stop=(c == NCD - 1)))
                                  for c in range(NCD)]
                            prev = P.stage("pe", mm, waits=[prev, wt] if mi == 0 else [prev])
                            if half == 1:
                                bcol = BIAS[:, lay * 52 + 22 + m: lay * 52 + 23 + m]
                                prev = P.stage("act", [lambda e, m=m, pm=pm, bcol=bcol: e.activation(
                                    GEL()[:, 0:512], pm[:, 0:512], AF.Gelu, bias=bcol)], waits=[prev])
                                prev = P.stage("dve", [lambda e, m=m: e.tensor_tensor(
                                    ACT()[:, m * 512:(m + 1) * 512], ACT()[:, m * 512:(m + 1) * 512],
                                    GEL()[:, 0:512], OP.mult)], waits=[prev])
                            else:
                                bcol = BIAS[:, lay * 52 + m: lay * 52 + 1 + m]
                                prev = P.stage("dve", [lambda e, m=m, pm=pm, bcol=bcol: e.tensor_scalar(
                                    ACT()[:, m * 512:(m + 1) * 512], pm[:, 0:512], bcol, None, OP.add)],
                                    waits=[prev])
                for mo in range(8):
                    buf = wbc if mo % 2 == 0 else wbd
                    def wfn2(e, lay=lay, mo=mo, buf=buf):
                        return e.dma_start(
                            buf[:, 0:HC * 128].rearrange("p (c n) -> p c n", c=HC),
                            w2p[lay].rearrange("(c p) n -> p c n", p=128)[:, :, mo * 128:(mo + 1) * 128])
                    P.emit("pool", wfn2, waits=[prev], inc="dqf", inc_by=16)
                    wt = ("dqf", P.cnt["dqf"])
                    pm = pb0 if mo % 2 == 0 else pb1
                    mm = [(lambda e, k=k, pm=pm, buf=buf: e.matmul(
                        pm[:, 0:512], buf[:, k * 128:(k + 1) * 128],
                        ACT()[:, k * 512:(k + 1) * 512], start=(k == 0), stop=(k == HC - 1)))
                          for k in range(HC)]
                    prev = P.stage("pe", mm, waits=[prev, wt])
                    b2col = BIAS[:, lay * 52 + 44 + mo: lay * 52 + 45 + mo]
                    prev = P.stage("dve", [
                        lambda e, pm=pm, b2col=b2col: e.tensor_scalar(
                            w1[:, 0:512], pm[:, 0:512], b2col, None, OP.add),
                        lambda e, mo=mo, t0=t0: e.tensor_tensor(
                            XW[:, mo * T + t0: mo * T + t0 + FST],
                            XW[:, mo * T + t0: mo * T + t0 + FST], w1[:, 0:512], OP.add),
                    ], waits=[prev])
                if debug and lay == 0 and ft == 0:
                    P.emit("pool", lambda e: e.dma_start(dbg["xn"], XN()[:]), waits=[prev], inc="dqo", inc_by=16)
                    P.emit("pool", lambda e: e.dma_start(dbg["act"], ACT()[:]), inc="dqo", inc_by=16)
                    prev = P.stage("dve", [lambda e: e.engine_nop()], waits=[P.tok("dqo")])

        for c in range(NCD):
            for pc in range(T // 512):
                buf = w0 if pc % 2 == 0 else w1
                P.emit("sp", lambda e, c=c, pc=pc, buf=buf: e.dma_start(
                    buf[:, 0:512], xin[c * 128:(c + 1) * 128, pc * 512:(pc + 1) * 512]),
                    waits=[prev], inc="dqa", inc_by=16)
                prev = P.stage("dve", [lambda e, c=c, pc=pc, buf=buf: e.tensor_tensor(
                    XW[:, c * T + pc * 512: c * T + (pc + 1) * 512],
                    XW[:, c * T + pc * 512: c * T + (pc + 1) * 512], buf[:, 0:512], OP.add)],
                    waits=[prev, ("dqa", P.cnt["dqa"])])
        for c in range(NCD):
            P.emit("sp", lambda e, c=c: e.dma_start(xout[c * 128:(c + 1) * 128, :],
                                                    XW[:, c * T:(c + 1) * T]),
                   waits=[prev] if c == 0 else (), inc="dqo2", inc_by=16)
        P.emit("sp", None, waits=[P.tok("dqo2")])
        if ctx is not None:
            ctx["prev"] = prev
            ctx["out_tok"] = P.tok("dqo2")
        import os
        if os.environ.get("PROG_STATS"):
            print("SEM COUNTS:", {k: v for k, v in P.cnt.items()})
            print("OPS:", {k: len(v) for k, v in P.ops.items()})
        if ctx is None:
            P.finalize()
    return nc


def run_l1_ffn(x, Wqkv, q_scale, k_scale, Wo, ln_g, ln_b, W1, b1, W2, b2, debug=False, **bk):
    nc = build_l1_ffn(debug=debug, **bk)
    DEPTH = 6
    f32 = np.float32
    sigma = np.concatenate([np.arange(32, 64), np.arange(0, 32)])
    perm1024 = np.concatenate([h * 64 + sigma for h in range(16)])
    wqkv5 = np.empty((DEPTH, 1024, 5 * 1024), f32)
    for i in range(DEPTH):
        q = Wqkv[i][:, 0:1024]; k = Wqkv[i][:, 1024:2048]; v = Wqkv[i][:, 2048:3072]
        wqkv5[i] = np.concatenate([q, q[:, perm1024], k, k[:, perm1024], v], axis=1)
    w1p = np.zeros((DEPTH, 1024, 2 * 2816), f32)
    w1p[:, :, 0:2730] = W1[:, :, 0:2730]
    w1p[:, :, 2816:2816 + 2730] = W1[:, :, 2730:5460]
    w2p = np.zeros((DEPTH, 2816, 1024), f32)
    w2p[:, 0:2730, :] = W2
    biasc = np.zeros((128, 52 * DEPTH), f32)
    lngb = np.zeros((128, 16 * DEPTH), f32)
    qscols = np.zeros((128, 4 * DEPTH), f32)
    for i in range(DEPTH):
        b1a = np.zeros(2816, f32); b1a[0:2730] = b1[i][0:2730]
        b1g = np.zeros(2816, f32); b1g[0:2730] = b1[i][2730:5460]
        biasc[:, i * 52 + 0:i * 52 + 22] = b1a.reshape(22, 128).T
        biasc[:, i * 52 + 22:i * 52 + 44] = b1g.reshape(22, 128).T
        biasc[:, i * 52 + 44:i * 52 + 52] = b2[i].reshape(8, 128).T
        lngb[:, i * 16:i * 16 + 8] = ln_g[i].reshape(8, 128).T
        lngb[:, i * 16 + 8:i * 16 + 16] = ln_b[i].reshape(8, 128).T
        qscols[:, i * 4 + 0] = np.tile(q_scale[i], 2)
        qscols[:, i * 4 + 1] = np.tile(q_scale[i][sigma], 2)
        qscols[:, i * 4 + 2] = np.tile(k_scale[i], 2)
        qscols[:, i * 4 + 3] = np.tile(k_scale[i][sigma], 2)
    DIM_HEAD = 64; WINDOW = 256
    half = np.arange(0, DIM_HEAD, 2, dtype=np.float64)
    inv_freq = 1.0 / 10000.0 ** (half / DIM_HEAD)
    t = np.arange(T, dtype=np.float64)
    freqs = t[:, None] * inv_freq[None, :]
    freqs = np.concatenate([freqs, freqs], -1)
    base_scale = (half + 0.4 * DIM_HEAD) / (1.4 * DIM_HEAD)
    power = (t - T // 2) / (WINDOW // 2)
    scale = base_scale[None, :] ** power[:, None]
    scale = np.concatenate([scale, scale], -1)
    sign = np.where(np.arange(64) < 32, -1.0, 1.0)
    tabA = (np.cos(freqs) * scale).T
    tabB = (np.sin(freqs) * scale).T * sign[:, None]
    tabC = (np.cos(freqs) / scale).T
    tabD = (np.sin(freqs) / scale).T * sign[:, None]
    tabs = np.concatenate([np.tile(tabA, (2, 1)), np.tile(tabB, (2, 1)),
                           np.tile(tabC, (2, 1)), np.tile(tabD, (2, 1))], axis=0).astype(f32)
    masks = np.zeros((512, 512), f32)
    for wflag in range(2):
        for mi in range(2):
            m = np.zeros((128, 512), f32)
            q_off = mi * 128 + np.arange(128)[:, None]
            j = np.arange(512)[None, :]
            allowed = (j >= q_off) & (j <= q_off + 256)
            if wflag == 1:
                allowed &= (j >= 256)
            m[~allowed] = -1e30
            masks[(wflag * 2 + mi) * 128:(wflag * 2 + mi + 1) * 128] = m
    onbm = np.zeros((128, 2), f32)
    onbm[0:64, 0] = 1.0; onbm[64:128, 1] = 1.0
    onbtm = np.ascontiguousarray(onbm.T)
    idn = np.eye(128, dtype=f32)
    in_maps = []
    for b in range(N_CORES):
        in_maps.append(dict(xin=np.ascontiguousarray(x[b].T).astype(f32),
                            wqkv5=wqkv5, wo=np.ascontiguousarray(Wo).astype(f32),
                            w1p=w1p, w2p=w2p, biasc=biasc, lngb=lngb, qscols=qscols,
                            tabs=tabs, masks=masks, onbm=onbm, onbtm=onbtm, idn=idn))
    res = run_bass_kernel_spmd(nc, in_maps, core_ids=list(range(N_CORES)))
    out = np.stack([np.ascontiguousarray(r["xout"].T) for r in res.results])
    if debug:
        return out, res.results[0]
    return out


def _prep_l0_attn(W_in, W_A, W_B, W_C, W_out, norm_w):
    DI = 2048
    wi = W_in.reshape(1024, 32, 128)
    order = []
    for mp in range(16):
        order += [mp, mp + 16]
    w_in = np.ascontiguousarray(wi[:, order, :].reshape(1024, 4096)).astype(np.float32)
    tau = np.arange(64)[:, None]; t = np.arange(64)[None, :]
    return dict(w_in=w_in,
                w_abc=np.ascontiguousarray(np.concatenate([W_A, W_B, W_C], axis=1)).astype(np.float32),
                w_out=np.ascontiguousarray(W_out).astype(np.float32),
                nwrep=np.ascontiguousarray(np.broadcast_to(norm_w[None, :], (128, DI))).astype(np.float32),
                trim=(tau <= t).astype(np.float32),
                idn=np.eye(128, dtype=np.float32))


def _prep_l0_ffn(W1, b1, W2, b2, scale):
    return dict(w1T=np.ascontiguousarray(W1.T).astype(np.float32),
                w2T=np.ascontiguousarray(W2.T).astype(np.float32),
                b1c=np.ascontiguousarray(b1.reshape(32, 128).T).astype(np.float32),
                b2c=np.ascontiguousarray(b2.reshape(8, 128).T).astype(np.float32),
                sclr=np.full((128, 1), np.float32(scale), np.float32),
                idn=np.eye(128, dtype=np.float32))


def _prep_l1_attn(W_in, W_x, W_dt, b_dt, gamma, beta):
    w_inT = np.ascontiguousarray(W_in.T).astype(np.float32)
    vecs = np.concatenate([
        W_x[0].reshape(NCD, 128).T,
        W_dt[:, 0].reshape(NCD, 128).T,
        b_dt.reshape(NCD, 128).T,
        gamma.reshape(NCD, 128).T,
        beta.reshape(NCD, 128).T,
    ], axis=1).astype(np.float32)
    return dict(w_inT=w_inT, vecs=vecs)


def _prep_l1_ffn(Wqkv, q_scale, k_scale, Wo, ln_g, ln_b, W1, b1, W2, b2):
    DEPTH = 6
    f32 = np.float32
    sigma = np.concatenate([np.arange(32, 64), np.arange(0, 32)])
    perm1024 = np.concatenate([h * 64 + sigma for h in range(16)])
    wqkv5 = np.empty((DEPTH, 1024, 5 * 1024), f32)
    for i in range(DEPTH):
        q = Wqkv[i][:, 0:1024]; k = Wqkv[i][:, 1024:2048]; v = Wqkv[i][:, 2048:3072]
        wqkv5[i] = np.concatenate([q, q[:, perm1024], k, k[:, perm1024], v], axis=1)
    w1p = np.zeros((DEPTH, 1024, 2 * 2816), f32)
    w1p[:, :, 0:2730] = W1[:, :, 0:2730]
    w1p[:, :, 2816:2816 + 2730] = W1[:, :, 2730:5460]
    w2p = np.zeros((DEPTH, 2816, 1024), f32)
    w2p[:, 0:2730, :] = W2
    biasc = np.zeros((128, 52 * DEPTH), f32)
    lngb = np.zeros((128, 16 * DEPTH), f32)
    qscols = np.zeros((128, 4 * DEPTH), f32)
    for i in range(DEPTH):
        b1a = np.zeros(2816, f32); b1a[0:2730] = b1[i][0:2730]
        b1g = np.zeros(2816, f32); b1g[0:2730] = b1[i][2730:5460]
        biasc[:, i * 52 + 0:i * 52 + 22] = b1a.reshape(22, 128).T
        biasc[:, i * 52 + 22:i * 52 + 44] = b1g.reshape(22, 128).T
        biasc[:, i * 52 + 44:i * 52 + 52] = b2[i].reshape(8, 128).T
        lngb[:, i * 16:i * 16 + 8] = ln_g[i].reshape(8, 128).T
        lngb[:, i * 16 + 8:i * 16 + 16] = ln_b[i].reshape(8, 128).T
        qscols[:, i * 4 + 0] = np.tile(q_scale[i], 2)
        qscols[:, i * 4 + 1] = np.tile(q_scale[i][sigma], 2)
        qscols[:, i * 4 + 2] = np.tile(k_scale[i], 2)
        qscols[:, i * 4 + 3] = np.tile(k_scale[i][sigma], 2)
    DIM_HEAD = 64; WINDOW = 256
    half = np.arange(0, DIM_HEAD, 2, dtype=np.float64)
    inv_freq = 1.0 / 10000.0 ** (half / DIM_HEAD)
    t = np.arange(T, dtype=np.float64)
    freqs = np.concatenate([t[:, None] * inv_freq[None, :]] * 2, -1)
    base_scale = (half + 0.4 * DIM_HEAD) / (1.4 * DIM_HEAD)
    power = (t - T // 2) / (WINDOW // 2)
    scale = np.concatenate([base_scale[None, :] ** power[:, None]] * 2, -1)
    sign = np.where(np.arange(64) < 32, -1.0, 1.0)
    tabA = (np.cos(freqs) * scale).T
    tabB = (np.sin(freqs) * scale).T * sign[:, None]
    tabC = (np.cos(freqs) / scale).T
    tabD = (np.sin(freqs) / scale).T * sign[:, None]
    tabs = np.concatenate([np.tile(tabA, (2, 1)), np.tile(tabB, (2, 1)),
                           np.tile(tabC, (2, 1)), np.tile(tabD, (2, 1))], axis=0).astype(f32)
    masks = np.zeros((512, 512), f32)
    for wflag in range(2):
        for mi in range(2):
            m = np.zeros((128, 512), f32)
            q_off = mi * 128 + np.arange(128)[:, None]
            j = np.arange(512)[None, :]
            allowed = (j >= q_off) & (j <= q_off + 256)
            if wflag == 1:
                allowed &= (j >= 256)
            m[~allowed] = -1e30
            masks[(wflag * 2 + mi) * 128:(wflag * 2 + mi + 1) * 128] = m
    onbm = np.zeros((128, 2), f32)
    onbm[0:64, 0] = 1.0; onbm[64:128, 1] = 1.0
    return dict(wqkv5=wqkv5, wo=np.ascontiguousarray(Wo).astype(f32), w1p=w1p, w2p=w2p,
                biasc=biasc, lngb=lngb, qscols=qscols, tabs=tabs, masks=masks,
                onbm=onbm, onbtm=np.ascontiguousarray(onbm.T), idn=np.eye(128, dtype=f32))


def build_fused():
    nc = bass.Bass("TRN2", target_bir_lowering=False)
    xin = nc.dram_tensor("xin", [D, T], F32, kind="ExternalInput").ap()
    xout = nc.dram_tensor("xout", [D, T], F32, kind="ExternalOutput").ap()
    xh1 = nc.dram_tensor("xh1", [D, T], F32, kind="Internal").ap()
    xh2 = nc.dram_tensor("xh2", [D, T], F32, kind="Internal").ap()
    xh3 = nc.dram_tensor("xh3", [D, T], F32, kind="Internal").ap()
    es = ExitStack()
    with es:
        P = Prog(nc, es)
        ctx = {"nc": nc, "P": P, "pref": "a_", "xin": xin, "xout": xh1, "prev": None}
        build_l0_attn(ctx=ctx)
        ctx.update(pref="b_", xin=xh1, xout=xh2)
        build_l0_ffn(ctx=ctx)
        ctx.update(pref="c_", xin=xh2, xout=xh3)
        build_l1_attn(ctx=ctx)
        ctx.update(pref="d_", xin=xh3, xout=xout)
        build_l1_ffn(ctx=ctx)
        P.finalize()
    return nc


_FUSED_CACHE = {}


def kernel(**inputs):
    d = inputs
    x = np.asarray(d["L0_attn_x"], np.float32)
    if "nc" not in _FUSED_CACHE:
        _FUSED_CACHE["nc"] = build_fused()
    nc = _FUSED_CACHE["nc"]
    am = _prep_l0_attn(d["L0_attn_W_in"], d["L0_attn_W_A"], d["L0_attn_W_B"],
                       d["L0_attn_W_C"], d["L0_attn_W_out"], d["L0_attn_norm_w"])
    bm = _prep_l0_ffn(d["L0_ffn_W1"], d["L0_ffn_b1"], d["L0_ffn_W2"], d["L0_ffn_b2"],
                      d["L0_ffn_scale"])
    cm = _prep_l1_attn(d["L1_attn_W_in"], d["L1_attn_W_x"], d["L1_attn_W_dt"],
                       d["L1_attn_b_dt"], d["L1_attn_gamma"], d["L1_attn_beta"])
    dm = _prep_l1_ffn(d["L1_ffn_Wqkv"], d["L1_ffn_q_scale"], d["L1_ffn_k_scale"],
                      d["L1_ffn_Wo"], d["L1_ffn_ln_g"], d["L1_ffn_ln_b"],
                      d["L1_ffn_W1"], d["L1_ffn_b1"], d["L1_ffn_W2"], d["L1_ffn_b2"])
    wm = {}
    for pref, mm in (("a_", am), ("b_", bm), ("c_", cm), ("d_", dm)):
        for k, v in mm.items():
            wm[pref + k] = v
    in_maps = []
    for b in range(N_CORES):
        m = dict(wm)
        m["xin"] = np.ascontiguousarray(x[b].T).astype(np.float32)
        in_maps.append(m)
    res = run_bass_kernel_spmd(nc, in_maps, core_ids=list(range(N_CORES)))
    return np.ascontiguousarray(
        np.stack([np.ascontiguousarray(r["xout"].T) for r in res.results]))


# version 04
# speedup vs baseline: 1.1629x; 1.1629x over previous
import numpy as np
from contextlib import ExitStack

import concourse.bass as bass
import concourse.mybir as mybir
from concourse.bass_utils import run_bass_kernel_spmd
from concourse.alu_op_type import AluOpType as OP

F32 = mybir.dt.float32
BF16 = mybir.dt.bfloat16
AF = mybir.ActivationFunctionType
AX = mybir.AxisListType

T = 4096
D = 1024
NCD = 8
N_CORES = 8

C_ONE, C_NEG1, C_NEGHALF, C_EPS_LN, C_EPS_F32, C_ZERO, C_HALF, C_NEG1E30 = range(8)
CPOOL_VALS = [1.0, -1.0, -0.5, 1e-5, float(np.finfo(np.float32).eps), 0.0, 0.5, -1e30]
DBG_TILE = 1


class Prog:
    ENGINES = ("sp", "pool", "act", "dve", "pe")

    def __init__(self, nc, es):
        self.nc = nc
        self.es = es
        self.ops = {e: [] for e in self.ENGINES}
        self.sems = {}
        self.cnt = {}

    def sem(self, name):
        if name not in self.sems:
            self.sems[name] = self.es.enter_context(self.nc.semaphore(name))
            self.cnt[name] = 0
        return self.sems[name]

    def emit(self, eng, fn, waits=(), inc=None, inc_by=1):
        s = None
        if inc is not None:
            s = self.sem(inc)
            self.cnt[inc] += inc_by
        wl = [(self.sem(n), v) for (n, v) in waits]

        def run(e, wl=wl, s=s, inc_by=inc_by, fn=fn):
            for sem, v in wl:
                e.wait_ge(sem, v)
            if fn is not None:
                ins = fn(e)
                if s is not None:
                    ins.then_inc(s, inc_by)
        self.ops[eng].append(run)

    def tok(self, name):
        return (name, self.cnt[name])

    def stage(self, eng, fns, waits=()):
        fns = list(fns)
        assert fns
        for i, fn in enumerate(fns):
            self.emit(eng, fn,
                      waits=waits if i == 0 else (),
                      inc=eng if i == len(fns) - 1 else None)
        return self.tok(eng)

    def finalize(self):
        with self.nc.Block() as block:
            for attr, e in (("sync", "sp"), ("gpsimd", "pool"), ("scalar", "act"),
                            ("vector", "dve"), ("tensor", "pe")):
                ops = self.ops[e]

                def body(eng, ops=ops):
                    for f in ops:
                        f(eng)
                getattr(block, attr)(body)


def _chunked_dma(P, queue, sem_name, dst_tile, src_ap, n_chunks, cols, waits=(), cast=False):
    for c in range(n_chunks):
        def fn(e, c=c):
            return e.dma_start(dst_tile[:, c * cols:(c + 1) * cols],
                               src_ap[c * 128:(c + 1) * 128, :])
        P.emit(queue, fn, waits=waits if c == 0 else (), inc=sem_name, inc_by=16)


def emit_ln(P, t, prev, src_tile, src_off, src_stride, n_chunks, inv_n,
            sq, pstat_x, pstat_x2, stat_mu, stat_rsig, stat_tmp, pbc_mu, pbc_rsig,
            cpool, ones_col, ones_row, out_f32=None, out_bf=None, out_cols=None,
            eps_col=C_EPS_LN, tmp_f32=None, add_to_src=False, ones_sq=None,
            ones_x=None, src_cast=None):
    oc = out_cols if out_cols is not None else t
    if ones_sq is None:
        ones_sq = ones_col
    if ones_x is None:
        ones_x = ones_col
    src = lambda c: src_tile[:, src_off + c * src_stride: src_off + c * src_stride + t]
    prev = P.stage("act", [
        (lambda e, c=c: e.activation(sq[:, c * t:(c + 1) * t], src(c), AF.Square))
        for c in range(n_chunks)], waits=[prev])
    mm = []
    for c in range(n_chunks):
        mm.append(lambda e, c=c: e.matmul(pstat_x[:, :t], ones_x[:], src(c),
                                          start=(c == 0), stop=(c == n_chunks - 1)))
    for c in range(n_chunks):
        mm.append(lambda e, c=c: e.matmul(pstat_x2[:, :t], ones_sq[:], sq[:, c * t:(c + 1) * t],
                                          start=(c == 0), stop=(c == n_chunks - 1)))
    prev = P.stage("pe", mm, waits=[prev])
    prev = P.stage("dve", [
        lambda e: e.tensor_scalar(stat_mu[:, :t], pstat_x[:, :t], inv_n, None, OP.mult),
        lambda e: e.tensor_scalar(stat_tmp[:, :t], pstat_x2[:, :t], inv_n, None, OP.mult),
        lambda e: e.tensor_tensor(stat_rsig[:, :t], stat_mu[:, :t], stat_mu[:, :t], OP.mult),
        lambda e: e.tensor_tensor(stat_tmp[:, :t], stat_tmp[:, :t], stat_rsig[:, :t], OP.subtract),
    ], waits=[prev])
    prev = P.stage("act", [
        lambda e: e.activation(stat_tmp[:, :t], stat_tmp[:, :t], AF.Ln, bias=cpool[0:1, eps_col:eps_col + 1]),
        lambda e: e.activation(stat_rsig[:, :t], stat_tmp[:, :t], AF.Exp, scale=cpool[0:1, C_NEGHALF:C_NEGHALF + 1]),
    ], waits=[prev])
    prev = P.stage("pe", [
        lambda e: e.matmul(pbc_mu[:, :t], ones_row[:], stat_mu[:, :t], start=True, stop=True),
        lambda e: e.matmul(pbc_rsig[:, :t], ones_row[:], stat_rsig[:, :t], start=True, stop=True),
    ], waits=[prev])
    fns = []
    for c in range(n_chunks):
        if tmp_f32 is not None:
            fns.append(lambda e, c=c: e.tensor_tensor(tmp_f32[:, :t], src(c), pbc_mu[:, :t], OP.subtract))
            fns.append(lambda e, c=c: e.tensor_tensor(tmp_f32[:, :t], tmp_f32[:, :t], pbc_rsig[:, :t], OP.mult))
            if out_bf is not None:
                fns.append(lambda e, c=c: e.tensor_copy(out_bf[:, c * oc:c * oc + t], tmp_f32[:, :t]))
            if add_to_src:
                fns.append(lambda e, c=c: e.tensor_tensor(src(c), src(c), tmp_f32[:, :t], OP.add))
        elif out_f32 is not None:
            fns.append(lambda e, c=c: e.tensor_tensor(out_f32[:, c * oc:c * oc + t], src(c), pbc_mu[:, :t], OP.subtract))
            fns.append(lambda e, c=c: e.tensor_tensor(out_f32[:, c * oc:c * oc + t], out_f32[:, c * oc:c * oc + t], pbc_rsig[:, :t], OP.mult))
            if out_bf is not None:
                fns.append(lambda e, c=c: e.tensor_copy(out_bf[:, c * oc:c * oc + t], out_f32[:, c * oc:c * oc + t]))
        elif src_cast is not None:
            fns.append(lambda e, c=c: e.tensor_copy(src_cast[:, :t], src(c)))
            fns.append(lambda e, c=c: e.tensor_tensor(src_cast[:, :t], src_cast[:, :t], pbc_mu[:, :t], OP.subtract))
            fns.append(lambda e, c=c: e.tensor_tensor(out_bf[:, c * oc:c * oc + t], src_cast[:, :t], pbc_rsig[:, :t], OP.mult))
        else:
            fns.append(lambda e, c=c: e.tensor_tensor(out_bf[:, c * oc:c * oc + t], src(c), pbc_mu[:, :t], OP.subtract))
            fns.append(lambda e, c=c: e.tensor_tensor(out_bf[:, c * oc:c * oc + t], out_bf[:, c * oc:c * oc + t], pbc_rsig[:, :t], OP.mult))
    prev = P.stage("dve", fns, waits=[prev])
    return prev


def emit_common_setup(P, es, nc, extra_waits=(), pref=""):
    S = lambda *a, **k: es.enter_context(nc.sbuf_tensor(pref + a[0], *a[1:], **k))
    cpool = S("cpool", [128, len(CPOOL_VALS)], F32)
    ones_col = S("ones_col", [128, 1], F32)
    ones_row = S("ones_row", [1, 128], F32)
    fns = []
    for i, v in enumerate(CPOOL_VALS):
        fns.append(lambda e, i=i, v=v: e.memset(cpool[:, i:i + 1], v))
    ones_bf = S("ones_bf", [128, 1], BF16)
    fns.append(lambda e: e.memset(ones_col[:], 1.0))
    fns.append(lambda e: e.memset(ones_row[:], 1.0))
    fns.append(lambda e: e.memset(ones_bf[:], 1.0))
    tok = P.stage("dve", fns, waits=list(extra_waits))
    return cpool, ones_col, ones_row, ones_bf, tok


def build_l1_attn(debug=False, ctx=None):
    if ctx is None:
        nc = bass.Bass("TRN2", target_bir_lowering=False)
        _pref = ""
    else:
        nc = ctx["nc"]
        _pref = ctx["pref"]
    xin = ctx["xin"] if ctx is not None else nc.dram_tensor("xin", [D, T], F32, kind="ExternalInput").ap()
    w_inT = nc.dram_tensor(_pref + "w_inT", [D, 2 * D], F32, kind="ExternalInput").ap()
    vecs = nc.dram_tensor(_pref + "vecs", [128, 5 * NCD], F32, kind="ExternalInput").ap()
    xout = ctx["xout"] if ctx is not None else nc.dram_tensor("xout", [D, T], F32, kind="ExternalOutput").ap()
    if debug:
        dbg = {k: nc.dram_tensor(_pref + "dbg_" + k, shp, F32, kind="ExternalOutput").ap()
               for k, shp in [("xinT", [128, 8 * 256]), ("zT", [128, 8 * 256]),
                              ("g8", [128, 8 * 256]), ("dtraw", [1, 256]),
                              ("mu", [1, 256]), ("rsig", [1, 256]),
                              ("carry", [128, 8])]}

    TT = 256
    NTILE = T // TT

    es = ExitStack()
    S = lambda *a, **k: es.enter_context(nc.sbuf_tensor(_pref + a[0], *a[1:], **k))
    PS = lambda *a, **k: es.enter_context(nc.psum_tensor(_pref + a[0], *a[1:], **k))
    with es:
        RX = S("RX", [128, NCD * T], F32)
        WIN = S("WIN", [128, NCD * 2 * D], BF16)
        VEC = S("VEC", [128, 5 * NCD], F32)
        xhat_b = S("xhat_b", [128, NCD * TT], BF16)
        sq = S("sq", [128, NCD * TT], F32)
        x_inT = S("x_inT", [128, NCD * TT], F32)
        zT = S("zT", [128, NCD * TT], F32)
        g8 = S("g8", [128, NCD * TT], F32)
        w0 = S("w0", [128, TT], F32)
        w1 = S("w1", [128, TT], F32)
        w2 = S("w2", [128, TT], F32)
        carry = S("carry", [128, NCD], F32)
        stat_mu = S("stat_mu", [1, TT], F32)
        stat_rsig = S("stat_rsig", [1, TT], F32)
        stat_tmp = S("stat_tmp", [1, TT], F32)
        dtraw = S("dtraw", [1, TT], F32)

        pbig0 = PS("pbig0", [128, TT], F32)
        pbig1 = PS("pbig1", [128, TT], F32)
        pstat_x = PS("pstat_x", [1, TT], F32)
        pstat_x2 = PS("pstat_x2", [1, TT], F32)
        pbc_mu = PS("pbc_mu", [128, TT], F32)
        pbc_rsig = PS("pbc_rsig", [128, TT], F32)
        pdtb = PS("pdtb", [128, TT], F32)

        P = ctx["P"] if ctx is not None else Prog(nc, es)
        _pw = [ctx["prev"]] if ctx is not None and ctx.get("prev") else []
        cpool, ones_col, ones_row, ones_bf, tok0 = emit_common_setup(P, es, nc, extra_waits=_pw, pref=_pref)
        P.stage("dve", [lambda e: e.memset(carry[:], 0.0)])

        _chunked_dma(P, "sp", "dqx", RX, xin, NCD, T)
        for c in range(NCD):
            P.emit("pool", lambda e, c=c: e.dma_start(WIN[:, c * 2 * D:(c + 1) * 2 * D],
                                                      w_inT[c * 128:(c + 1) * 128, :]),
                   waits=_pw if c == 0 else (), inc="dqw", inc_by=16)
        P.emit("pool", lambda e: e.dma_start(VEC[:], vecs), inc="dqw", inc_by=16)
        wx0 = lambda c: VEC[:, 0 * NCD + c:0 * NCD + c + 1]
        wdt = lambda c: VEC[:, 1 * NCD + c:1 * NCD + c + 1]
        bdt = lambda c: VEC[:, 2 * NCD + c:2 * NCD + c + 1]
        gam = lambda c: VEC[:, 3 * NCD + c:3 * NCD + c + 1]
        bet = lambda c: VEC[:, 4 * NCD + c:4 * NCD + c + 1]

        loads_tok = [("dqx", P.cnt["dqx"]), ("dqw", P.cnt["dqw"]), P.tok("dve")]

        prev = P.stage("dve", [lambda e: e.engine_nop()], waits=loads_tok)

        for it in range(NTILE):
            t0 = it * TT
            prev = emit_ln(P, TT, prev, RX, t0, T, NCD, 1.0 / D,
                           sq, pstat_x, pstat_x2, stat_mu, stat_rsig, stat_tmp,
                           pbc_mu, pbc_rsig, cpool, ones_col, ones_row,
                           out_bf=xhat_b, tmp_f32=w0, add_to_src=True)
            if debug and it == DBG_TILE:
                P.emit("sp", lambda e: e.dma_start(dbg["mu"], stat_mu[:]), waits=[prev], inc="dqo", inc_by=16)
                P.emit("sp", lambda e: e.dma_start(dbg["rsig"], stat_rsig[:]), inc="dqo", inc_by=16)
                prev = P.stage("dve", [lambda e: e.engine_nop()], waits=[P.tok("dqo")])
            for m in range(16):
                pb = pbig0 if m % 2 == 0 else pbig1
                mm = [(lambda e, c=c, m=m, pb=pb: e.matmul(
                    pb[:], WIN[:, c * 2 * D + m * 128: c * 2 * D + (m + 1) * 128],
                    xhat_b[:, c * TT:(c + 1) * TT], start=(c == 0), stop=(c == NCD - 1)))
                    for c in range(NCD)]
                prev = P.stage("pe", mm, waits=[prev])
                if m < 8:
                    prev = P.stage("act", [lambda e, m=m, pb=pb: e.activation(
                        x_inT[:, m * TT:(m + 1) * TT], pb[:], AF.Tanh)], waits=[prev])
                else:
                    prev = P.stage("act", [lambda e, m=m, pb=pb: e.activation(
                        zT[:, (m - 8) * TT:(m - 7) * TT], pb[:], AF.Sigmoid)], waits=[prev])
            mm = [(lambda e, c=c: e.matmul(pstat_x[:], wx0(c), x_inT[:, c * TT:(c + 1) * TT],
                                           start=(c == 0), stop=(c == NCD - 1)))
                  for c in range(NCD)]
            prev = P.stage("pe", mm, waits=[prev])
            prev = P.stage("dve", [lambda e: e.tensor_copy(dtraw[:], pstat_x[:])], waits=[prev])
            prev = P.stage("pe", [lambda e: e.matmul(pdtb[:], ones_row[:], dtraw[:],
                                                     start=True, stop=True)], waits=[prev])
            for c in range(NCD):
                prev = P.stage("act", [
                    lambda e, c=c: e.activation(w0[:], pdtb[:], AF.Sigmoid, scale=wdt(c), bias=bdt(c)),
                    lambda e, c=c: e.activation(w0[:], w0[:], AF.Sigmoid, scale=cpool[:, C_NEG1:C_NEG1 + 1]),
                ], waits=[prev])
                prev = P.stage("dve", [
                    lambda e, c=c: e.tensor_scalar(w1[:], w0[:], -1.0, 1.0, OP.mult, OP.add),
                    lambda e, c=c: e.tensor_tensor(w1[:], w1[:], x_inT[:, c * TT:(c + 1) * TT], OP.mult),
                    lambda e, c=c: e.tensor_tensor_scan(w2[:], w0[:], w1[:], carry[:, c:c + 1], OP.mult, OP.add),
                ], waits=[prev])
                prev = P.stage("dve", [
                    lambda e, c=c: e.tensor_copy(carry[:, c:c + 1], w2[:, TT - 1:TT]),
                    lambda e, c=c: e.tensor_tensor(g8[:, c * TT:(c + 1) * TT], w2[:], zT[:, c * TT:(c + 1) * TT], OP.mult),
                ], waits=[prev])
            if debug and it == DBG_TILE:
                P.emit("sp", lambda e: e.dma_start(dbg["xinT"], x_inT[:]), waits=[prev], inc="dqo", inc_by=16)
                P.emit("sp", lambda e: e.dma_start(dbg["zT"], zT[:]), inc="dqo", inc_by=16)
                P.emit("sp", lambda e: e.dma_start(dbg["g8"], g8[:]), inc="dqo", inc_by=16)
                P.emit("sp", lambda e: e.dma_start(dbg["dtraw"], dtraw[:]), inc="dqo", inc_by=16)
                P.emit("sp", lambda e: e.dma_start(dbg["carry"], carry[:]), inc="dqo", inc_by=16)
                prev = P.stage("dve", [lambda e: e.engine_nop()], waits=[P.tok("dqo")])
            prev = emit_ln(P, TT, prev, g8, 0, TT, NCD, 1.0 / D,
                           sq, pstat_x, pstat_x2, stat_mu, stat_rsig, stat_tmp,
                           pbc_mu, pbc_rsig, cpool, ones_col, ones_row,
                           out_f32=None, out_bf=sq)
            fns = []
            for c in range(NCD):
                fns.append(lambda e, c=c: e.tensor_scalar(
                    sq[:, c * TT:(c + 1) * TT], sq[:, c * TT:(c + 1) * TT],
                    gam(c), bet(c), OP.mult, OP.add))
                fns.append(lambda e, c=c, t0=t0: e.tensor_tensor(
                    RX[:, c * T + t0: c * T + t0 + TT], RX[:, c * T + t0: c * T + t0 + TT],
                    sq[:, c * TT:(c + 1) * TT], OP.add))
            prev = P.stage("dve", fns, waits=[prev])

        for c in range(NCD):
            P.emit("sp", lambda e, c=c: e.dma_start(xout[c * 128:(c + 1) * 128, :],
                                                    RX[:, c * T:(c + 1) * T]),
                   waits=[prev] if c == 0 else (), inc="dqo", inc_by=16)
        P.emit("sp", None, waits=[P.tok("dqo")])
        if ctx is not None:
            ctx["prev"] = prev
            ctx["out_tok"] = P.tok("dqo")
        else:
            P.finalize()
    return nc


def run_l1_attn(x, W_in, W_x, W_dt, b_dt, gamma, beta, debug=False):
    nc = build_l1_attn(debug=debug)
    w_inT = np.ascontiguousarray(W_in.T).astype(np.float32)
    vecs = np.concatenate([
        W_x[0].reshape(NCD, 128).T,
        W_dt[:, 0].reshape(NCD, 128).T,
        b_dt.reshape(NCD, 128).T,
        gamma.reshape(NCD, 128).T,
        beta.reshape(NCD, 128).T,
    ], axis=1).astype(np.float32)
    in_maps = []
    for b in range(N_CORES):
        in_maps.append(dict(xin=np.ascontiguousarray(x[b].T).astype(np.float32),
                            w_inT=w_inT, vecs=vecs))
    res = run_bass_kernel_spmd(nc, in_maps, core_ids=list(range(N_CORES)))
    out = np.stack([np.ascontiguousarray(r["xout"].T) for r in res.results])
    if debug:
        return out, res.results[0]
    return out


def build_l0_ffn(debug=False, ctx=None):
    if ctx is None:
        nc = bass.Bass("TRN2", target_bir_lowering=False)
        _pref = ""
    else:
        nc = ctx["nc"]
        _pref = ctx["pref"]
    H = 4096
    xin = ctx["xin"] if ctx is not None else nc.dram_tensor("xin", [D, T], F32, kind="ExternalInput").ap()
    w1T = nc.dram_tensor(_pref + "w1T", [D, H], F32, kind="ExternalInput").ap()
    w2T = nc.dram_tensor(_pref + "w2T", [H, D], F32, kind="ExternalInput").ap()
    b1c = nc.dram_tensor(_pref + "b1c", [128, 32], F32, kind="ExternalInput").ap()
    b2c = nc.dram_tensor(_pref + "b2c", [128, 8], F32, kind="ExternalInput").ap()
    sclr = nc.dram_tensor(_pref + "sclr", [128, 1], F32, kind="ExternalInput").ap()
    xout = ctx["xout"] if ctx is not None else nc.dram_tensor("xout", [D, T], F32, kind="ExternalOutput").ap()
    e2scr = nc.dram_tensor(_pref + "e2scr", [16, D], BF16, kind="Internal").ap()
    if debug:
        dbg = {k: nc.dram_tensor(_pref + "dbg_" + k, shp, BF16, kind="ExternalOutput").ap()
               for k, shp in [("xhat", [128, 8 * 256]), ("urow", [1, 4 * 256]),
                              ("hact", [128, 32 * 256]), ("t4", [4, 4 * 256]),
                              ("e2", [4, 4 * 256]), ("evn", [1, 4 * 1024]),
                              ("o2", [128, 32 * 16]), ("lt", [128, 8 * 16])]}

    TT = 256
    NTILE = T // TT
    NH = 32

    es = ExitStack()
    S = lambda *a, **k: es.enter_context(nc.sbuf_tensor(_pref + a[0], *a[1:], **k))
    PS = lambda *a, **k: es.enter_context(nc.psum_tensor(_pref + a[0], *a[1:], **k))
    with es:
        RX = S("RX", [128, NCD * T], F32)
        xhat_b = S("xhat_b", [128, NCD * TT], BF16)
        h_act = S("h_act", [128, NH * TT], BF16)
        sq = S("sq", [128, NCD * TT], BF16)
        w0 = S("w0", [128, TT], F32)
        B1 = S("B1", [128, 32], F32)
        B2 = S("B2", [128, 8], F32)
        SCL = S("SCL", [128, 1], F32)
        LT = S("LT", [128, NCD * 16], BF16)
        LTOK = S("LTOK", [16, D], BF16)
        O2 = S("O2", [128, NH * 16], BF16)
        E2 = S("E2", [4, 4 * 256], BF16)
        EVN = S("EVN", [1, 4 * D], BF16)
        IDN = S("IDN", [128, 128], BF16)
        UROW = S("UROW", [1, 4 * TT], BF16)
        T4 = S("T4", [4, 4 * TT], BF16)
        wb1a = S("wb1a", [128, NCD * 512], BF16)
        wb1b = S("wb1b", [128, NCD * 512], BF16)
        wb2a = wb1a.rearrange("p (c n) -> p (c n)", c=1) if False else wb1a
        wb2b = wb1b
        stat_mu = S("stat_mu", [1, TT], F32)
        stat_rsig = S("stat_rsig", [1, TT], F32)
        stat_tmp = S("stat_tmp", [1, TT], F32)

        PP = [PS("P%d" % i, [128, 512], F32) for i in range(7)]
        PTR = PS("PTR", [16, 1024], BF16)
        pb = lambda i: PP[i]

        P = ctx["P"] if ctx is not None else Prog(nc, es)
        _pw = [ctx["prev"]] if ctx is not None and ctx.get("prev") else []
        cpool, ones_col, ones_row, ones_bf, tok0 = emit_common_setup(P, es, nc, extra_waits=_pw, pref=_pref)

        _chunked_dma(P, "sp", "dqx", RX, xin, NCD, T)
        P.emit("pool", lambda e: e.dma_start(B1[:], b1c), waits=_pw, inc="dqw", inc_by=16)
        P.emit("pool", lambda e: e.dma_start(B2[:], b2c), inc="dqw", inc_by=16)
        P.emit("pool", lambda e: e.dma_start(SCL[:], sclr), inc="dqw", inc_by=16)
        loads_tok = [("dqx", P.cnt["dqx"]), ("dqw", P.cnt["dqw"]), P.tok("dve")]
        idn = nc.dram_tensor(_pref + "idn", [128, 128], F32, kind="ExternalInput").ap()
        P.emit("pool", lambda e: e.dma_start(IDN[:], idn), inc="dqw", inc_by=16)
        loads_tok = [("dqx", P.cnt["dqx"]), ("dqw", P.cnt["dqw"]), P.tok("dve")]

        prev = P.stage("dve", [lambda e: e.engine_nop()], waits=loads_tok)

        for it in range(NTILE):
            t0 = it * TT
            prev = emit_ln(P, TT, prev, RX, t0, T, NCD, 1.0 / D,
                           sq, pb(4)[0:1], pb(5)[0:1], stat_mu, stat_rsig, stat_tmp,
                           pb(2), pb(3), cpool, ones_col, ones_row, out_bf=xhat_b,
                           ones_sq=ones_bf)
            if it == 0:
                fns = [(lambda e, c=c: e.tensor_copy(LT[:, c * 16:(c + 1) * 16],
                                                     xhat_b[:, c * TT + 1:c * TT + 17]))
                       for c in range(NCD)]
                prev = P.stage("dve", fns, waits=[prev])
                mm = [(lambda e, c=c: e.transpose(PTR[:, c * 128:(c + 1) * 128],
                                                  LT[:, c * 16:(c + 1) * 16], IDN[:]))
                      for c in range(NCD)]
                prev = P.stage("pe", mm, waits=[prev])
                prev = P.stage("dve", [lambda e: e.tensor_copy(LTOK[:], PTR[:])], waits=[prev])
                fns = [lambda e: e.memset(O2[:], 0.0)]
                for c2 in range(4):
                    for r in range(4):
                        for j in range(NCD):
                            k = c2 * 8 + j
                            fns.append(lambda e, k=k, r=r, c2=c2, j=j: e.tensor_copy(
                                O2[:, k * 16 + r * 4 + c2: k * 16 + r * 4 + c2 + 1],
                                LT[:, j * 16 + 9 + 2 * r: j * 16 + 10 + 2 * r]))
                prev = P.stage("dve", fns, waits=[prev])
                P.emit("pool", lambda e: e.dma_start(e2scr, LTOK[:]), waits=[prev], inc="dqe", inc_by=16)
                for r in range(4):
                    P.emit("pool", lambda e, r=r: e.dma_start(
                        E2[:, r * 256:(r + 1) * 256],
                        e2scr[8 + 2 * r:9 + 2 * r, :].rearrange("a (u c) -> (a c) u", c=4),
                        allow_slow_non_contiguous=True), waits=[("dqe", 16)], inc="dqe", inc_by=16)
                    P.emit("pool", lambda e, r=r: e.dma_start(
                        EVN[:, r * D:(r + 1) * D], e2scr[2 * r:2 * r + 1, :]),
                        inc="dqe", inc_by=16)
                prev = P.stage("dve", [lambda e: e.engine_nop()], waits=[("dqe", P.cnt["dqe"])])

            for r in range(4):
                mm = [(lambda e, c=c, r=r: e.matmul(pb(6)[0:1, :TT],
                                                    LT[:, c * 16 + 1 + 2 * r: c * 16 + 2 + 2 * r],
                                                    xhat_b[:, c * TT:(c + 1) * TT],
                                                    start=(c == 0), stop=(c == NCD - 1)))
                      for c in range(NCD)]
                prev = P.stage("pe", mm, waits=[prev])
                prev = P.stage("dve", [lambda e, r=r: e.tensor_scalar(
                    UROW[:, r * TT:(r + 1) * TT], pb(6)[0:1, :TT], SCL[0:1, 0:1], None, OP.mult)],
                    waits=[prev])

            for mg in range(8):
                wb = wb1a if mg % 2 == 0 else wb1b
                P.emit("pool", lambda e, mg=mg, wb=wb: e.dma_start(
                    wb[:].rearrange("p (c n) -> p c n", c=NCD),
                    w1T.rearrange("(c p) n -> p c n", p=128)[:, :, mg * 512:(mg + 1) * 512]),
                    waits=[prev], inc="dqw1", inc_by=16)
                wtok = ("dqw1", P.cnt["dqw1"])
                for mi in range(4):
                    m = mg * 4 + mi
                    r = m // 8
                    pm = pb(mi % 2)
                    mm = [(lambda e, c=c, mi=mi, wb=wb, pm=pm: e.matmul(
                        pm[:, :TT], wb[:, c * 512 + mi * 128: c * 512 + (mi + 1) * 128],
                        xhat_b[:, c * TT:(c + 1) * TT], start=(c == 0), stop=False))
                          for c in range(NCD)]
                    mm.append(lambda e, r=r, m=m, pm=pm: e.matmul(
                        pm[:, :TT], EVN[:, r * D + (m % 8) * 128: r * D + (m % 8 + 1) * 128],
                        UROW[:, r * TT:(r + 1) * TT], start=False, stop=True))
                    prev = P.stage("pe", mm, waits=[prev, wtok] if mi == 0 else [prev])
                    prev = P.stage("act", [lambda e, m=m, pm=pm: e.activation(
                        h_act[:, m * TT:(m + 1) * TT], pm[:, :TT], AF.Gelu,
                        bias=B1[:, m:m + 1])], waits=[prev])

            for r in range(4):
                mm = [(lambda e, k=k, r=r: e.matmul(pb(6)[0:4, :TT],
                                                    O2[:, k * 16 + r * 4: k * 16 + r * 4 + 4],
                                                    h_act[:, k * TT:(k + 1) * TT],
                                                    start=(k == 0), stop=(k == NH - 1)))
                      for k in range(NH)]
                prev = P.stage("pe", mm, waits=[prev])
                prev = P.stage("dve", [lambda e, r=r: e.tensor_scalar(
                    T4[:, r * TT:(r + 1) * TT], pb(6)[0:4, :TT], SCL[0:4, 0:1], None, OP.mult)],
                    waits=[prev])

            if debug and it == 0:
                for k, tile in [("xhat", xhat_b), ("urow", UROW), ("hact", h_act),
                                ("t4", T4), ("e2", E2), ("evn", EVN), ("o2", O2), ("lt", LT)]:
                    P.emit("sp", lambda e, k=k, tile=tile: e.dma_start(dbg[k], tile[:]),
                           waits=[prev], inc="dqo", inc_by=16)
                prev = P.stage("dve", [lambda e: e.engine_nop()], waits=[P.tok("dqo")])
            for mo in range(8):
                wb = wb2a if mo % 2 == 0 else wb2b
                P.emit("pool", lambda e, mo=mo, wb=wb: e.dma_start(
                    wb[:].rearrange("p (c n) -> p c n", c=NH),
                    w2T.rearrange("(c p) n -> p c n", p=128)[:, :, mo * 128:(mo + 1) * 128]),
                    waits=[prev], inc="dqw2", inc_by=16)
                wtok = ("dqw2", P.cnt["dqw2"])
                pm = pb(mo % 2)
                r = mo // 2
                mm = [(lambda e, k=k, wb=wb, pm=pm: e.matmul(
                    pm[:, :TT], wb[:, k * 128:(k + 1) * 128],
                    h_act[:, k * TT:(k + 1) * TT], start=(k == 0), stop=False))
                      for k in range(NH)]
                mm.append(lambda e, r=r, mo=mo, pm=pm: e.matmul(
                    pm[:, :TT], E2[:, r * 256 + (mo % 2) * 128: r * 256 + (mo % 2 + 1) * 128],
                    T4[:, r * TT:(r + 1) * TT], start=False, stop=True))
                prev = P.stage("pe", mm, waits=[prev, wtok])
                prev = P.stage("dve", [
                    lambda e, mo=mo, pm=pm: e.tensor_scalar(w0[:], pm[:, :TT], B2[:, mo:mo + 1], None, OP.add),
                    lambda e, mo=mo, t0=t0: e.tensor_tensor(
                        RX[:, mo * T + t0: mo * T + t0 + TT], RX[:, mo * T + t0: mo * T + t0 + TT],
                        w0[:], OP.add),
                ], waits=[prev])

        for c in range(NCD):
            P.emit("sp", lambda e, c=c: e.dma_start(xout[c * 128:(c + 1) * 128, :],
                                                    RX[:, c * T:(c + 1) * T]),
                   waits=[prev] if c == 0 else (), inc="dqo", inc_by=16)
        P.emit("sp", None, waits=[P.tok("dqo")])
        if ctx is not None:
            ctx["prev"] = prev
            ctx["out_tok"] = P.tok("dqo")
        else:
            P.finalize()
    return nc


def run_l0_ffn(x, W1, b1, W2, b2, scale, debug=False):
    nc = build_l0_ffn(debug=debug)
    w1T = np.ascontiguousarray(W1.T).astype(np.float32)
    w2T = np.ascontiguousarray(W2.T).astype(np.float32)
    b1c = np.ascontiguousarray(b1.reshape(32, 128).T).astype(np.float32)
    b2c = np.ascontiguousarray(b2.reshape(8, 128).T).astype(np.float32)
    sclr = np.full((128, 1), np.float32(scale), np.float32)
    idn = np.eye(128, dtype=np.float32)
    in_maps = []
    for b in range(N_CORES):
        in_maps.append(dict(xin=np.ascontiguousarray(x[b].T).astype(np.float32),
                            w1T=w1T, w2T=w2T, b1c=b1c, b2c=b2c, sclr=sclr, idn=idn))
    res = run_bass_kernel_spmd(nc, in_maps, core_ids=list(range(N_CORES)))
    out = np.stack([np.ascontiguousarray(r["xout"].T) for r in res.results])
    if debug:
        return out, res.results[0]
    return out



def build_l0_attn(debug=False, ctx=None):
    if ctx is None:
        nc = bass.Bass("TRN2", target_bir_lowering=False)
        _pref = ""
    else:
        nc = ctx["nc"]
        _pref = ctx["pref"]
    DI = 2048
    NDI = 16
    Q = 64
    xin = ctx["xin"] if ctx is not None else nc.dram_tensor("xin", [D, T], F32, kind="ExternalInput").ap()
    w_in = nc.dram_tensor(_pref + "w_in", [D, 2 * DI], F32, kind="ExternalInput").ap()
    w_abc = nc.dram_tensor(_pref + "w_abc", [DI, 48], F32, kind="ExternalInput").ap()
    w_out = nc.dram_tensor(_pref + "w_out", [DI, D], F32, kind="ExternalInput").ap()
    nwrep = nc.dram_tensor(_pref + "nwrep", [128, DI], F32, kind="ExternalInput").ap()
    trim = nc.dram_tensor(_pref + "trim", [Q, Q], F32, kind="ExternalInput").ap()
    idn = nc.dram_tensor(_pref + "idn", [128, 128], F32, kind="ExternalInput").ap()
    xout = ctx["xout"] if ctx is not None else nc.dram_tensor("xout", [D, T], F32, kind="ExternalOutput").ap()
    if debug:
        dbg = {k: nc.dram_tensor(_pref + "dbg_" + k, shp, F32, kind="ExternalOutput").ap()
               for k, shp in [("abc", [64, 3 * 32]), ("cum", [64, 16]), ("y", [64, 2048]),
                              ("yn", [64, 2048]), ("sst", [16, 2048]), ("xtok", [64, 2 * 2048])]}

    TT = 128
    NTILE = T // TT
    NQ = TT // Q

    es = ExitStack()
    S = lambda *a, **k: es.enter_context(nc.sbuf_tensor(_pref + a[0], *a[1:], **k))
    PS = lambda *a, **k: es.enter_context(nc.psum_tensor(_pref + a[0], *a[1:], **k))
    with es:
        RX = S("RX", [128, NCD * T], F32)
        xhat_b = S("xhat_b", [128, NCD * TT], BF16)
        x_inT = S("x_inT", [128, NDI * TT], BF16)
        XT64 = S("XT64", [64, 2 * DI], BF16)
        SCRB = S("SCRB", [128, 2048], BF16)
        WABC = S("WABC", [128, NDI * 48], BF16)
        NW = S("NW", [128, DI], BF16)
        TRI = S("TRI", [Q, Q], F32)
        IDN = S("IDN", [128, 128], BF16)
        AT = S("AT", [64, 2 * 16], F32)
        BT = S("BT", [64, 2 * 16], F32)
        CTl = S("CTl", [64, 2 * 16], F32)
        E1 = S("E1", [Q, 16], F32)
        E2t = S("E2t", [Q, 16], F32)
        BTq = S("BTq", [Q, 16], BF16)
        CTq = S("CTq", [Q, 16], BF16)
        BTt = S("BTt", [16, Q], BF16)
        CTt = S("CTt", [16, Q], BF16)
        MM = S("MM", [Q, Q], BF16)
        SST = S("SST", [16, DI], BF16)
        EAE = S("EAE", [16, 1], F32)
        GBF = S("GBF", [16, 512], BF16)
        YN = S("YN", [Q, DI], BF16)
        YF = S("YF", [128, NDI * TT], BF16)
        RMS = S("RMS", [Q, 1], F32)
        wba = S("wba", [128, NCD * 512], BF16)
        wbb = S("wbb", [128, NCD * 512], BF16)
        w0 = S("w0", [128, TT], F32)
        w1 = S("w1", [128, TT], F32)
        stat_mu = S("stat_mu", [1, TT], F32)
        stat_rsig = S("stat_rsig", [1, TT], F32)
        stat_tmp = S("stat_tmp", [1, TT], F32)

        PSY = PS("PSY", [Q, DI], F32)
        pb0 = PS("pb0", [128, 512], F32)
        pb1 = PS("pb1", [128, 512], F32)
        pb2 = PS("pb2", [128, 512], F32)
        PTB = PS("PTB", [128, 1024], BF16)

        P = ctx["P"] if ctx is not None else Prog(nc, es)
        _pw = [ctx["prev"]] if ctx is not None and ctx.get("prev") else []
        cpool, ones_col, ones_row, ones_bf, tok0 = emit_common_setup(P, es, nc, extra_waits=_pw, pref=_pref)
        P.stage("dve", [lambda e: e.memset(SST[:], 0.0)])

        _chunked_dma(P, "sp", "dqx", RX, xin, NCD, T)
        for c in range(NDI):
            P.emit("pool", lambda e, c=c: e.dma_start(WABC[:, c * 48:(c + 1) * 48],
                                                      w_abc[c * 128:(c + 1) * 128, :]),
                   waits=_pw if c == 0 else (), inc="dqw", inc_by=16)
        P.emit("pool", lambda e: e.dma_start(NW[:], nwrep), inc="dqw", inc_by=16)
        P.emit("pool", lambda e: e.dma_start(TRI[:], trim), inc="dqw", inc_by=16)
        P.emit("pool", lambda e: e.dma_start(IDN[:], idn), inc="dqw", inc_by=16)
        wa = lambda c: WABC[:, c * 48 + 0: c * 48 + 16]
        wb_ = lambda c: WABC[:, c * 48 + 16: c * 48 + 32]
        wc_ = lambda c: WABC[:, c * 48 + 32: c * 48 + 48]
        loads_tok = [("dqx", P.cnt["dqx"]), ("dqw", P.cnt["dqw"]), P.tok("dve")]

        prev = P.stage("dve", [lambda e: e.engine_nop()], waits=loads_tok)

        for it in range(NTILE):
            t0 = it * TT
            prev = emit_ln(P, TT, prev, RX, t0, T, NCD, 1.0 / D,
                           SCRB, pb2[0:1, 0:TT], pb2[0:1, TT:2 * TT], stat_mu, stat_rsig,
                           stat_tmp, pb0, pb1, cpool, ones_col, ones_row,
                           out_bf=xhat_b, ones_sq=ones_bf)
            for g in range(8):
                wbuf = wba if g % 2 == 0 else wbb
                P.emit("pool", lambda e, g=g, wbuf=wbuf: e.dma_start(
                    wbuf[:].rearrange("p (c n) -> p c n", c=NCD),
                    w_in.rearrange("(c p) n -> p c n", p=128)[:, :, g * 512:(g + 1) * 512]),
                    waits=[prev], inc="dqwi", inc_by=16)
                wtok = ("dqwi", P.cnt["dqwi"])
                for pi in range(2):
                    mp = g * 2 + pi
                    mm = [(lambda e, c=c, pi=pi, wbuf=wbuf: e.matmul(
                        pb0[:, :TT], wbuf[:, c * 512 + pi * 256: c * 512 + pi * 256 + 128],
                        xhat_b[:, c * TT:(c + 1) * TT], start=(c == 0), stop=(c == NCD - 1)))
                          for c in range(NCD)]
                    mm += [(lambda e, c=c, pi=pi, wbuf=wbuf: e.matmul(
                        pb1[:, :TT], wbuf[:, c * 512 + pi * 256 + 128: c * 512 + pi * 256 + 256],
                        xhat_b[:, c * TT:(c + 1) * TT], start=(c == 0), stop=(c == NCD - 1)))
                           for c in range(NCD)]
                    prev = P.stage("pe", mm, waits=[prev, wtok] if pi == 0 else [prev])
                    prev = P.stage("act", [
                        lambda e: e.activation(w0[:, :TT], pb0[:, :TT], AF.Silu),
                        lambda e: e.activation(w1[:, :TT], pb1[:, :TT], AF.Sigmoid),
                    ], waits=[prev])
                    prev = P.stage("dve", [lambda e, mp=mp: e.tensor_tensor(
                        x_inT[:, mp * TT:(mp + 1) * TT], w0[:, :TT], w1[:, :TT], OP.mult)],
                        waits=[prev])
                    mm = [(lambda e, q=q, mp=mp: e.transpose(
                        PTB[0:64, q * 128:(q + 1) * 128],
                        x_inT[:, mp * TT + q * 64: mp * TT + (q + 1) * 64], IDN[:]))
                          for q in range(NQ)]
                    prev = P.stage("pe", mm, waits=[prev])
                    prev = P.stage("dve", [(lambda e, q=q, mp=mp: e.tensor_copy(
                        XT64[:, q * DI + mp * 128: q * DI + (mp + 1) * 128],
                        PTB[0:64, q * 128:(q + 1) * 128])) for q in range(NQ)], waits=[prev])
            for q in range(NQ):
                mm = []
                for proj, pbx in ((wa, pb0), (wb_, pb1), (wc_, pb2)):
                    for c in range(NDI):
                        mm.append(lambda e, c=c, q=q, proj=proj, pbx=pbx: e.matmul(
                            pbx[0:64, 0:16], x_inT[:, c * TT + q * 64: c * TT + (q + 1) * 64],
                            proj(c), start=(c == 0), stop=(c == NDI - 1)))
                prev = P.stage("pe", mm, waits=[prev])
                prev = P.stage("dve", [
                    lambda e, q=q: e.tensor_scalar(AT[:, q * 16:(q + 1) * 16], pb0[0:64, 0:16],
                                                   0.0, -5.0, OP.min, OP.max),
                    lambda e, q=q: e.tensor_copy(BT[:, q * 16:(q + 1) * 16], pb1[0:64, 0:16]),
                    lambda e, q=q: e.tensor_copy(CTl[:, q * 16:(q + 1) * 16], pb2[0:64, 0:16]),
                ], waits=[prev])
                prev = P.stage("act", [lambda e, q=q: e.activation(
                    AT[:, q * 16:(q + 1) * 16], AT[:, q * 16:(q + 1) * 16], AF.Exp)],
                    waits=[prev])
                prev = P.stage("dve", [lambda e, q=q: e.tensor_scalar(
                    AT[:, q * 16:(q + 1) * 16], AT[:, q * 16:(q + 1) * 16], -1.0, None, OP.mult)],
                    waits=[prev])
            if debug and it == 0:
                P.emit("sp", lambda e: e.dma_start(dbg["abc"][:, 0:32], AT[:]), waits=[prev], inc="dqo", inc_by=16)
                P.emit("sp", lambda e: e.dma_start(dbg["abc"][:, 32:64], BT[:]), inc="dqo", inc_by=16)
                P.emit("sp", lambda e: e.dma_start(dbg["abc"][:, 64:96], CTl[:]), inc="dqo", inc_by=16)
                P.emit("pool", lambda e: e.dma_start(dbg["xtok"], XT64[:]), waits=[prev], inc="dqo", inc_by=16)
                prev = P.stage("dve", [lambda e: e.engine_nop()], waits=[P.tok("dqo")])
            for q in range(NQ):
                atq = AT[:, q * 16:(q + 1) * 16]
                btq = BT[:, q * 16:(q + 1) * 16]
                ctq = CTl[:, q * 16:(q + 1) * 16]
                xq = lambda ns, q=q: XT64[:, q * DI + ns * 512: q * DI + (ns + 1) * 512]
                prev = P.stage("pe", [
                    lambda e, atq=atq: e.matmul(pb0[0:64, 0:16], TRI[:], atq, start=True, stop=True),
                    lambda e, atq=atq: e.matmul(pb1[0:16, 0:1], atq, ones_col[0:64, :], start=True, stop=True),
                ], waits=[prev])
                prev = P.stage("act", [
                    lambda e: e.activation(E1[:], pb0[0:64, 0:16], AF.Exp),
                    lambda e: e.activation(E2t[:], pb0[0:64, 0:16], AF.Exp, scale=cpool[0:64, C_NEG1:C_NEG1 + 1]),
                    lambda e: e.activation(EAE[:], pb1[0:16, 0:1], AF.Exp),
                ], waits=[prev])
                prev = P.stage("dve", [
                    lambda e, btq=btq: e.tensor_tensor(BTq[:], btq, E2t[:], OP.mult),
                    lambda e, ctq=ctq: e.tensor_tensor(CTq[:], ctq, E1[:], OP.mult),
                ], waits=[prev])
                prev = P.stage("pe", [
                    lambda e: e.transpose(PTB[0:16, 0:64], BTq[:], IDN[0:64, 0:64]),
                    lambda e: e.transpose(PTB[0:16, 64:128], CTq[:], IDN[0:64, 0:64]),
                ], waits=[prev])
                prev = P.stage("dve", [
                    lambda e: e.tensor_copy(BTt[:], PTB[0:16, 0:64]),
                    lambda e: e.tensor_copy(CTt[:], PTB[0:16, 64:128]),
                ], waits=[prev])
                prev = P.stage("pe", [lambda e: e.matmul(pb0[0:64, 0:64], BTt[:], CTt[:],
                                                         start=True, stop=True)], waits=[prev])
                prev = P.stage("dve", [lambda e: e.tensor_tensor(MM[:], pb0[0:64, 0:64], TRI[:], OP.mult)],
                               waits=[prev])
                mm = []
                for ns in range(4):
                    mm.append(lambda e, ns=ns, xq=xq: e.matmul(
                        PSY[:, ns * 512:(ns + 1) * 512], MM[:], xq(ns), start=True, stop=False))
                    mm.append(lambda e, ns=ns: e.matmul(
                        PSY[:, ns * 512:(ns + 1) * 512], CTt[:], SST[:, ns * 512:(ns + 1) * 512],
                        start=False, stop=True))
                prev = P.stage("pe", mm, waits=[prev])
                for ns in range(4):
                    prev = P.stage("pe", [lambda e, ns=ns, xq=xq: e.matmul(
                        pb2[0:16, 0:512], BTq[:], xq(ns), start=True, stop=True)], waits=[prev])
                    prev = P.stage("dve", [
                        lambda e: e.tensor_scalar(GBF[:], pb2[0:16, 0:512], EAE[0:16, 0:1], None, OP.mult),
                        lambda e, ns=ns: e.tensor_scalar(SST[:, ns * 512:(ns + 1) * 512],
                                                         SST[:, ns * 512:(ns + 1) * 512],
                                                         EAE[0:16, 0:1], None, OP.mult),
                        lambda e, ns=ns: e.tensor_tensor(SST[:, ns * 512:(ns + 1) * 512],
                                                         SST[:, ns * 512:(ns + 1) * 512],
                                                         GBF[:], OP.add),
                    ], waits=[prev])
                if debug and it == 0 and q == 0:
                    P.emit("sp", lambda e: e.dma_start(dbg["cum"], E1[:]), waits=[prev], inc="dqo", inc_by=16)
                    P.emit("pool", lambda e: e.dma_start(dbg["sst"], SST[:]), waits=[prev], inc="dqo", inc_by=16)
                    prev = P.stage("dve", [lambda e: e.engine_nop()], waits=[P.tok("dqo")])
                prev = P.stage("act", [lambda e: e.activation(
                    SCRB[0:64, 0:2048], PSY[:], AF.Square, accum_out=RMS[:])], waits=[prev])
                prev = P.stage("dve", [lambda e: e.tensor_scalar(RMS[:], RMS[:], 1.0 / DI, None, OP.mult)],
                               waits=[prev])
                prev = P.stage("act", [
                    lambda e: e.activation(RMS[:], RMS[:], AF.Ln, bias=cpool[0:64, C_EPS_F32:C_EPS_F32 + 1]),
                    lambda e: e.activation(RMS[:], RMS[:], AF.Exp, scale=cpool[0:64, C_NEGHALF:C_NEGHALF + 1]),
                ], waits=[prev])
                prev = P.stage("dve", [
                    lambda e: e.tensor_scalar(YN[:], PSY[:], RMS[0:64, 0:1], None, OP.mult),
                    lambda e: e.tensor_tensor(YN[:], YN[:], NW[0:64, :], OP.mult),
                ], waits=[prev])
                if debug and it == 0 and q == 0:
                    P.emit("pool", lambda e: e.dma_start(dbg["yn"], YN[:]), waits=[prev], inc="dqo", inc_by=16)
                    prev = P.stage("dve", [lambda e: e.engine_nop()], waits=[P.tok("dqo")])
                for half in range(2):
                    mm = [(lambda e, k=k, half=half: e.transpose(
                        PTB[:, k * 64:(k + 1) * 64],
                        YN[:, (half * 8 + k) * 128:(half * 8 + k + 1) * 128], IDN[0:64, 0:64]))
                          for k in range(8)]
                    prev = P.stage("pe", mm, waits=[prev])
                    prev = P.stage("dve", [(lambda e, k=k, half=half, q=q: e.tensor_copy(
                        YF[:, (half * 8 + k) * TT + q * 64: (half * 8 + k) * TT + (q + 1) * 64],
                        PTB[:, k * 64:(k + 1) * 64])) for k in range(8)], waits=[prev])
            for og in range(4):
                wbuf = wba if og % 2 == 0 else wbb
                P.emit("pool", lambda e, og=og, wbuf=wbuf: e.dma_start(
                    wbuf[:, 0:NDI * 256].rearrange("p (c n) -> p c n", c=NDI),
                    w_out.rearrange("(c p) n -> p c n", p=128)[:, :, og * 256:(og + 1) * 256]),
                    waits=[prev], inc="dqwo", inc_by=16)
                wtok = ("dqwo", P.cnt["dqwo"])
                for oi in range(2):
                    mo = og * 2 + oi
                    pm = pb0 if oi == 0 else pb1
                    mm = [(lambda e, k=k, oi=oi, wbuf=wbuf, pm=pm: e.matmul(
                        pm[:, :TT], wbuf[:, k * 256 + oi * 128: k * 256 + (oi + 1) * 128],
                        YF[:, k * TT:(k + 1) * TT], start=(k == 0), stop=(k == NDI - 1)))
                          for k in range(NDI)]
                    prev = P.stage("pe", mm, waits=[prev, wtok] if oi == 0 else [prev])
                    prev = P.stage("dve", [lambda e, mo=mo, t0=t0, pm=pm: e.tensor_tensor(
                        RX[:, mo * T + t0: mo * T + t0 + TT], RX[:, mo * T + t0: mo * T + t0 + TT],
                        pm[:, :TT], OP.add)], waits=[prev])

        for c in range(NCD):
            P.emit("sp", lambda e, c=c: e.dma_start(xout[c * 128:(c + 1) * 128, :],
                                                    RX[:, c * T:(c + 1) * T]),
                   waits=[prev] if c == 0 else (), inc="dqo", inc_by=16)
        P.emit("sp", None, waits=[P.tok("dqo")])
        if ctx is not None:
            ctx["prev"] = prev
            ctx["out_tok"] = P.tok("dqo")
        else:
            P.finalize()
    return nc


def run_l0_attn(x, W_in, W_A, W_B, W_C, W_out, norm_w, debug=False):
    nc = build_l0_attn(debug=debug)
    DI = 2048
    wi = W_in.reshape(1024, 32, 128)
    order = []
    for mp in range(16):
        order += [mp, mp + 16]
    w_in = np.ascontiguousarray(wi[:, order, :].reshape(1024, 4096)).astype(np.float32)
    w_abc = np.ascontiguousarray(np.concatenate([W_A, W_B, W_C], axis=1)).astype(np.float32)
    w_outc = np.ascontiguousarray(W_out).astype(np.float32)
    nwrep = np.ascontiguousarray(np.broadcast_to(norm_w[None, :], (128, DI))).astype(np.float32)
    tau = np.arange(64)[:, None]; t = np.arange(64)[None, :]
    trim = (tau <= t).astype(np.float32)
    idn = np.eye(128, dtype=np.float32)
    in_maps = []
    for b in range(N_CORES):
        in_maps.append(dict(xin=np.ascontiguousarray(x[b].T).astype(np.float32),
                            w_in=w_in, w_abc=w_abc, w_out=w_outc, nwrep=nwrep,
                            trim=trim, idn=idn))
    res = run_bass_kernel_spmd(nc, in_maps, core_ids=list(range(N_CORES)))
    out = np.stack([np.ascontiguousarray(r["xout"].T) for r in res.results])
    if debug:
        return out, res.results[0]
    return out



def build_l1_ffn(debug=False, ctx=None, DEPTH_RUN=6, NW_RUN=16, FT_RUN=8):
    if ctx is None:
        nc = bass.Bass("TRN2", target_bir_lowering=False)
        _pref = ""
    else:
        nc = ctx["nc"]
        _pref = ctx["pref"]
    DEPTH = 6
    W = 256
    NW = T // W
    HC = 22
    FST = 512
    xin = ctx["xin"] if ctx is not None else nc.dram_tensor("xin", [D, T], F32, kind="ExternalInput").ap()
    wqkv5 = nc.dram_tensor(_pref + "wqkv5", [DEPTH, D, 5 * D], F32, kind="ExternalInput").ap()
    wo = nc.dram_tensor(_pref + "wo", [DEPTH, D, D], F32, kind="ExternalInput").ap()
    w1p = nc.dram_tensor(_pref + "w1p", [DEPTH, D, 2 * 2816], F32, kind="ExternalInput").ap()
    w2p = nc.dram_tensor(_pref + "w2p", [DEPTH, 2816, D], F32, kind="ExternalInput").ap()
    biasc = nc.dram_tensor(_pref + "biasc", [128, 52 * DEPTH], F32, kind="ExternalInput").ap()
    lngb = nc.dram_tensor(_pref + "lngb", [128, 16 * DEPTH], F32, kind="ExternalInput").ap()
    qscols = nc.dram_tensor(_pref + "qscols", [128, 4 * DEPTH], F32, kind="ExternalInput").ap()
    tabs = nc.dram_tensor(_pref + "tabs", [4 * 128, T], F32, kind="ExternalInput").ap()
    masks = nc.dram_tensor(_pref + "masks", [4 * 128, 2 * W], F32, kind="ExternalInput").ap()
    onbm = nc.dram_tensor(_pref + "onbm", [128, 2], F32, kind="ExternalInput").ap()
    onbtm = nc.dram_tensor(_pref + "onbtm", [2, 128], F32, kind="ExternalInput").ap()
    idn = nc.dram_tensor(_pref + "idn", [128, 128], F32, kind="ExternalInput").ap()
    xout = ctx["xout"] if ctx is not None else nc.dram_tensor("xout", [D, T], F32, kind="ExternalOutput").ap()
    if debug:
        dbg = {k: nc.dram_tensor(_pref + "dbg_" + k, shp, F32, kind="ExternalOutput").ap()
               for k, shp in [("qr", [128, 8 * W]), ("kr", [128, 8 * W]),
                              ("att", [128, 2 * W]), ("outf", [128, 8 * W]),
                              ("xw1", [128, W]), ("nsa", [128, 1]), ("nsb", [128, 1]), ("nmx", [128, 1]), ("sim", [128, 2 * W]), ("attp", [128, 2 * W]), ("xn", [128, 8 * FST]),
                              ("act", [128, HC * FST])]}

    es = ExitStack()
    S = lambda *a, **k: es.enter_context(nc.sbuf_tensor(_pref + a[0], *a[1:], **k))
    PS = lambda *a, **k: es.enter_context(nc.psum_tensor(_pref + a[0], *a[1:], **k))
    with es:
        XW = S("XW", [128, NCD * T], F32)
        ARN = S("ARN", [128, 23104], BF16)
        XS = lambda: ARN[:, 0:8 * 264]
        QT = lambda: ARN[:, 2112:2112 + 8 * 256]
        QST = lambda: ARN[:, 4160:4160 + 8 * 256]
        KT = lambda: ARN[:, 6208:6208 + 8 * 256]
        KST = lambda: ARN[:, 8256:8256 + 8 * 256]
        KR = lambda: ARN[:, 10304:10304 + 8 * 512]
        VR = lambda: ARN[:, 14400:14400 + 2 * 2 * 1024]
        OUTF = lambda: ARN[:, 18496:18496 + 8 * 256]
        ATT = lambda: ARN[:, 20544:20544 + 4 * 128]
        SQB = lambda: ARN[:, 21056:21056 + 512]
        INVB = lambda: ARN[:, 21568:21568 + 512]
        TAB = lambda: ARN[:, 22080:22080 + 4 * 256]
        ATL = lambda: ARN[:, 0:512]
        XSF = lambda: ARN[:, 0:8 * 520]
        XN = lambda: ARN[:, 4160:4160 + 8 * 512]
        ACT = lambda: ARN[:, 8256:8256 + HC * 512]
        GEL = lambda: ARN[:, 19584:19584 + 1024]
        MASKS = S("MASKS", [128, 2 * 2 * W], F32)
        IDN_BF = S("IDN_BF", [128, 128], BF16)
        INV2B = S("INV2B", [2, 256], BF16)
        ONB = S("ONB", [128, 2], BF16)
        ONBT = S("ONBT", [2, 128], BF16)
        BIAS = S("BIAS", [128, 52 * DEPTH], F32)
        LNGB = S("LNGB", [128, 16 * DEPTH], F32)
        QSC = S("QSC", [128, 4 * DEPTH], F32)
        wba = S("wba", [128, NCD * 512], BF16)
        wbb = S("wbb", [128, NCD * 512], BF16)
        wbc = wba
        wbd = wbb
        w0 = S("w0", [128, 512], F32)
        w1 = S("w1", [128, 512], F32)
        nmax = S("nmax", [128, 1], F32)
        nsum = S("nsum", [128, 1], F32)
        inv2 = S("inv2", [2, 256], F32)
        stat_mu = S("stat_mu", [1, FST], F32)
        stat_rsig = S("stat_rsig", [1, FST], F32)
        stat_tmp = S("stat_tmp", [1, FST], F32)

        pb0 = PS("pb0", [128, 512], F32)
        pb1 = PS("pb1", [128, 512], F32)
        pb2 = PS("pb2", [128, 512], F32)
        pb3 = PS("pb3", [128, 512], F32)
        PTB = PS("PTB", [128, 1024], BF16)

        P = ctx["P"] if ctx is not None else Prog(nc, es)
        _pw = [ctx["prev"]] if ctx is not None and ctx.get("prev") else []
        cpool, ones_col, ones_row, ones_bf, tok0 = emit_common_setup(P, es, nc, extra_waits=_pw, pref=_pref)

        P.emit("pool", lambda e: e.dma_start(ONB[:], onbm), waits=_pw, inc="dqw", inc_by=16)
        P.emit("pool", lambda e: e.dma_start(ONBT[:], onbtm), inc="dqw", inc_by=16)
        P.emit("pool", lambda e: e.dma_start(BIAS[:], biasc), inc="dqw", inc_by=16)
        P.emit("pool", lambda e: e.dma_start(LNGB[:], lngb), inc="dqw", inc_by=16)
        P.emit("pool", lambda e: e.dma_start(QSC[:], qscols), inc="dqw", inc_by=16)
        P.emit("pool", lambda e: e.dma_start(IDN_BF[:], idn), inc="dqw", inc_by=16)
        loads_tok = [("dqw", P.cnt["dqw"]), P.tok("dve")]
        prev = P.stage("dve", [lambda e: e.engine_nop()], waits=loads_tok)

        import os as _os0
        if _os0.environ.get("SKIP_INITLN"):
            _chunked_dma(P, "sp", "dqx", XW, xin, NCD, T, waits=[prev])
            prev = P.stage("dve", [lambda e: e.engine_nop()], waits=[("dqx", P.cnt["dqx"])])
        for it in range(0 if _os0.environ.get("SKIP_INITLN") else T // 256):
            t0 = it * 256
            mmq = []
            for c in range(NCD):
                P.emit("sp", lambda e, c=c, t0=t0: e.dma_start(
                    w0[:, 0:256] if c % 2 == 0 else w1[:, 0:256],
                    xin[c * 128:(c + 1) * 128, t0:t0 + 256]),
                    waits=[prev], inc="dqx", inc_by=16)
                xt = ("dqx", P.cnt["dqx"])
                src = (w0 if c % 2 == 0 else w1)
                prev = P.stage("act", [lambda e, src=src: e.activation(
                    SQB()[:, 0:256], src[:, 0:256], AF.Square)], waits=[xt, prev])
                prev = P.stage("pe", [
                    lambda e, c=c, src=src: e.matmul(pb2[0:1, 0:256], ones_col[:], src[:, 0:256],
                                                     start=(c == 0), stop=(c == NCD - 1)),
                    lambda e, c=c: e.matmul(pb2[0:1, 256:512], ones_bf[:], SQB()[:, 0:256],
                                            start=(c == 0), stop=(c == NCD - 1)),
                ], waits=[prev])
            prev = P.stage("dve", [
                lambda e: e.tensor_scalar(stat_mu[:, 0:256], pb2[0:1, 0:256], 1.0 / D, None, OP.mult),
                lambda e: e.tensor_scalar(stat_tmp[:, 0:256], pb2[0:1, 256:512], 1.0 / D, None, OP.mult),
                lambda e: e.tensor_tensor(stat_rsig[:, 0:256], stat_mu[:, 0:256], stat_mu[:, 0:256], OP.mult),
                lambda e: e.tensor_tensor(stat_tmp[:, 0:256], stat_tmp[:, 0:256], stat_rsig[:, 0:256], OP.subtract),
            ], waits=[prev])
            prev = P.stage("act", [
                lambda e: e.activation(stat_tmp[:, 0:256], stat_tmp[:, 0:256], AF.Ln, bias=cpool[0:1, C_EPS_LN:C_EPS_LN + 1]),
                lambda e: e.activation(stat_rsig[:, 0:256], stat_tmp[:, 0:256], AF.Exp, scale=cpool[0:1, C_NEGHALF:C_NEGHALF + 1]),
            ], waits=[prev])
            prev = P.stage("pe", [
                lambda e: e.matmul(pb0[:, 0:256], ones_row[:], stat_mu[:, 0:256], start=True, stop=True),
                lambda e: e.matmul(pb1[:, 0:256], ones_row[:], stat_rsig[:, 0:256], start=True, stop=True),
            ], waits=[prev])
            for c in range(NCD):
                P.emit("sp", lambda e, c=c, t0=t0: e.dma_start(
                    w0[:, 0:256] if c % 2 == 0 else w1[:, 0:256],
                    xin[c * 128:(c + 1) * 128, t0:t0 + 256]),
                    waits=[prev], inc="dqx", inc_by=16)
                xt = ("dqx", P.cnt["dqx"])
                src = (w0 if c % 2 == 0 else w1)
                prev = P.stage("dve", [
                    lambda e, c=c, t0=t0, src=src: e.tensor_tensor(
                        XW[:, c * T + t0: c * T + t0 + 256], src[:, 0:256], pb0[:, 0:256], OP.subtract),
                    lambda e, c=c, t0=t0: e.tensor_tensor(
                        XW[:, c * T + t0: c * T + t0 + 256], XW[:, c * T + t0: c * T + t0 + 256],
                        pb1[:, 0:256], OP.mult),
                ], waits=[xt, prev])

        def stream_w(buf, src3, lay, col0, ncols, wsem):
            def fn(e, buf=buf, src3=src3, lay=lay, col0=col0, ncols=ncols):
                return e.dma_start(
                    buf[:, 0:NCD * ncols].rearrange("p (c n) -> p c n", c=NCD),
                    src3[lay].rearrange("(c p) n -> p c n", p=128)[:, :, col0:col0 + ncols])
            P.emit("pool", fn, waits=[prev], inc=wsem, inc_by=16)
            return (wsem, P.cnt[wsem])

        for lay in range(DEPTH_RUN):
            qs_c = QSC[:, lay * 4 + 0: lay * 4 + 1]
            qsw_c = QSC[:, lay * 4 + 1: lay * 4 + 2]
            ks_c = QSC[:, lay * 4 + 2: lay * 4 + 3]
            ksw_c = QSC[:, lay * 4 + 3: lay * 4 + 4]
            prev = P.stage("dve", [lambda e: e.memset(KR()[:], 0.0),
                                   lambda e: e.memset(VR()[:], 0.0)], waits=[prev])
            for w in range(NW_RUN):
                t0 = w * W
                sl = w % 2
                fns = []
                for c in range(NCD):
                    off = 0 if c < 4 else 1
                    if t0 - off < 0:
                        fns.append(lambda e, c=c: e.memset(XS()[:, c * 264:c * 264 + 1], 0.0))
                        fns.append(lambda e, c=c, t0=t0: e.tensor_copy(
                            XS()[:, c * 264 + 1:c * 264 + 256], XW[:, c * T: c * T + 255]))
                    else:
                        fns.append(lambda e, c=c, t0=t0, off=off: e.tensor_copy(
                            XS()[:, c * 264:c * 264 + 256], XW[:, c * T + t0 - off: c * T + t0 + 256 - off]))
                prev = P.stage("dve", fns, waits=[prev])
                import os as _os
                parts = [("q", 0, QT), ("qs", 1024, QST), ("k", 2048, KT), ("ks", 3072, KST)]
                if _os.environ.get("SKIP_QKV"):
                    parts = []
                for pname, col0, dstf in parts:
                    buf = wba if pname in ("q", "k") else wbb
                    for hf in range(2):
                        wt = stream_w(buf, wqkv5, lay, col0 + hf * 512, 512, "dqq")
                        for m4 in range(4):
                            m = hf * 4 + m4
                            pm = pb0 if m % 2 == 0 else pb1
                            mm = [(lambda e, c=c, m4=m4, pm=pm, buf=buf: e.matmul(
                                pm[:, 0:W], buf[:, c * 512 + m4 * 128: c * 512 + (m4 + 1) * 128],
                                XS()[:, c * 264:c * 264 + 256], start=(c == 0), stop=(c == NCD - 1)))
                                  for c in range(NCD)]
                            prev = P.stage("pe", mm, waits=[prev, wt] if m4 == 0 else [prev])
                            prev = P.stage("dve", [lambda e, m=m, pm=pm, dstf=dstf: e.tensor_copy(
                                dstf()[:, m * W:(m + 1) * W], pm[:, 0:W])], waits=[prev])
                for nsl in range(2 if not _os.environ.get("SKIP_QKV") else 0):
                    wt = stream_w(wba, wqkv5, lay, 4096 + nsl * 512, 512, "dqq")
                    for tc in range(2):
                        pm = pb0 if tc == 0 else pb1
                        mm = [(lambda e, c=c, tc=tc, pm=pm: e.matmul(
                            pm[:, 0:512], XS()[:, c * 264 + tc * 128: c * 264 + (tc + 1) * 128],
                            wba[:, c * 512:(c + 1) * 512],
                            start=(c == 0), stop=(c == NCD - 1)))
                              for c in range(NCD)]
                        prev = P.stage("pe", mm, waits=[prev, wt] if tc == 0 else [prev])
                        prev = P.stage("dve", [lambda e, tc=tc, nsl=nsl, sl=sl, pm=pm: e.tensor_copy(
                            VR()[:, (sl * 2 + tc) * 1024 + nsl * 512: (sl * 2 + tc) * 1024 + (nsl + 1) * 512],
                            pm[:, 0:512])], waits=[prev])
                if not _os.environ.get("SKIP_TABS"):
                    P.emit("pool", lambda e, t0=t0: e.dma_start(
                        TAB()[:, 0:256], tabs[0:128, t0:t0 + 256]), waits=[prev], inc="dqt", inc_by=16)
                    P.emit("pool", lambda e, t0=t0: e.dma_start(
                        TAB()[:, 256:512], tabs[128:256, t0:t0 + 256]), inc="dqt", inc_by=16)
                    P.emit("pool", lambda e, t0=t0: e.dma_start(
                        TAB()[:, 512:768], tabs[256:384, t0:t0 + 256]), inc="dqt", inc_by=16)
                    P.emit("pool", lambda e, t0=t0: e.dma_start(
                        TAB()[:, 768:1024], tabs[384:512, t0:t0 + 256]), inc="dqt", inc_by=16)
                    moff = 0 if w > 0 else 2
                    P.emit("pool", lambda e, moff=moff: e.dma_start(
                        MASKS[:, 0:512], masks[moff * 128:(moff + 1) * 128, :]), inc="dqt", inc_by=16)
                    P.emit("pool", lambda e, moff=moff: e.dma_start(
                        MASKS[:, 512:1024], masks[(moff + 1) * 128:(moff + 2) * 128, :]), inc="dqt", inc_by=16)
                    prev = P.stage("dve", [
                        lambda e, qs_c=qs_c: e.tensor_scalar(TAB()[:, 0:256], TAB()[:, 0:256], qs_c, None, OP.mult),
                        lambda e, qsw_c=qsw_c: e.tensor_scalar(TAB()[:, 256:512], TAB()[:, 256:512], qsw_c, None, OP.mult),
                        lambda e, ks_c=ks_c: e.tensor_scalar(TAB()[:, 512:768], TAB()[:, 512:768], ks_c, None, OP.mult),
                        lambda e, ksw_c=ksw_c: e.tensor_scalar(TAB()[:, 768:1024], TAB()[:, 768:1024], ksw_c, None, OP.mult),
                    ], waits=[prev, ("dqt", P.cnt["dqt"])])
                import os as _os
                _skip_rot = _os.environ.get("SKIP_ROT")
                for side, base_t, swap_t, taba, tabb in (() if _skip_rot else (
                        ("q", QT, QST, 0, 256), ("k", KT, KST, 512, 768))):
                    for c in range(NCD):
                        prev = P.stage("act", [lambda e, c=c, base_t=base_t: e.activation(
                            SQB()[:, 0:W], base_t()[:, c * W:(c + 1) * W], AF.Square)], waits=[prev])
                        prev = P.stage("pe", [lambda e: e.matmul(
                            pb2[0:2, 0:W], ONB[:], SQB()[:, 0:W], start=True, stop=True)], waits=[prev])
                        prev = P.stage("act", [lambda e: e.activation(
                            inv2[:, 0:W], pb2[0:2, 0:W], AF.Ln)], waits=[prev])
                        prev = P.stage("act", [lambda e: e.activation(
                            inv2[:, 0:W], inv2[:, 0:W], AF.Exp, scale=cpool[0:2, C_HALF:C_HALF + 1])], waits=[prev])
                        prev = P.stage("dve", [
                            lambda e: e.tensor_scalar(inv2[:, 0:W], inv2[:, 0:W], 1e-12, None, OP.max),
                            lambda e: e.reciprocal(inv2[:, 0:W], inv2[:, 0:W]),
                        ], waits=[prev])
                        prev = P.stage("dve", [
                            lambda e: e.tensor_copy(INV2B[:, 0:W], inv2[:, 0:W]),
                        ], waits=[prev])
                        prev = P.stage("pe", [lambda e: e.matmul(
                            pb2[:, 0:W], ONBT[:], INV2B[:, 0:W], start=True, stop=True)], waits=[prev])
                        prev = P.stage("dve", [lambda e: e.tensor_copy(
                            INVB()[:, 0:W], pb2[:, 0:W])], waits=[prev])
                        prev = P.stage("dve", [
                            lambda e, c=c, base_t=base_t, taba=taba: e.tensor_tensor(
                                base_t()[:, c * W:(c + 1) * W], base_t()[:, c * W:(c + 1) * W],
                                TAB()[:, taba:taba + W], OP.mult),
                            lambda e, c=c, swap_t=swap_t, tabb=tabb: e.tensor_tensor(
                                swap_t()[:, c * W:(c + 1) * W], swap_t()[:, c * W:(c + 1) * W],
                                TAB()[:, tabb:tabb + W], OP.mult),
                            lambda e, c=c, base_t=base_t, swap_t=swap_t: e.tensor_tensor(
                                base_t()[:, c * W:(c + 1) * W], base_t()[:, c * W:(c + 1) * W],
                                swap_t()[:, c * W:(c + 1) * W], OP.add),
                            lambda e, c=c, base_t=base_t: e.tensor_tensor(
                                base_t()[:, c * W:(c + 1) * W], base_t()[:, c * W:(c + 1) * W],
                                INVB()[:, 0:W], OP.mult),
                        ], waits=[prev])
                prev = P.stage("dve", [(lambda e, c=c, sl=sl: e.tensor_copy(
                    KR()[:, c * 512 + sl * 256: c * 512 + (sl + 1) * 256], KT()[:, c * W:(c + 1) * W]))
                    for c in range(NCD)], waits=[prev])
                if debug and lay == 0 and w == 0:
                    P.emit("pool", lambda e: e.dma_start(dbg["qr"], QT()[:]), waits=[prev], inc="dqo", inc_by=16)
                    P.emit("pool", lambda e: e.dma_start(dbg["kr"], KT()[:]), inc="dqo", inc_by=16)
                    prev = P.stage("dve", [lambda e: e.engine_nop()], waits=[P.tok("dqo")])
                import os as _os
                _skip_heads = _os.environ.get("SKIP_HEADS")
                for c in range(NCD if not _skip_heads else 0):
                    for hh in range(2):
                        h = 2 * c + hh
                        rb = hh * 64
                        for mi in range(2):
                            prev = P.stage("pe", [
                                (lambda e, c=c, rb=rb, mi=mi, sl=sl: e.matmul(
                                    pb3[0:128, 0:256],
                                    QT()[rb:rb + 64, c * W + mi * 128: c * W + (mi + 1) * 128],
                                    KR()[rb:rb + 64, c * 512 + (1 - sl) * 256: c * 512 + (2 - sl) * 256],
                                    start=True, stop=True)),
                                (lambda e, c=c, rb=rb, mi=mi, sl=sl: e.matmul(
                                    pb3[0:128, 256:512],
                                    QT()[rb:rb + 64, c * W + mi * 128: c * W + (mi + 1) * 128],
                                    KR()[rb:rb + 64, c * 512 + sl * 256: c * 512 + (sl + 1) * 256],
                                    start=True, stop=True)),
                            ], waits=[prev])
                            prev = P.stage("dve", [
                                lambda e, mi=mi: e.scalar_tensor_tensor(
                                    w0[:, 0:512], pb3[0:128, 0:512], 8.0,
                                    MASKS[:, mi * 512:(mi + 1) * 512], OP.mult, OP.add),
                            ], waits=[prev])
                            prev = P.stage("dve", [
                                lambda e: e.tensor_reduce(nmax[:, 0:1], w0[:, 0:512],
                                                          mybir.AxisListType.X, OP.max, negate=True),
                            ], waits=[prev])
                            if debug and lay == 0 and w == 0 and c == 0 and hh == 0 and mi == 0:
                                P.emit("sp", lambda e: e.dma_start(dbg["sim"], w0[:, 0:512]), waits=[prev], inc="dqo", inc_by=16)
                                prev = P.stage("dve", [lambda e: e.engine_nop()], waits=[P.tok("dqo")])
                            prev = P.stage("act", [lambda e: e.activation(
                                ATL()[:, 0:512], w0[:, 0:512], AF.Exp, bias=nmax[:, 0:1])],
                                waits=[prev])
                            prev = P.stage("dve", [
                                lambda e: e.tensor_reduce(nsum[:, 0:1], ATL()[:, 0:512],
                                                          mybir.AxisListType.X, OP.add),
                            ], waits=[prev])
                            prev = P.stage("dve", [
                                lambda e: e.reciprocal(nsum[:, 0:1], nsum[:, 0:1]),
                            ], waits=[prev])
                            if debug and lay == 0 and w == 0 and c == 0 and hh == 0 and mi == 0:
                                P.emit("sp", lambda e: e.dma_start(dbg["nsa"], nsum[:, 0:1]), waits=[prev], inc="dqo", inc_by=16)
                                P.emit("sp", lambda e: e.dma_start(dbg["nmx"], nmax[:, 0:1]), inc="dqo", inc_by=16)
                                prev = P.stage("dve", [lambda e: e.engine_nop()], waits=[P.tok("dqo")])
                            prev = P.stage("dve", [
                                lambda e: e.tensor_scalar(ATL()[:, 0:512], ATL()[:, 0:512],
                                                          nsum[:, 0:1], None, OP.mult),
                            ], waits=[prev])
                            if debug and lay == 0 and w == 0 and c == 0 and hh == 0 and mi == 0:
                                P.emit("pool", lambda e: e.dma_start(dbg["attp"], ATL()[:, 0:512]), waits=[prev], inc="dqo", inc_by=16)
                                prev = P.stage("dve", [lambda e: e.engine_nop()], waits=[P.tok("dqo")])
                            mm = [(lambda e, kc=kc: e.transpose(
                                PTB[:, kc * 128:(kc + 1) * 128],
                                ATL()[:, kc * 128:(kc + 1) * 128], IDN_BF[:]))
                                  for kc in range(4)]
                            prev = P.stage("pe", mm, waits=[prev])
                            prev = P.stage("dve", [(lambda e, kc=kc: e.tensor_copy(
                                ATT()[:, kc * 128:(kc + 1) * 128], PTB[:, kc * 128:(kc + 1) * 128]))
                                for kc in range(4)], waits=[prev])
                            mm = []
                            for kc in range(4):
                                vsl = (1 - sl) if kc < 2 else sl
                                vtc = kc % 2
                                mm.append(lambda e, kc=kc, vsl=vsl, vtc=vtc, h=h, rb=rb: e.matmul(
                                    pb3[rb:rb + 64, 0:128],
                                    VR()[:, (vsl * 2 + vtc) * 1024 + h * 64: (vsl * 2 + vtc) * 1024 + (h + 1) * 64],
                                    ATT()[:, kc * 128:(kc + 1) * 128],
                                    start=(kc == 0), stop=(kc == 3),
                                    tile_position=(0, 64) if rb == 64 else None))
                            prev = P.stage("pe", mm, waits=[prev])
                            prev = P.stage("dve", [lambda e, c=c, mi=mi, rb=rb: e.tensor_copy(
                                OUTF()[rb:rb + 64, c * W + mi * 128: c * W + (mi + 1) * 128],
                                pb3[rb:rb + 64, 0:128])], waits=[prev])
                if debug and lay == 0 and w == 0:
                    P.emit("pool", lambda e: e.dma_start(dbg["outf"], OUTF()[:]), waits=[prev], inc="dqo", inc_by=16)
                    prev = P.stage("dve", [lambda e: e.engine_nop()], waits=[P.tok("dqo")])
                war_o = {0: prev, 1: prev}
                evtok_o = {}
                first_pe_o = True
                for og in range(2 if not _os.environ.get("SKIP_WO") else 0):
                    buf = wba if og % 2 == 0 else wbb
                    def wfno(e, buf=buf, lay=lay, og=og):
                        return e.dma_start(
                            buf[:, 0:NCD * 512].rearrange("p (c n) -> p c n", c=NCD),
                            wo[lay].rearrange("(c p) n -> p c n", p=128)[:, :, og * 512:(og + 1) * 512])
                    P.emit("pool", wfno, waits=[war_o[og % 2]], inc="dqq", inc_by=16)
                    wt = ("dqq", P.cnt["dqq"])
                    for oi in range(4):
                        mo = og * 4 + oi
                        pm = pb0 if oi % 2 == 0 else pb1
                        mm = [(lambda e, c=c, oi=oi, pm=pm, buf=buf: e.matmul(
                            pm[:, 0:W], buf[:, c * 512 + oi * 128: c * 512 + (oi + 1) * 128],
                            OUTF()[:, c * W:(c + 1) * W], start=(c == 0), stop=(c == NCD - 1)))
                              for c in range(NCD)]
                        w_ = [wt] if oi == 0 else []
                        if first_pe_o:
                            w_.append(prev); first_pe_o = False
                        if evtok_o.get(oi % 2):
                            w_.append(evtok_o[oi % 2])
                        ptok = P.stage("pe", mm, waits=w_)
                        evtok_o[oi % 2] = P.stage("dve", [lambda e, mo=mo, t0=t0, pm=pm: e.tensor_tensor(
                            XW[:, mo * T + t0: mo * T + t0 + W], XW[:, mo * T + t0: mo * T + t0 + W],
                            pm[:, 0:W], OP.add)], waits=[ptok])
                        war_o[og % 2] = ptok
                prev = P.stage("dve", [lambda e: e.engine_nop()],
                               waits=[t for t in evtok_o.values() if t])
                if debug and lay == 0 and w == 0:
                    P.emit("sp", lambda e: e.dma_start(dbg["xw1"], XW[:, 0:256]), waits=[prev], inc="dqo", inc_by=16)
                    prev = P.stage("dve", [lambda e: e.engine_nop()], waits=[P.tok("dqo")])
            for ft in range(FT_RUN):
                t0 = ft * FST
                fns = []
                for c in range(NCD):
                    off = 0 if c < 4 else 1
                    if t0 - off < 0:
                        fns.append(lambda e, c=c: e.memset(XSF()[:, c * 520:c * 520 + 1], 0.0))
                        fns.append(lambda e, c=c: e.tensor_copy(
                            XSF()[:, c * 520 + 1:c * 520 + FST], XW[:, c * T: c * T + FST - 1]))
                    else:
                        fns.append(lambda e, c=c, t0=t0, off=off: e.tensor_copy(
                            XSF()[:, c * 520:c * 520 + FST], XW[:, c * T + t0 - off: c * T + t0 + FST - off]))
                prev = P.stage("dve", fns, waits=[prev])
                prev = emit_ln(P, FST, prev, ARN, 0, 520, NCD, 1.0 / D,
                               ARN[:, 8256:8256 + 8 * 512], pb2[0:1, 0:FST], pb3[0:1, 0:FST],
                               stat_mu, stat_rsig, stat_tmp, pb0, pb1, cpool, ones_col, ones_row,
                               out_bf=XN(), out_cols=512, ones_sq=ones_bf, ones_x=ones_bf,
                               src_cast=w0)
                fns = []
                for c in range(NCD):
                    g_c = LNGB[:, lay * 16 + c: lay * 16 + c + 1]
                    b_c = LNGB[:, lay * 16 + 8 + c: lay * 16 + 8 + c + 1]
                    fns.append(lambda e, c=c, g_c=g_c, b_c=b_c: e.tensor_scalar(
                        XN()[:, c * 512:(c + 1) * 512], XN()[:, c * 512:(c + 1) * 512],
                        g_c, b_c, OP.mult, OP.add))
                prev = P.stage("dve", fns, waits=[prev])
                war_f = {0: prev, 1: prev}
                evtok_f = {}
                geltok = {}
                first_pe_f = True
                for half in range(2):
                    for mg in range(11):
                        gslot = mg % 2
                        buf = wbc if gslot == 0 else wbd
                        def wfn(e, lay=lay, half=half, mg=mg, buf=buf):
                            return e.dma_start(
                                buf[:, 0:NCD * 256].rearrange("p (c n) -> p c n", c=NCD),
                                w1p[lay].rearrange("(c p) n -> p c n", p=128)[:, :, half * 2816 + mg * 256: half * 2816 + (mg + 1) * 256])
                        P.emit("pool", wfn, waits=[war_f[gslot]], inc="dqf", inc_by=16)
                        wt = ("dqf", P.cnt["dqf"])
                        for mi in range(2):
                            m = mg * 2 + mi
                            pm = pb0 if mi == 0 else pb1
                            mm = [(lambda e, c=c, mi=mi, pm=pm, buf=buf: e.matmul(
                                pm[:, 0:512], buf[:, c * 256 + mi * 128: c * 256 + (mi + 1) * 128],
                                XN()[:, c * 512:(c + 1) * 512], start=(c == 0), stop=(c == NCD - 1)))
                                  for c in range(NCD)]
                            w_ = [wt] if mi == 0 else []
                            if first_pe_f:
                                w_.append(prev); first_pe_f = False
                            if evtok_f.get(mi):
                                w_.append(evtok_f[mi])
                            ptok = P.stage("pe", mm, waits=w_)
                            if half == 1:
                                bcol = BIAS[:, lay * 52 + 22 + m: lay * 52 + 23 + m]
                                gslot2 = m % 2
                                gw = [ptok]
                                if geltok.get(gslot2):
                                    gw.append(geltok[gslot2])
                                atok = P.stage("act", [lambda e, m=m, pm=pm, bcol=bcol, gslot2=gslot2: e.activation(
                                    GEL()[:, gslot2 * 512:(gslot2 + 1) * 512], pm[:, 0:512], AF.Gelu, bias=bcol)],
                                    waits=gw)
                                evtok_f[mi] = geltok[gslot2] = P.stage("dve", [lambda e, m=m, gslot2=gslot2: e.tensor_tensor(
                                    ACT()[:, m * 512:(m + 1) * 512], ACT()[:, m * 512:(m + 1) * 512],
                                    GEL()[:, gslot2 * 512:(gslot2 + 1) * 512], OP.mult)], waits=[atok])
                            else:
                                bcol = BIAS[:, lay * 52 + m: lay * 52 + 1 + m]
                                evtok_f[mi] = P.stage("dve", [lambda e, m=m, pm=pm, bcol=bcol: e.tensor_scalar(
                                    ACT()[:, m * 512:(m + 1) * 512], pm[:, 0:512], bcol, None, OP.add)],
                                    waits=[ptok])
                        war_f[gslot] = ptok
                prev = P.stage("dve", [lambda e: e.engine_nop()],
                               waits=[t for t in evtok_f.values() if t] + [t for t in geltok.values() if t])
                war_2 = {0: prev, 1: prev}
                evtok_2 = {}
                first_pe_2 = True
                for mo in range(8):
                    gslot = mo % 2
                    buf = wbc if gslot == 0 else wbd
                    def wfn2(e, lay=lay, mo=mo, buf=buf):
                        return e.dma_start(
                            buf[:, 0:HC * 128].rearrange("p (c n) -> p c n", c=HC),
                            w2p[lay].rearrange("(c p) n -> p c n", p=128)[:, :, mo * 128:(mo + 1) * 128])
                    P.emit("pool", wfn2, waits=[war_2[gslot]], inc="dqf", inc_by=16)
                    wt = ("dqf", P.cnt["dqf"])
                    pm = pb0 if gslot == 0 else pb1
                    wbuf2 = w0 if gslot == 0 else w1
                    w_ = [wt]
                    if first_pe_2:
                        w_.append(prev); first_pe_2 = False
                    if evtok_2.get(gslot):
                        w_.append(evtok_2[gslot])
                    mm = [(lambda e, k=k, pm=pm, buf=buf: e.matmul(
                        pm[:, 0:512], buf[:, k * 128:(k + 1) * 128],
                        ACT()[:, k * 512:(k + 1) * 512], start=(k == 0), stop=(k == HC - 1)))
                          for k in range(HC)]
                    ptok = P.stage("pe", mm, waits=w_)
                    war_2[gslot] = ptok
                    b2col = BIAS[:, lay * 52 + 44 + mo: lay * 52 + 45 + mo]
                    evtok_2[gslot] = P.stage("dve", [
                        lambda e, pm=pm, b2col=b2col, wbuf2=wbuf2: e.tensor_scalar(
                            wbuf2[:, 0:512], pm[:, 0:512], b2col, None, OP.add),
                        lambda e, mo=mo, t0=t0, wbuf2=wbuf2: e.tensor_tensor(
                            XW[:, mo * T + t0: mo * T + t0 + FST],
                            XW[:, mo * T + t0: mo * T + t0 + FST], wbuf2[:, 0:512], OP.add),
                    ], waits=[ptok])
                prev = P.stage("dve", [lambda e: e.engine_nop()],
                               waits=[t for t in evtok_2.values() if t])
                if debug and lay == 0 and ft == 0:
                    P.emit("pool", lambda e: e.dma_start(dbg["xn"], XN()[:]), waits=[prev], inc="dqo", inc_by=16)
                    P.emit("pool", lambda e: e.dma_start(dbg["act"], ACT()[:]), inc="dqo", inc_by=16)
                    prev = P.stage("dve", [lambda e: e.engine_nop()], waits=[P.tok("dqo")])

        for c in range(NCD):
            for pc in range(T // 512):
                buf = w0 if pc % 2 == 0 else w1
                P.emit("sp", lambda e, c=c, pc=pc, buf=buf: e.dma_start(
                    buf[:, 0:512], xin[c * 128:(c + 1) * 128, pc * 512:(pc + 1) * 512]),
                    waits=[prev], inc="dqa", inc_by=16)
                prev = P.stage("dve", [lambda e, c=c, pc=pc, buf=buf: e.tensor_tensor(
                    XW[:, c * T + pc * 512: c * T + (pc + 1) * 512],
                    XW[:, c * T + pc * 512: c * T + (pc + 1) * 512], buf[:, 0:512], OP.add)],
                    waits=[prev, ("dqa", P.cnt["dqa"])])
        for c in range(NCD):
            P.emit("sp", lambda e, c=c: e.dma_start(xout[c * 128:(c + 1) * 128, :],
                                                    XW[:, c * T:(c + 1) * T]),
                   waits=[prev] if c == 0 else (), inc="dqo2", inc_by=16)
        P.emit("sp", None, waits=[P.tok("dqo2")])
        if ctx is not None:
            ctx["prev"] = prev
            ctx["out_tok"] = P.tok("dqo2")
        import os
        if os.environ.get("PROG_STATS"):
            print("SEM COUNTS:", {k: v for k, v in P.cnt.items()})
            print("OPS:", {k: len(v) for k, v in P.ops.items()})
        if ctx is None:
            P.finalize()
    return nc


def run_l1_ffn(x, Wqkv, q_scale, k_scale, Wo, ln_g, ln_b, W1, b1, W2, b2, debug=False, **bk):
    nc = build_l1_ffn(debug=debug, **bk)
    DEPTH = 6
    f32 = np.float32
    sigma = np.concatenate([np.arange(32, 64), np.arange(0, 32)])
    perm1024 = np.concatenate([h * 64 + sigma for h in range(16)])
    wqkv5 = np.empty((DEPTH, 1024, 5 * 1024), f32)
    for i in range(DEPTH):
        q = Wqkv[i][:, 0:1024]; k = Wqkv[i][:, 1024:2048]; v = Wqkv[i][:, 2048:3072]
        wqkv5[i] = np.concatenate([q, q[:, perm1024], k, k[:, perm1024], v], axis=1)
    w1p = np.zeros((DEPTH, 1024, 2 * 2816), f32)
    w1p[:, :, 0:2730] = W1[:, :, 0:2730]
    w1p[:, :, 2816:2816 + 2730] = W1[:, :, 2730:5460]
    w2p = np.zeros((DEPTH, 2816, 1024), f32)
    w2p[:, 0:2730, :] = W2
    biasc = np.zeros((128, 52 * DEPTH), f32)
    lngb = np.zeros((128, 16 * DEPTH), f32)
    qscols = np.zeros((128, 4 * DEPTH), f32)
    for i in range(DEPTH):
        b1a = np.zeros(2816, f32); b1a[0:2730] = b1[i][0:2730]
        b1g = np.zeros(2816, f32); b1g[0:2730] = b1[i][2730:5460]
        biasc[:, i * 52 + 0:i * 52 + 22] = b1a.reshape(22, 128).T
        biasc[:, i * 52 + 22:i * 52 + 44] = b1g.reshape(22, 128).T
        biasc[:, i * 52 + 44:i * 52 + 52] = b2[i].reshape(8, 128).T
        lngb[:, i * 16:i * 16 + 8] = ln_g[i].reshape(8, 128).T
        lngb[:, i * 16 + 8:i * 16 + 16] = ln_b[i].reshape(8, 128).T
        qscols[:, i * 4 + 0] = np.tile(q_scale[i], 2)
        qscols[:, i * 4 + 1] = np.tile(q_scale[i][sigma], 2)
        qscols[:, i * 4 + 2] = np.tile(k_scale[i], 2)
        qscols[:, i * 4 + 3] = np.tile(k_scale[i][sigma], 2)
    DIM_HEAD = 64; WINDOW = 256
    half = np.arange(0, DIM_HEAD, 2, dtype=np.float64)
    inv_freq = 1.0 / 10000.0 ** (half / DIM_HEAD)
    t = np.arange(T, dtype=np.float64)
    freqs = t[:, None] * inv_freq[None, :]
    freqs = np.concatenate([freqs, freqs], -1)
    base_scale = (half + 0.4 * DIM_HEAD) / (1.4 * DIM_HEAD)
    power = (t - T // 2) / (WINDOW // 2)
    scale = base_scale[None, :] ** power[:, None]
    scale = np.concatenate([scale, scale], -1)
    sign = np.where(np.arange(64) < 32, -1.0, 1.0)
    tabA = (np.cos(freqs) * scale).T
    tabB = (np.sin(freqs) * scale).T * sign[:, None]
    tabC = (np.cos(freqs) / scale).T
    tabD = (np.sin(freqs) / scale).T * sign[:, None]
    tabs = np.concatenate([np.tile(tabA, (2, 1)), np.tile(tabB, (2, 1)),
                           np.tile(tabC, (2, 1)), np.tile(tabD, (2, 1))], axis=0).astype(f32)
    masks = np.zeros((512, 512), f32)
    for wflag in range(2):
        for mi in range(2):
            m = np.zeros((128, 512), f32)
            q_off = mi * 128 + np.arange(128)[:, None]
            j = np.arange(512)[None, :]
            allowed = (j >= q_off) & (j <= q_off + 256)
            if wflag == 1:
                allowed &= (j >= 256)
            m[~allowed] = -1e30
            masks[(wflag * 2 + mi) * 128:(wflag * 2 + mi + 1) * 128] = m
    onbm = np.zeros((128, 2), f32)
    onbm[0:64, 0] = 1.0; onbm[64:128, 1] = 1.0
    onbtm = np.ascontiguousarray(onbm.T)
    idn = np.eye(128, dtype=f32)
    in_maps = []
    for b in range(N_CORES):
        in_maps.append(dict(xin=np.ascontiguousarray(x[b].T).astype(f32),
                            wqkv5=wqkv5, wo=np.ascontiguousarray(Wo).astype(f32),
                            w1p=w1p, w2p=w2p, biasc=biasc, lngb=lngb, qscols=qscols,
                            tabs=tabs, masks=masks, onbm=onbm, onbtm=onbtm, idn=idn))
    res = run_bass_kernel_spmd(nc, in_maps, core_ids=list(range(N_CORES)))
    out = np.stack([np.ascontiguousarray(r["xout"].T) for r in res.results])
    if debug:
        return out, res.results[0]
    return out


def _prep_l0_attn(W_in, W_A, W_B, W_C, W_out, norm_w):
    DI = 2048
    wi = W_in.reshape(1024, 32, 128)
    order = []
    for mp in range(16):
        order += [mp, mp + 16]
    w_in = np.ascontiguousarray(wi[:, order, :].reshape(1024, 4096)).astype(np.float32)
    tau = np.arange(64)[:, None]; t = np.arange(64)[None, :]
    return dict(w_in=w_in,
                w_abc=np.ascontiguousarray(np.concatenate([W_A, W_B, W_C], axis=1)).astype(np.float32),
                w_out=np.ascontiguousarray(W_out).astype(np.float32),
                nwrep=np.ascontiguousarray(np.broadcast_to(norm_w[None, :], (128, DI))).astype(np.float32),
                trim=(tau <= t).astype(np.float32),
                idn=np.eye(128, dtype=np.float32))


def _prep_l0_ffn(W1, b1, W2, b2, scale):
    return dict(w1T=np.ascontiguousarray(W1.T).astype(np.float32),
                w2T=np.ascontiguousarray(W2.T).astype(np.float32),
                b1c=np.ascontiguousarray(b1.reshape(32, 128).T).astype(np.float32),
                b2c=np.ascontiguousarray(b2.reshape(8, 128).T).astype(np.float32),
                sclr=np.full((128, 1), np.float32(scale), np.float32),
                idn=np.eye(128, dtype=np.float32))


def _prep_l1_attn(W_in, W_x, W_dt, b_dt, gamma, beta):
    w_inT = np.ascontiguousarray(W_in.T).astype(np.float32)
    vecs = np.concatenate([
        W_x[0].reshape(NCD, 128).T,
        W_dt[:, 0].reshape(NCD, 128).T,
        b_dt.reshape(NCD, 128).T,
        gamma.reshape(NCD, 128).T,
        beta.reshape(NCD, 128).T,
    ], axis=1).astype(np.float32)
    return dict(w_inT=w_inT, vecs=vecs)


def _prep_l1_ffn(Wqkv, q_scale, k_scale, Wo, ln_g, ln_b, W1, b1, W2, b2):
    DEPTH = 6
    f32 = np.float32
    sigma = np.concatenate([np.arange(32, 64), np.arange(0, 32)])
    perm1024 = np.concatenate([h * 64 + sigma for h in range(16)])
    wqkv5 = np.empty((DEPTH, 1024, 5 * 1024), f32)
    for i in range(DEPTH):
        q = Wqkv[i][:, 0:1024]; k = Wqkv[i][:, 1024:2048]; v = Wqkv[i][:, 2048:3072]
        wqkv5[i] = np.concatenate([q, q[:, perm1024], k, k[:, perm1024], v], axis=1)
    w1p = np.zeros((DEPTH, 1024, 2 * 2816), f32)
    w1p[:, :, 0:2730] = W1[:, :, 0:2730]
    w1p[:, :, 2816:2816 + 2730] = W1[:, :, 2730:5460]
    w2p = np.zeros((DEPTH, 2816, 1024), f32)
    w2p[:, 0:2730, :] = W2
    biasc = np.zeros((128, 52 * DEPTH), f32)
    lngb = np.zeros((128, 16 * DEPTH), f32)
    qscols = np.zeros((128, 4 * DEPTH), f32)
    for i in range(DEPTH):
        b1a = np.zeros(2816, f32); b1a[0:2730] = b1[i][0:2730]
        b1g = np.zeros(2816, f32); b1g[0:2730] = b1[i][2730:5460]
        biasc[:, i * 52 + 0:i * 52 + 22] = b1a.reshape(22, 128).T
        biasc[:, i * 52 + 22:i * 52 + 44] = b1g.reshape(22, 128).T
        biasc[:, i * 52 + 44:i * 52 + 52] = b2[i].reshape(8, 128).T
        lngb[:, i * 16:i * 16 + 8] = ln_g[i].reshape(8, 128).T
        lngb[:, i * 16 + 8:i * 16 + 16] = ln_b[i].reshape(8, 128).T
        qscols[:, i * 4 + 0] = np.tile(q_scale[i], 2)
        qscols[:, i * 4 + 1] = np.tile(q_scale[i][sigma], 2)
        qscols[:, i * 4 + 2] = np.tile(k_scale[i], 2)
        qscols[:, i * 4 + 3] = np.tile(k_scale[i][sigma], 2)
    DIM_HEAD = 64; WINDOW = 256
    half = np.arange(0, DIM_HEAD, 2, dtype=np.float64)
    inv_freq = 1.0 / 10000.0 ** (half / DIM_HEAD)
    t = np.arange(T, dtype=np.float64)
    freqs = np.concatenate([t[:, None] * inv_freq[None, :]] * 2, -1)
    base_scale = (half + 0.4 * DIM_HEAD) / (1.4 * DIM_HEAD)
    power = (t - T // 2) / (WINDOW // 2)
    scale = np.concatenate([base_scale[None, :] ** power[:, None]] * 2, -1)
    sign = np.where(np.arange(64) < 32, -1.0, 1.0)
    tabA = (np.cos(freqs) * scale).T
    tabB = (np.sin(freqs) * scale).T * sign[:, None]
    tabC = (np.cos(freqs) / scale).T
    tabD = (np.sin(freqs) / scale).T * sign[:, None]
    tabs = np.concatenate([np.tile(tabA, (2, 1)), np.tile(tabB, (2, 1)),
                           np.tile(tabC, (2, 1)), np.tile(tabD, (2, 1))], axis=0).astype(f32)
    masks = np.zeros((512, 512), f32)
    for wflag in range(2):
        for mi in range(2):
            m = np.zeros((128, 512), f32)
            q_off = mi * 128 + np.arange(128)[:, None]
            j = np.arange(512)[None, :]
            allowed = (j >= q_off) & (j <= q_off + 256)
            if wflag == 1:
                allowed &= (j >= 256)
            m[~allowed] = -1e30
            masks[(wflag * 2 + mi) * 128:(wflag * 2 + mi + 1) * 128] = m
    onbm = np.zeros((128, 2), f32)
    onbm[0:64, 0] = 1.0; onbm[64:128, 1] = 1.0
    return dict(wqkv5=wqkv5, wo=np.ascontiguousarray(Wo).astype(f32), w1p=w1p, w2p=w2p,
                biasc=biasc, lngb=lngb, qscols=qscols, tabs=tabs, masks=masks,
                onbm=onbm, onbtm=np.ascontiguousarray(onbm.T), idn=np.eye(128, dtype=f32))


def build_fused():
    nc = bass.Bass("TRN2", target_bir_lowering=False)
    xin = nc.dram_tensor("xin", [D, T], F32, kind="ExternalInput").ap()
    xout = nc.dram_tensor("xout", [D, T], F32, kind="ExternalOutput").ap()
    xh1 = nc.dram_tensor("xh1", [D, T], F32, kind="Internal").ap()
    xh2 = nc.dram_tensor("xh2", [D, T], F32, kind="Internal").ap()
    xh3 = nc.dram_tensor("xh3", [D, T], F32, kind="Internal").ap()
    es = ExitStack()
    with es:
        P = Prog(nc, es)
        ctx = {"nc": nc, "P": P, "pref": "a_", "xin": xin, "xout": xh1, "prev": None}
        build_l0_attn(ctx=ctx)
        ctx.update(pref="b_", xin=xh1, xout=xh2)
        build_l0_ffn(ctx=ctx)
        ctx.update(pref="c_", xin=xh2, xout=xh3)
        build_l1_attn(ctx=ctx)
        ctx.update(pref="d_", xin=xh3, xout=xout)
        build_l1_ffn(ctx=ctx)
        P.finalize()
    return nc


_FUSED_CACHE = {}


def kernel(**inputs):
    d = inputs
    x = np.asarray(d["L0_attn_x"], np.float32)
    if "nc" not in _FUSED_CACHE:
        _FUSED_CACHE["nc"] = build_fused()
    nc = _FUSED_CACHE["nc"]
    am = _prep_l0_attn(d["L0_attn_W_in"], d["L0_attn_W_A"], d["L0_attn_W_B"],
                       d["L0_attn_W_C"], d["L0_attn_W_out"], d["L0_attn_norm_w"])
    bm = _prep_l0_ffn(d["L0_ffn_W1"], d["L0_ffn_b1"], d["L0_ffn_W2"], d["L0_ffn_b2"],
                      d["L0_ffn_scale"])
    cm = _prep_l1_attn(d["L1_attn_W_in"], d["L1_attn_W_x"], d["L1_attn_W_dt"],
                       d["L1_attn_b_dt"], d["L1_attn_gamma"], d["L1_attn_beta"])
    dm = _prep_l1_ffn(d["L1_ffn_Wqkv"], d["L1_ffn_q_scale"], d["L1_ffn_k_scale"],
                      d["L1_ffn_Wo"], d["L1_ffn_ln_g"], d["L1_ffn_ln_b"],
                      d["L1_ffn_W1"], d["L1_ffn_b1"], d["L1_ffn_W2"], d["L1_ffn_b2"])
    wm = {}
    for pref, mm in (("a_", am), ("b_", bm), ("c_", cm), ("d_", dm)):
        for k, v in mm.items():
            wm[pref + k] = v
    in_maps = []
    for b in range(N_CORES):
        m = dict(wm)
        m["xin"] = np.ascontiguousarray(x[b].T).astype(np.float32)
        in_maps.append(m)
    res = run_bass_kernel_spmd(nc, in_maps, core_ids=list(range(N_CORES)))
    return np.ascontiguousarray(
        np.stack([np.ascontiguousarray(r["xout"].T) for r in res.results]))


# version 06
# speedup vs baseline: 1.1877x; 1.0213x over previous
import numpy as np
from contextlib import ExitStack

import concourse.bass as bass
import concourse.mybir as mybir
from concourse.bass_utils import run_bass_kernel_spmd
from concourse.alu_op_type import AluOpType as OP

F32 = mybir.dt.float32
BF16 = mybir.dt.bfloat16
AF = mybir.ActivationFunctionType
AX = mybir.AxisListType

T = 4096
D = 1024
NCD = 8
N_CORES = 8

C_ONE, C_NEG1, C_NEGHALF, C_EPS_LN, C_EPS_F32, C_ZERO, C_HALF, C_NEG1E30 = range(8)
CPOOL_VALS = [1.0, -1.0, -0.5, 1e-5, float(np.finfo(np.float32).eps), 0.0, 0.5, -1e30]
DBG_TILE = 1


class Prog:
    ENGINES = ("sp", "pool", "act", "dve", "pe")

    def __init__(self, nc, es):
        self.nc = nc
        self.es = es
        self.ops = {e: [] for e in self.ENGINES}
        self.sems = {}
        self.cnt = {}

    def sem(self, name):
        if name not in self.sems:
            self.sems[name] = self.es.enter_context(self.nc.semaphore(name))
            self.cnt[name] = 0
        return self.sems[name]

    def emit(self, eng, fn, waits=(), inc=None, inc_by=1):
        s = None
        if inc is not None:
            s = self.sem(inc)
            self.cnt[inc] += inc_by
        wl = [(self.sem(n), v) for (n, v) in waits]

        def run(e, wl=wl, s=s, inc_by=inc_by, fn=fn):
            for sem, v in wl:
                e.wait_ge(sem, v)
            if fn is not None:
                ins = fn(e)
                if s is not None:
                    ins.then_inc(s, inc_by)
        self.ops[eng].append(run)

    def tok(self, name):
        return (name, self.cnt[name])

    def stage(self, eng, fns, waits=()):
        fns = list(fns)
        assert fns
        for i, fn in enumerate(fns):
            self.emit(eng, fn,
                      waits=waits if i == 0 else (),
                      inc=eng if i == len(fns) - 1 else None)
        return self.tok(eng)

    def finalize(self):
        with self.nc.Block() as block:
            for attr, e in (("sync", "sp"), ("gpsimd", "pool"), ("scalar", "act"),
                            ("vector", "dve"), ("tensor", "pe")):
                ops = self.ops[e]

                def body(eng, ops=ops):
                    for f in ops:
                        f(eng)
                getattr(block, attr)(body)


def _chunked_dma(P, queue, sem_name, dst_tile, src_ap, n_chunks, cols, waits=(), cast=False):
    for c in range(n_chunks):
        def fn(e, c=c):
            return e.dma_start(dst_tile[:, c * cols:(c + 1) * cols],
                               src_ap[c * 128:(c + 1) * 128, :])
        P.emit(queue, fn, waits=waits if c == 0 else (), inc=sem_name, inc_by=16)


def emit_ln(P, t, prev, src_tile, src_off, src_stride, n_chunks, inv_n,
            sq, pstat_x, pstat_x2, stat_mu, stat_rsig, stat_tmp, pbc_mu, pbc_rsig,
            cpool, ones_col, ones_row, out_f32=None, out_bf=None, out_cols=None,
            eps_col=C_EPS_LN, tmp_f32=None, add_to_src=False, ones_sq=None,
            ones_x=None, src_cast=None):
    oc = out_cols if out_cols is not None else t
    if ones_sq is None:
        ones_sq = ones_col
    if ones_x is None:
        ones_x = ones_col
    src = lambda c: src_tile[:, src_off + c * src_stride: src_off + c * src_stride + t]
    prev = P.stage("act", [
        (lambda e, c=c: e.activation(sq[:, c * t:(c + 1) * t], src(c), AF.Square))
        for c in range(n_chunks)], waits=[prev])
    mm = []
    for c in range(n_chunks):
        mm.append(lambda e, c=c: e.matmul(pstat_x[:, :t], ones_x[:], src(c),
                                          start=(c == 0), stop=(c == n_chunks - 1)))
    for c in range(n_chunks):
        mm.append(lambda e, c=c: e.matmul(pstat_x2[:, :t], ones_sq[:], sq[:, c * t:(c + 1) * t],
                                          start=(c == 0), stop=(c == n_chunks - 1)))
    prev = P.stage("pe", mm, waits=[prev])
    prev = P.stage("dve", [
        lambda e: e.tensor_scalar(stat_mu[:, :t], pstat_x[:, :t], inv_n, None, OP.mult),
        lambda e: e.tensor_scalar(stat_tmp[:, :t], pstat_x2[:, :t], inv_n, None, OP.mult),
        lambda e: e.tensor_tensor(stat_rsig[:, :t], stat_mu[:, :t], stat_mu[:, :t], OP.mult),
        lambda e: e.tensor_tensor(stat_tmp[:, :t], stat_tmp[:, :t], stat_rsig[:, :t], OP.subtract),
    ], waits=[prev])
    prev = P.stage("act", [
        lambda e: e.activation(stat_tmp[:, :t], stat_tmp[:, :t], AF.Ln, bias=cpool[0:1, eps_col:eps_col + 1]),
        lambda e: e.activation(stat_rsig[:, :t], stat_tmp[:, :t], AF.Exp, scale=cpool[0:1, C_NEGHALF:C_NEGHALF + 1]),
    ], waits=[prev])
    prev = P.stage("pe", [
        lambda e: e.matmul(pbc_mu[:, :t], ones_row[:], stat_mu[:, :t], start=True, stop=True),
        lambda e: e.matmul(pbc_rsig[:, :t], ones_row[:], stat_rsig[:, :t], start=True, stop=True),
    ], waits=[prev])
    fns = []
    for c in range(n_chunks):
        if tmp_f32 is not None:
            fns.append(lambda e, c=c: e.tensor_tensor(tmp_f32[:, :t], src(c), pbc_mu[:, :t], OP.subtract))
            fns.append(lambda e, c=c: e.tensor_tensor(tmp_f32[:, :t], tmp_f32[:, :t], pbc_rsig[:, :t], OP.mult))
            if out_bf is not None:
                fns.append(lambda e, c=c: e.tensor_copy(out_bf[:, c * oc:c * oc + t], tmp_f32[:, :t]))
            if add_to_src:
                fns.append(lambda e, c=c: e.tensor_tensor(src(c), src(c), tmp_f32[:, :t], OP.add))
        elif out_f32 is not None:
            fns.append(lambda e, c=c: e.tensor_tensor(out_f32[:, c * oc:c * oc + t], src(c), pbc_mu[:, :t], OP.subtract))
            fns.append(lambda e, c=c: e.tensor_tensor(out_f32[:, c * oc:c * oc + t], out_f32[:, c * oc:c * oc + t], pbc_rsig[:, :t], OP.mult))
            if out_bf is not None:
                fns.append(lambda e, c=c: e.tensor_copy(out_bf[:, c * oc:c * oc + t], out_f32[:, c * oc:c * oc + t]))
        elif src_cast is not None:
            fns.append(lambda e, c=c: e.tensor_copy(src_cast[:, :t], src(c)))
            fns.append(lambda e, c=c: e.tensor_tensor(src_cast[:, :t], src_cast[:, :t], pbc_mu[:, :t], OP.subtract))
            fns.append(lambda e, c=c: e.tensor_tensor(out_bf[:, c * oc:c * oc + t], src_cast[:, :t], pbc_rsig[:, :t], OP.mult))
        else:
            fns.append(lambda e, c=c: e.tensor_tensor(out_bf[:, c * oc:c * oc + t], src(c), pbc_mu[:, :t], OP.subtract))
            fns.append(lambda e, c=c: e.tensor_tensor(out_bf[:, c * oc:c * oc + t], out_bf[:, c * oc:c * oc + t], pbc_rsig[:, :t], OP.mult))
    prev = P.stage("dve", fns, waits=[prev])
    return prev


def emit_common_setup(P, es, nc, extra_waits=(), pref=""):
    S = lambda *a, **k: es.enter_context(nc.sbuf_tensor(pref + a[0], *a[1:], **k))
    cpool = S("cpool", [128, len(CPOOL_VALS)], F32)
    ones_col = S("ones_col", [128, 1], F32)
    ones_row = S("ones_row", [1, 128], F32)
    fns = []
    for i, v in enumerate(CPOOL_VALS):
        fns.append(lambda e, i=i, v=v: e.memset(cpool[:, i:i + 1], v))
    ones_bf = S("ones_bf", [128, 1], BF16)
    fns.append(lambda e: e.memset(ones_col[:], 1.0))
    fns.append(lambda e: e.memset(ones_row[:], 1.0))
    fns.append(lambda e: e.memset(ones_bf[:], 1.0))
    tok = P.stage("dve", fns, waits=list(extra_waits))
    return cpool, ones_col, ones_row, ones_bf, tok


def build_l1_attn(debug=False, ctx=None):
    if ctx is None:
        nc = bass.Bass("TRN2", target_bir_lowering=False)
        _pref = ""
    else:
        nc = ctx["nc"]
        _pref = ctx["pref"]
    xin = ctx["xin"] if ctx is not None else nc.dram_tensor("xin", [D, T], F32, kind="ExternalInput").ap()
    w_inT = nc.dram_tensor(_pref + "w_inT", [D, 2 * D], F32, kind="ExternalInput").ap()
    vecs = nc.dram_tensor(_pref + "vecs", [128, 5 * NCD], F32, kind="ExternalInput").ap()
    xout = ctx["xout"] if ctx is not None else nc.dram_tensor("xout", [D, T], F32, kind="ExternalOutput").ap()
    if debug:
        dbg = {k: nc.dram_tensor(_pref + "dbg_" + k, shp, F32, kind="ExternalOutput").ap()
               for k, shp in [("xinT", [128, 8 * 256]), ("zT", [128, 8 * 256]),
                              ("g8", [128, 8 * 256]), ("dtraw", [1, 256]),
                              ("mu", [1, 256]), ("rsig", [1, 256]),
                              ("carry", [128, 8])]}

    TT = 256
    NTILE = T // TT

    es = ExitStack()
    S = lambda *a, **k: es.enter_context(nc.sbuf_tensor(_pref + a[0], *a[1:], **k))
    PS = lambda *a, **k: es.enter_context(nc.psum_tensor(_pref + a[0], *a[1:], **k))
    with es:
        RX = S("RX", [128, NCD * T], F32)
        WIN = S("WIN", [128, NCD * 2 * D], BF16)
        VEC = S("VEC", [128, 5 * NCD], F32)
        xhat_b = S("xhat_b", [128, NCD * TT], BF16)
        sq = S("sq", [128, NCD * TT], F32)
        x_inT = S("x_inT", [128, NCD * TT], F32)
        zT = S("zT", [128, NCD * TT], F32)
        g8 = S("g8", [128, NCD * TT], F32)
        w0 = S("w0", [128, TT], F32)
        w1 = S("w1", [128, TT], F32)
        w2 = S("w2", [128, TT], F32)
        carry = S("carry", [128, NCD], F32)
        stat_mu = S("stat_mu", [1, TT], F32)
        stat_rsig = S("stat_rsig", [1, TT], F32)
        stat_tmp = S("stat_tmp", [1, TT], F32)
        dtraw = S("dtraw", [1, TT], F32)

        pbig0 = PS("pbig0", [128, TT], F32)
        pbig1 = PS("pbig1", [128, TT], F32)
        pstat_x = PS("pstat_x", [1, TT], F32)
        pstat_x2 = PS("pstat_x2", [1, TT], F32)
        pbc_mu = PS("pbc_mu", [128, TT], F32)
        pbc_rsig = PS("pbc_rsig", [128, TT], F32)
        pdtb = PS("pdtb", [128, TT], F32)

        P = ctx["P"] if ctx is not None else Prog(nc, es)
        _pw = [ctx["prev"]] if ctx is not None and ctx.get("prev") else []
        cpool, ones_col, ones_row, ones_bf, tok0 = emit_common_setup(P, es, nc, extra_waits=_pw, pref=_pref)
        P.stage("dve", [lambda e: e.memset(carry[:], 0.0)])

        _chunked_dma(P, "sp", "dqx", RX, xin, NCD, T)
        for c in range(NCD):
            P.emit("pool", lambda e, c=c: e.dma_start(WIN[:, c * 2 * D:(c + 1) * 2 * D],
                                                      w_inT[c * 128:(c + 1) * 128, :]),
                   waits=_pw if c == 0 else (), inc="dqw", inc_by=16)
        P.emit("pool", lambda e: e.dma_start(VEC[:], vecs), inc="dqw", inc_by=16)
        wx0 = lambda c: VEC[:, 0 * NCD + c:0 * NCD + c + 1]
        wdt = lambda c: VEC[:, 1 * NCD + c:1 * NCD + c + 1]
        bdt = lambda c: VEC[:, 2 * NCD + c:2 * NCD + c + 1]
        gam = lambda c: VEC[:, 3 * NCD + c:3 * NCD + c + 1]
        bet = lambda c: VEC[:, 4 * NCD + c:4 * NCD + c + 1]

        loads_tok = [("dqx", P.cnt["dqx"]), ("dqw", P.cnt["dqw"]), P.tok("dve")]

        prev = P.stage("dve", [lambda e: e.engine_nop()], waits=loads_tok)

        for it in range(NTILE):
            t0 = it * TT
            prev = emit_ln(P, TT, prev, RX, t0, T, NCD, 1.0 / D,
                           sq, pstat_x, pstat_x2, stat_mu, stat_rsig, stat_tmp,
                           pbc_mu, pbc_rsig, cpool, ones_col, ones_row,
                           out_bf=xhat_b, tmp_f32=w0, add_to_src=True)
            if debug and it == DBG_TILE:
                P.emit("sp", lambda e: e.dma_start(dbg["mu"], stat_mu[:]), waits=[prev], inc="dqo", inc_by=16)
                P.emit("sp", lambda e: e.dma_start(dbg["rsig"], stat_rsig[:]), inc="dqo", inc_by=16)
                prev = P.stage("dve", [lambda e: e.engine_nop()], waits=[P.tok("dqo")])
            for m in range(16):
                pb = pbig0 if m % 2 == 0 else pbig1
                mm = [(lambda e, c=c, m=m, pb=pb: e.matmul(
                    pb[:], WIN[:, c * 2 * D + m * 128: c * 2 * D + (m + 1) * 128],
                    xhat_b[:, c * TT:(c + 1) * TT], start=(c == 0), stop=(c == NCD - 1)))
                    for c in range(NCD)]
                prev = P.stage("pe", mm, waits=[prev])
                if m < 8:
                    prev = P.stage("act", [lambda e, m=m, pb=pb: e.activation(
                        x_inT[:, m * TT:(m + 1) * TT], pb[:], AF.Tanh)], waits=[prev])
                else:
                    prev = P.stage("act", [lambda e, m=m, pb=pb: e.activation(
                        zT[:, (m - 8) * TT:(m - 7) * TT], pb[:], AF.Sigmoid)], waits=[prev])
            mm = [(lambda e, c=c: e.matmul(pstat_x[:], wx0(c), x_inT[:, c * TT:(c + 1) * TT],
                                           start=(c == 0), stop=(c == NCD - 1)))
                  for c in range(NCD)]
            prev = P.stage("pe", mm, waits=[prev])
            prev = P.stage("dve", [lambda e: e.tensor_copy(dtraw[:], pstat_x[:])], waits=[prev])
            prev = P.stage("pe", [lambda e: e.matmul(pdtb[:], ones_row[:], dtraw[:],
                                                     start=True, stop=True)], waits=[prev])
            for c in range(NCD):
                prev = P.stage("act", [
                    lambda e, c=c: e.activation(w0[:], pdtb[:], AF.Sigmoid, scale=wdt(c), bias=bdt(c)),
                    lambda e, c=c: e.activation(w0[:], w0[:], AF.Sigmoid, scale=cpool[:, C_NEG1:C_NEG1 + 1]),
                ], waits=[prev])
                prev = P.stage("dve", [
                    lambda e, c=c: e.tensor_scalar(w1[:], w0[:], -1.0, 1.0, OP.mult, OP.add),
                    lambda e, c=c: e.tensor_tensor(w1[:], w1[:], x_inT[:, c * TT:(c + 1) * TT], OP.mult),
                    lambda e, c=c: e.tensor_tensor_scan(w2[:], w0[:], w1[:], carry[:, c:c + 1], OP.mult, OP.add),
                ], waits=[prev])
                prev = P.stage("dve", [
                    lambda e, c=c: e.tensor_copy(carry[:, c:c + 1], w2[:, TT - 1:TT]),
                    lambda e, c=c: e.tensor_tensor(g8[:, c * TT:(c + 1) * TT], w2[:], zT[:, c * TT:(c + 1) * TT], OP.mult),
                ], waits=[prev])
            if debug and it == DBG_TILE:
                P.emit("sp", lambda e: e.dma_start(dbg["xinT"], x_inT[:]), waits=[prev], inc="dqo", inc_by=16)
                P.emit("sp", lambda e: e.dma_start(dbg["zT"], zT[:]), inc="dqo", inc_by=16)
                P.emit("sp", lambda e: e.dma_start(dbg["g8"], g8[:]), inc="dqo", inc_by=16)
                P.emit("sp", lambda e: e.dma_start(dbg["dtraw"], dtraw[:]), inc="dqo", inc_by=16)
                P.emit("sp", lambda e: e.dma_start(dbg["carry"], carry[:]), inc="dqo", inc_by=16)
                prev = P.stage("dve", [lambda e: e.engine_nop()], waits=[P.tok("dqo")])
            prev = emit_ln(P, TT, prev, g8, 0, TT, NCD, 1.0 / D,
                           sq, pstat_x, pstat_x2, stat_mu, stat_rsig, stat_tmp,
                           pbc_mu, pbc_rsig, cpool, ones_col, ones_row,
                           out_f32=None, out_bf=sq)
            fns = []
            for c in range(NCD):
                fns.append(lambda e, c=c: e.tensor_scalar(
                    sq[:, c * TT:(c + 1) * TT], sq[:, c * TT:(c + 1) * TT],
                    gam(c), bet(c), OP.mult, OP.add))
                fns.append(lambda e, c=c, t0=t0: e.tensor_tensor(
                    RX[:, c * T + t0: c * T + t0 + TT], RX[:, c * T + t0: c * T + t0 + TT],
                    sq[:, c * TT:(c + 1) * TT], OP.add))
            prev = P.stage("dve", fns, waits=[prev])

        for c in range(NCD):
            P.emit("sp", lambda e, c=c: e.dma_start(xout[c * 128:(c + 1) * 128, :],
                                                    RX[:, c * T:(c + 1) * T]),
                   waits=[prev] if c == 0 else (), inc="dqo", inc_by=16)
        P.emit("sp", None, waits=[P.tok("dqo")])
        if ctx is not None:
            ctx["prev"] = prev
            ctx["out_tok"] = P.tok("dqo")
        else:
            P.finalize()
    return nc


def run_l1_attn(x, W_in, W_x, W_dt, b_dt, gamma, beta, debug=False):
    nc = build_l1_attn(debug=debug)
    w_inT = np.ascontiguousarray(W_in.T).astype(np.float32)
    vecs = np.concatenate([
        W_x[0].reshape(NCD, 128).T,
        W_dt[:, 0].reshape(NCD, 128).T,
        b_dt.reshape(NCD, 128).T,
        gamma.reshape(NCD, 128).T,
        beta.reshape(NCD, 128).T,
    ], axis=1).astype(np.float32)
    in_maps = []
    for b in range(N_CORES):
        in_maps.append(dict(xin=np.ascontiguousarray(x[b].T).astype(np.float32),
                            w_inT=w_inT, vecs=vecs))
    res = run_bass_kernel_spmd(nc, in_maps, core_ids=list(range(N_CORES)))
    out = np.stack([np.ascontiguousarray(r["xout"].T) for r in res.results])
    if debug:
        return out, res.results[0]
    return out


def build_l0_ffn(debug=False, ctx=None):
    if ctx is None:
        nc = bass.Bass("TRN2", target_bir_lowering=False)
        _pref = ""
    else:
        nc = ctx["nc"]
        _pref = ctx["pref"]
    H = 4096
    xin = ctx["xin"] if ctx is not None else nc.dram_tensor("xin", [D, T], F32, kind="ExternalInput").ap()
    w1T = nc.dram_tensor(_pref + "w1T", [D, H], F32, kind="ExternalInput").ap()
    w2T = nc.dram_tensor(_pref + "w2T", [H, D], F32, kind="ExternalInput").ap()
    b1c = nc.dram_tensor(_pref + "b1c", [128, 32], F32, kind="ExternalInput").ap()
    b2c = nc.dram_tensor(_pref + "b2c", [128, 8], F32, kind="ExternalInput").ap()
    sclr = nc.dram_tensor(_pref + "sclr", [128, 1], F32, kind="ExternalInput").ap()
    xout = ctx["xout"] if ctx is not None else nc.dram_tensor("xout", [D, T], F32, kind="ExternalOutput").ap()
    e2scr = nc.dram_tensor(_pref + "e2scr", [16, D], BF16, kind="Internal").ap()
    if debug:
        dbg = {k: nc.dram_tensor(_pref + "dbg_" + k, shp, BF16, kind="ExternalOutput").ap()
               for k, shp in [("xhat", [128, 8 * 256]), ("urow", [1, 4 * 256]),
                              ("hact", [128, 32 * 256]), ("t4", [4, 4 * 256]),
                              ("e2", [4, 4 * 256]), ("evn", [1, 4 * 1024]),
                              ("o2", [128, 32 * 16]), ("lt", [128, 8 * 16])]}

    TT = 256
    NTILE = T // TT
    NH = 32

    es = ExitStack()
    S = lambda *a, **k: es.enter_context(nc.sbuf_tensor(_pref + a[0], *a[1:], **k))
    PS = lambda *a, **k: es.enter_context(nc.psum_tensor(_pref + a[0], *a[1:], **k))
    with es:
        RX = S("RX", [128, NCD * T], F32)
        xhat_b = S("xhat_b", [128, NCD * TT], BF16)
        h_act = S("h_act", [128, NH * TT], BF16)
        sq = S("sq", [128, NCD * TT], BF16)
        w0 = S("w0", [128, TT], F32)
        B1 = S("B1", [128, 32], F32)
        B2 = S("B2", [128, 8], F32)
        SCL = S("SCL", [128, 1], F32)
        LT = S("LT", [128, NCD * 16], BF16)
        LTOK = S("LTOK", [16, D], BF16)
        O2 = S("O2", [128, NH * 16], BF16)
        E2 = S("E2", [4, 4 * 256], BF16)
        EVN = S("EVN", [1, 4 * D], BF16)
        IDN = S("IDN", [128, 128], BF16)
        UROW = S("UROW", [1, 4 * TT], BF16)
        T4 = S("T4", [4, 4 * TT], BF16)
        wb1a = S("wb1a", [128, NCD * 512], BF16)
        wb1b = S("wb1b", [128, NCD * 512], BF16)
        wb2a = wb1a.rearrange("p (c n) -> p (c n)", c=1) if False else wb1a
        wb2b = wb1b
        stat_mu = S("stat_mu", [1, TT], F32)
        stat_rsig = S("stat_rsig", [1, TT], F32)
        stat_tmp = S("stat_tmp", [1, TT], F32)

        PP = [PS("P%d" % i, [128, 512], F32) for i in range(7)]
        PTR = PS("PTR", [16, 1024], BF16)
        pb = lambda i: PP[i]

        P = ctx["P"] if ctx is not None else Prog(nc, es)
        _pw = [ctx["prev"]] if ctx is not None and ctx.get("prev") else []
        cpool, ones_col, ones_row, ones_bf, tok0 = emit_common_setup(P, es, nc, extra_waits=_pw, pref=_pref)

        _chunked_dma(P, "sp", "dqx", RX, xin, NCD, T)
        P.emit("pool", lambda e: e.dma_start(B1[:], b1c), waits=_pw, inc="dqw", inc_by=16)
        P.emit("pool", lambda e: e.dma_start(B2[:], b2c), inc="dqw", inc_by=16)
        P.emit("pool", lambda e: e.dma_start(SCL[:], sclr), inc="dqw", inc_by=16)
        loads_tok = [("dqx", P.cnt["dqx"]), ("dqw", P.cnt["dqw"]), P.tok("dve")]
        idn = nc.dram_tensor(_pref + "idn", [128, 128], F32, kind="ExternalInput").ap()
        P.emit("pool", lambda e: e.dma_start(IDN[:], idn), inc="dqw", inc_by=16)
        loads_tok = [("dqx", P.cnt["dqx"]), ("dqw", P.cnt["dqw"]), P.tok("dve")]

        prev = P.stage("dve", [lambda e: e.engine_nop()], waits=loads_tok)

        for it in range(NTILE):
            t0 = it * TT
            prev = emit_ln(P, TT, prev, RX, t0, T, NCD, 1.0 / D,
                           sq, pb(4)[0:1], pb(5)[0:1], stat_mu, stat_rsig, stat_tmp,
                           pb(2), pb(3), cpool, ones_col, ones_row, out_bf=xhat_b,
                           ones_sq=ones_bf)
            if it == 0:
                fns = [(lambda e, c=c: e.tensor_copy(LT[:, c * 16:(c + 1) * 16],
                                                     xhat_b[:, c * TT + 1:c * TT + 17]))
                       for c in range(NCD)]
                prev = P.stage("dve", fns, waits=[prev])
                mm = [(lambda e, c=c: e.transpose(PTR[:, c * 128:(c + 1) * 128],
                                                  LT[:, c * 16:(c + 1) * 16], IDN[:]))
                      for c in range(NCD)]
                prev = P.stage("pe", mm, waits=[prev])
                prev = P.stage("dve", [lambda e: e.tensor_copy(LTOK[:], PTR[:])], waits=[prev])
                fns = [lambda e: e.memset(O2[:], 0.0)]
                for c2 in range(4):
                    for r in range(4):
                        for j in range(NCD):
                            k = c2 * 8 + j
                            fns.append(lambda e, k=k, r=r, c2=c2, j=j: e.tensor_copy(
                                O2[:, k * 16 + r * 4 + c2: k * 16 + r * 4 + c2 + 1],
                                LT[:, j * 16 + 9 + 2 * r: j * 16 + 10 + 2 * r]))
                prev = P.stage("dve", fns, waits=[prev])
                P.emit("pool", lambda e: e.dma_start(e2scr, LTOK[:]), waits=[prev], inc="dqe", inc_by=16)
                for r in range(4):
                    P.emit("pool", lambda e, r=r: e.dma_start(
                        E2[:, r * 256:(r + 1) * 256],
                        e2scr[8 + 2 * r:9 + 2 * r, :].rearrange("a (u c) -> (a c) u", c=4),
                        allow_slow_non_contiguous=True), waits=[("dqe", 16)], inc="dqe", inc_by=16)
                    P.emit("pool", lambda e, r=r: e.dma_start(
                        EVN[:, r * D:(r + 1) * D], e2scr[2 * r:2 * r + 1, :]),
                        inc="dqe", inc_by=16)
                prev = P.stage("dve", [lambda e: e.engine_nop()], waits=[("dqe", P.cnt["dqe"])])

            for r in range(4):
                mm = [(lambda e, c=c, r=r: e.matmul(pb(6)[0:1, :TT],
                                                    LT[:, c * 16 + 1 + 2 * r: c * 16 + 2 + 2 * r],
                                                    xhat_b[:, c * TT:(c + 1) * TT],
                                                    start=(c == 0), stop=(c == NCD - 1)))
                      for c in range(NCD)]
                prev = P.stage("pe", mm, waits=[prev])
                prev = P.stage("dve", [lambda e, r=r: e.tensor_scalar(
                    UROW[:, r * TT:(r + 1) * TT], pb(6)[0:1, :TT], SCL[0:1, 0:1], None, OP.mult)],
                    waits=[prev])

            for mg in range(8):
                wb = wb1a if mg % 2 == 0 else wb1b
                P.emit("pool", lambda e, mg=mg, wb=wb: e.dma_start(
                    wb[:].rearrange("p (c n) -> p c n", c=NCD),
                    w1T.rearrange("(c p) n -> p c n", p=128)[:, :, mg * 512:(mg + 1) * 512]),
                    waits=[prev], inc="dqw1", inc_by=16)
                wtok = ("dqw1", P.cnt["dqw1"])
                for mi in range(4):
                    m = mg * 4 + mi
                    r = m // 8
                    pm = pb(mi % 2)
                    mm = [(lambda e, c=c, mi=mi, wb=wb, pm=pm: e.matmul(
                        pm[:, :TT], wb[:, c * 512 + mi * 128: c * 512 + (mi + 1) * 128],
                        xhat_b[:, c * TT:(c + 1) * TT], start=(c == 0), stop=False))
                          for c in range(NCD)]
                    mm.append(lambda e, r=r, m=m, pm=pm: e.matmul(
                        pm[:, :TT], EVN[:, r * D + (m % 8) * 128: r * D + (m % 8 + 1) * 128],
                        UROW[:, r * TT:(r + 1) * TT], start=False, stop=True))
                    prev = P.stage("pe", mm, waits=[prev, wtok] if mi == 0 else [prev])
                    prev = P.stage("act", [lambda e, m=m, pm=pm: e.activation(
                        h_act[:, m * TT:(m + 1) * TT], pm[:, :TT], AF.Gelu,
                        bias=B1[:, m:m + 1])], waits=[prev])

            for r in range(4):
                mm = [(lambda e, k=k, r=r: e.matmul(pb(6)[0:4, :TT],
                                                    O2[:, k * 16 + r * 4: k * 16 + r * 4 + 4],
                                                    h_act[:, k * TT:(k + 1) * TT],
                                                    start=(k == 0), stop=(k == NH - 1)))
                      for k in range(NH)]
                prev = P.stage("pe", mm, waits=[prev])
                prev = P.stage("dve", [lambda e, r=r: e.tensor_scalar(
                    T4[:, r * TT:(r + 1) * TT], pb(6)[0:4, :TT], SCL[0:4, 0:1], None, OP.mult)],
                    waits=[prev])

            if debug and it == 0:
                for k, tile in [("xhat", xhat_b), ("urow", UROW), ("hact", h_act),
                                ("t4", T4), ("e2", E2), ("evn", EVN), ("o2", O2), ("lt", LT)]:
                    P.emit("sp", lambda e, k=k, tile=tile: e.dma_start(dbg[k], tile[:]),
                           waits=[prev], inc="dqo", inc_by=16)
                prev = P.stage("dve", [lambda e: e.engine_nop()], waits=[P.tok("dqo")])
            for mo in range(8):
                wb = wb2a if mo % 2 == 0 else wb2b
                P.emit("pool", lambda e, mo=mo, wb=wb: e.dma_start(
                    wb[:].rearrange("p (c n) -> p c n", c=NH),
                    w2T.rearrange("(c p) n -> p c n", p=128)[:, :, mo * 128:(mo + 1) * 128]),
                    waits=[prev], inc="dqw2", inc_by=16)
                wtok = ("dqw2", P.cnt["dqw2"])
                pm = pb(mo % 2)
                r = mo // 2
                mm = [(lambda e, k=k, wb=wb, pm=pm: e.matmul(
                    pm[:, :TT], wb[:, k * 128:(k + 1) * 128],
                    h_act[:, k * TT:(k + 1) * TT], start=(k == 0), stop=False))
                      for k in range(NH)]
                mm.append(lambda e, r=r, mo=mo, pm=pm: e.matmul(
                    pm[:, :TT], E2[:, r * 256 + (mo % 2) * 128: r * 256 + (mo % 2 + 1) * 128],
                    T4[:, r * TT:(r + 1) * TT], start=False, stop=True))
                prev = P.stage("pe", mm, waits=[prev, wtok])
                prev = P.stage("dve", [
                    lambda e, mo=mo, pm=pm: e.tensor_scalar(w0[:], pm[:, :TT], B2[:, mo:mo + 1], None, OP.add),
                    lambda e, mo=mo, t0=t0: e.tensor_tensor(
                        RX[:, mo * T + t0: mo * T + t0 + TT], RX[:, mo * T + t0: mo * T + t0 + TT],
                        w0[:], OP.add),
                ], waits=[prev])

        for c in range(NCD):
            P.emit("sp", lambda e, c=c: e.dma_start(xout[c * 128:(c + 1) * 128, :],
                                                    RX[:, c * T:(c + 1) * T]),
                   waits=[prev] if c == 0 else (), inc="dqo", inc_by=16)
        P.emit("sp", None, waits=[P.tok("dqo")])
        if ctx is not None:
            ctx["prev"] = prev
            ctx["out_tok"] = P.tok("dqo")
        else:
            P.finalize()
    return nc


def run_l0_ffn(x, W1, b1, W2, b2, scale, debug=False):
    nc = build_l0_ffn(debug=debug)
    w1T = np.ascontiguousarray(W1.T).astype(np.float32)
    w2T = np.ascontiguousarray(W2.T).astype(np.float32)
    b1c = np.ascontiguousarray(b1.reshape(32, 128).T).astype(np.float32)
    b2c = np.ascontiguousarray(b2.reshape(8, 128).T).astype(np.float32)
    sclr = np.full((128, 1), np.float32(scale), np.float32)
    idn = np.eye(128, dtype=np.float32)
    in_maps = []
    for b in range(N_CORES):
        in_maps.append(dict(xin=np.ascontiguousarray(x[b].T).astype(np.float32),
                            w1T=w1T, w2T=w2T, b1c=b1c, b2c=b2c, sclr=sclr, idn=idn))
    res = run_bass_kernel_spmd(nc, in_maps, core_ids=list(range(N_CORES)))
    out = np.stack([np.ascontiguousarray(r["xout"].T) for r in res.results])
    if debug:
        return out, res.results[0]
    return out



def build_l0_attn(debug=False, ctx=None):
    if ctx is None:
        nc = bass.Bass("TRN2", target_bir_lowering=False)
        _pref = ""
    else:
        nc = ctx["nc"]
        _pref = ctx["pref"]
    DI = 2048
    NDI = 16
    Q = 64
    xin = ctx["xin"] if ctx is not None else nc.dram_tensor("xin", [D, T], F32, kind="ExternalInput").ap()
    w_in = nc.dram_tensor(_pref + "w_in", [D, 2 * DI], F32, kind="ExternalInput").ap()
    w_abc = nc.dram_tensor(_pref + "w_abc", [DI, 48], F32, kind="ExternalInput").ap()
    w_out = nc.dram_tensor(_pref + "w_out", [DI, D], F32, kind="ExternalInput").ap()
    nwrep = nc.dram_tensor(_pref + "nwrep", [128, DI], F32, kind="ExternalInput").ap()
    trim = nc.dram_tensor(_pref + "trim", [Q, Q], F32, kind="ExternalInput").ap()
    idn = nc.dram_tensor(_pref + "idn", [128, 128], F32, kind="ExternalInput").ap()
    xout = ctx["xout"] if ctx is not None else nc.dram_tensor("xout", [D, T], F32, kind="ExternalOutput").ap()
    if debug:
        dbg = {k: nc.dram_tensor(_pref + "dbg_" + k, shp, F32, kind="ExternalOutput").ap()
               for k, shp in [("abc", [64, 3 * 32]), ("cum", [64, 16]), ("y", [64, 2048]),
                              ("yn", [64, 2048]), ("sst", [16, 2048]), ("xtok", [64, 2 * 2048])]}

    TT = 128
    NTILE = T // TT
    NQ = TT // Q

    es = ExitStack()
    S = lambda *a, **k: es.enter_context(nc.sbuf_tensor(_pref + a[0], *a[1:], **k))
    PS = lambda *a, **k: es.enter_context(nc.psum_tensor(_pref + a[0], *a[1:], **k))
    with es:
        RX = S("RX", [128, NCD * T], F32)
        xhat_b = S("xhat_b", [128, NCD * TT], BF16)
        x_inT = S("x_inT", [128, NDI * TT], BF16)
        XT64 = S("XT64", [64, 2 * DI], BF16)
        SCRB = S("SCRB", [128, 2048], BF16)
        WABC = S("WABC", [128, NDI * 48], BF16)
        NW = S("NW", [128, DI], BF16)
        TRI = S("TRI", [Q, Q], F32)
        IDN = S("IDN", [128, 128], BF16)
        AT = S("AT", [64, 2 * 16], F32)
        BT = S("BT", [64, 2 * 16], F32)
        CTl = S("CTl", [64, 2 * 16], F32)
        E1 = S("E1", [Q, 16], F32)
        E2t = S("E2t", [Q, 16], F32)
        BTq = S("BTq", [Q, 16], BF16)
        CTq = S("CTq", [Q, 16], BF16)
        BTt = S("BTt", [16, Q], BF16)
        CTt = S("CTt", [16, Q], BF16)
        MM = S("MM", [Q, Q], BF16)
        SST = S("SST", [16, DI], BF16)
        EAE = S("EAE", [16, 1], F32)
        GBF = S("GBF", [16, 512], BF16)
        YN = S("YN", [Q, DI], BF16)
        YF = S("YF", [128, NDI * TT], BF16)
        RMS = S("RMS", [Q, 1], F32)
        wba = S("wba", [128, NCD * 512], BF16)
        wbb = S("wbb", [128, NCD * 512], BF16)
        w0 = S("w0", [128, TT], F32)
        w1 = S("w1", [128, TT], F32)
        stat_mu = S("stat_mu", [1, TT], F32)
        stat_rsig = S("stat_rsig", [1, TT], F32)
        stat_tmp = S("stat_tmp", [1, TT], F32)

        PSY = PS("PSY", [Q, DI], F32)
        pb0 = PS("pb0", [128, 512], F32)
        pb1 = PS("pb1", [128, 512], F32)
        pb2 = PS("pb2", [128, 512], F32)
        PTB = PS("PTB", [128, 1024], BF16)

        P = ctx["P"] if ctx is not None else Prog(nc, es)
        _pw = [ctx["prev"]] if ctx is not None and ctx.get("prev") else []
        cpool, ones_col, ones_row, ones_bf, tok0 = emit_common_setup(P, es, nc, extra_waits=_pw, pref=_pref)
        P.stage("dve", [lambda e: e.memset(SST[:], 0.0)])

        _chunked_dma(P, "sp", "dqx", RX, xin, NCD, T)
        for c in range(NDI):
            P.emit("pool", lambda e, c=c: e.dma_start(WABC[:, c * 48:(c + 1) * 48],
                                                      w_abc[c * 128:(c + 1) * 128, :]),
                   waits=_pw if c == 0 else (), inc="dqw", inc_by=16)
        P.emit("pool", lambda e: e.dma_start(NW[:], nwrep), inc="dqw", inc_by=16)
        P.emit("pool", lambda e: e.dma_start(TRI[:], trim), inc="dqw", inc_by=16)
        P.emit("pool", lambda e: e.dma_start(IDN[:], idn), inc="dqw", inc_by=16)
        wa = lambda c: WABC[:, c * 48 + 0: c * 48 + 16]
        wb_ = lambda c: WABC[:, c * 48 + 16: c * 48 + 32]
        wc_ = lambda c: WABC[:, c * 48 + 32: c * 48 + 48]
        loads_tok = [("dqx", P.cnt["dqx"]), ("dqw", P.cnt["dqw"]), P.tok("dve")]

        prev = P.stage("dve", [lambda e: e.engine_nop()], waits=loads_tok)

        for it in range(NTILE):
            t0 = it * TT
            prev = emit_ln(P, TT, prev, RX, t0, T, NCD, 1.0 / D,
                           SCRB, pb2[0:1, 0:TT], pb2[0:1, TT:2 * TT], stat_mu, stat_rsig,
                           stat_tmp, pb0, pb1, cpool, ones_col, ones_row,
                           out_bf=xhat_b, ones_sq=ones_bf)
            for g in range(8):
                wbuf = wba if g % 2 == 0 else wbb
                P.emit("pool", lambda e, g=g, wbuf=wbuf: e.dma_start(
                    wbuf[:].rearrange("p (c n) -> p c n", c=NCD),
                    w_in.rearrange("(c p) n -> p c n", p=128)[:, :, g * 512:(g + 1) * 512]),
                    waits=[prev], inc="dqwi", inc_by=16)
                wtok = ("dqwi", P.cnt["dqwi"])
                for pi in range(2):
                    mp = g * 2 + pi
                    mm = [(lambda e, c=c, pi=pi, wbuf=wbuf: e.matmul(
                        pb0[:, :TT], wbuf[:, c * 512 + pi * 256: c * 512 + pi * 256 + 128],
                        xhat_b[:, c * TT:(c + 1) * TT], start=(c == 0), stop=(c == NCD - 1)))
                          for c in range(NCD)]
                    mm += [(lambda e, c=c, pi=pi, wbuf=wbuf: e.matmul(
                        pb1[:, :TT], wbuf[:, c * 512 + pi * 256 + 128: c * 512 + pi * 256 + 256],
                        xhat_b[:, c * TT:(c + 1) * TT], start=(c == 0), stop=(c == NCD - 1)))
                           for c in range(NCD)]
                    prev = P.stage("pe", mm, waits=[prev, wtok] if pi == 0 else [prev])
                    prev = P.stage("act", [
                        lambda e: e.activation(w0[:, :TT], pb0[:, :TT], AF.Silu),
                        lambda e: e.activation(w1[:, :TT], pb1[:, :TT], AF.Sigmoid),
                    ], waits=[prev])
                    prev = P.stage("dve", [lambda e, mp=mp: e.tensor_tensor(
                        x_inT[:, mp * TT:(mp + 1) * TT], w0[:, :TT], w1[:, :TT], OP.mult)],
                        waits=[prev])
                    mm = [(lambda e, q=q, mp=mp: e.transpose(
                        PTB[0:64, q * 128:(q + 1) * 128],
                        x_inT[:, mp * TT + q * 64: mp * TT + (q + 1) * 64], IDN[:]))
                          for q in range(NQ)]
                    prev = P.stage("pe", mm, waits=[prev])
                    prev = P.stage("dve", [(lambda e, q=q, mp=mp: e.tensor_copy(
                        XT64[:, q * DI + mp * 128: q * DI + (mp + 1) * 128],
                        PTB[0:64, q * 128:(q + 1) * 128])) for q in range(NQ)], waits=[prev])
            for q in range(NQ):
                mm = []
                for proj, pbx in ((wa, pb0), (wb_, pb1), (wc_, pb2)):
                    for c in range(NDI):
                        mm.append(lambda e, c=c, q=q, proj=proj, pbx=pbx: e.matmul(
                            pbx[0:64, 0:16], x_inT[:, c * TT + q * 64: c * TT + (q + 1) * 64],
                            proj(c), start=(c == 0), stop=(c == NDI - 1)))
                prev = P.stage("pe", mm, waits=[prev])
                prev = P.stage("dve", [
                    lambda e, q=q: e.tensor_scalar(AT[:, q * 16:(q + 1) * 16], pb0[0:64, 0:16],
                                                   0.0, -5.0, OP.min, OP.max),
                    lambda e, q=q: e.tensor_copy(BT[:, q * 16:(q + 1) * 16], pb1[0:64, 0:16]),
                    lambda e, q=q: e.tensor_copy(CTl[:, q * 16:(q + 1) * 16], pb2[0:64, 0:16]),
                ], waits=[prev])
                prev = P.stage("act", [lambda e, q=q: e.activation(
                    AT[:, q * 16:(q + 1) * 16], AT[:, q * 16:(q + 1) * 16], AF.Exp)],
                    waits=[prev])
                prev = P.stage("dve", [lambda e, q=q: e.tensor_scalar(
                    AT[:, q * 16:(q + 1) * 16], AT[:, q * 16:(q + 1) * 16], -1.0, None, OP.mult)],
                    waits=[prev])
            if debug and it == 0:
                P.emit("sp", lambda e: e.dma_start(dbg["abc"][:, 0:32], AT[:]), waits=[prev], inc="dqo", inc_by=16)
                P.emit("sp", lambda e: e.dma_start(dbg["abc"][:, 32:64], BT[:]), inc="dqo", inc_by=16)
                P.emit("sp", lambda e: e.dma_start(dbg["abc"][:, 64:96], CTl[:]), inc="dqo", inc_by=16)
                P.emit("pool", lambda e: e.dma_start(dbg["xtok"], XT64[:]), waits=[prev], inc="dqo", inc_by=16)
                prev = P.stage("dve", [lambda e: e.engine_nop()], waits=[P.tok("dqo")])
            for q in range(NQ):
                atq = AT[:, q * 16:(q + 1) * 16]
                btq = BT[:, q * 16:(q + 1) * 16]
                ctq = CTl[:, q * 16:(q + 1) * 16]
                xq = lambda ns, q=q: XT64[:, q * DI + ns * 512: q * DI + (ns + 1) * 512]
                prev = P.stage("pe", [
                    lambda e, atq=atq: e.matmul(pb0[0:64, 0:16], TRI[:], atq, start=True, stop=True),
                    lambda e, atq=atq: e.matmul(pb1[0:16, 0:1], atq, ones_col[0:64, :], start=True, stop=True),
                ], waits=[prev])
                prev = P.stage("act", [
                    lambda e: e.activation(E1[:], pb0[0:64, 0:16], AF.Exp),
                    lambda e: e.activation(E2t[:], pb0[0:64, 0:16], AF.Exp, scale=cpool[0:64, C_NEG1:C_NEG1 + 1]),
                    lambda e: e.activation(EAE[:], pb1[0:16, 0:1], AF.Exp),
                ], waits=[prev])
                prev = P.stage("dve", [
                    lambda e, btq=btq: e.tensor_tensor(BTq[:], btq, E2t[:], OP.mult),
                    lambda e, ctq=ctq: e.tensor_tensor(CTq[:], ctq, E1[:], OP.mult),
                ], waits=[prev])
                prev = P.stage("pe", [
                    lambda e: e.transpose(PTB[0:16, 0:64], BTq[:], IDN[0:64, 0:64]),
                    lambda e: e.transpose(PTB[0:16, 64:128], CTq[:], IDN[0:64, 0:64]),
                ], waits=[prev])
                prev = P.stage("dve", [
                    lambda e: e.tensor_copy(BTt[:], PTB[0:16, 0:64]),
                    lambda e: e.tensor_copy(CTt[:], PTB[0:16, 64:128]),
                ], waits=[prev])
                prev = P.stage("pe", [lambda e: e.matmul(pb0[0:64, 0:64], BTt[:], CTt[:],
                                                         start=True, stop=True)], waits=[prev])
                prev = P.stage("dve", [lambda e: e.tensor_tensor(MM[:], pb0[0:64, 0:64], TRI[:], OP.mult)],
                               waits=[prev])
                mm = []
                for ns in range(4):
                    mm.append(lambda e, ns=ns, xq=xq: e.matmul(
                        PSY[:, ns * 512:(ns + 1) * 512], MM[:], xq(ns), start=True, stop=False))
                    mm.append(lambda e, ns=ns: e.matmul(
                        PSY[:, ns * 512:(ns + 1) * 512], CTt[:], SST[:, ns * 512:(ns + 1) * 512],
                        start=False, stop=True))
                prev = P.stage("pe", mm, waits=[prev])
                for ns in range(4):
                    prev = P.stage("pe", [lambda e, ns=ns, xq=xq: e.matmul(
                        pb2[0:16, 0:512], BTq[:], xq(ns), start=True, stop=True)], waits=[prev])
                    prev = P.stage("dve", [
                        lambda e: e.tensor_scalar(GBF[:], pb2[0:16, 0:512], EAE[0:16, 0:1], None, OP.mult),
                        lambda e, ns=ns: e.tensor_scalar(SST[:, ns * 512:(ns + 1) * 512],
                                                         SST[:, ns * 512:(ns + 1) * 512],
                                                         EAE[0:16, 0:1], None, OP.mult),
                        lambda e, ns=ns: e.tensor_tensor(SST[:, ns * 512:(ns + 1) * 512],
                                                         SST[:, ns * 512:(ns + 1) * 512],
                                                         GBF[:], OP.add),
                    ], waits=[prev])
                if debug and it == 0 and q == 0:
                    P.emit("sp", lambda e: e.dma_start(dbg["cum"], E1[:]), waits=[prev], inc="dqo", inc_by=16)
                    P.emit("pool", lambda e: e.dma_start(dbg["sst"], SST[:]), waits=[prev], inc="dqo", inc_by=16)
                    prev = P.stage("dve", [lambda e: e.engine_nop()], waits=[P.tok("dqo")])
                prev = P.stage("act", [lambda e: e.activation(
                    SCRB[0:64, 0:2048], PSY[:], AF.Square, accum_out=RMS[:])], waits=[prev])
                prev = P.stage("dve", [lambda e: e.tensor_scalar(RMS[:], RMS[:], 1.0 / DI, None, OP.mult)],
                               waits=[prev])
                prev = P.stage("act", [
                    lambda e: e.activation(RMS[:], RMS[:], AF.Ln, bias=cpool[0:64, C_EPS_F32:C_EPS_F32 + 1]),
                    lambda e: e.activation(RMS[:], RMS[:], AF.Exp, scale=cpool[0:64, C_NEGHALF:C_NEGHALF + 1]),
                ], waits=[prev])
                prev = P.stage("dve", [
                    lambda e: e.tensor_scalar(YN[:], PSY[:], RMS[0:64, 0:1], None, OP.mult),
                    lambda e: e.tensor_tensor(YN[:], YN[:], NW[0:64, :], OP.mult),
                ], waits=[prev])
                if debug and it == 0 and q == 0:
                    P.emit("pool", lambda e: e.dma_start(dbg["yn"], YN[:]), waits=[prev], inc="dqo", inc_by=16)
                    prev = P.stage("dve", [lambda e: e.engine_nop()], waits=[P.tok("dqo")])
                for half in range(2):
                    mm = [(lambda e, k=k, half=half: e.transpose(
                        PTB[:, k * 64:(k + 1) * 64],
                        YN[:, (half * 8 + k) * 128:(half * 8 + k + 1) * 128], IDN[0:64, 0:64]))
                          for k in range(8)]
                    prev = P.stage("pe", mm, waits=[prev])
                    prev = P.stage("dve", [(lambda e, k=k, half=half, q=q: e.tensor_copy(
                        YF[:, (half * 8 + k) * TT + q * 64: (half * 8 + k) * TT + (q + 1) * 64],
                        PTB[:, k * 64:(k + 1) * 64])) for k in range(8)], waits=[prev])
            for og in range(4):
                wbuf = wba if og % 2 == 0 else wbb
                P.emit("pool", lambda e, og=og, wbuf=wbuf: e.dma_start(
                    wbuf[:, 0:NDI * 256].rearrange("p (c n) -> p c n", c=NDI),
                    w_out.rearrange("(c p) n -> p c n", p=128)[:, :, og * 256:(og + 1) * 256]),
                    waits=[prev], inc="dqwo", inc_by=16)
                wtok = ("dqwo", P.cnt["dqwo"])
                for oi in range(2):
                    mo = og * 2 + oi
                    pm = pb0 if oi == 0 else pb1
                    mm = [(lambda e, k=k, oi=oi, wbuf=wbuf, pm=pm: e.matmul(
                        pm[:, :TT], wbuf[:, k * 256 + oi * 128: k * 256 + (oi + 1) * 128],
                        YF[:, k * TT:(k + 1) * TT], start=(k == 0), stop=(k == NDI - 1)))
                          for k in range(NDI)]
                    prev = P.stage("pe", mm, waits=[prev, wtok] if oi == 0 else [prev])
                    prev = P.stage("dve", [lambda e, mo=mo, t0=t0, pm=pm: e.tensor_tensor(
                        RX[:, mo * T + t0: mo * T + t0 + TT], RX[:, mo * T + t0: mo * T + t0 + TT],
                        pm[:, :TT], OP.add)], waits=[prev])

        for c in range(NCD):
            P.emit("sp", lambda e, c=c: e.dma_start(xout[c * 128:(c + 1) * 128, :],
                                                    RX[:, c * T:(c + 1) * T]),
                   waits=[prev] if c == 0 else (), inc="dqo", inc_by=16)
        P.emit("sp", None, waits=[P.tok("dqo")])
        if ctx is not None:
            ctx["prev"] = prev
            ctx["out_tok"] = P.tok("dqo")
        else:
            P.finalize()
    return nc


def run_l0_attn(x, W_in, W_A, W_B, W_C, W_out, norm_w, debug=False):
    nc = build_l0_attn(debug=debug)
    DI = 2048
    wi = W_in.reshape(1024, 32, 128)
    order = []
    for mp in range(16):
        order += [mp, mp + 16]
    w_in = np.ascontiguousarray(wi[:, order, :].reshape(1024, 4096)).astype(np.float32)
    w_abc = np.ascontiguousarray(np.concatenate([W_A, W_B, W_C], axis=1)).astype(np.float32)
    w_outc = np.ascontiguousarray(W_out).astype(np.float32)
    nwrep = np.ascontiguousarray(np.broadcast_to(norm_w[None, :], (128, DI))).astype(np.float32)
    tau = np.arange(64)[:, None]; t = np.arange(64)[None, :]
    trim = (tau <= t).astype(np.float32)
    idn = np.eye(128, dtype=np.float32)
    in_maps = []
    for b in range(N_CORES):
        in_maps.append(dict(xin=np.ascontiguousarray(x[b].T).astype(np.float32),
                            w_in=w_in, w_abc=w_abc, w_out=w_outc, nwrep=nwrep,
                            trim=trim, idn=idn))
    res = run_bass_kernel_spmd(nc, in_maps, core_ids=list(range(N_CORES)))
    out = np.stack([np.ascontiguousarray(r["xout"].T) for r in res.results])
    if debug:
        return out, res.results[0]
    return out



def build_l1_ffn(debug=False, ctx=None, DEPTH_RUN=6, NW_RUN=16, FT_RUN=8):
    if ctx is None:
        nc = bass.Bass("TRN2", target_bir_lowering=False)
        _pref = ""
    else:
        nc = ctx["nc"]
        _pref = ctx["pref"]
    DEPTH = 6
    W = 256
    NW = T // W
    HC = 22
    FST = 512
    xin = ctx["xin"] if ctx is not None else nc.dram_tensor("xin", [D, T], F32, kind="ExternalInput").ap()
    wqkv5 = nc.dram_tensor(_pref + "wqkv5", [DEPTH, D, 5 * D], F32, kind="ExternalInput").ap()
    wo = nc.dram_tensor(_pref + "wo", [DEPTH, D, D], F32, kind="ExternalInput").ap()
    w1p = nc.dram_tensor(_pref + "w1p", [DEPTH, D, 2 * 2816], F32, kind="ExternalInput").ap()
    w2p = nc.dram_tensor(_pref + "w2p", [DEPTH, 2816, D], F32, kind="ExternalInput").ap()
    biasc = nc.dram_tensor(_pref + "biasc", [128, 52 * DEPTH], F32, kind="ExternalInput").ap()
    lngb = nc.dram_tensor(_pref + "lngb", [128, 16 * DEPTH], F32, kind="ExternalInput").ap()
    qscols = nc.dram_tensor(_pref + "qscols", [128, 4 * DEPTH], F32, kind="ExternalInput").ap()
    tabs = nc.dram_tensor(_pref + "tabs", [4 * 128, T], F32, kind="ExternalInput").ap()
    masks = nc.dram_tensor(_pref + "masks", [4 * 128, 2 * W], F32, kind="ExternalInput").ap()
    onbm = nc.dram_tensor(_pref + "onbm", [128, 2], F32, kind="ExternalInput").ap()
    onbtm = nc.dram_tensor(_pref + "onbtm", [2, 128], F32, kind="ExternalInput").ap()
    idn = nc.dram_tensor(_pref + "idn", [128, 128], F32, kind="ExternalInput").ap()
    xout = ctx["xout"] if ctx is not None else nc.dram_tensor("xout", [D, T], F32, kind="ExternalOutput").ap()
    if debug:
        dbg = {k: nc.dram_tensor(_pref + "dbg_" + k, shp, F32, kind="ExternalOutput").ap()
               for k, shp in [("qr", [128, 8 * W]), ("kr", [128, 8 * W]),
                              ("att", [128, 2 * W]), ("outf", [128, 8 * W]),
                              ("xw1", [128, W]), ("nsa", [128, 1]), ("nsb", [128, 1]), ("nmx", [128, 1]), ("sim", [128, 2 * W]), ("attp", [128, 2 * W]), ("xn", [128, 8 * FST]),
                              ("act", [128, HC * FST])]}

    es = ExitStack()
    S = lambda *a, **k: es.enter_context(nc.sbuf_tensor(_pref + a[0], *a[1:], **k))
    PS = lambda *a, **k: es.enter_context(nc.psum_tensor(_pref + a[0], *a[1:], **k))
    with es:
        XW = S("XW", [128, NCD * T], F32)
        ARN = S("ARN", [128, 22080], BF16)
        XS = lambda: ARN[:, 0:8 * 264]
        QT = lambda: ARN[:, 2112:2112 + 8 * 256]
        QST = lambda: ARN[:, 4160:4160 + 8 * 256]
        KT = lambda: ARN[:, 6208:6208 + 8 * 256]
        KST = lambda: ARN[:, 8256:8256 + 8 * 256]
        KR = lambda: ARN[:, 10304:10304 + 8 * 512]
        VR = lambda: ARN[:, 14400:14400 + 2 * 2 * 1024]
        OUTF = lambda: ARN[:, 18496:18496 + 8 * 256]
        ATT = lambda: ARN[:, 20544:20544 + 4 * 128]
        SQB = lambda: ARN[:, 21056:21056 + 512]
        INVB = lambda: ARN[:, 21568:21568 + 512]
        TAB = lambda: ARN[:, 1024:2048]
        ATL = lambda: ARN[:, 0:512]
        XSF = lambda: ARN[:, 0:8 * 520]
        XN = lambda: ARN[:, 4160:4160 + 8 * 512]
        ACT = lambda: ARN[:, 8256:8256 + HC * 512]
        GEL = lambda: ARN[:, 19584:19584 + 1024]
        MASKS = S("MASKS", [128, 2 * 2 * W], F32)
        IDN_BF = S("IDN_BF", [128, 128], BF16)
        INV2B = S("INV2B", [2, 512], BF16)
        ONB = S("ONB", [128, 2], BF16)
        ONBT = S("ONBT", [2, 128], BF16)
        BIAS = S("BIAS", [128, 52 * DEPTH], F32)
        LNGB = S("LNGB", [128, 16 * DEPTH], F32)
        QSC = S("QSC", [128, 4 * DEPTH], F32)
        wba = S("wba", [128, NCD * 512], BF16)
        wbb = S("wbb", [128, NCD * 512], BF16)
        wbc = wba
        wbd = wbb
        w0 = S("w0", [128, 512], F32)
        w1 = S("w1", [128, 512], F32)
        nmax = S("nmax", [128, 2], F32)
        nsum = S("nsum", [128, 2], F32)
        inv2 = S("inv2", [2, 512], F32)
        stat_mu = S("stat_mu", [1, FST], F32)
        stat_rsig = S("stat_rsig", [1, FST], F32)
        stat_tmp = S("stat_tmp", [1, FST], F32)

        pb0 = PS("pb0", [128, 512], F32)
        pb1 = PS("pb1", [128, 512], F32)
        pb2 = PS("pb2", [128, 512], F32)
        pb3 = PS("pb3", [128, 512], F32)
        PTB = PS("PTB", [128, 1024], BF16)

        P = ctx["P"] if ctx is not None else Prog(nc, es)
        _pw = [ctx["prev"]] if ctx is not None and ctx.get("prev") else []
        cpool, ones_col, ones_row, ones_bf, tok0 = emit_common_setup(P, es, nc, extra_waits=_pw, pref=_pref)

        P.emit("pool", lambda e: e.dma_start(ONB[:], onbm), waits=_pw, inc="dqw", inc_by=16)
        P.emit("pool", lambda e: e.dma_start(ONBT[:], onbtm), inc="dqw", inc_by=16)
        P.emit("pool", lambda e: e.dma_start(BIAS[:], biasc), inc="dqw", inc_by=16)
        P.emit("pool", lambda e: e.dma_start(LNGB[:], lngb), inc="dqw", inc_by=16)
        P.emit("pool", lambda e: e.dma_start(QSC[:], qscols), inc="dqw", inc_by=16)
        P.emit("pool", lambda e: e.dma_start(IDN_BF[:], idn), inc="dqw", inc_by=16)
        loads_tok = [("dqw", P.cnt["dqw"]), P.tok("dve")]
        prev = P.stage("dve", [lambda e: e.engine_nop()], waits=loads_tok)

        import os as _os0
        if _os0.environ.get("SKIP_INITLN"):
            _chunked_dma(P, "sp", "dqx", XW, xin, NCD, T, waits=[prev])
            prev = P.stage("dve", [lambda e: e.engine_nop()], waits=[("dqx", P.cnt["dqx"])])
        for it in range(0 if _os0.environ.get("SKIP_INITLN") else T // 256):
            t0 = it * 256
            mmq = []
            for c in range(NCD):
                P.emit("sp", lambda e, c=c, t0=t0: e.dma_start(
                    w0[:, 0:256] if c % 2 == 0 else w1[:, 0:256],
                    xin[c * 128:(c + 1) * 128, t0:t0 + 256]),
                    waits=[prev], inc="dqx", inc_by=16)
                xt = ("dqx", P.cnt["dqx"])
                src = (w0 if c % 2 == 0 else w1)
                prev = P.stage("act", [lambda e, src=src: e.activation(
                    SQB()[:, 0:256], src[:, 0:256], AF.Square)], waits=[xt, prev])
                prev = P.stage("pe", [
                    lambda e, c=c, src=src: e.matmul(pb2[0:1, 0:256], ones_col[:], src[:, 0:256],
                                                     start=(c == 0), stop=(c == NCD - 1)),
                    lambda e, c=c: e.matmul(pb2[0:1, 256:512], ones_bf[:], SQB()[:, 0:256],
                                            start=(c == 0), stop=(c == NCD - 1)),
                ], waits=[prev])
            prev = P.stage("dve", [
                lambda e: e.tensor_scalar(stat_mu[:, 0:256], pb2[0:1, 0:256], 1.0 / D, None, OP.mult),
                lambda e: e.tensor_scalar(stat_tmp[:, 0:256], pb2[0:1, 256:512], 1.0 / D, None, OP.mult),
                lambda e: e.tensor_tensor(stat_rsig[:, 0:256], stat_mu[:, 0:256], stat_mu[:, 0:256], OP.mult),
                lambda e: e.tensor_tensor(stat_tmp[:, 0:256], stat_tmp[:, 0:256], stat_rsig[:, 0:256], OP.subtract),
            ], waits=[prev])
            prev = P.stage("act", [
                lambda e: e.activation(stat_tmp[:, 0:256], stat_tmp[:, 0:256], AF.Ln, bias=cpool[0:1, C_EPS_LN:C_EPS_LN + 1]),
                lambda e: e.activation(stat_rsig[:, 0:256], stat_tmp[:, 0:256], AF.Exp, scale=cpool[0:1, C_NEGHALF:C_NEGHALF + 1]),
            ], waits=[prev])
            prev = P.stage("pe", [
                lambda e: e.matmul(pb0[:, 0:256], ones_row[:], stat_mu[:, 0:256], start=True, stop=True),
                lambda e: e.matmul(pb1[:, 0:256], ones_row[:], stat_rsig[:, 0:256], start=True, stop=True),
            ], waits=[prev])
            for c in range(NCD):
                P.emit("sp", lambda e, c=c, t0=t0: e.dma_start(
                    w0[:, 0:256] if c % 2 == 0 else w1[:, 0:256],
                    xin[c * 128:(c + 1) * 128, t0:t0 + 256]),
                    waits=[prev], inc="dqx", inc_by=16)
                xt = ("dqx", P.cnt["dqx"])
                src = (w0 if c % 2 == 0 else w1)
                prev = P.stage("dve", [
                    lambda e, c=c, t0=t0, src=src: e.tensor_tensor(
                        XW[:, c * T + t0: c * T + t0 + 256], src[:, 0:256], pb0[:, 0:256], OP.subtract),
                    lambda e, c=c, t0=t0: e.tensor_tensor(
                        XW[:, c * T + t0: c * T + t0 + 256], XW[:, c * T + t0: c * T + t0 + 256],
                        pb1[:, 0:256], OP.mult),
                ], waits=[xt, prev])

        def stream_w(buf, src3, lay, col0, ncols, wsem):
            def fn(e, buf=buf, src3=src3, lay=lay, col0=col0, ncols=ncols):
                return e.dma_start(
                    buf[:, 0:NCD * ncols].rearrange("p (c n) -> p c n", c=NCD),
                    src3[lay].rearrange("(c p) n -> p c n", p=128)[:, :, col0:col0 + ncols])
            P.emit("pool", fn, waits=[prev], inc=wsem, inc_by=16)
            return (wsem, P.cnt[wsem])

        for lay in range(DEPTH_RUN):
            qs_c = QSC[:, lay * 4 + 0: lay * 4 + 1]
            qsw_c = QSC[:, lay * 4 + 1: lay * 4 + 2]
            ks_c = QSC[:, lay * 4 + 2: lay * 4 + 3]
            ksw_c = QSC[:, lay * 4 + 3: lay * 4 + 4]
            prev = P.stage("dve", [lambda e: e.memset(KR()[:], 0.0),
                                   lambda e: e.memset(VR()[:], 0.0)], waits=[prev])
            for w in range(NW_RUN):
                t0 = w * W
                sl = w % 2
                fns = []
                for c in range(NCD):
                    off = 0 if c < 4 else 1
                    if t0 - off < 0:
                        fns.append(lambda e, c=c: e.memset(XS()[:, c * 264:c * 264 + 1], 0.0))
                        fns.append(lambda e, c=c, t0=t0: e.tensor_copy(
                            XS()[:, c * 264 + 1:c * 264 + 256], XW[:, c * T: c * T + 255]))
                    else:
                        fns.append(lambda e, c=c, t0=t0, off=off: e.tensor_copy(
                            XS()[:, c * 264:c * 264 + 256], XW[:, c * T + t0 - off: c * T + t0 + 256 - off]))
                prev = P.stage("dve", fns, waits=[prev])
                import os as _os
                parts = [("q", 0, QT), ("qs", 1024, QST), ("k", 2048, KT), ("ks", 3072, KST)]
                if _os.environ.get("SKIP_QKV"):
                    parts = []
                for pname, col0, dstf in parts:
                    buf = wba if pname in ("q", "k") else wbb
                    for hf in range(2):
                        wt = stream_w(buf, wqkv5, lay, col0 + hf * 512, 512, "dqq")
                        for m4 in range(4):
                            m = hf * 4 + m4
                            pm = pb0 if m % 2 == 0 else pb1
                            mm = [(lambda e, c=c, m4=m4, pm=pm, buf=buf: e.matmul(
                                pm[:, 0:W], buf[:, c * 512 + m4 * 128: c * 512 + (m4 + 1) * 128],
                                XS()[:, c * 264:c * 264 + 256], start=(c == 0), stop=(c == NCD - 1)))
                                  for c in range(NCD)]
                            prev = P.stage("pe", mm, waits=[prev, wt] if m4 == 0 else [prev])
                            prev = P.stage("dve", [lambda e, m=m, pm=pm, dstf=dstf: e.tensor_copy(
                                dstf()[:, m * W:(m + 1) * W], pm[:, 0:W])], waits=[prev])
                for nsl in range(2 if not _os.environ.get("SKIP_QKV") else 0):
                    wt = stream_w(wba, wqkv5, lay, 4096 + nsl * 512, 512, "dqq")
                    for tc in range(2):
                        pm = pb0 if tc == 0 else pb1
                        mm = [(lambda e, c=c, tc=tc, pm=pm: e.matmul(
                            pm[:, 0:512], XS()[:, c * 264 + tc * 128: c * 264 + (tc + 1) * 128],
                            wba[:, c * 512:(c + 1) * 512],
                            start=(c == 0), stop=(c == NCD - 1)))
                              for c in range(NCD)]
                        prev = P.stage("pe", mm, waits=[prev, wt] if tc == 0 else [prev])
                        prev = P.stage("dve", [lambda e, tc=tc, nsl=nsl, sl=sl, pm=pm: e.tensor_copy(
                            VR()[:, (sl * 2 + tc) * 1024 + nsl * 512: (sl * 2 + tc) * 1024 + (nsl + 1) * 512],
                            pm[:, 0:512])], waits=[prev])
                if not _os.environ.get("SKIP_TABS"):
                    P.emit("pool", lambda e, t0=t0: e.dma_start(
                        TAB()[:, 0:256], tabs[0:128, t0:t0 + 256]), waits=[prev], inc="dqt", inc_by=16)
                    P.emit("pool", lambda e, t0=t0: e.dma_start(
                        TAB()[:, 256:512], tabs[128:256, t0:t0 + 256]), inc="dqt", inc_by=16)
                    P.emit("pool", lambda e, t0=t0: e.dma_start(
                        TAB()[:, 512:768], tabs[256:384, t0:t0 + 256]), inc="dqt", inc_by=16)
                    P.emit("pool", lambda e, t0=t0: e.dma_start(
                        TAB()[:, 768:1024], tabs[384:512, t0:t0 + 256]), inc="dqt", inc_by=16)
                    moff = 0 if w > 0 else 2
                    P.emit("pool", lambda e, moff=moff: e.dma_start(
                        MASKS[:, 0:512], masks[moff * 128:(moff + 1) * 128, :]), inc="dqt", inc_by=16)
                    P.emit("pool", lambda e, moff=moff: e.dma_start(
                        MASKS[:, 512:1024], masks[(moff + 1) * 128:(moff + 2) * 128, :]), inc="dqt", inc_by=16)
                    prev = P.stage("dve", [
                        lambda e, qs_c=qs_c: e.tensor_scalar(TAB()[:, 0:256], TAB()[:, 0:256], qs_c, None, OP.mult),
                        lambda e, qsw_c=qsw_c: e.tensor_scalar(TAB()[:, 256:512], TAB()[:, 256:512], qsw_c, None, OP.mult),
                        lambda e, ks_c=ks_c: e.tensor_scalar(TAB()[:, 512:768], TAB()[:, 512:768], ks_c, None, OP.mult),
                        lambda e, ksw_c=ksw_c: e.tensor_scalar(TAB()[:, 768:1024], TAB()[:, 768:1024], ksw_c, None, OP.mult),
                    ], waits=[prev, ("dqt", P.cnt["dqt"])])
                import os as _os
                _skip_rot = _os.environ.get("SKIP_ROT")
                onbpb = (pb2, pb3)
                invpb = (pb0, pb1)
                rot_tok = {}; invb_tok = {}
                lj = 0
                ln_pe_prev = prev
                for side, base_t, swap_t, taba, tabb in (() if _skip_rot else (
                        ("q", QT, QST, 0, 256), ("k", KT, KST, 512, 768))):
                    for c in range(NCD):
                        s = lj % 2
                        pon = onbpb[s]
                        pin = invpb[s]
                        so = s * 256
                        t1 = P.stage("act", [lambda e, c=c, base_t=base_t, so=so: e.activation(
                            SQB()[:, so:so + W], base_t()[:, c * W:(c + 1) * W], AF.Square)],
                            waits=[ln_pe_prev] if lj == 0 else ([rot_tok[s]] if rot_tok.get(s) else []))
                        t2 = P.stage("pe", [lambda e, so=so, pon=pon: e.matmul(
                            pon[0:2, 0:W], ONB[:], SQB()[:, so:so + W], start=True, stop=True)], waits=[t1])
                        t3 = P.stage("act", [
                            lambda e, so=so, pon=pon: e.activation(inv2[:, so:so + W], pon[0:2, 0:W], AF.Ln),
                            lambda e, so=so: e.activation(inv2[:, so:so + W], inv2[:, so:so + W], AF.Exp,
                                                          scale=cpool[0:2, C_HALF:C_HALF + 1]),
                        ], waits=[t2])
                        t4 = P.stage("dve", [
                            lambda e, so=so: e.tensor_scalar(inv2[:, so:so + W], inv2[:, so:so + W], 1e-12, None, OP.max),
                            lambda e, so=so: e.reciprocal(inv2[:, so:so + W], inv2[:, so:so + W]),
                        ], waits=[t3])
                        t5 = P.stage("dve", [
                            lambda e, so=so: e.tensor_copy(INV2B[:, so:so + W], inv2[:, so:so + W]),
                        ], waits=[t4])
                        t6 = P.stage("pe", [lambda e, so=so, pin=pin: e.matmul(
                            pin[:, 0:W], ONBT[:], INV2B[:, so:so + W], start=True, stop=True)], waits=[t5])
                        invb_tok[s] = t7 = P.stage("act", [lambda e, so=so, pin=pin: e.copy(
                            INVB()[:, so:so + W], pin[:, 0:W])], waits=[t6])
                        rot_tok[s] = P.stage("dve", [
                            lambda e, c=c, base_t=base_t, taba=taba: e.tensor_tensor(
                                base_t()[:, c * W:(c + 1) * W], base_t()[:, c * W:(c + 1) * W],
                                TAB()[:, taba:taba + W], OP.mult),
                            lambda e, c=c, swap_t=swap_t, tabb=tabb: e.tensor_tensor(
                                swap_t()[:, c * W:(c + 1) * W], swap_t()[:, c * W:(c + 1) * W],
                                TAB()[:, tabb:tabb + W], OP.mult),
                            lambda e, c=c, base_t=base_t, swap_t=swap_t: e.tensor_tensor(
                                base_t()[:, c * W:(c + 1) * W], base_t()[:, c * W:(c + 1) * W],
                                swap_t()[:, c * W:(c + 1) * W], OP.add),
                            lambda e, c=c, base_t=base_t, so=so: e.tensor_tensor(
                                base_t()[:, c * W:(c + 1) * W], base_t()[:, c * W:(c + 1) * W],
                                INVB()[:, so:so + W], OP.mult),
                        ], waits=[t7])
                        lj += 1
                if not _skip_rot:
                    prev = P.stage("dve", [lambda e: e.engine_nop()],
                                   waits=[t for t in rot_tok.values() if t])
                prev = P.stage("dve", [(lambda e, c=c, sl=sl: e.tensor_copy(
                    KR()[:, c * 512 + sl * 256: c * 512 + (sl + 1) * 256], KT()[:, c * W:(c + 1) * W]))
                    for c in range(NCD)], waits=[prev])
                if debug and lay == 0 and w == 0:
                    P.emit("pool", lambda e: e.dma_start(dbg["qr"], QT()[:]), waits=[prev], inc="dqo", inc_by=16)
                    P.emit("pool", lambda e: e.dma_start(dbg["kr"], KT()[:]), inc="dqo", inc_by=16)
                    prev = P.stage("dve", [lambda e: e.engine_nop()], waits=[P.tok("dqo")])
                import os as _os
                _skip_heads = _os.environ.get("SKIP_HEADS")
                ATL2 = lambda s: ARN[:, s * 512:(s + 1) * 512]
                ATT2 = lambda s: ARN[:, 20544 + s * 512:20544 + (s + 1) * 512]
                simpb = (pb3, pb1)
                avpb = (pb2, pb0)
                stt_tok = {}; evout_tok = {}; attcp_tok = {}
                it_i = 0
                head_pe_prev = prev
                for c in range(NCD if not _skip_heads else 0):
                    for hh in range(2):
                        h = 2 * c + hh
                        rb = hh * 64
                        for mi in range(2):
                            s = it_i % 2
                            ps = simpb[s]
                            pa = avpb[s]
                            wbuf = w0 if s == 0 else w1
                            w_ = [head_pe_prev] if it_i == 0 else []
                            if stt_tok.get(s):
                                w_.append(stt_tok[s])
                            ptok = P.stage("pe", [
                                (lambda e, c=c, rb=rb, mi=mi, sl=sl, ps=ps: e.matmul(
                                    ps[0:128, 0:256],
                                    QT()[rb:rb + 64, c * W + mi * 128: c * W + (mi + 1) * 128],
                                    KR()[rb:rb + 64, c * 512 + (1 - sl) * 256: c * 512 + (2 - sl) * 256],
                                    start=True, stop=True)),
                                (lambda e, c=c, rb=rb, mi=mi, sl=sl, ps=ps: e.matmul(
                                    ps[0:128, 256:512],
                                    QT()[rb:rb + 64, c * W + mi * 128: c * W + (mi + 1) * 128],
                                    KR()[rb:rb + 64, c * 512 + sl * 256: c * 512 + (sl + 1) * 256],
                                    start=True, stop=True)),
                            ], waits=w_)
                            stt_tok[s] = t1 = P.stage("dve", [
                                lambda e, mi=mi, ps=ps, wbuf=wbuf: e.scalar_tensor_tensor(
                                    wbuf[:, 0:512], ps[0:128, 0:512], 8.0,
                                    MASKS[:, mi * 512:(mi + 1) * 512], OP.mult, OP.add),
                            ], waits=[ptok])
                            t2 = P.stage("dve", [
                                lambda e, s=s, wbuf=wbuf: e.tensor_reduce(nmax[:, s:s + 1], wbuf[:, 0:512],
                                                          mybir.AxisListType.X, OP.max, negate=True),
                            ], waits=[t1])
                            t3 = P.stage("act", [lambda e, s=s, wbuf=wbuf: e.activation(
                                ATL2(s)[:, 0:512], wbuf[:, 0:512], AF.Exp, bias=nmax[:, s:s + 1])],
                                waits=[t2])
                            t4 = P.stage("dve", [
                                lambda e, s=s: e.tensor_reduce(nsum[:, s:s + 1], ATL2(s)[:, 0:512],
                                                          mybir.AxisListType.X, OP.add),
                            ], waits=[t3])
                            t5 = P.stage("dve", [
                                lambda e, s=s: e.reciprocal(nsum[:, s:s + 1], nsum[:, s:s + 1]),
                            ], waits=[t4])
                            t6 = P.stage("dve", [
                                lambda e, s=s: e.tensor_scalar(ATL2(s)[:, 0:512], ATL2(s)[:, 0:512],
                                                          nsum[:, s:s + 1], None, OP.mult),
                            ], waits=[t5])
                            ttw = [t6]
                            if attcp_tok.get(s):
                                ttw.append(attcp_tok[s])
                            ptt = P.stage("pe", [(lambda e, kc=kc, s=s: e.transpose(
                                PTB[:, s * 512 + kc * 128: s * 512 + (kc + 1) * 128],
                                ATL2(s)[:, kc * 128:(kc + 1) * 128], IDN_BF[:]))
                                  for kc in range(4)], waits=ttw)
                            attcp_tok[s] = t7 = P.stage("act", [(lambda e, kc=kc, s=s: e.copy(
                                ATT2(s)[:, kc * 128:(kc + 1) * 128],
                                PTB[:, s * 512 + kc * 128: s * 512 + (kc + 1) * 128]))
                                for kc in range(4)], waits=[ptt])
                            avw = [t7]
                            if evout_tok.get(s):
                                avw.append(evout_tok[s])
                            mm = []
                            for kc in range(4):
                                vsl = (1 - sl) if kc < 2 else sl
                                vtc = kc % 2
                                mm.append(lambda e, kc=kc, vsl=vsl, vtc=vtc, h=h, rb=rb, s=s, pa=pa: e.matmul(
                                    pa[rb:rb + 64, 0:128],
                                    VR()[:, (vsl * 2 + vtc) * 1024 + h * 64: (vsl * 2 + vtc) * 1024 + (h + 1) * 64],
                                    ATT2(s)[:, kc * 128:(kc + 1) * 128],
                                    start=(kc == 0), stop=(kc == 3),
                                    tile_position=(0, 64) if rb == 64 else None))
                            pav = P.stage("pe", mm, waits=avw)
                            evout_tok[s] = P.stage("dve", [lambda e, c=c, mi=mi, rb=rb, pa=pa: e.tensor_copy(
                                OUTF()[rb:rb + 64, c * W + mi * 128: c * W + (mi + 1) * 128],
                                pa[rb:rb + 64, 0:128])], waits=[pav])
                            it_i += 1
                prev = P.stage("dve", [lambda e: e.engine_nop()],
                               waits=[t for t in list(evout_tok.values()) + list(stt_tok.values()) if t])
                if debug and lay == 0 and w == 0:
                    P.emit("pool", lambda e: e.dma_start(dbg["outf"], OUTF()[:]), waits=[prev], inc="dqo", inc_by=16)
                    prev = P.stage("dve", [lambda e: e.engine_nop()], waits=[P.tok("dqo")])
                war_o = {0: prev, 1: prev}
                evtok_o = {}
                first_pe_o = True
                for og in range(2 if not _os.environ.get("SKIP_WO") else 0):
                    buf = wba if og % 2 == 0 else wbb
                    def wfno(e, buf=buf, lay=lay, og=og):
                        return e.dma_start(
                            buf[:, 0:NCD * 512].rearrange("p (c n) -> p c n", c=NCD),
                            wo[lay].rearrange("(c p) n -> p c n", p=128)[:, :, og * 512:(og + 1) * 512])
                    P.emit("pool", wfno, waits=[war_o[og % 2]], inc="dqq", inc_by=16)
                    wt = ("dqq", P.cnt["dqq"])
                    for oi in range(4):
                        mo = og * 4 + oi
                        pm = pb0 if oi % 2 == 0 else pb1
                        mm = [(lambda e, c=c, oi=oi, pm=pm, buf=buf: e.matmul(
                            pm[:, 0:W], buf[:, c * 512 + oi * 128: c * 512 + (oi + 1) * 128],
                            OUTF()[:, c * W:(c + 1) * W], start=(c == 0), stop=(c == NCD - 1)))
                              for c in range(NCD)]
                        w_ = [wt] if oi == 0 else []
                        if first_pe_o:
                            w_.append(prev); first_pe_o = False
                        if evtok_o.get(oi % 2):
                            w_.append(evtok_o[oi % 2])
                        ptok = P.stage("pe", mm, waits=w_)
                        evtok_o[oi % 2] = P.stage("dve", [lambda e, mo=mo, t0=t0, pm=pm: e.tensor_tensor(
                            XW[:, mo * T + t0: mo * T + t0 + W], XW[:, mo * T + t0: mo * T + t0 + W],
                            pm[:, 0:W], OP.add)], waits=[ptok])
                        war_o[og % 2] = ptok
                prev = P.stage("dve", [lambda e: e.engine_nop()],
                               waits=[t for t in evtok_o.values() if t])
                if debug and lay == 0 and w == 0:
                    P.emit("sp", lambda e: e.dma_start(dbg["xw1"], XW[:, 0:256]), waits=[prev], inc="dqo", inc_by=16)
                    prev = P.stage("dve", [lambda e: e.engine_nop()], waits=[P.tok("dqo")])
            for ft in range(FT_RUN):
                t0 = ft * FST
                fns = []
                for c in range(NCD):
                    off = 0 if c < 4 else 1
                    if t0 - off < 0:
                        fns.append(lambda e, c=c: e.memset(XSF()[:, c * 520:c * 520 + 1], 0.0))
                        fns.append(lambda e, c=c: e.tensor_copy(
                            XSF()[:, c * 520 + 1:c * 520 + FST], XW[:, c * T: c * T + FST - 1]))
                    else:
                        fns.append(lambda e, c=c, t0=t0, off=off: e.tensor_copy(
                            XSF()[:, c * 520:c * 520 + FST], XW[:, c * T + t0 - off: c * T + t0 + FST - off]))
                prev = P.stage("dve", fns, waits=[prev])
                prev = emit_ln(P, FST, prev, ARN, 0, 520, NCD, 1.0 / D,
                               ARN[:, 8256:8256 + 8 * 512], pb2[0:1, 0:FST], pb3[0:1, 0:FST],
                               stat_mu, stat_rsig, stat_tmp, pb0, pb1, cpool, ones_col, ones_row,
                               out_bf=XN(), out_cols=512, ones_sq=ones_bf, ones_x=ones_bf,
                               src_cast=w0)
                fns = []
                for c in range(NCD):
                    g_c = LNGB[:, lay * 16 + c: lay * 16 + c + 1]
                    b_c = LNGB[:, lay * 16 + 8 + c: lay * 16 + 8 + c + 1]
                    fns.append(lambda e, c=c, g_c=g_c, b_c=b_c: e.tensor_scalar(
                        XN()[:, c * 512:(c + 1) * 512], XN()[:, c * 512:(c + 1) * 512],
                        g_c, b_c, OP.mult, OP.add))
                prev = P.stage("dve", fns, waits=[prev])
                war_f = {0: prev, 1: prev}
                evtok_f = {}
                geltok = {}
                first_pe_f = True
                for half in range(2):
                    for mg in range(11):
                        gslot = mg % 2
                        buf = wbc if gslot == 0 else wbd
                        def wfn(e, lay=lay, half=half, mg=mg, buf=buf):
                            return e.dma_start(
                                buf[:, 0:NCD * 256].rearrange("p (c n) -> p c n", c=NCD),
                                w1p[lay].rearrange("(c p) n -> p c n", p=128)[:, :, half * 2816 + mg * 256: half * 2816 + (mg + 1) * 256])
                        P.emit("pool", wfn, waits=[war_f[gslot]], inc="dqf", inc_by=16)
                        wt = ("dqf", P.cnt["dqf"])
                        for mi in range(2):
                            m = mg * 2 + mi
                            pm = pb0 if mi == 0 else pb1
                            mm = [(lambda e, c=c, mi=mi, pm=pm, buf=buf: e.matmul(
                                pm[:, 0:512], buf[:, c * 256 + mi * 128: c * 256 + (mi + 1) * 128],
                                XN()[:, c * 512:(c + 1) * 512], start=(c == 0), stop=(c == NCD - 1)))
                                  for c in range(NCD)]
                            w_ = [wt] if mi == 0 else []
                            if first_pe_f:
                                w_.append(prev); first_pe_f = False
                            if evtok_f.get(mi):
                                w_.append(evtok_f[mi])
                            ptok = P.stage("pe", mm, waits=w_)
                            if half == 1:
                                bcol = BIAS[:, lay * 52 + 22 + m: lay * 52 + 23 + m]
                                gslot2 = m % 2
                                gw = [ptok]
                                if geltok.get(gslot2):
                                    gw.append(geltok[gslot2])
                                atok = P.stage("act", [lambda e, m=m, pm=pm, bcol=bcol, gslot2=gslot2: e.activation(
                                    GEL()[:, gslot2 * 512:(gslot2 + 1) * 512], pm[:, 0:512], AF.Gelu, bias=bcol)],
                                    waits=gw)
                                evtok_f[mi] = geltok[gslot2] = P.stage("dve", [lambda e, m=m, gslot2=gslot2: e.tensor_tensor(
                                    ACT()[:, m * 512:(m + 1) * 512], ACT()[:, m * 512:(m + 1) * 512],
                                    GEL()[:, gslot2 * 512:(gslot2 + 1) * 512], OP.mult)], waits=[atok])
                            else:
                                bcol = BIAS[:, lay * 52 + m: lay * 52 + 1 + m]
                                evtok_f[mi] = P.stage("dve", [lambda e, m=m, pm=pm, bcol=bcol: e.tensor_scalar(
                                    ACT()[:, m * 512:(m + 1) * 512], pm[:, 0:512], bcol, None, OP.add)],
                                    waits=[ptok])
                        war_f[gslot] = ptok
                prev = P.stage("dve", [lambda e: e.engine_nop()],
                               waits=[t for t in evtok_f.values() if t] + [t for t in geltok.values() if t])
                war_2 = {0: prev, 1: prev}
                evtok_2 = {}
                first_pe_2 = True
                for mo in range(8):
                    gslot = mo % 2
                    buf = wbc if gslot == 0 else wbd
                    def wfn2(e, lay=lay, mo=mo, buf=buf):
                        return e.dma_start(
                            buf[:, 0:HC * 128].rearrange("p (c n) -> p c n", c=HC),
                            w2p[lay].rearrange("(c p) n -> p c n", p=128)[:, :, mo * 128:(mo + 1) * 128])
                    P.emit("pool", wfn2, waits=[war_2[gslot]], inc="dqf", inc_by=16)
                    wt = ("dqf", P.cnt["dqf"])
                    pm = pb0 if gslot == 0 else pb1
                    wbuf2 = w0 if gslot == 0 else w1
                    w_ = [wt]
                    if first_pe_2:
                        w_.append(prev); first_pe_2 = False
                    if evtok_2.get(gslot):
                        w_.append(evtok_2[gslot])
                    mm = [(lambda e, k=k, pm=pm, buf=buf: e.matmul(
                        pm[:, 0:512], buf[:, k * 128:(k + 1) * 128],
                        ACT()[:, k * 512:(k + 1) * 512], start=(k == 0), stop=(k == HC - 1)))
                          for k in range(HC)]
                    ptok = P.stage("pe", mm, waits=w_)
                    war_2[gslot] = ptok
                    b2col = BIAS[:, lay * 52 + 44 + mo: lay * 52 + 45 + mo]
                    evtok_2[gslot] = P.stage("dve", [
                        lambda e, pm=pm, b2col=b2col, wbuf2=wbuf2: e.tensor_scalar(
                            wbuf2[:, 0:512], pm[:, 0:512], b2col, None, OP.add),
                        lambda e, mo=mo, t0=t0, wbuf2=wbuf2: e.tensor_tensor(
                            XW[:, mo * T + t0: mo * T + t0 + FST],
                            XW[:, mo * T + t0: mo * T + t0 + FST], wbuf2[:, 0:512], OP.add),
                    ], waits=[ptok])
                prev = P.stage("dve", [lambda e: e.engine_nop()],
                               waits=[t for t in evtok_2.values() if t])
                if debug and lay == 0 and ft == 0:
                    P.emit("pool", lambda e: e.dma_start(dbg["xn"], XN()[:]), waits=[prev], inc="dqo", inc_by=16)
                    P.emit("pool", lambda e: e.dma_start(dbg["act"], ACT()[:]), inc="dqo", inc_by=16)
                    prev = P.stage("dve", [lambda e: e.engine_nop()], waits=[P.tok("dqo")])

        for c in range(NCD):
            for pc in range(T // 512):
                buf = w0 if pc % 2 == 0 else w1
                P.emit("sp", lambda e, c=c, pc=pc, buf=buf: e.dma_start(
                    buf[:, 0:512], xin[c * 128:(c + 1) * 128, pc * 512:(pc + 1) * 512]),
                    waits=[prev], inc="dqa", inc_by=16)
                prev = P.stage("dve", [lambda e, c=c, pc=pc, buf=buf: e.tensor_tensor(
                    XW[:, c * T + pc * 512: c * T + (pc + 1) * 512],
                    XW[:, c * T + pc * 512: c * T + (pc + 1) * 512], buf[:, 0:512], OP.add)],
                    waits=[prev, ("dqa", P.cnt["dqa"])])
        for c in range(NCD):
            P.emit("sp", lambda e, c=c: e.dma_start(xout[c * 128:(c + 1) * 128, :],
                                                    XW[:, c * T:(c + 1) * T]),
                   waits=[prev] if c == 0 else (), inc="dqo2", inc_by=16)
        P.emit("sp", None, waits=[P.tok("dqo2")])
        if ctx is not None:
            ctx["prev"] = prev
            ctx["out_tok"] = P.tok("dqo2")
        import os
        if os.environ.get("PROG_STATS"):
            print("SEM COUNTS:", {k: v for k, v in P.cnt.items()})
            print("OPS:", {k: len(v) for k, v in P.ops.items()})
        if ctx is None:
            P.finalize()
    return nc


def run_l1_ffn(x, Wqkv, q_scale, k_scale, Wo, ln_g, ln_b, W1, b1, W2, b2, debug=False, **bk):
    nc = build_l1_ffn(debug=debug, **bk)
    DEPTH = 6
    f32 = np.float32
    sigma = np.concatenate([np.arange(32, 64), np.arange(0, 32)])
    perm1024 = np.concatenate([h * 64 + sigma for h in range(16)])
    wqkv5 = np.empty((DEPTH, 1024, 5 * 1024), f32)
    for i in range(DEPTH):
        q = Wqkv[i][:, 0:1024]; k = Wqkv[i][:, 1024:2048]; v = Wqkv[i][:, 2048:3072]
        wqkv5[i] = np.concatenate([q, q[:, perm1024], k, k[:, perm1024], v], axis=1)
    w1p = np.zeros((DEPTH, 1024, 2 * 2816), f32)
    w1p[:, :, 0:2730] = W1[:, :, 0:2730]
    w1p[:, :, 2816:2816 + 2730] = W1[:, :, 2730:5460]
    w2p = np.zeros((DEPTH, 2816, 1024), f32)
    w2p[:, 0:2730, :] = W2
    biasc = np.zeros((128, 52 * DEPTH), f32)
    lngb = np.zeros((128, 16 * DEPTH), f32)
    qscols = np.zeros((128, 4 * DEPTH), f32)
    for i in range(DEPTH):
        b1a = np.zeros(2816, f32); b1a[0:2730] = b1[i][0:2730]
        b1g = np.zeros(2816, f32); b1g[0:2730] = b1[i][2730:5460]
        biasc[:, i * 52 + 0:i * 52 + 22] = b1a.reshape(22, 128).T
        biasc[:, i * 52 + 22:i * 52 + 44] = b1g.reshape(22, 128).T
        biasc[:, i * 52 + 44:i * 52 + 52] = b2[i].reshape(8, 128).T
        lngb[:, i * 16:i * 16 + 8] = ln_g[i].reshape(8, 128).T
        lngb[:, i * 16 + 8:i * 16 + 16] = ln_b[i].reshape(8, 128).T
        qscols[:, i * 4 + 0] = np.tile(q_scale[i], 2)
        qscols[:, i * 4 + 1] = np.tile(q_scale[i][sigma], 2)
        qscols[:, i * 4 + 2] = np.tile(k_scale[i], 2)
        qscols[:, i * 4 + 3] = np.tile(k_scale[i][sigma], 2)
    DIM_HEAD = 64; WINDOW = 256
    half = np.arange(0, DIM_HEAD, 2, dtype=np.float64)
    inv_freq = 1.0 / 10000.0 ** (half / DIM_HEAD)
    t = np.arange(T, dtype=np.float64)
    freqs = t[:, None] * inv_freq[None, :]
    freqs = np.concatenate([freqs, freqs], -1)
    base_scale = (half + 0.4 * DIM_HEAD) / (1.4 * DIM_HEAD)
    power = (t - T // 2) / (WINDOW // 2)
    scale = base_scale[None, :] ** power[:, None]
    scale = np.concatenate([scale, scale], -1)
    sign = np.where(np.arange(64) < 32, -1.0, 1.0)
    tabA = (np.cos(freqs) * scale).T
    tabB = (np.sin(freqs) * scale).T * sign[:, None]
    tabC = (np.cos(freqs) / scale).T
    tabD = (np.sin(freqs) / scale).T * sign[:, None]
    tabs = np.concatenate([np.tile(tabA, (2, 1)), np.tile(tabB, (2, 1)),
                           np.tile(tabC, (2, 1)), np.tile(tabD, (2, 1))], axis=0).astype(f32)
    masks = np.zeros((512, 512), f32)
    for wflag in range(2):
        for mi in range(2):
            m = np.zeros((128, 512), f32)
            q_off = mi * 128 + np.arange(128)[:, None]
            j = np.arange(512)[None, :]
            allowed = (j >= q_off) & (j <= q_off + 256)
            if wflag == 1:
                allowed &= (j >= 256)
            m[~allowed] = -1e30
            masks[(wflag * 2 + mi) * 128:(wflag * 2 + mi + 1) * 128] = m
    onbm = np.zeros((128, 2), f32)
    onbm[0:64, 0] = 1.0; onbm[64:128, 1] = 1.0
    onbtm = np.ascontiguousarray(onbm.T)
    idn = np.eye(128, dtype=f32)
    in_maps = []
    for b in range(N_CORES):
        in_maps.append(dict(xin=np.ascontiguousarray(x[b].T).astype(f32),
                            wqkv5=wqkv5, wo=np.ascontiguousarray(Wo).astype(f32),
                            w1p=w1p, w2p=w2p, biasc=biasc, lngb=lngb, qscols=qscols,
                            tabs=tabs, masks=masks, onbm=onbm, onbtm=onbtm, idn=idn))
    res = run_bass_kernel_spmd(nc, in_maps, core_ids=list(range(N_CORES)))
    out = np.stack([np.ascontiguousarray(r["xout"].T) for r in res.results])
    if debug:
        return out, res.results[0]
    return out


def _prep_l0_attn(W_in, W_A, W_B, W_C, W_out, norm_w):
    DI = 2048
    wi = W_in.reshape(1024, 32, 128)
    order = []
    for mp in range(16):
        order += [mp, mp + 16]
    w_in = np.ascontiguousarray(wi[:, order, :].reshape(1024, 4096)).astype(np.float32)
    tau = np.arange(64)[:, None]; t = np.arange(64)[None, :]
    return dict(w_in=w_in,
                w_abc=np.ascontiguousarray(np.concatenate([W_A, W_B, W_C], axis=1)).astype(np.float32),
                w_out=np.ascontiguousarray(W_out).astype(np.float32),
                nwrep=np.ascontiguousarray(np.broadcast_to(norm_w[None, :], (128, DI))).astype(np.float32),
                trim=(tau <= t).astype(np.float32),
                idn=np.eye(128, dtype=np.float32))


def _prep_l0_ffn(W1, b1, W2, b2, scale):
    return dict(w1T=np.ascontiguousarray(W1.T).astype(np.float32),
                w2T=np.ascontiguousarray(W2.T).astype(np.float32),
                b1c=np.ascontiguousarray(b1.reshape(32, 128).T).astype(np.float32),
                b2c=np.ascontiguousarray(b2.reshape(8, 128).T).astype(np.float32),
                sclr=np.full((128, 1), np.float32(scale), np.float32),
                idn=np.eye(128, dtype=np.float32))


def _prep_l1_attn(W_in, W_x, W_dt, b_dt, gamma, beta):
    w_inT = np.ascontiguousarray(W_in.T).astype(np.float32)
    vecs = np.concatenate([
        W_x[0].reshape(NCD, 128).T,
        W_dt[:, 0].reshape(NCD, 128).T,
        b_dt.reshape(NCD, 128).T,
        gamma.reshape(NCD, 128).T,
        beta.reshape(NCD, 128).T,
    ], axis=1).astype(np.float32)
    return dict(w_inT=w_inT, vecs=vecs)


def _prep_l1_ffn(Wqkv, q_scale, k_scale, Wo, ln_g, ln_b, W1, b1, W2, b2):
    DEPTH = 6
    f32 = np.float32
    sigma = np.concatenate([np.arange(32, 64), np.arange(0, 32)])
    perm1024 = np.concatenate([h * 64 + sigma for h in range(16)])
    wqkv5 = np.empty((DEPTH, 1024, 5 * 1024), f32)
    for i in range(DEPTH):
        q = Wqkv[i][:, 0:1024]; k = Wqkv[i][:, 1024:2048]; v = Wqkv[i][:, 2048:3072]
        wqkv5[i] = np.concatenate([q, q[:, perm1024], k, k[:, perm1024], v], axis=1)
    w1p = np.zeros((DEPTH, 1024, 2 * 2816), f32)
    w1p[:, :, 0:2730] = W1[:, :, 0:2730]
    w1p[:, :, 2816:2816 + 2730] = W1[:, :, 2730:5460]
    w2p = np.zeros((DEPTH, 2816, 1024), f32)
    w2p[:, 0:2730, :] = W2
    biasc = np.zeros((128, 52 * DEPTH), f32)
    lngb = np.zeros((128, 16 * DEPTH), f32)
    qscols = np.zeros((128, 4 * DEPTH), f32)
    for i in range(DEPTH):
        b1a = np.zeros(2816, f32); b1a[0:2730] = b1[i][0:2730]
        b1g = np.zeros(2816, f32); b1g[0:2730] = b1[i][2730:5460]
        biasc[:, i * 52 + 0:i * 52 + 22] = b1a.reshape(22, 128).T
        biasc[:, i * 52 + 22:i * 52 + 44] = b1g.reshape(22, 128).T
        biasc[:, i * 52 + 44:i * 52 + 52] = b2[i].reshape(8, 128).T
        lngb[:, i * 16:i * 16 + 8] = ln_g[i].reshape(8, 128).T
        lngb[:, i * 16 + 8:i * 16 + 16] = ln_b[i].reshape(8, 128).T
        qscols[:, i * 4 + 0] = np.tile(q_scale[i], 2)
        qscols[:, i * 4 + 1] = np.tile(q_scale[i][sigma], 2)
        qscols[:, i * 4 + 2] = np.tile(k_scale[i], 2)
        qscols[:, i * 4 + 3] = np.tile(k_scale[i][sigma], 2)
    DIM_HEAD = 64; WINDOW = 256
    half = np.arange(0, DIM_HEAD, 2, dtype=np.float64)
    inv_freq = 1.0 / 10000.0 ** (half / DIM_HEAD)
    t = np.arange(T, dtype=np.float64)
    freqs = np.concatenate([t[:, None] * inv_freq[None, :]] * 2, -1)
    base_scale = (half + 0.4 * DIM_HEAD) / (1.4 * DIM_HEAD)
    power = (t - T // 2) / (WINDOW // 2)
    scale = np.concatenate([base_scale[None, :] ** power[:, None]] * 2, -1)
    sign = np.where(np.arange(64) < 32, -1.0, 1.0)
    tabA = (np.cos(freqs) * scale).T
    tabB = (np.sin(freqs) * scale).T * sign[:, None]
    tabC = (np.cos(freqs) / scale).T
    tabD = (np.sin(freqs) / scale).T * sign[:, None]
    tabs = np.concatenate([np.tile(tabA, (2, 1)), np.tile(tabB, (2, 1)),
                           np.tile(tabC, (2, 1)), np.tile(tabD, (2, 1))], axis=0).astype(f32)
    masks = np.zeros((512, 512), f32)
    for wflag in range(2):
        for mi in range(2):
            m = np.zeros((128, 512), f32)
            q_off = mi * 128 + np.arange(128)[:, None]
            j = np.arange(512)[None, :]
            allowed = (j >= q_off) & (j <= q_off + 256)
            if wflag == 1:
                allowed &= (j >= 256)
            m[~allowed] = -1e30
            masks[(wflag * 2 + mi) * 128:(wflag * 2 + mi + 1) * 128] = m
    onbm = np.zeros((128, 2), f32)
    onbm[0:64, 0] = 1.0; onbm[64:128, 1] = 1.0
    return dict(wqkv5=wqkv5, wo=np.ascontiguousarray(Wo).astype(f32), w1p=w1p, w2p=w2p,
                biasc=biasc, lngb=lngb, qscols=qscols, tabs=tabs, masks=masks,
                onbm=onbm, onbtm=np.ascontiguousarray(onbm.T), idn=np.eye(128, dtype=f32))


def build_fused():
    nc = bass.Bass("TRN2", target_bir_lowering=False)
    xin = nc.dram_tensor("xin", [D, T], F32, kind="ExternalInput").ap()
    xout = nc.dram_tensor("xout", [D, T], F32, kind="ExternalOutput").ap()
    xh1 = nc.dram_tensor("xh1", [D, T], F32, kind="Internal").ap()
    xh2 = nc.dram_tensor("xh2", [D, T], F32, kind="Internal").ap()
    xh3 = nc.dram_tensor("xh3", [D, T], F32, kind="Internal").ap()
    es = ExitStack()
    with es:
        P = Prog(nc, es)
        ctx = {"nc": nc, "P": P, "pref": "a_", "xin": xin, "xout": xh1, "prev": None}
        build_l0_attn(ctx=ctx)
        ctx.update(pref="b_", xin=xh1, xout=xh2)
        build_l0_ffn(ctx=ctx)
        ctx.update(pref="c_", xin=xh2, xout=xh3)
        build_l1_attn(ctx=ctx)
        ctx.update(pref="d_", xin=xh3, xout=xout)
        build_l1_ffn(ctx=ctx)
        P.finalize()
    return nc


_FUSED_CACHE = {}


def kernel(**inputs):
    d = inputs
    x = np.asarray(d["L0_attn_x"], np.float32)
    if "nc" not in _FUSED_CACHE:
        _FUSED_CACHE["nc"] = build_fused()
    nc = _FUSED_CACHE["nc"]
    am = _prep_l0_attn(d["L0_attn_W_in"], d["L0_attn_W_A"], d["L0_attn_W_B"],
                       d["L0_attn_W_C"], d["L0_attn_W_out"], d["L0_attn_norm_w"])
    bm = _prep_l0_ffn(d["L0_ffn_W1"], d["L0_ffn_b1"], d["L0_ffn_W2"], d["L0_ffn_b2"],
                      d["L0_ffn_scale"])
    cm = _prep_l1_attn(d["L1_attn_W_in"], d["L1_attn_W_x"], d["L1_attn_W_dt"],
                       d["L1_attn_b_dt"], d["L1_attn_gamma"], d["L1_attn_beta"])
    dm = _prep_l1_ffn(d["L1_ffn_Wqkv"], d["L1_ffn_q_scale"], d["L1_ffn_k_scale"],
                      d["L1_ffn_Wo"], d["L1_ffn_ln_g"], d["L1_ffn_ln_b"],
                      d["L1_ffn_W1"], d["L1_ffn_b1"], d["L1_ffn_W2"], d["L1_ffn_b2"])
    wm = {}
    for pref, mm in (("a_", am), ("b_", bm), ("c_", cm), ("d_", dm)):
        for k, v in mm.items():
            wm[pref + k] = v
    in_maps = []
    for b in range(N_CORES):
        m = dict(wm)
        m["xin"] = np.ascontiguousarray(x[b].T).astype(np.float32)
        in_maps.append(m)
    res = run_bass_kernel_spmd(nc, in_maps, core_ids=list(range(N_CORES)))
    return np.ascontiguousarray(
        np.stack([np.ascontiguousarray(r["xout"].T) for r in res.results]))


# version 10 (final)
# speedup vs baseline: 1.5073x; 1.2691x over previous
import numpy as np
from contextlib import ExitStack

import concourse.bass as bass
import concourse.mybir as mybir
from concourse.bass_utils import run_bass_kernel_spmd
from concourse.alu_op_type import AluOpType as OP

F32 = mybir.dt.float32
BF16 = mybir.dt.bfloat16
AF = mybir.ActivationFunctionType
AX = mybir.AxisListType

T = 4096
D = 1024
NCD = 8
N_CORES = 8

C_ONE, C_NEG1, C_NEGHALF, C_EPS_LN, C_EPS_F32, C_ZERO, C_HALF, C_NEG1E30, C_EPS_L2 = range(9)
CPOOL_VALS = [1.0, -1.0, -0.5, 1e-5, float(np.finfo(np.float32).eps), 0.0, 0.5, -1e30, 1e-24]
DBG_TILE = 1


class Prog:
    ENGINES = ("sp", "pool", "act", "dve", "pe")

    def __init__(self, nc, es):
        self.nc = nc
        self.es = es
        self.ops = {e: [] for e in self.ENGINES}
        self.sems = {}
        self.cnt = {}

    def sem(self, name):
        if name not in self.sems:
            self.sems[name] = self.es.enter_context(self.nc.semaphore(name))
            self.cnt[name] = 0
        return self.sems[name]

    def emit(self, eng, fn, waits=(), inc=None, inc_by=1):
        s = None
        if inc is not None:
            s = self.sem(inc)
            self.cnt[inc] += inc_by
        wl = [(self.sem(n), v) for (n, v) in waits]

        def run(e, wl=wl, s=s, inc_by=inc_by, fn=fn):
            for sem, v in wl:
                e.wait_ge(sem, v)
            if fn is not None:
                ins = fn(e)
                if s is not None:
                    ins.then_inc(s, inc_by)
        self.ops[eng].append(run)

    def tok(self, name):
        return (name, self.cnt[name])

    def stage(self, eng, fns, waits=()):
        fns = list(fns)
        assert fns
        for i, fn in enumerate(fns):
            self.emit(eng, fn,
                      waits=waits if i == 0 else (),
                      inc=eng if i == len(fns) - 1 else None)
        return self.tok(eng)

    def finalize(self):
        with self.nc.Block() as block:
            for attr, e in (("sync", "sp"), ("gpsimd", "pool"), ("scalar", "act"),
                            ("vector", "dve"), ("tensor", "pe")):
                ops = self.ops[e]

                def body(eng, ops=ops):
                    for f in ops:
                        f(eng)
                getattr(block, attr)(body)


def _chunked_dma(P, queue, sem_name, dst_tile, src_ap, n_chunks, cols, waits=(), cast=False):
    for c in range(n_chunks):
        def fn(e, c=c):
            return e.dma_start(dst_tile[:, c * cols:(c + 1) * cols],
                               src_ap[c * 128:(c + 1) * 128, :])
        P.emit(queue, fn, waits=waits if c == 0 else (), inc=sem_name, inc_by=16)


def emit_ln(P, t, prev, src_tile, src_off, src_stride, n_chunks, inv_n,
            sq, pstat_x, pstat_x2, stat_mu, stat_rsig, stat_tmp, pbc_mu, pbc_rsig,
            cpool, ones_col, ones_row, out_f32=None, out_bf=None, out_cols=None,
            eps_col=C_EPS_LN, tmp_f32=None, add_to_src=False, ones_sq=None,
            ones_x=None, src_cast=None):
    oc = out_cols if out_cols is not None else t
    if ones_sq is None:
        ones_sq = ones_col
    if ones_x is None:
        ones_x = ones_col
    src = lambda c: src_tile[:, src_off + c * src_stride: src_off + c * src_stride + t]
    prev = P.stage("act", [
        (lambda e, c=c: e.activation(sq[:, c * t:(c + 1) * t], src(c), AF.Square))
        for c in range(n_chunks)], waits=[prev])
    mm = []
    for c in range(n_chunks):
        mm.append(lambda e, c=c: e.matmul(pstat_x[:, :t], ones_x[:], src(c),
                                          start=(c == 0), stop=(c == n_chunks - 1)))
    for c in range(n_chunks):
        mm.append(lambda e, c=c: e.matmul(pstat_x2[:, :t], ones_sq[:], sq[:, c * t:(c + 1) * t],
                                          start=(c == 0), stop=(c == n_chunks - 1)))
    prev = P.stage("pe", mm, waits=[prev])
    prev = P.stage("dve", [
        lambda e: e.tensor_scalar(stat_mu[:, :t], pstat_x[:, :t], inv_n, None, OP.mult),
        lambda e: e.tensor_scalar(stat_tmp[:, :t], pstat_x2[:, :t], inv_n, None, OP.mult),
        lambda e: e.tensor_tensor(stat_rsig[:, :t], stat_mu[:, :t], stat_mu[:, :t], OP.mult),
        lambda e: e.tensor_tensor(stat_tmp[:, :t], stat_tmp[:, :t], stat_rsig[:, :t], OP.subtract),
    ], waits=[prev])
    prev = P.stage("act", [
        lambda e: e.activation(stat_tmp[:, :t], stat_tmp[:, :t], AF.Ln, bias=cpool[0:1, eps_col:eps_col + 1]),
        lambda e: e.activation(stat_rsig[:, :t], stat_tmp[:, :t], AF.Exp, scale=cpool[0:1, C_NEGHALF:C_NEGHALF + 1]),
    ], waits=[prev])
    prev = P.stage("pe", [
        lambda e: e.matmul(pbc_mu[:, :t], ones_row[:], stat_mu[:, :t], start=True, stop=True),
        lambda e: e.matmul(pbc_rsig[:, :t], ones_row[:], stat_rsig[:, :t], start=True, stop=True),
    ], waits=[prev])
    fns = []
    for c in range(n_chunks):
        if tmp_f32 is not None:
            fns.append(lambda e, c=c: e.tensor_tensor(tmp_f32[:, :t], src(c), pbc_mu[:, :t], OP.subtract))
            fns.append(lambda e, c=c: e.tensor_tensor(tmp_f32[:, :t], tmp_f32[:, :t], pbc_rsig[:, :t], OP.mult))
            if out_bf is not None:
                fns.append(lambda e, c=c: e.tensor_copy(out_bf[:, c * oc:c * oc + t], tmp_f32[:, :t]))
            if add_to_src:
                fns.append(lambda e, c=c: e.tensor_tensor(src(c), src(c), tmp_f32[:, :t], OP.add))
        elif out_f32 is not None:
            fns.append(lambda e, c=c: e.tensor_tensor(out_f32[:, c * oc:c * oc + t], src(c), pbc_mu[:, :t], OP.subtract))
            fns.append(lambda e, c=c: e.tensor_tensor(out_f32[:, c * oc:c * oc + t], out_f32[:, c * oc:c * oc + t], pbc_rsig[:, :t], OP.mult))
            if out_bf is not None:
                fns.append(lambda e, c=c: e.tensor_copy(out_bf[:, c * oc:c * oc + t], out_f32[:, c * oc:c * oc + t]))
        elif src_cast is not None:
            fns.append(lambda e, c=c: e.tensor_copy(src_cast[:, :t], src(c)))
            fns.append(lambda e, c=c: e.tensor_tensor(src_cast[:, :t], src_cast[:, :t], pbc_mu[:, :t], OP.subtract))
            fns.append(lambda e, c=c: e.tensor_tensor(out_bf[:, c * oc:c * oc + t], src_cast[:, :t], pbc_rsig[:, :t], OP.mult))
        else:
            fns.append(lambda e, c=c: e.tensor_tensor(out_bf[:, c * oc:c * oc + t], src(c), pbc_mu[:, :t], OP.subtract))
            fns.append(lambda e, c=c: e.tensor_tensor(out_bf[:, c * oc:c * oc + t], out_bf[:, c * oc:c * oc + t], pbc_rsig[:, :t], OP.mult))
    prev = P.stage("dve", fns, waits=[prev])
    return prev


def emit_common_setup(P, es, nc, extra_waits=(), pref=""):
    S = lambda *a, **k: es.enter_context(nc.sbuf_tensor(pref + a[0], *a[1:], **k))
    cpool = S("cpool", [128, len(CPOOL_VALS)], F32)
    ones_col = S("ones_col", [128, 1], F32)
    ones_row = S("ones_row", [1, 128], F32)
    fns = []
    for i, v in enumerate(CPOOL_VALS):
        fns.append(lambda e, i=i, v=v: e.memset(cpool[:, i:i + 1], v))
    ones_bf = S("ones_bf", [128, 1], BF16)
    fns.append(lambda e: e.memset(ones_col[:], 1.0))
    fns.append(lambda e: e.memset(ones_row[:], 1.0))
    fns.append(lambda e: e.memset(ones_bf[:], 1.0))
    tok = P.stage("dve", fns, waits=list(extra_waits))
    return cpool, ones_col, ones_row, ones_bf, tok


def build_l1_attn(debug=False, ctx=None):
    if ctx is None:
        nc = bass.Bass("TRN2", target_bir_lowering=False)
        _pref = ""
    else:
        nc = ctx["nc"]
        _pref = ctx["pref"]
    xin = ctx["xin"] if ctx is not None else nc.dram_tensor("xin", [D, T], F32, kind="ExternalInput").ap()
    w_inT = nc.dram_tensor(_pref + "w_inT", [D, 2 * D], F32, kind="ExternalInput").ap()
    vecs = nc.dram_tensor(_pref + "vecs", [128, 5 * NCD], F32, kind="ExternalInput").ap()
    xout = ctx["xout"] if ctx is not None else nc.dram_tensor("xout", [D, T], F32, kind="ExternalOutput").ap()
    if debug:
        dbg = {k: nc.dram_tensor(_pref + "dbg_" + k, shp, F32, kind="ExternalOutput").ap()
               for k, shp in [("xinT", [128, 8 * 256]), ("zT", [128, 8 * 256]),
                              ("g8", [128, 8 * 256]), ("dtraw", [1, 256]),
                              ("mu", [1, 256]), ("rsig", [1, 256]),
                              ("carry", [128, 8])]}

    TT = 256
    NTILE = T // TT

    es = ExitStack()
    S = lambda *a, **k: es.enter_context(nc.sbuf_tensor(_pref + a[0], *a[1:], **k))
    PS = lambda *a, **k: es.enter_context(nc.psum_tensor(_pref + a[0], *a[1:], **k))
    with es:
        RX = S("RX", [128, NCD * T], F32)
        WIN = S("WIN", [128, NCD * 2 * D], BF16)
        VEC = S("VEC", [128, 5 * NCD], F32)
        xhat_b = S("xhat_b", [128, NCD * TT], BF16)
        sq = S("sq", [128, NCD * TT], F32)
        x_inT = S("x_inT", [128, NCD * TT], F32)
        zT = S("zT", [128, NCD * TT], F32)
        g8 = S("g8", [128, NCD * TT], F32)
        w0 = S("w0", [128, TT], F32)
        w1 = S("w1", [128, TT], F32)
        w2 = S("w2", [128, TT], F32)
        carry = S("carry", [128, NCD], F32)
        stat_mu = S("stat_mu", [1, TT], F32)
        stat_rsig = S("stat_rsig", [1, TT], F32)
        stat_tmp = S("stat_tmp", [1, TT], F32)
        dtraw = S("dtraw", [1, TT], F32)

        pbig0 = PS("pbig0", [128, TT], F32)
        pbig1 = PS("pbig1", [128, TT], F32)
        pstat_x = PS("pstat_x", [1, TT], F32)
        pstat_x2 = PS("pstat_x2", [1, TT], F32)
        pbc_mu = PS("pbc_mu", [128, TT], F32)
        pbc_rsig = PS("pbc_rsig", [128, TT], F32)
        pdtb = PS("pdtb", [128, TT], F32)

        P = ctx["P"] if ctx is not None else Prog(nc, es)
        _pw = [ctx["prev"]] if ctx is not None and ctx.get("prev") else []
        cpool, ones_col, ones_row, ones_bf, tok0 = emit_common_setup(P, es, nc, extra_waits=_pw, pref=_pref)
        P.stage("dve", [lambda e: e.memset(carry[:], 0.0)])

        _chunked_dma(P, "sp", "dqx", RX, xin, NCD, T)
        for c in range(NCD):
            P.emit("pool", lambda e, c=c: e.dma_start(WIN[:, c * 2 * D:(c + 1) * 2 * D],
                                                      w_inT[c * 128:(c + 1) * 128, :]),
                   waits=_pw if c == 0 else (), inc="dqw", inc_by=16)
        P.emit("pool", lambda e: e.dma_start(VEC[:], vecs), inc="dqw", inc_by=16)
        wx0 = lambda c: VEC[:, 0 * NCD + c:0 * NCD + c + 1]
        wdt = lambda c: VEC[:, 1 * NCD + c:1 * NCD + c + 1]
        bdt = lambda c: VEC[:, 2 * NCD + c:2 * NCD + c + 1]
        gam = lambda c: VEC[:, 3 * NCD + c:3 * NCD + c + 1]
        bet = lambda c: VEC[:, 4 * NCD + c:4 * NCD + c + 1]

        loads_tok = [("dqx", P.cnt["dqx"]), ("dqw", P.cnt["dqw"]), P.tok("dve")]

        prev = P.stage("dve", [lambda e: e.engine_nop()], waits=loads_tok)

        for it in range(NTILE):
            t0 = it * TT
            prev = emit_ln(P, TT, prev, RX, t0, T, NCD, 1.0 / D,
                           sq, pstat_x, pstat_x2, stat_mu, stat_rsig, stat_tmp,
                           pbc_mu, pbc_rsig, cpool, ones_col, ones_row,
                           out_bf=xhat_b, tmp_f32=w0, add_to_src=True)
            if debug and it == DBG_TILE:
                P.emit("sp", lambda e: e.dma_start(dbg["mu"], stat_mu[:]), waits=[prev], inc="dqo", inc_by=16)
                P.emit("sp", lambda e: e.dma_start(dbg["rsig"], stat_rsig[:]), inc="dqo", inc_by=16)
                prev = P.stage("dve", [lambda e: e.engine_nop()], waits=[P.tok("dqo")])
            for m in range(16):
                pb = pbig0 if m % 2 == 0 else pbig1
                mm = [(lambda e, c=c, m=m, pb=pb: e.matmul(
                    pb[:], WIN[:, c * 2 * D + m * 128: c * 2 * D + (m + 1) * 128],
                    xhat_b[:, c * TT:(c + 1) * TT], start=(c == 0), stop=(c == NCD - 1)))
                    for c in range(NCD)]
                prev = P.stage("pe", mm, waits=[prev])
                if m < 8:
                    prev = P.stage("act", [lambda e, m=m, pb=pb: e.activation(
                        x_inT[:, m * TT:(m + 1) * TT], pb[:], AF.Tanh)], waits=[prev])
                else:
                    prev = P.stage("act", [lambda e, m=m, pb=pb: e.activation(
                        zT[:, (m - 8) * TT:(m - 7) * TT], pb[:], AF.Sigmoid)], waits=[prev])
            mm = [(lambda e, c=c: e.matmul(pstat_x[:], wx0(c), x_inT[:, c * TT:(c + 1) * TT],
                                           start=(c == 0), stop=(c == NCD - 1)))
                  for c in range(NCD)]
            prev = P.stage("pe", mm, waits=[prev])
            prev = P.stage("dve", [lambda e: e.tensor_copy(dtraw[:], pstat_x[:])], waits=[prev])
            prev = P.stage("pe", [lambda e: e.matmul(pdtb[:], ones_row[:], dtraw[:],
                                                     start=True, stop=True)], waits=[prev])
            for c in range(NCD):
                prev = P.stage("act", [
                    lambda e, c=c: e.activation(w0[:], pdtb[:], AF.Sigmoid, scale=wdt(c), bias=bdt(c)),
                    lambda e, c=c: e.activation(w0[:], w0[:], AF.Sigmoid, scale=cpool[:, C_NEG1:C_NEG1 + 1]),
                ], waits=[prev])
                prev = P.stage("dve", [
                    lambda e, c=c: e.tensor_scalar(w1[:], w0[:], -1.0, 1.0, OP.mult, OP.add),
                    lambda e, c=c: e.tensor_tensor(w1[:], w1[:], x_inT[:, c * TT:(c + 1) * TT], OP.mult),
                    lambda e, c=c: e.tensor_tensor_scan(w2[:], w0[:], w1[:], carry[:, c:c + 1], OP.mult, OP.add),
                ], waits=[prev])
                prev = P.stage("dve", [
                    lambda e, c=c: e.tensor_copy(carry[:, c:c + 1], w2[:, TT - 1:TT]),
                    lambda e, c=c: e.tensor_tensor(g8[:, c * TT:(c + 1) * TT], w2[:], zT[:, c * TT:(c + 1) * TT], OP.mult),
                ], waits=[prev])
            if debug and it == DBG_TILE:
                P.emit("sp", lambda e: e.dma_start(dbg["xinT"], x_inT[:]), waits=[prev], inc="dqo", inc_by=16)
                P.emit("sp", lambda e: e.dma_start(dbg["zT"], zT[:]), inc="dqo", inc_by=16)
                P.emit("sp", lambda e: e.dma_start(dbg["g8"], g8[:]), inc="dqo", inc_by=16)
                P.emit("sp", lambda e: e.dma_start(dbg["dtraw"], dtraw[:]), inc="dqo", inc_by=16)
                P.emit("sp", lambda e: e.dma_start(dbg["carry"], carry[:]), inc="dqo", inc_by=16)
                prev = P.stage("dve", [lambda e: e.engine_nop()], waits=[P.tok("dqo")])
            prev = emit_ln(P, TT, prev, g8, 0, TT, NCD, 1.0 / D,
                           sq, pstat_x, pstat_x2, stat_mu, stat_rsig, stat_tmp,
                           pbc_mu, pbc_rsig, cpool, ones_col, ones_row,
                           out_f32=None, out_bf=sq)
            fns = []
            for c in range(NCD):
                fns.append(lambda e, c=c: e.tensor_scalar(
                    sq[:, c * TT:(c + 1) * TT], sq[:, c * TT:(c + 1) * TT],
                    gam(c), bet(c), OP.mult, OP.add))
                fns.append(lambda e, c=c, t0=t0: e.tensor_tensor(
                    RX[:, c * T + t0: c * T + t0 + TT], RX[:, c * T + t0: c * T + t0 + TT],
                    sq[:, c * TT:(c + 1) * TT], OP.add))
            prev = P.stage("dve", fns, waits=[prev])

        for c in range(NCD):
            P.emit("sp", lambda e, c=c: e.dma_start(xout[c * 128:(c + 1) * 128, :],
                                                    RX[:, c * T:(c + 1) * T]),
                   waits=[prev] if c == 0 else (), inc="dqo", inc_by=16)
        P.emit("sp", None, waits=[P.tok("dqo")])
        if ctx is not None:
            ctx["prev"] = prev
            ctx["out_tok"] = P.tok("dqo")
        else:
            P.finalize()
    return nc


def run_l1_attn(x, W_in, W_x, W_dt, b_dt, gamma, beta, debug=False):
    nc = build_l1_attn(debug=debug)
    w_inT = np.ascontiguousarray(W_in.T).astype(np.float32)
    vecs = np.concatenate([
        W_x[0].reshape(NCD, 128).T,
        W_dt[:, 0].reshape(NCD, 128).T,
        b_dt.reshape(NCD, 128).T,
        gamma.reshape(NCD, 128).T,
        beta.reshape(NCD, 128).T,
    ], axis=1).astype(np.float32)
    in_maps = []
    for b in range(N_CORES):
        in_maps.append(dict(xin=np.ascontiguousarray(x[b].T).astype(np.float32),
                            w_inT=w_inT, vecs=vecs))
    res = run_bass_kernel_spmd(nc, in_maps, core_ids=list(range(N_CORES)))
    out = np.stack([np.ascontiguousarray(r["xout"].T) for r in res.results])
    if debug:
        return out, res.results[0]
    return out


def build_l0_ffn(debug=False, ctx=None):
    if ctx is None:
        nc = bass.Bass("TRN2", target_bir_lowering=False)
        _pref = ""
    else:
        nc = ctx["nc"]
        _pref = ctx["pref"]
    H = 4096
    xin = ctx["xin"] if ctx is not None else nc.dram_tensor("xin", [D, T], F32, kind="ExternalInput").ap()
    w1T = nc.dram_tensor(_pref + "w1T", [D, H], F32, kind="ExternalInput").ap()
    w2T = nc.dram_tensor(_pref + "w2T", [H, D], F32, kind="ExternalInput").ap()
    b1c = nc.dram_tensor(_pref + "b1c", [128, 32], F32, kind="ExternalInput").ap()
    b2c = nc.dram_tensor(_pref + "b2c", [128, 8], F32, kind="ExternalInput").ap()
    sclr = nc.dram_tensor(_pref + "sclr", [128, 1], F32, kind="ExternalInput").ap()
    xout = ctx["xout"] if ctx is not None else nc.dram_tensor("xout", [D, T], F32, kind="ExternalOutput").ap()
    e2scr = nc.dram_tensor(_pref + "e2scr", [16, D], BF16, kind="Internal").ap()
    if debug:
        dbg = {k: nc.dram_tensor(_pref + "dbg_" + k, shp, BF16, kind="ExternalOutput").ap()
               for k, shp in [("xhat", [128, 8 * 256]), ("urow", [1, 4 * 256]),
                              ("hact", [128, 32 * 256]), ("t4", [4, 4 * 256]),
                              ("e2", [4, 4 * 256]), ("evn", [1, 4 * 1024]),
                              ("o2", [128, 32 * 16]), ("lt", [128, 8 * 16])]}

    TT = 256
    NTILE = T // TT
    NH = 32

    es = ExitStack()
    S = lambda *a, **k: es.enter_context(nc.sbuf_tensor(_pref + a[0], *a[1:], **k))
    PS = lambda *a, **k: es.enter_context(nc.psum_tensor(_pref + a[0], *a[1:], **k))
    with es:
        RX = S("RX", [128, NCD * T], F32)
        xhat_b = S("xhat_b", [128, NCD * TT], BF16)
        h_act = S("h_act", [128, NH * TT], BF16)
        sq = S("sq", [128, NCD * TT], BF16)
        w0 = S("w0", [128, TT], F32)
        B1 = S("B1", [128, 32], F32)
        B2 = S("B2", [128, 8], F32)
        SCL = S("SCL", [128, 1], F32)
        LT = S("LT", [128, NCD * 16], BF16)
        LTOK = S("LTOK", [16, D], BF16)
        O2 = S("O2", [128, NH * 16], BF16)
        E2 = S("E2", [4, 4 * 256], BF16)
        EVN = S("EVN", [1, 4 * D], BF16)
        IDN = S("IDN", [128, 128], BF16)
        UROW = S("UROW", [1, 4 * TT], BF16)
        T4 = S("T4", [4, 4 * TT], BF16)
        wb1a = S("wb1a", [128, NCD * 512], BF16)
        wb1b = S("wb1b", [128, NCD * 512], BF16)
        wb2a = wb1a.rearrange("p (c n) -> p (c n)", c=1) if False else wb1a
        wb2b = wb1b
        stat_mu = S("stat_mu", [1, TT], F32)
        stat_rsig = S("stat_rsig", [1, TT], F32)
        stat_tmp = S("stat_tmp", [1, TT], F32)

        PP = [PS("P%d" % i, [128, 512], F32) for i in range(7)]
        PTR = PS("PTR", [16, 1024], BF16)
        pb = lambda i: PP[i]

        P = ctx["P"] if ctx is not None else Prog(nc, es)
        _pw = [ctx["prev"]] if ctx is not None and ctx.get("prev") else []
        cpool, ones_col, ones_row, ones_bf, tok0 = emit_common_setup(P, es, nc, extra_waits=_pw, pref=_pref)

        _chunked_dma(P, "sp", "dqx", RX, xin, NCD, T)
        P.emit("pool", lambda e: e.dma_start(B1[:], b1c), waits=_pw, inc="dqw", inc_by=16)
        P.emit("pool", lambda e: e.dma_start(B2[:], b2c), inc="dqw", inc_by=16)
        P.emit("pool", lambda e: e.dma_start(SCL[:], sclr), inc="dqw", inc_by=16)
        loads_tok = [("dqx", P.cnt["dqx"]), ("dqw", P.cnt["dqw"]), P.tok("dve")]
        idn = nc.dram_tensor(_pref + "idn", [128, 128], F32, kind="ExternalInput").ap()
        P.emit("pool", lambda e: e.dma_start(IDN[:], idn), inc="dqw", inc_by=16)
        loads_tok = [("dqx", P.cnt["dqx"]), ("dqw", P.cnt["dqw"]), P.tok("dve")]

        prev = P.stage("dve", [lambda e: e.engine_nop()], waits=loads_tok)

        for it in range(NTILE):
            t0 = it * TT
            prev = emit_ln(P, TT, prev, RX, t0, T, NCD, 1.0 / D,
                           sq, pb(4)[0:1], pb(5)[0:1], stat_mu, stat_rsig, stat_tmp,
                           pb(2), pb(3), cpool, ones_col, ones_row, out_bf=xhat_b,
                           ones_sq=ones_bf)
            if it == 0:
                fns = [(lambda e, c=c: e.tensor_copy(LT[:, c * 16:(c + 1) * 16],
                                                     xhat_b[:, c * TT + 1:c * TT + 17]))
                       for c in range(NCD)]
                prev = P.stage("dve", fns, waits=[prev])
                mm = [(lambda e, c=c: e.transpose(PTR[:, c * 128:(c + 1) * 128],
                                                  LT[:, c * 16:(c + 1) * 16], IDN[:]))
                      for c in range(NCD)]
                prev = P.stage("pe", mm, waits=[prev])
                prev = P.stage("dve", [lambda e: e.tensor_copy(LTOK[:], PTR[:])], waits=[prev])
                fns = [lambda e: e.memset(O2[:], 0.0)]
                for c2 in range(4):
                    for r in range(4):
                        for j in range(NCD):
                            k = c2 * 8 + j
                            fns.append(lambda e, k=k, r=r, c2=c2, j=j: e.tensor_copy(
                                O2[:, k * 16 + r * 4 + c2: k * 16 + r * 4 + c2 + 1],
                                LT[:, j * 16 + 9 + 2 * r: j * 16 + 10 + 2 * r]))
                prev = P.stage("dve", fns, waits=[prev])
                P.emit("pool", lambda e: e.dma_start(e2scr, LTOK[:]), waits=[prev], inc="dqe", inc_by=16)
                for r in range(4):
                    P.emit("pool", lambda e, r=r: e.dma_start(
                        E2[:, r * 256:(r + 1) * 256],
                        e2scr[8 + 2 * r:9 + 2 * r, :].rearrange("a (u c) -> (a c) u", c=4),
                        allow_slow_non_contiguous=True), waits=[("dqe", 16)], inc="dqe", inc_by=16)
                    P.emit("pool", lambda e, r=r: e.dma_start(
                        EVN[:, r * D:(r + 1) * D], e2scr[2 * r:2 * r + 1, :]),
                        inc="dqe", inc_by=16)
                prev = P.stage("dve", [lambda e: e.engine_nop()], waits=[("dqe", P.cnt["dqe"])])

            for r in range(4):
                mm = [(lambda e, c=c, r=r: e.matmul(pb(6)[0:1, :TT],
                                                    LT[:, c * 16 + 1 + 2 * r: c * 16 + 2 + 2 * r],
                                                    xhat_b[:, c * TT:(c + 1) * TT],
                                                    start=(c == 0), stop=(c == NCD - 1)))
                      for c in range(NCD)]
                prev = P.stage("pe", mm, waits=[prev])
                prev = P.stage("dve", [lambda e, r=r: e.tensor_scalar(
                    UROW[:, r * TT:(r + 1) * TT], pb(6)[0:1, :TT], SCL[0:1, 0:1], None, OP.mult)],
                    waits=[prev])

            for mg in range(8):
                wb = wb1a if mg % 2 == 0 else wb1b
                P.emit("pool", lambda e, mg=mg, wb=wb: e.dma_start(
                    wb[:].rearrange("p (c n) -> p c n", c=NCD),
                    w1T.rearrange("(c p) n -> p c n", p=128)[:, :, mg * 512:(mg + 1) * 512]),
                    waits=[prev], inc="dqw1", inc_by=16)
                wtok = ("dqw1", P.cnt["dqw1"])
                for mi in range(4):
                    m = mg * 4 + mi
                    r = m // 8
                    pm = pb(mi % 2)
                    mm = [(lambda e, c=c, mi=mi, wb=wb, pm=pm: e.matmul(
                        pm[:, :TT], wb[:, c * 512 + mi * 128: c * 512 + (mi + 1) * 128],
                        xhat_b[:, c * TT:(c + 1) * TT], start=(c == 0), stop=False))
                          for c in range(NCD)]
                    mm.append(lambda e, r=r, m=m, pm=pm: e.matmul(
                        pm[:, :TT], EVN[:, r * D + (m % 8) * 128: r * D + (m % 8 + 1) * 128],
                        UROW[:, r * TT:(r + 1) * TT], start=False, stop=True))
                    prev = P.stage("pe", mm, waits=[prev, wtok] if mi == 0 else [prev])
                    prev = P.stage("act", [lambda e, m=m, pm=pm: e.activation(
                        h_act[:, m * TT:(m + 1) * TT], pm[:, :TT], AF.Gelu,
                        bias=B1[:, m:m + 1])], waits=[prev])

            for r in range(4):
                mm = [(lambda e, k=k, r=r: e.matmul(pb(6)[0:4, :TT],
                                                    O2[:, k * 16 + r * 4: k * 16 + r * 4 + 4],
                                                    h_act[:, k * TT:(k + 1) * TT],
                                                    start=(k == 0), stop=(k == NH - 1)))
                      for k in range(NH)]
                prev = P.stage("pe", mm, waits=[prev])
                prev = P.stage("dve", [lambda e, r=r: e.tensor_scalar(
                    T4[:, r * TT:(r + 1) * TT], pb(6)[0:4, :TT], SCL[0:4, 0:1], None, OP.mult)],
                    waits=[prev])

            if debug and it == 0:
                for k, tile in [("xhat", xhat_b), ("urow", UROW), ("hact", h_act),
                                ("t4", T4), ("e2", E2), ("evn", EVN), ("o2", O2), ("lt", LT)]:
                    P.emit("sp", lambda e, k=k, tile=tile: e.dma_start(dbg[k], tile[:]),
                           waits=[prev], inc="dqo", inc_by=16)
                prev = P.stage("dve", [lambda e: e.engine_nop()], waits=[P.tok("dqo")])
            for mo in range(8):
                wb = wb2a if mo % 2 == 0 else wb2b
                P.emit("pool", lambda e, mo=mo, wb=wb: e.dma_start(
                    wb[:].rearrange("p (c n) -> p c n", c=NH),
                    w2T.rearrange("(c p) n -> p c n", p=128)[:, :, mo * 128:(mo + 1) * 128]),
                    waits=[prev], inc="dqw2", inc_by=16)
                wtok = ("dqw2", P.cnt["dqw2"])
                pm = pb(mo % 2)
                r = mo // 2
                mm = [(lambda e, k=k, wb=wb, pm=pm: e.matmul(
                    pm[:, :TT], wb[:, k * 128:(k + 1) * 128],
                    h_act[:, k * TT:(k + 1) * TT], start=(k == 0), stop=False))
                      for k in range(NH)]
                mm.append(lambda e, r=r, mo=mo, pm=pm: e.matmul(
                    pm[:, :TT], E2[:, r * 256 + (mo % 2) * 128: r * 256 + (mo % 2 + 1) * 128],
                    T4[:, r * TT:(r + 1) * TT], start=False, stop=True))
                prev = P.stage("pe", mm, waits=[prev, wtok])
                prev = P.stage("dve", [
                    lambda e, mo=mo, pm=pm: e.tensor_scalar(w0[:], pm[:, :TT], B2[:, mo:mo + 1], None, OP.add),
                    lambda e, mo=mo, t0=t0: e.tensor_tensor(
                        RX[:, mo * T + t0: mo * T + t0 + TT], RX[:, mo * T + t0: mo * T + t0 + TT],
                        w0[:], OP.add),
                ], waits=[prev])

        for c in range(NCD):
            P.emit("sp", lambda e, c=c: e.dma_start(xout[c * 128:(c + 1) * 128, :],
                                                    RX[:, c * T:(c + 1) * T]),
                   waits=[prev] if c == 0 else (), inc="dqo", inc_by=16)
        P.emit("sp", None, waits=[P.tok("dqo")])
        if ctx is not None:
            ctx["prev"] = prev
            ctx["out_tok"] = P.tok("dqo")
        else:
            P.finalize()
    return nc


def run_l0_ffn(x, W1, b1, W2, b2, scale, debug=False):
    nc = build_l0_ffn(debug=debug)
    w1T = np.ascontiguousarray(W1.T).astype(np.float32)
    w2T = np.ascontiguousarray(W2.T).astype(np.float32)
    b1c = np.ascontiguousarray(b1.reshape(32, 128).T).astype(np.float32)
    b2c = np.ascontiguousarray(b2.reshape(8, 128).T).astype(np.float32)
    sclr = np.full((128, 1), np.float32(scale), np.float32)
    idn = np.eye(128, dtype=np.float32)
    in_maps = []
    for b in range(N_CORES):
        in_maps.append(dict(xin=np.ascontiguousarray(x[b].T).astype(np.float32),
                            w1T=w1T, w2T=w2T, b1c=b1c, b2c=b2c, sclr=sclr, idn=idn))
    res = run_bass_kernel_spmd(nc, in_maps, core_ids=list(range(N_CORES)))
    out = np.stack([np.ascontiguousarray(r["xout"].T) for r in res.results])
    if debug:
        return out, res.results[0]
    return out



def build_l0_attn(debug=False, ctx=None):
    if ctx is None:
        nc = bass.Bass("TRN2", target_bir_lowering=False)
        _pref = ""
    else:
        nc = ctx["nc"]
        _pref = ctx["pref"]
    DI = 2048
    NDI = 16
    Q = 64
    xin = ctx["xin"] if ctx is not None else nc.dram_tensor("xin", [D, T], F32, kind="ExternalInput").ap()
    w_in = nc.dram_tensor(_pref + "w_in", [D, 2 * DI], F32, kind="ExternalInput").ap()
    w_abc = nc.dram_tensor(_pref + "w_abc", [DI, 48], F32, kind="ExternalInput").ap()
    w_out = nc.dram_tensor(_pref + "w_out", [DI, D], F32, kind="ExternalInput").ap()
    nwrep = nc.dram_tensor(_pref + "nwrep", [128, DI], F32, kind="ExternalInput").ap()
    trim = nc.dram_tensor(_pref + "trim", [Q, Q], F32, kind="ExternalInput").ap()
    idn = nc.dram_tensor(_pref + "idn", [128, 128], F32, kind="ExternalInput").ap()
    xout = ctx["xout"] if ctx is not None else nc.dram_tensor("xout", [D, T], F32, kind="ExternalOutput").ap()
    if debug:
        dbg = {k: nc.dram_tensor(_pref + "dbg_" + k, shp, F32, kind="ExternalOutput").ap()
               for k, shp in [("abc", [64, 3 * 32]), ("cum", [64, 16]), ("y", [64, 2048]),
                              ("yn", [64, 2048]), ("sst", [16, 2048]), ("xtok", [64, 2 * 2048])]}

    TT = 128
    NTILE = T // TT
    NQ = TT // Q

    es = ExitStack()
    S = lambda *a, **k: es.enter_context(nc.sbuf_tensor(_pref + a[0], *a[1:], **k))
    PS = lambda *a, **k: es.enter_context(nc.psum_tensor(_pref + a[0], *a[1:], **k))
    with es:
        RX = S("RX", [128, NCD * T], F32)
        xhat_b = S("xhat_b", [128, NCD * TT], BF16)
        x_inT = S("x_inT", [128, NDI * TT], BF16)
        XT64 = S("XT64", [64, 2 * DI], BF16)
        SCRB = S("SCRB", [128, 2048], BF16)
        WABC = S("WABC", [128, NDI * 48], BF16)
        NW = S("NW", [128, DI], BF16)
        TRI = S("TRI", [Q, Q], F32)
        IDN = S("IDN", [128, 128], BF16)
        AT = S("AT", [64, 2 * 16], F32)
        BT = S("BT", [64, 2 * 16], F32)
        CTl = S("CTl", [64, 2 * 16], F32)
        E1 = S("E1", [Q, 16], F32)
        E2t = S("E2t", [Q, 16], F32)
        BTq = S("BTq", [Q, 16], BF16)
        CTq = S("CTq", [Q, 16], BF16)
        BTt = S("BTt", [16, Q], BF16)
        CTt = S("CTt", [16, Q], BF16)
        MM = S("MM", [Q, Q], BF16)
        SST = S("SST", [16, DI], BF16)
        EAE = S("EAE", [16, 1], F32)
        GBF = S("GBF", [16, 512], BF16)
        YN = S("YN", [Q, DI], BF16)
        YF = S("YF", [128, NDI * TT], BF16)
        RMS = S("RMS", [Q, 1], F32)
        wba = S("wba", [128, NCD * 512], BF16)
        wbb = S("wbb", [128, NCD * 512], BF16)
        w0 = S("w0", [128, TT], F32)
        w1 = S("w1", [128, TT], F32)
        stat_mu = S("stat_mu", [1, TT], F32)
        stat_rsig = S("stat_rsig", [1, TT], F32)
        stat_tmp = S("stat_tmp", [1, TT], F32)

        PSY = PS("PSY", [Q, DI], F32)
        pb0 = PS("pb0", [128, 512], F32)
        pb1 = PS("pb1", [128, 512], F32)
        pb2 = PS("pb2", [128, 512], F32)
        PTB = PS("PTB", [128, 1024], BF16)

        P = ctx["P"] if ctx is not None else Prog(nc, es)
        _pw = [ctx["prev"]] if ctx is not None and ctx.get("prev") else []
        cpool, ones_col, ones_row, ones_bf, tok0 = emit_common_setup(P, es, nc, extra_waits=_pw, pref=_pref)
        P.stage("dve", [lambda e: e.memset(SST[:], 0.0)])

        _chunked_dma(P, "sp", "dqx", RX, xin, NCD, T)
        for c in range(NDI):
            P.emit("pool", lambda e, c=c: e.dma_start(WABC[:, c * 48:(c + 1) * 48],
                                                      w_abc[c * 128:(c + 1) * 128, :]),
                   waits=_pw if c == 0 else (), inc="dqw", inc_by=16)
        P.emit("pool", lambda e: e.dma_start(NW[:], nwrep), inc="dqw", inc_by=16)
        P.emit("pool", lambda e: e.dma_start(TRI[:], trim), inc="dqw", inc_by=16)
        P.emit("pool", lambda e: e.dma_start(IDN[:], idn), inc="dqw", inc_by=16)
        wa = lambda c: WABC[:, c * 48 + 0: c * 48 + 16]
        wb_ = lambda c: WABC[:, c * 48 + 16: c * 48 + 32]
        wc_ = lambda c: WABC[:, c * 48 + 32: c * 48 + 48]
        loads_tok = [("dqx", P.cnt["dqx"]), ("dqw", P.cnt["dqw"]), P.tok("dve")]

        prev = P.stage("dve", [lambda e: e.engine_nop()], waits=loads_tok)

        for it in range(NTILE):
            t0 = it * TT
            prev = emit_ln(P, TT, prev, RX, t0, T, NCD, 1.0 / D,
                           SCRB, pb2[0:1, 0:TT], pb2[0:1, TT:2 * TT], stat_mu, stat_rsig,
                           stat_tmp, pb0, pb1, cpool, ones_col, ones_row,
                           out_bf=xhat_b, ones_sq=ones_bf)
            for g in range(8):
                wbuf = wba if g % 2 == 0 else wbb
                P.emit("pool", lambda e, g=g, wbuf=wbuf: e.dma_start(
                    wbuf[:].rearrange("p (c n) -> p c n", c=NCD),
                    w_in.rearrange("(c p) n -> p c n", p=128)[:, :, g * 512:(g + 1) * 512]),
                    waits=[prev], inc="dqwi", inc_by=16)
                wtok = ("dqwi", P.cnt["dqwi"])
                for pi in range(2):
                    mp = g * 2 + pi
                    mm = [(lambda e, c=c, pi=pi, wbuf=wbuf: e.matmul(
                        pb0[:, :TT], wbuf[:, c * 512 + pi * 256: c * 512 + pi * 256 + 128],
                        xhat_b[:, c * TT:(c + 1) * TT], start=(c == 0), stop=(c == NCD - 1)))
                          for c in range(NCD)]
                    mm += [(lambda e, c=c, pi=pi, wbuf=wbuf: e.matmul(
                        pb1[:, :TT], wbuf[:, c * 512 + pi * 256 + 128: c * 512 + pi * 256 + 256],
                        xhat_b[:, c * TT:(c + 1) * TT], start=(c == 0), stop=(c == NCD - 1)))
                           for c in range(NCD)]
                    prev = P.stage("pe", mm, waits=[prev, wtok] if pi == 0 else [prev])
                    prev = P.stage("act", [
                        lambda e: e.activation(w0[:, :TT], pb0[:, :TT], AF.Silu),
                        lambda e: e.activation(w1[:, :TT], pb1[:, :TT], AF.Sigmoid),
                    ], waits=[prev])
                    prev = P.stage("dve", [lambda e, mp=mp: e.tensor_tensor(
                        x_inT[:, mp * TT:(mp + 1) * TT], w0[:, :TT], w1[:, :TT], OP.mult)],
                        waits=[prev])
                    mm = [(lambda e, q=q, mp=mp: e.transpose(
                        PTB[0:64, q * 128:(q + 1) * 128],
                        x_inT[:, mp * TT + q * 64: mp * TT + (q + 1) * 64], IDN[:]))
                          for q in range(NQ)]
                    prev = P.stage("pe", mm, waits=[prev])
                    prev = P.stage("dve", [(lambda e, q=q, mp=mp: e.tensor_copy(
                        XT64[:, q * DI + mp * 128: q * DI + (mp + 1) * 128],
                        PTB[0:64, q * 128:(q + 1) * 128])) for q in range(NQ)], waits=[prev])
            for q in range(NQ):
                mm = []
                for proj, pbx in ((wa, pb0), (wb_, pb1), (wc_, pb2)):
                    for c in range(NDI):
                        mm.append(lambda e, c=c, q=q, proj=proj, pbx=pbx: e.matmul(
                            pbx[0:64, 0:16], x_inT[:, c * TT + q * 64: c * TT + (q + 1) * 64],
                            proj(c), start=(c == 0), stop=(c == NDI - 1)))
                prev = P.stage("pe", mm, waits=[prev])
                prev = P.stage("dve", [
                    lambda e, q=q: e.tensor_scalar(AT[:, q * 16:(q + 1) * 16], pb0[0:64, 0:16],
                                                   0.0, -5.0, OP.min, OP.max),
                    lambda e, q=q: e.tensor_copy(BT[:, q * 16:(q + 1) * 16], pb1[0:64, 0:16]),
                    lambda e, q=q: e.tensor_copy(CTl[:, q * 16:(q + 1) * 16], pb2[0:64, 0:16]),
                ], waits=[prev])
                prev = P.stage("act", [lambda e, q=q: e.activation(
                    AT[:, q * 16:(q + 1) * 16], AT[:, q * 16:(q + 1) * 16], AF.Exp)],
                    waits=[prev])
                prev = P.stage("dve", [lambda e, q=q: e.tensor_scalar(
                    AT[:, q * 16:(q + 1) * 16], AT[:, q * 16:(q + 1) * 16], -1.0, None, OP.mult)],
                    waits=[prev])
            if debug and it == 0:
                P.emit("sp", lambda e: e.dma_start(dbg["abc"][:, 0:32], AT[:]), waits=[prev], inc="dqo", inc_by=16)
                P.emit("sp", lambda e: e.dma_start(dbg["abc"][:, 32:64], BT[:]), inc="dqo", inc_by=16)
                P.emit("sp", lambda e: e.dma_start(dbg["abc"][:, 64:96], CTl[:]), inc="dqo", inc_by=16)
                P.emit("pool", lambda e: e.dma_start(dbg["xtok"], XT64[:]), waits=[prev], inc="dqo", inc_by=16)
                prev = P.stage("dve", [lambda e: e.engine_nop()], waits=[P.tok("dqo")])
            for q in range(NQ):
                atq = AT[:, q * 16:(q + 1) * 16]
                btq = BT[:, q * 16:(q + 1) * 16]
                ctq = CTl[:, q * 16:(q + 1) * 16]
                xq = lambda ns, q=q: XT64[:, q * DI + ns * 512: q * DI + (ns + 1) * 512]
                prev = P.stage("pe", [
                    lambda e, atq=atq: e.matmul(pb0[0:64, 0:16], TRI[:], atq, start=True, stop=True),
                    lambda e, atq=atq: e.matmul(pb1[0:16, 0:1], atq, ones_col[0:64, :], start=True, stop=True),
                ], waits=[prev])
                prev = P.stage("act", [
                    lambda e: e.activation(E1[:], pb0[0:64, 0:16], AF.Exp),
                    lambda e: e.activation(E2t[:], pb0[0:64, 0:16], AF.Exp, scale=cpool[0:64, C_NEG1:C_NEG1 + 1]),
                    lambda e: e.activation(EAE[:], pb1[0:16, 0:1], AF.Exp),
                ], waits=[prev])
                prev = P.stage("dve", [
                    lambda e, btq=btq: e.tensor_tensor(BTq[:], btq, E2t[:], OP.mult),
                    lambda e, ctq=ctq: e.tensor_tensor(CTq[:], ctq, E1[:], OP.mult),
                ], waits=[prev])
                prev = P.stage("pe", [
                    lambda e: e.transpose(PTB[0:16, 0:64], BTq[:], IDN[0:64, 0:64]),
                    lambda e: e.transpose(PTB[0:16, 64:128], CTq[:], IDN[0:64, 0:64]),
                ], waits=[prev])
                prev = P.stage("dve", [
                    lambda e: e.tensor_copy(BTt[:], PTB[0:16, 0:64]),
                    lambda e: e.tensor_copy(CTt[:], PTB[0:16, 64:128]),
                ], waits=[prev])
                prev = P.stage("pe", [lambda e: e.matmul(pb0[0:64, 0:64], BTt[:], CTt[:],
                                                         start=True, stop=True)], waits=[prev])
                prev = P.stage("dve", [lambda e: e.tensor_tensor(MM[:], pb0[0:64, 0:64], TRI[:], OP.mult)],
                               waits=[prev])
                mm = []
                for ns in range(4):
                    mm.append(lambda e, ns=ns, xq=xq: e.matmul(
                        PSY[:, ns * 512:(ns + 1) * 512], MM[:], xq(ns), start=True, stop=False))
                    mm.append(lambda e, ns=ns: e.matmul(
                        PSY[:, ns * 512:(ns + 1) * 512], CTt[:], SST[:, ns * 512:(ns + 1) * 512],
                        start=False, stop=True))
                prev = P.stage("pe", mm, waits=[prev])
                for ns in range(4):
                    prev = P.stage("pe", [lambda e, ns=ns, xq=xq: e.matmul(
                        pb2[0:16, 0:512], BTq[:], xq(ns), start=True, stop=True)], waits=[prev])
                    prev = P.stage("dve", [
                        lambda e: e.tensor_scalar(GBF[:], pb2[0:16, 0:512], EAE[0:16, 0:1], None, OP.mult),
                        lambda e, ns=ns: e.tensor_scalar(SST[:, ns * 512:(ns + 1) * 512],
                                                         SST[:, ns * 512:(ns + 1) * 512],
                                                         EAE[0:16, 0:1], None, OP.mult),
                        lambda e, ns=ns: e.tensor_tensor(SST[:, ns * 512:(ns + 1) * 512],
                                                         SST[:, ns * 512:(ns + 1) * 512],
                                                         GBF[:], OP.add),
                    ], waits=[prev])
                if debug and it == 0 and q == 0:
                    P.emit("sp", lambda e: e.dma_start(dbg["cum"], E1[:]), waits=[prev], inc="dqo", inc_by=16)
                    P.emit("pool", lambda e: e.dma_start(dbg["sst"], SST[:]), waits=[prev], inc="dqo", inc_by=16)
                    prev = P.stage("dve", [lambda e: e.engine_nop()], waits=[P.tok("dqo")])
                prev = P.stage("act", [lambda e: e.activation(
                    SCRB[0:64, 0:2048], PSY[:], AF.Square, accum_out=RMS[:])], waits=[prev])
                prev = P.stage("dve", [lambda e: e.tensor_scalar(RMS[:], RMS[:], 1.0 / DI, None, OP.mult)],
                               waits=[prev])
                prev = P.stage("act", [
                    lambda e: e.activation(RMS[:], RMS[:], AF.Ln, bias=cpool[0:64, C_EPS_F32:C_EPS_F32 + 1]),
                    lambda e: e.activation(RMS[:], RMS[:], AF.Exp, scale=cpool[0:64, C_NEGHALF:C_NEGHALF + 1]),
                ], waits=[prev])
                prev = P.stage("dve", [
                    lambda e: e.tensor_scalar(YN[:], PSY[:], RMS[0:64, 0:1], None, OP.mult),
                    lambda e: e.tensor_tensor(YN[:], YN[:], NW[0:64, :], OP.mult),
                ], waits=[prev])
                if debug and it == 0 and q == 0:
                    P.emit("pool", lambda e: e.dma_start(dbg["yn"], YN[:]), waits=[prev], inc="dqo", inc_by=16)
                    prev = P.stage("dve", [lambda e: e.engine_nop()], waits=[P.tok("dqo")])
                for half in range(2):
                    mm = [(lambda e, k=k, half=half: e.transpose(
                        PTB[:, k * 64:(k + 1) * 64],
                        YN[:, (half * 8 + k) * 128:(half * 8 + k + 1) * 128], IDN[0:64, 0:64]))
                          for k in range(8)]
                    prev = P.stage("pe", mm, waits=[prev])
                    prev = P.stage("dve", [(lambda e, k=k, half=half, q=q: e.tensor_copy(
                        YF[:, (half * 8 + k) * TT + q * 64: (half * 8 + k) * TT + (q + 1) * 64],
                        PTB[:, k * 64:(k + 1) * 64])) for k in range(8)], waits=[prev])
            for og in range(4):
                wbuf = wba if og % 2 == 0 else wbb
                P.emit("pool", lambda e, og=og, wbuf=wbuf: e.dma_start(
                    wbuf[:, 0:NDI * 256].rearrange("p (c n) -> p c n", c=NDI),
                    w_out.rearrange("(c p) n -> p c n", p=128)[:, :, og * 256:(og + 1) * 256]),
                    waits=[prev], inc="dqwo", inc_by=16)
                wtok = ("dqwo", P.cnt["dqwo"])
                for oi in range(2):
                    mo = og * 2 + oi
                    pm = pb0 if oi == 0 else pb1
                    mm = [(lambda e, k=k, oi=oi, wbuf=wbuf, pm=pm: e.matmul(
                        pm[:, :TT], wbuf[:, k * 256 + oi * 128: k * 256 + (oi + 1) * 128],
                        YF[:, k * TT:(k + 1) * TT], start=(k == 0), stop=(k == NDI - 1)))
                          for k in range(NDI)]
                    prev = P.stage("pe", mm, waits=[prev, wtok] if oi == 0 else [prev])
                    prev = P.stage("dve", [lambda e, mo=mo, t0=t0, pm=pm: e.tensor_tensor(
                        RX[:, mo * T + t0: mo * T + t0 + TT], RX[:, mo * T + t0: mo * T + t0 + TT],
                        pm[:, :TT], OP.add)], waits=[prev])

        for c in range(NCD):
            P.emit("sp", lambda e, c=c: e.dma_start(xout[c * 128:(c + 1) * 128, :],
                                                    RX[:, c * T:(c + 1) * T]),
                   waits=[prev] if c == 0 else (), inc="dqo", inc_by=16)
        P.emit("sp", None, waits=[P.tok("dqo")])
        if ctx is not None:
            ctx["prev"] = prev
            ctx["out_tok"] = P.tok("dqo")
        else:
            P.finalize()
    return nc


def run_l0_attn(x, W_in, W_A, W_B, W_C, W_out, norm_w, debug=False):
    nc = build_l0_attn(debug=debug)
    DI = 2048
    wi = W_in.reshape(1024, 32, 128)
    order = []
    for mp in range(16):
        order += [mp, mp + 16]
    w_in = np.ascontiguousarray(wi[:, order, :].reshape(1024, 4096)).astype(np.float32)
    w_abc = np.ascontiguousarray(np.concatenate([W_A, W_B, W_C], axis=1)).astype(np.float32)
    w_outc = np.ascontiguousarray(W_out).astype(np.float32)
    nwrep = np.ascontiguousarray(np.broadcast_to(norm_w[None, :], (128, DI))).astype(np.float32)
    tau = np.arange(64)[:, None]; t = np.arange(64)[None, :]
    trim = (tau <= t).astype(np.float32)
    idn = np.eye(128, dtype=np.float32)
    in_maps = []
    for b in range(N_CORES):
        in_maps.append(dict(xin=np.ascontiguousarray(x[b].T).astype(np.float32),
                            w_in=w_in, w_abc=w_abc, w_out=w_outc, nwrep=nwrep,
                            trim=trim, idn=idn))
    res = run_bass_kernel_spmd(nc, in_maps, core_ids=list(range(N_CORES)))
    out = np.stack([np.ascontiguousarray(r["xout"].T) for r in res.results])
    if debug:
        return out, res.results[0]
    return out



def build_l1_ffn(debug=False, ctx=None, DEPTH_RUN=6, NW_RUN=16, FT_RUN=8):
    if ctx is None:
        nc = bass.Bass("TRN2", target_bir_lowering=False)
        _pref = ""
    else:
        nc = ctx["nc"]
        _pref = ctx["pref"]
    DEPTH = 6
    W = 256
    NW = T // W
    HC = 22
    FST = 512
    xin = ctx["xin"] if ctx is not None else nc.dram_tensor("xin", [D, T], F32, kind="ExternalInput").ap()
    wqkv5 = nc.dram_tensor(_pref + "wqkv5", [DEPTH, D, 5 * D], F32, kind="ExternalInput").ap()
    wo = nc.dram_tensor(_pref + "wo", [DEPTH, D, D], F32, kind="ExternalInput").ap()
    w1p = nc.dram_tensor(_pref + "w1p", [DEPTH, D, 2 * 2816], F32, kind="ExternalInput").ap()
    w2p = nc.dram_tensor(_pref + "w2p", [DEPTH, 2816, D], F32, kind="ExternalInput").ap()
    biasc = nc.dram_tensor(_pref + "biasc", [128, 52 * DEPTH], F32, kind="ExternalInput").ap()
    lngb = nc.dram_tensor(_pref + "lngb", [128, 16 * DEPTH], F32, kind="ExternalInput").ap()
    qscols = nc.dram_tensor(_pref + "qscols", [128, 4 * DEPTH], F32, kind="ExternalInput").ap()
    tabs = nc.dram_tensor(_pref + "tabs", [4 * 128, T], F32, kind="ExternalInput").ap()
    masks = nc.dram_tensor(_pref + "masks", [4 * 128, 2 * W], F32, kind="ExternalInput").ap()
    onbm = nc.dram_tensor(_pref + "onbm", [128, 2], F32, kind="ExternalInput").ap()
    onbtm = nc.dram_tensor(_pref + "onbtm", [2, 128], F32, kind="ExternalInput").ap()
    idn = nc.dram_tensor(_pref + "idn", [128, 128], F32, kind="ExternalInput").ap()
    xout = ctx["xout"] if ctx is not None else nc.dram_tensor("xout", [D, T], F32, kind="ExternalOutput").ap()
    if debug:
        dbg = {k: nc.dram_tensor(_pref + "dbg_" + k, shp, F32, kind="ExternalOutput").ap()
               for k, shp in [("qr", [128, 8 * W]), ("kr", [128, 8 * W]),
                              ("att", [128, 2 * W]), ("outf", [128, 8 * W]),
                              ("xw1", [128, W]), ("nsa", [128, 1]), ("nsb", [128, 1]), ("nmx", [128, 1]), ("sim", [128, 2 * W]), ("attp", [128, 2 * W]), ("xn", [128, 8 * FST]),
                              ("act", [128, HC * FST])]}

    es = ExitStack()
    S = lambda *a, **k: es.enter_context(nc.sbuf_tensor(_pref + a[0], *a[1:], **k))
    PS = lambda *a, **k: es.enter_context(nc.psum_tensor(_pref + a[0], *a[1:], **k))
    with es:
        XW = S("XW", [128, NCD * T], F32)
        ARN = S("ARN", [128, 22080], BF16)
        XS = lambda: ARN[:, 0:8 * 264]
        QT = lambda: ARN[:, 2112:2112 + 8 * 256]
        QST = lambda: ARN[:, 4160:4160 + 8 * 256]
        KT = lambda: ARN[:, 6208:6208 + 8 * 256]
        KST = lambda: ARN[:, 8256:8256 + 8 * 256]
        KR = lambda: ARN[:, 10304:10304 + 8 * 512]
        VR = lambda: ARN[:, 14400:14400 + 2 * 2 * 1024]
        OUTF = lambda: ARN[:, 18496:18496 + 8 * 256]
        ATT = lambda: ARN[:, 20544:20544 + 4 * 128]
        SQB = lambda: ARN[:, 21056:21056 + 512]
        INVB = lambda: ARN[:, 21568:21568 + 512]
        TAB = lambda: ARN[:, 1024:2048]
        ATL = lambda: ARN[:, 0:512]
        XSF = lambda: ARN[:, 0:8 * 520]
        XN = lambda: ARN[:, 4160:4160 + 8 * 512]
        ACT = lambda: ARN[:, 8256:8256 + HC * 512]
        GEL = lambda: ARN[:, 19584:19584 + 1024]
        MASKS = S("MASKS", [128, 2 * 2 * W], BF16)
        IDN_BF = S("IDN_BF", [128, 128], BF16)
        INV2B = S("INV2B", [2, 2048], BF16)
        ONB = S("ONB", [128, 2], BF16)
        ONBT = S("ONBT", [2, 128], BF16)
        BIAS = S("BIAS", [128, 52 * DEPTH], F32)
        LNGB = S("LNGB", [128, 16 * DEPTH], F32)
        QSC = S("QSC", [128, 4 * DEPTH], F32)
        wba = S("wba", [128, NCD * 512], BF16)
        wbb = S("wbb", [128, NCD * 512], BF16)
        wbc = wba
        wbd = wbb
        w0 = S("w0", [128, 512], F32)
        w1 = S("w1", [128, 512], F32)
        nsum = S("nsum", [128, 4], F32)
        nmax = S("nmax", [128, 4], F32)
        stat_mu = S("stat_mu", [1, FST], F32)
        stat_rsig = S("stat_rsig", [1, FST], F32)
        stat_tmp = S("stat_tmp", [1, FST], F32)

        pb0 = PS("pb0", [128, 512], F32)
        pb1 = PS("pb1", [128, 512], F32)
        pb2 = PS("pb2", [128, 512], F32)
        pb3 = PS("pb3", [128, 512], F32)
        pb4 = PS("pb4", [128, 512], F32)
        pb5 = PS("pb5", [128, 512], F32)
        PTB = PS("PTB", [128, 1024], BF16)
        PTB2 = PS("PTB2", [128, 512], BF16)

        P = ctx["P"] if ctx is not None else Prog(nc, es)
        _pw = [ctx["prev"]] if ctx is not None and ctx.get("prev") else []
        cpool, ones_col, ones_row, ones_bf, tok0 = emit_common_setup(P, es, nc, extra_waits=_pw, pref=_pref)

        P.emit("pool", lambda e: e.dma_start(ONB[:], onbm), waits=_pw, inc="dqw", inc_by=16)
        P.emit("pool", lambda e: e.dma_start(ONBT[:], onbtm), inc="dqw", inc_by=16)
        P.emit("pool", lambda e: e.dma_start(BIAS[:], biasc), inc="dqw", inc_by=16)
        P.emit("pool", lambda e: e.dma_start(LNGB[:], lngb), inc="dqw", inc_by=16)
        P.emit("pool", lambda e: e.dma_start(QSC[:], qscols), inc="dqw", inc_by=16)
        P.emit("pool", lambda e: e.dma_start(IDN_BF[:], idn), inc="dqw", inc_by=16)
        loads_tok = [("dqw", P.cnt["dqw"]), P.tok("dve")]
        prev = P.stage("dve", [lambda e: e.engine_nop()], waits=loads_tok)

        import os as _os0
        if _os0.environ.get("SKIP_INITLN"):
            _chunked_dma(P, "sp", "dqx", XW, xin, NCD, T, waits=[prev])
            prev = P.stage("dve", [lambda e: e.engine_nop()], waits=[("dqx", P.cnt["dqx"])])
        for it in range(0 if _os0.environ.get("SKIP_INITLN") else T // 256):
            t0 = it * 256
            mmq = []
            for c in range(NCD):
                P.emit("sp", lambda e, c=c, t0=t0: e.dma_start(
                    w0[:, 0:256] if c % 2 == 0 else w1[:, 0:256],
                    xin[c * 128:(c + 1) * 128, t0:t0 + 256]),
                    waits=[prev], inc="dqx", inc_by=16)
                xt = ("dqx", P.cnt["dqx"])
                src = (w0 if c % 2 == 0 else w1)
                prev = P.stage("act", [lambda e, src=src: e.activation(
                    SQB()[:, 0:256], src[:, 0:256], AF.Square)], waits=[xt, prev])
                prev = P.stage("pe", [
                    lambda e, c=c, src=src: e.matmul(pb2[0:1, 0:256], ones_col[:], src[:, 0:256],
                                                     start=(c == 0), stop=(c == NCD - 1)),
                    lambda e, c=c: e.matmul(pb2[0:1, 256:512], ones_bf[:], SQB()[:, 0:256],
                                            start=(c == 0), stop=(c == NCD - 1)),
                ], waits=[prev])
            prev = P.stage("dve", [
                lambda e: e.tensor_scalar(stat_mu[:, 0:256], pb2[0:1, 0:256], 1.0 / D, None, OP.mult),
                lambda e: e.tensor_scalar(stat_tmp[:, 0:256], pb2[0:1, 256:512], 1.0 / D, None, OP.mult),
                lambda e: e.tensor_tensor(stat_rsig[:, 0:256], stat_mu[:, 0:256], stat_mu[:, 0:256], OP.mult),
                lambda e: e.tensor_tensor(stat_tmp[:, 0:256], stat_tmp[:, 0:256], stat_rsig[:, 0:256], OP.subtract),
            ], waits=[prev])
            prev = P.stage("act", [
                lambda e: e.activation(stat_tmp[:, 0:256], stat_tmp[:, 0:256], AF.Ln, bias=cpool[0:1, C_EPS_LN:C_EPS_LN + 1]),
                lambda e: e.activation(stat_rsig[:, 0:256], stat_tmp[:, 0:256], AF.Exp, scale=cpool[0:1, C_NEGHALF:C_NEGHALF + 1]),
            ], waits=[prev])
            prev = P.stage("pe", [
                lambda e: e.matmul(pb0[:, 0:256], ones_row[:], stat_mu[:, 0:256], start=True, stop=True),
                lambda e: e.matmul(pb1[:, 0:256], ones_row[:], stat_rsig[:, 0:256], start=True, stop=True),
            ], waits=[prev])
            for c in range(NCD):
                P.emit("sp", lambda e, c=c, t0=t0: e.dma_start(
                    w0[:, 0:256] if c % 2 == 0 else w1[:, 0:256],
                    xin[c * 128:(c + 1) * 128, t0:t0 + 256]),
                    waits=[prev], inc="dqx", inc_by=16)
                xt = ("dqx", P.cnt["dqx"])
                src = (w0 if c % 2 == 0 else w1)
                prev = P.stage("dve", [
                    lambda e, c=c, t0=t0, src=src: e.tensor_tensor(
                        XW[:, c * T + t0: c * T + t0 + 256], src[:, 0:256], pb0[:, 0:256], OP.subtract),
                    lambda e, c=c, t0=t0: e.tensor_tensor(
                        XW[:, c * T + t0: c * T + t0 + 256], XW[:, c * T + t0: c * T + t0 + 256],
                        pb1[:, 0:256], OP.mult),
                ], waits=[xt, prev])

        def stream_w(buf, src3, lay, col0, ncols, wsem, waits):
            def fn(e, buf=buf, src3=src3, lay=lay, col0=col0, ncols=ncols):
                return e.dma_start(
                    buf[:, 0:NCD * ncols].rearrange("p (c n) -> p c n", c=NCD),
                    src3[lay].rearrange("(c p) n -> p c n", p=128)[:, :, col0:col0 + ncols])
            P.emit("pool", fn, waits=[t for t in waits if t], inc=wsem, inc_by=16)
            return (wsem, P.cnt[wsem])

        def _w(*toks):
            return [t for t in toks if t]

        wbuf_war = {}

        for lay in range(DEPTH_RUN):
            qs_c = QSC[:, lay * 4 + 0: lay * 4 + 1]
            qsw_c = QSC[:, lay * 4 + 1: lay * 4 + 2]
            ks_c = QSC[:, lay * 4 + 2: lay * 4 + 3]
            ksw_c = QSC[:, lay * 4 + 3: lay * 4 + 4]
            prev = P.stage("dve", [lambda e: e.memset(KR()[:], 0.0),
                                   lambda e: e.memset(VR()[:], 0.0)], waits=[prev])
            for w in range(NW_RUN):
                t0 = w * W
                sl = w % 2
                fns = []
                for c in range(NCD):
                    off = 0 if c < 4 else 1
                    if t0 - off < 0:
                        fns.append(lambda e, c=c: e.memset(XS()[:, c * 264:c * 264 + 1], 0.0))
                        fns.append(lambda e, c=c, t0=t0: e.tensor_copy(
                            XS()[:, c * 264 + 1:c * 264 + 256], XW[:, c * T: c * T + 255]))
                    else:
                        fns.append(lambda e, c=c, t0=t0, off=off: e.tensor_copy(
                            XS()[:, c * 264:c * 264 + 256], XW[:, c * T + t0 - off: c * T + t0 + 256 - off]))
                prev = P.stage("dve", fns, waits=[prev])
                import os as _os
                parts = [("q", 0, QT), ("qs", 1024, QST), ("k", 2048, KT), ("ks", 3072, KST)]
                if _os.environ.get("SKIP_QKV"):
                    parts = []
                evq = {}
                xs_tok = prev
                qi = 0
                for pname, col0, dstf in parts:
                    for hf in range(2):
                        buf = (wba, wbb)[qi % 2]
                        wt = stream_w(buf, wqkv5, lay, col0 + hf * 512, 512, "dqq",
                                      waits=[wbuf_war.get(qi % 2, xs_tok)])
                        for m4 in range(4):
                            m = hf * 4 + m4
                            pm = pb0 if m % 2 == 0 else pb1
                            mm = [(lambda e, c=c, m4=m4, pm=pm, buf=buf: e.matmul(
                                pm[:, 0:W], buf[:, c * 512 + m4 * 128: c * 512 + (m4 + 1) * 128],
                                XS()[:, c * 264:c * 264 + 256], start=(c == 0), stop=(c == NCD - 1)))
                                  for c in range(NCD)]
                            ptok = P.stage("pe", mm, waits=_w(
                                wt if m4 == 0 else None,
                                xs_tok if qi == 0 and m4 == 0 else None,
                                evq.get(m % 2)))
                            evq[m % 2] = P.stage("dve", [lambda e, m=m, pm=pm, dstf=dstf: e.tensor_copy(
                                dstf()[:, m * W:(m + 1) * W], pm[:, 0:W])], waits=[ptok])
                        wbuf_war[qi % 2] = ptok
                        qi += 1
                for nsl in range(2 if not _os.environ.get("SKIP_QKV") else 0):
                    buf = (wba, wbb)[qi % 2]
                    wt = stream_w(buf, wqkv5, lay, 4096 + nsl * 512, 512, "dqq",
                                  waits=[wbuf_war.get(qi % 2, xs_tok)])
                    for tc in range(2):
                        pm = pb0 if tc == 0 else pb1
                        mm = [(lambda e, c=c, tc=tc, pm=pm, buf=buf: e.matmul(
                            pm[:, 0:512], XS()[:, c * 264 + tc * 128: c * 264 + (tc + 1) * 128],
                            buf[:, c * 512:(c + 1) * 512],
                            start=(c == 0), stop=(c == NCD - 1)))
                              for c in range(NCD)]
                        ptok = P.stage("pe", mm, waits=_w(
                            wt if tc == 0 else None, evq.get(tc)))
                        evq[tc] = P.stage("dve", [lambda e, tc=tc, nsl=nsl, sl=sl, pm=pm: e.tensor_copy(
                            VR()[:, (sl * 2 + tc) * 1024 + nsl * 512: (sl * 2 + tc) * 1024 + (nsl + 1) * 512],
                            pm[:, 0:512])], waits=[ptok])
                    wbuf_war[qi % 2] = ptok
                    qi += 1
                prev = P.stage("dve", [lambda e: e.engine_nop()])
                vjoin = prev
                if not _os.environ.get("SKIP_TABS"):
                    P.emit("pool", lambda e, t0=t0: e.dma_start(
                        TAB()[:, 0:256], tabs[0:128, t0:t0 + 256]), waits=[prev], inc="dqt", inc_by=16)
                    P.emit("pool", lambda e, t0=t0: e.dma_start(
                        TAB()[:, 256:512], tabs[128:256, t0:t0 + 256]), inc="dqt", inc_by=16)
                    P.emit("pool", lambda e, t0=t0: e.dma_start(
                        TAB()[:, 512:768], tabs[256:384, t0:t0 + 256]), inc="dqt", inc_by=16)
                    P.emit("pool", lambda e, t0=t0: e.dma_start(
                        TAB()[:, 768:1024], tabs[384:512, t0:t0 + 256]), inc="dqt", inc_by=16)
                    moff = 0 if w > 0 else 2
                    P.emit("pool", lambda e, moff=moff: e.dma_start(
                        MASKS[:, 0:512], masks[moff * 128:(moff + 1) * 128, :]), inc="dqt", inc_by=16)
                    P.emit("pool", lambda e, moff=moff: e.dma_start(
                        MASKS[:, 512:1024], masks[(moff + 1) * 128:(moff + 2) * 128, :]), inc="dqt", inc_by=16)
                    prev = P.stage("dve", [
                        lambda e, qs_c=qs_c: e.tensor_scalar(TAB()[:, 0:256], TAB()[:, 0:256], qs_c, None, OP.mult),
                        lambda e, qsw_c=qsw_c: e.tensor_scalar(TAB()[:, 256:512], TAB()[:, 256:512], qsw_c, None, OP.mult),
                        lambda e, ks_c=ks_c: e.tensor_scalar(TAB()[:, 512:768], TAB()[:, 512:768], ks_c, None, OP.mult),
                        lambda e, ksw_c=ksw_c: e.tensor_scalar(TAB()[:, 768:1024], TAB()[:, 768:1024], ksw_c, None, OP.mult),
                    ], waits=[prev, ("dqt", P.cnt["dqt"])])
                import os as _os
                _skip_rot = _os.environ.get("SKIP_ROT")
                tab_tok = prev
                rotpb = (pb0, pb1, pb2, pb3)
                rot_prev = None
                for side, base_t, swap_t, taba, tabb in (() if _skip_rot else (
                        ("q", QT, QST, 0, 256), ("k", KT, KST, 512, 768))):
                    sq_t = P.stage("act", [lambda e, base_t=base_t: e.activation(
                        OUTF()[:, 0:2048], base_t()[:, 0:2048], AF.Square)],
                        waits=_w(vjoin if side == "q" else None))
                    mm_t = P.stage("pe", [(lambda e, j=j: e.matmul(
                        rotpb[j][0:2, 0:512], ONB[:], OUTF()[:, j * 512:(j + 1) * 512],
                        start=True, stop=True)) for j in range(4)],
                        waits=_w(sq_t, rot_prev))
                    ln_fns = []
                    for j in range(4):
                        wsl = w0 if j % 2 == 0 else w1
                        ln_fns.append(lambda e, j=j, wsl=wsl: e.activation(
                            wsl[0:2, 0:512], rotpb[j][0:2, 0:512], AF.Ln,
                            bias=cpool[0:2, C_EPS_L2:C_EPS_L2 + 1]))
                        ln_fns.append(lambda e, j=j, wsl=wsl: e.activation(
                            INV2B[:, j * 512:(j + 1) * 512], wsl[0:2, 0:512], AF.Exp,
                            scale=cpool[0:2, C_NEGHALF:C_NEGHALF + 1]))
                    ln_t = P.stage("act", ln_fns, waits=[mm_t])
                    bc_t = P.stage("pe", [(lambda e, j=j: e.matmul(
                        rotpb[j][:, 0:512], ONBT[:], INV2B[0:2, j * 512:(j + 1) * 512],
                        start=True, stop=True)) for j in range(4)], waits=[ln_t])
                    fns = []
                    for c in range(NCD):
                        fns.append(lambda e, c=c, base_t=base_t, taba=taba: e.tensor_tensor(
                            base_t()[:, c * W:(c + 1) * W], base_t()[:, c * W:(c + 1) * W],
                            TAB()[:, taba:taba + W], OP.mult))
                        fns.append(lambda e, c=c, swap_t=swap_t, tabb=tabb: e.tensor_tensor(
                            swap_t()[:, c * W:(c + 1) * W], swap_t()[:, c * W:(c + 1) * W],
                            TAB()[:, tabb:tabb + W], OP.mult))
                        fns.append(lambda e, c=c, base_t=base_t, swap_t=swap_t: e.tensor_tensor(
                            base_t()[:, c * W:(c + 1) * W], base_t()[:, c * W:(c + 1) * W],
                            swap_t()[:, c * W:(c + 1) * W], OP.add))
                        fns.append(lambda e, c=c, base_t=base_t: e.tensor_tensor(
                            base_t()[:, c * W:(c + 1) * W], base_t()[:, c * W:(c + 1) * W],
                            rotpb[c // 2][:, (c % 2) * 256:(c % 2) * 256 + 256], OP.mult))
                    rot_prev = P.stage("dve", fns, waits=_w(bc_t, tab_tok))
                if not _skip_rot:
                    prev = rot_prev
                prev = P.stage("dve", [(lambda e, c=c, sl=sl: e.tensor_copy(
                    KR()[:, c * 512 + sl * 256: c * 512 + (sl + 1) * 256], KT()[:, c * W:(c + 1) * W]))
                    for c in range(NCD)], waits=[prev])
                if debug and lay == 0 and w == 0:
                    P.emit("pool", lambda e: e.dma_start(dbg["qr"], QT()[:]), waits=[prev], inc="dqo", inc_by=16)
                    P.emit("pool", lambda e: e.dma_start(dbg["kr"], KT()[:]), inc="dqo", inc_by=16)
                    prev = P.stage("dve", [lambda e: e.engine_nop()], waits=[P.tok("dqo")])
                import os as _os
                _skip_heads = _os.environ.get("SKIP_HEADS")
                ATL2 = lambda s: ARN[:, s * 512:(s + 1) * 512]
                ATT2 = lambda s: ARN[:, 20544 + s * 512:20544 + (s + 1) * 512]
                simpb = (pb3, pb1)
                avpb = (pb2, pb0)
                stt_tok = {}; evout_tok = {}; attcp_tok = {}
                it_i = 0
                head_pe_prev = prev
                for c in range(NCD if not _skip_heads else 0):
                    for hh in range(2):
                        h = 2 * c + hh
                        rb = hh * 64
                        for mi in range(2):
                            s = it_i % 2
                            ps = simpb[s]
                            pa = avpb[s]
                            wbuf = w0 if s == 0 else w1
                            w_ = [head_pe_prev] if it_i == 0 else []
                            if stt_tok.get(s):
                                w_.append(stt_tok[s])
                            ptok = P.stage("pe", [
                                (lambda e, c=c, rb=rb, mi=mi, sl=sl, ps=ps: e.matmul(
                                    ps[0:128, 0:256],
                                    QT()[rb:rb + 64, c * W + mi * 128: c * W + (mi + 1) * 128],
                                    KR()[rb:rb + 64, c * 512 + (1 - sl) * 256: c * 512 + (2 - sl) * 256],
                                    start=True, stop=True)),
                                (lambda e, c=c, rb=rb, mi=mi, sl=sl, ps=ps: e.matmul(
                                    ps[0:128, 256:512],
                                    QT()[rb:rb + 64, c * W + mi * 128: c * W + (mi + 1) * 128],
                                    KR()[rb:rb + 64, c * 512 + sl * 256: c * 512 + (sl + 1) * 256],
                                    start=True, stop=True)),
                            ], waits=w_)
                            stt_tok[s] = t1 = P.stage("dve", [
                                lambda e, mi=mi, ps=ps, wbuf=wbuf: e.scalar_tensor_tensor(
                                    wbuf[:, 0:512], ps[0:128, 0:512], 8.0,
                                    MASKS[:, mi * 512:(mi + 1) * 512], OP.mult, OP.add),
                            ], waits=[ptok])
                            t2 = P.stage("dve", [
                                lambda e, s=s, wbuf=wbuf: e.tensor_reduce(nmax[:, s:s + 1], wbuf[:, 0:512],
                                                          mybir.AxisListType.X, OP.max, negate=True),
                            ], waits=[t1])
                            t3 = P.stage("act", [lambda e, s=s, wbuf=wbuf: e.activation(
                                ATL2(s)[:, 0:512], wbuf[:, 0:512], AF.Exp, bias=nmax[:, s:s + 1])],
                                waits=[t2])
                            t4 = P.stage("dve", [
                                lambda e, s=s: e.tensor_reduce(nsum[:, s:s + 1], ATL2(s)[:, 0:512],
                                                          mybir.AxisListType.X, OP.add),
                            ], waits=[t3])
                            t5 = P.stage("dve", [
                                lambda e, s=s: e.reciprocal(nsum[:, s:s + 1], nsum[:, s:s + 1]),
                            ], waits=[t4])
                            t6 = P.stage("dve", [
                                lambda e, s=s: e.tensor_scalar(ATL2(s)[:, 0:512], ATL2(s)[:, 0:512],
                                                          nsum[:, s:s + 1], None, OP.mult),
                            ], waits=[t5])
                            ttw = [t6]
                            if attcp_tok.get(s):
                                ttw.append(attcp_tok[s])
                            ptt = P.stage("pe", [(lambda e, kc=kc, s=s: e.transpose(
                                PTB[:, s * 512 + kc * 128: s * 512 + (kc + 1) * 128],
                                ATL2(s)[:, kc * 128:(kc + 1) * 128], IDN_BF[:]))
                                  for kc in range(4)], waits=ttw)
                            attcp_tok[s] = t7 = P.stage("act", [(lambda e, kc=kc, s=s: e.copy(
                                ATT2(s)[:, kc * 128:(kc + 1) * 128],
                                PTB[:, s * 512 + kc * 128: s * 512 + (kc + 1) * 128]))
                                for kc in range(4)], waits=[ptt])
                            avw = [t7]
                            if evout_tok.get(s):
                                avw.append(evout_tok[s])
                            mm = []
                            for kc in range(4):
                                vsl = (1 - sl) if kc < 2 else sl
                                vtc = kc % 2
                                mm.append(lambda e, kc=kc, vsl=vsl, vtc=vtc, h=h, rb=rb, s=s, pa=pa: e.matmul(
                                    pa[rb:rb + 64, 0:128],
                                    VR()[:, (vsl * 2 + vtc) * 1024 + h * 64: (vsl * 2 + vtc) * 1024 + (h + 1) * 64],
                                    ATT2(s)[:, kc * 128:(kc + 1) * 128],
                                    start=(kc == 0), stop=(kc == 3),
                                    tile_position=(0, 64) if rb == 64 else None))
                            pav = P.stage("pe", mm, waits=avw)
                            evout_tok[s] = P.stage("dve", [lambda e, c=c, mi=mi, rb=rb, pa=pa: e.tensor_copy(
                                OUTF()[rb:rb + 64, c * W + mi * 128: c * W + (mi + 1) * 128],
                                pa[rb:rb + 64, 0:128])], waits=[pav])
                            it_i += 1
                prev = P.stage("dve", [lambda e: e.engine_nop()],
                               waits=[t for t in list(evout_tok.values()) + list(stt_tok.values()) if t])
                if debug and lay == 0 and w == 0:
                    P.emit("pool", lambda e: e.dma_start(dbg["outf"], OUTF()[:]), waits=[prev], inc="dqo", inc_by=16)
                    prev = P.stage("dve", [lambda e: e.engine_nop()], waits=[P.tok("dqo")])
                war_o = {0: prev, 1: prev}
                evtok_o = {}
                first_pe_o = True
                for og in range(2 if not _os.environ.get("SKIP_WO") else 0):
                    buf = wba if og % 2 == 0 else wbb
                    def wfno(e, buf=buf, lay=lay, og=og):
                        return e.dma_start(
                            buf[:, 0:NCD * 512].rearrange("p (c n) -> p c n", c=NCD),
                            wo[lay].rearrange("(c p) n -> p c n", p=128)[:, :, og * 512:(og + 1) * 512])
                    P.emit("pool", wfno, waits=[wbuf_war.get(og % 2, war_o[og % 2])],
                           inc="dqq", inc_by=16)
                    wt = ("dqq", P.cnt["dqq"])
                    for oi in range(4):
                        mo = og * 4 + oi
                        pm = pb0 if oi % 2 == 0 else pb1
                        mm = [(lambda e, c=c, oi=oi, pm=pm, buf=buf: e.matmul(
                            pm[:, 0:W], buf[:, c * 512 + oi * 128: c * 512 + (oi + 1) * 128],
                            OUTF()[:, c * W:(c + 1) * W], start=(c == 0), stop=(c == NCD - 1)))
                              for c in range(NCD)]
                        w_ = [wt] if oi == 0 else []
                        if first_pe_o:
                            w_.append(prev); first_pe_o = False
                        if evtok_o.get(oi % 2):
                            w_.append(evtok_o[oi % 2])
                        ptok = P.stage("pe", mm, waits=w_)
                        evtok_o[oi % 2] = P.stage("dve", [lambda e, mo=mo, t0=t0, pm=pm: e.tensor_tensor(
                            XW[:, mo * T + t0: mo * T + t0 + W], XW[:, mo * T + t0: mo * T + t0 + W],
                            pm[:, 0:W], OP.add)], waits=[ptok])
                        war_o[og % 2] = ptok
                        wbuf_war[og % 2] = ptok
                prev = P.stage("dve", [lambda e: e.engine_nop()],
                               waits=[t for t in evtok_o.values() if t])
                if debug and lay == 0 and w == 0:
                    P.emit("sp", lambda e: e.dma_start(dbg["xw1"], XW[:, 0:256]), waits=[prev], inc="dqo", inc_by=16)
                    prev = P.stage("dve", [lambda e: e.engine_nop()], waits=[P.tok("dqo")])
            for ft in range(FT_RUN):
                t0 = ft * FST
                fns = []
                for c in range(NCD):
                    off = 0 if c < 4 else 1
                    if t0 - off < 0:
                        fns.append(lambda e, c=c: e.memset(XSF()[:, c * 520:c * 520 + 1], 0.0))
                        fns.append(lambda e, c=c: e.tensor_copy(
                            XSF()[:, c * 520 + 1:c * 520 + FST], XW[:, c * T: c * T + FST - 1]))
                    else:
                        fns.append(lambda e, c=c, t0=t0, off=off: e.tensor_copy(
                            XSF()[:, c * 520:c * 520 + FST], XW[:, c * T + t0 - off: c * T + t0 + FST - off]))
                prev = P.stage("dve", fns, waits=[prev])
                prev = emit_ln(P, FST, prev, ARN, 0, 520, NCD, 1.0 / D,
                               ARN[:, 8256:8256 + 8 * 512], pb2[0:1, 0:FST], pb3[0:1, 0:FST],
                               stat_mu, stat_rsig, stat_tmp, pb0, pb1, cpool, ones_col, ones_row,
                               out_bf=XN(), out_cols=512, ones_sq=ones_bf, ones_x=ones_bf,
                               src_cast=w0)
                fns = []
                for c in range(NCD):
                    g_c = LNGB[:, lay * 16 + c: lay * 16 + c + 1]
                    b_c = LNGB[:, lay * 16 + 8 + c: lay * 16 + 8 + c + 1]
                    fns.append(lambda e, c=c, g_c=g_c, b_c=b_c: e.tensor_scalar(
                        XN()[:, c * 512:(c + 1) * 512], XN()[:, c * 512:(c + 1) * 512],
                        g_c, b_c, OP.mult, OP.add))
                prev = P.stage("dve", fns, waits=[prev])
                war_f = {0: prev, 1: prev}
                evtok_f = {}
                geltok = {}
                first_pe_f = True
                for half in range(2):
                    for mg in range(11):
                        gslot = mg % 2
                        buf = wbc if gslot == 0 else wbd
                        def wfn(e, lay=lay, half=half, mg=mg, buf=buf):
                            return e.dma_start(
                                buf[:, 0:NCD * 256].rearrange("p (c n) -> p c n", c=NCD),
                                w1p[lay].rearrange("(c p) n -> p c n", p=128)[:, :, half * 2816 + mg * 256: half * 2816 + (mg + 1) * 256])
                        P.emit("pool", wfn, waits=[wbuf_war.get(gslot, war_f[gslot])],
                               inc="dqf", inc_by=16)
                        wt = ("dqf", P.cnt["dqf"])
                        for mi in range(2):
                            m = mg * 2 + mi
                            pm = pb0 if mi == 0 else pb1
                            mm = [(lambda e, c=c, mi=mi, pm=pm, buf=buf: e.matmul(
                                pm[:, 0:512], buf[:, c * 256 + mi * 128: c * 256 + (mi + 1) * 128],
                                XN()[:, c * 512:(c + 1) * 512], start=(c == 0), stop=(c == NCD - 1)))
                                  for c in range(NCD)]
                            w_ = [wt] if mi == 0 else []
                            if first_pe_f:
                                w_.append(prev); first_pe_f = False
                            if evtok_f.get(mi):
                                w_.append(evtok_f[mi])
                            ptok = P.stage("pe", mm, waits=w_)
                            if half == 1:
                                bcol = BIAS[:, lay * 52 + 22 + m: lay * 52 + 23 + m]
                                gslot2 = m % 2
                                gw = [ptok]
                                if geltok.get(gslot2):
                                    gw.append(geltok[gslot2])
                                atok = P.stage("act", [lambda e, m=m, pm=pm, bcol=bcol, gslot2=gslot2: e.activation(
                                    GEL()[:, gslot2 * 512:(gslot2 + 1) * 512], pm[:, 0:512], AF.Gelu, bias=bcol)],
                                    waits=gw)
                                evtok_f[mi] = geltok[gslot2] = P.stage("dve", [lambda e, m=m, gslot2=gslot2: e.tensor_tensor(
                                    ACT()[:, m * 512:(m + 1) * 512], ACT()[:, m * 512:(m + 1) * 512],
                                    GEL()[:, gslot2 * 512:(gslot2 + 1) * 512], OP.mult)], waits=[atok])
                            else:
                                bcol = BIAS[:, lay * 52 + m: lay * 52 + 1 + m]
                                evtok_f[mi] = P.stage("dve", [lambda e, m=m, pm=pm, bcol=bcol: e.tensor_scalar(
                                    ACT()[:, m * 512:(m + 1) * 512], pm[:, 0:512], bcol, None, OP.add)],
                                    waits=[ptok])
                        war_f[gslot] = ptok
                        wbuf_war[gslot] = ptok
                prev = P.stage("dve", [lambda e: e.engine_nop()],
                               waits=[t for t in evtok_f.values() if t] + [t for t in geltok.values() if t])
                war_2 = {0: prev, 1: prev}
                evtok_2 = {}
                first_pe_2 = True
                for mo in range(8):
                    gslot = mo % 2
                    buf = wbc if gslot == 0 else wbd
                    def wfn2(e, lay=lay, mo=mo, buf=buf):
                        return e.dma_start(
                            buf[:, 0:HC * 128].rearrange("p (c n) -> p c n", c=HC),
                            w2p[lay].rearrange("(c p) n -> p c n", p=128)[:, :, mo * 128:(mo + 1) * 128])
                    P.emit("pool", wfn2, waits=[wbuf_war.get(gslot, war_2[gslot])],
                           inc="dqf", inc_by=16)
                    wt = ("dqf", P.cnt["dqf"])
                    pm = pb0 if gslot == 0 else pb1
                    wbuf2 = w0 if gslot == 0 else w1
                    w_ = [wt]
                    if first_pe_2:
                        w_.append(prev); first_pe_2 = False
                    if evtok_2.get(gslot):
                        w_.append(evtok_2[gslot])
                    mm = [(lambda e, k=k, pm=pm, buf=buf: e.matmul(
                        pm[:, 0:512], buf[:, k * 128:(k + 1) * 128],
                        ACT()[:, k * 512:(k + 1) * 512], start=(k == 0), stop=(k == HC - 1)))
                          for k in range(HC)]
                    ptok = P.stage("pe", mm, waits=w_)
                    war_2[gslot] = ptok
                    wbuf_war[gslot] = ptok
                    b2col = BIAS[:, lay * 52 + 44 + mo: lay * 52 + 45 + mo]
                    evtok_2[gslot] = P.stage("dve", [
                        lambda e, pm=pm, b2col=b2col, wbuf2=wbuf2: e.tensor_scalar(
                            wbuf2[:, 0:512], pm[:, 0:512], b2col, None, OP.add),
                        lambda e, mo=mo, t0=t0, wbuf2=wbuf2: e.tensor_tensor(
                            XW[:, mo * T + t0: mo * T + t0 + FST],
                            XW[:, mo * T + t0: mo * T + t0 + FST], wbuf2[:, 0:512], OP.add),
                    ], waits=[ptok])
                prev = P.stage("dve", [lambda e: e.engine_nop()],
                               waits=[t for t in evtok_2.values() if t])
                if debug and lay == 0 and ft == 0:
                    P.emit("pool", lambda e: e.dma_start(dbg["xn"], XN()[:]), waits=[prev], inc="dqo", inc_by=16)
                    P.emit("pool", lambda e: e.dma_start(dbg["act"], ACT()[:]), inc="dqo", inc_by=16)
                    prev = P.stage("dve", [lambda e: e.engine_nop()], waits=[P.tok("dqo")])

        for c in range(NCD):
            for pc in range(T // 512):
                buf = w0 if pc % 2 == 0 else w1
                P.emit("sp", lambda e, c=c, pc=pc, buf=buf: e.dma_start(
                    buf[:, 0:512], xin[c * 128:(c + 1) * 128, pc * 512:(pc + 1) * 512]),
                    waits=[prev], inc="dqa", inc_by=16)
                prev = P.stage("dve", [lambda e, c=c, pc=pc, buf=buf: e.tensor_tensor(
                    XW[:, c * T + pc * 512: c * T + (pc + 1) * 512],
                    XW[:, c * T + pc * 512: c * T + (pc + 1) * 512], buf[:, 0:512], OP.add)],
                    waits=[prev, ("dqa", P.cnt["dqa"])])
        for c in range(NCD):
            P.emit("sp", lambda e, c=c: e.dma_start(xout[c * 128:(c + 1) * 128, :],
                                                    XW[:, c * T:(c + 1) * T]),
                   waits=[prev] if c == 0 else (), inc="dqo2", inc_by=16)
        P.emit("sp", None, waits=[P.tok("dqo2")])
        if ctx is not None:
            ctx["prev"] = prev
            ctx["out_tok"] = P.tok("dqo2")
        import os
        if os.environ.get("PROG_STATS"):
            print("SEM COUNTS:", {k: v for k, v in P.cnt.items()})
            print("OPS:", {k: len(v) for k, v in P.ops.items()})
        if ctx is None:
            P.finalize()
    return nc


def run_l1_ffn(x, Wqkv, q_scale, k_scale, Wo, ln_g, ln_b, W1, b1, W2, b2, debug=False, **bk):
    nc = build_l1_ffn(debug=debug, **bk)
    DEPTH = 6
    f32 = np.float32
    sigma = np.concatenate([np.arange(32, 64), np.arange(0, 32)])
    perm1024 = np.concatenate([h * 64 + sigma for h in range(16)])
    wqkv5 = np.empty((DEPTH, 1024, 5 * 1024), f32)
    for i in range(DEPTH):
        q = Wqkv[i][:, 0:1024]; k = Wqkv[i][:, 1024:2048]; v = Wqkv[i][:, 2048:3072]
        wqkv5[i] = np.concatenate([q, q[:, perm1024], k, k[:, perm1024], v], axis=1)
    w1p = np.zeros((DEPTH, 1024, 2 * 2816), f32)
    w1p[:, :, 0:2730] = W1[:, :, 0:2730]
    w1p[:, :, 2816:2816 + 2730] = W1[:, :, 2730:5460]
    w2p = np.zeros((DEPTH, 2816, 1024), f32)
    w2p[:, 0:2730, :] = W2
    biasc = np.zeros((128, 52 * DEPTH), f32)
    lngb = np.zeros((128, 16 * DEPTH), f32)
    qscols = np.zeros((128, 4 * DEPTH), f32)
    for i in range(DEPTH):
        b1a = np.zeros(2816, f32); b1a[0:2730] = b1[i][0:2730]
        b1g = np.zeros(2816, f32); b1g[0:2730] = b1[i][2730:5460]
        biasc[:, i * 52 + 0:i * 52 + 22] = b1a.reshape(22, 128).T
        biasc[:, i * 52 + 22:i * 52 + 44] = b1g.reshape(22, 128).T
        biasc[:, i * 52 + 44:i * 52 + 52] = b2[i].reshape(8, 128).T
        lngb[:, i * 16:i * 16 + 8] = ln_g[i].reshape(8, 128).T
        lngb[:, i * 16 + 8:i * 16 + 16] = ln_b[i].reshape(8, 128).T
        qscols[:, i * 4 + 0] = np.tile(q_scale[i], 2)
        qscols[:, i * 4 + 1] = np.tile(q_scale[i][sigma], 2)
        qscols[:, i * 4 + 2] = np.tile(k_scale[i], 2)
        qscols[:, i * 4 + 3] = np.tile(k_scale[i][sigma], 2)
    DIM_HEAD = 64; WINDOW = 256
    half = np.arange(0, DIM_HEAD, 2, dtype=np.float64)
    inv_freq = 1.0 / 10000.0 ** (half / DIM_HEAD)
    t = np.arange(T, dtype=np.float64)
    freqs = t[:, None] * inv_freq[None, :]
    freqs = np.concatenate([freqs, freqs], -1)
    base_scale = (half + 0.4 * DIM_HEAD) / (1.4 * DIM_HEAD)
    power = (t - T // 2) / (WINDOW // 2)
    scale = base_scale[None, :] ** power[:, None]
    scale = np.concatenate([scale, scale], -1)
    sign = np.where(np.arange(64) < 32, -1.0, 1.0)
    tabA = (np.cos(freqs) * scale).T
    tabB = (np.sin(freqs) * scale).T * sign[:, None]
    tabC = (np.cos(freqs) / scale).T
    tabD = (np.sin(freqs) / scale).T * sign[:, None]
    tabs = np.concatenate([np.tile(tabA, (2, 1)), np.tile(tabB, (2, 1)),
                           np.tile(tabC, (2, 1)), np.tile(tabD, (2, 1))], axis=0).astype(f32)
    masks = np.zeros((512, 512), f32)
    for wflag in range(2):
        for mi in range(2):
            m = np.zeros((128, 512), f32)
            q_off = mi * 128 + np.arange(128)[:, None]
            j = np.arange(512)[None, :]
            allowed = (j >= q_off) & (j <= q_off + 256)
            if wflag == 1:
                allowed &= (j >= 256)
            m[~allowed] = -1e30
            masks[(wflag * 2 + mi) * 128:(wflag * 2 + mi + 1) * 128] = m
    onbm = np.zeros((128, 2), f32)
    onbm[0:64, 0] = 1.0; onbm[64:128, 1] = 1.0
    onbtm = np.ascontiguousarray(onbm.T)
    idn = np.eye(128, dtype=f32)
    in_maps = []
    for b in range(N_CORES):
        in_maps.append(dict(xin=np.ascontiguousarray(x[b].T).astype(f32),
                            wqkv5=wqkv5, wo=np.ascontiguousarray(Wo).astype(f32),
                            w1p=w1p, w2p=w2p, biasc=biasc, lngb=lngb, qscols=qscols,
                            tabs=tabs, masks=masks, onbm=onbm, onbtm=onbtm, idn=idn))
    res = run_bass_kernel_spmd(nc, in_maps, core_ids=list(range(N_CORES)))
    out = np.stack([np.ascontiguousarray(r["xout"].T) for r in res.results])
    if debug:
        return out, res.results[0]
    return out


def _prep_l0_attn(W_in, W_A, W_B, W_C, W_out, norm_w):
    DI = 2048
    wi = W_in.reshape(1024, 32, 128)
    order = []
    for mp in range(16):
        order += [mp, mp + 16]
    w_in = np.ascontiguousarray(wi[:, order, :].reshape(1024, 4096)).astype(np.float32)
    tau = np.arange(64)[:, None]; t = np.arange(64)[None, :]
    return dict(w_in=w_in,
                w_abc=np.ascontiguousarray(np.concatenate([W_A, W_B, W_C], axis=1)).astype(np.float32),
                w_out=np.ascontiguousarray(W_out).astype(np.float32),
                nwrep=np.ascontiguousarray(np.broadcast_to(norm_w[None, :], (128, DI))).astype(np.float32),
                trim=(tau <= t).astype(np.float32),
                idn=np.eye(128, dtype=np.float32))


def _prep_l0_ffn(W1, b1, W2, b2, scale):
    return dict(w1T=np.ascontiguousarray(W1.T).astype(np.float32),
                w2T=np.ascontiguousarray(W2.T).astype(np.float32),
                b1c=np.ascontiguousarray(b1.reshape(32, 128).T).astype(np.float32),
                b2c=np.ascontiguousarray(b2.reshape(8, 128).T).astype(np.float32),
                sclr=np.full((128, 1), np.float32(scale), np.float32),
                idn=np.eye(128, dtype=np.float32))


def _prep_l1_attn(W_in, W_x, W_dt, b_dt, gamma, beta):
    w_inT = np.ascontiguousarray(W_in.T).astype(np.float32)
    vecs = np.concatenate([
        W_x[0].reshape(NCD, 128).T,
        W_dt[:, 0].reshape(NCD, 128).T,
        b_dt.reshape(NCD, 128).T,
        gamma.reshape(NCD, 128).T,
        beta.reshape(NCD, 128).T,
    ], axis=1).astype(np.float32)
    return dict(w_inT=w_inT, vecs=vecs)


def _prep_l1_ffn(Wqkv, q_scale, k_scale, Wo, ln_g, ln_b, W1, b1, W2, b2):
    DEPTH = 6
    f32 = np.float32
    sigma = np.concatenate([np.arange(32, 64), np.arange(0, 32)])
    perm1024 = np.concatenate([h * 64 + sigma for h in range(16)])
    wqkv5 = np.empty((DEPTH, 1024, 5 * 1024), f32)
    for i in range(DEPTH):
        q = Wqkv[i][:, 0:1024]; k = Wqkv[i][:, 1024:2048]; v = Wqkv[i][:, 2048:3072]
        wqkv5[i] = np.concatenate([q, q[:, perm1024], k, k[:, perm1024], v], axis=1)
    w1p = np.zeros((DEPTH, 1024, 2 * 2816), f32)
    w1p[:, :, 0:2730] = W1[:, :, 0:2730]
    w1p[:, :, 2816:2816 + 2730] = W1[:, :, 2730:5460]
    w2p = np.zeros((DEPTH, 2816, 1024), f32)
    w2p[:, 0:2730, :] = W2
    biasc = np.zeros((128, 52 * DEPTH), f32)
    lngb = np.zeros((128, 16 * DEPTH), f32)
    qscols = np.zeros((128, 4 * DEPTH), f32)
    for i in range(DEPTH):
        b1a = np.zeros(2816, f32); b1a[0:2730] = b1[i][0:2730]
        b1g = np.zeros(2816, f32); b1g[0:2730] = b1[i][2730:5460]
        biasc[:, i * 52 + 0:i * 52 + 22] = b1a.reshape(22, 128).T
        biasc[:, i * 52 + 22:i * 52 + 44] = b1g.reshape(22, 128).T
        biasc[:, i * 52 + 44:i * 52 + 52] = b2[i].reshape(8, 128).T
        lngb[:, i * 16:i * 16 + 8] = ln_g[i].reshape(8, 128).T
        lngb[:, i * 16 + 8:i * 16 + 16] = ln_b[i].reshape(8, 128).T
        qscols[:, i * 4 + 0] = np.tile(q_scale[i], 2)
        qscols[:, i * 4 + 1] = np.tile(q_scale[i][sigma], 2)
        qscols[:, i * 4 + 2] = np.tile(k_scale[i], 2)
        qscols[:, i * 4 + 3] = np.tile(k_scale[i][sigma], 2)
    DIM_HEAD = 64; WINDOW = 256
    half = np.arange(0, DIM_HEAD, 2, dtype=np.float64)
    inv_freq = 1.0 / 10000.0 ** (half / DIM_HEAD)
    t = np.arange(T, dtype=np.float64)
    freqs = np.concatenate([t[:, None] * inv_freq[None, :]] * 2, -1)
    base_scale = (half + 0.4 * DIM_HEAD) / (1.4 * DIM_HEAD)
    power = (t - T // 2) / (WINDOW // 2)
    scale = np.concatenate([base_scale[None, :] ** power[:, None]] * 2, -1)
    sign = np.where(np.arange(64) < 32, -1.0, 1.0)
    tabA = (np.cos(freqs) * scale).T
    tabB = (np.sin(freqs) * scale).T * sign[:, None]
    tabC = (np.cos(freqs) / scale).T
    tabD = (np.sin(freqs) / scale).T * sign[:, None]
    tabs = np.concatenate([np.tile(tabA, (2, 1)), np.tile(tabB, (2, 1)),
                           np.tile(tabC, (2, 1)), np.tile(tabD, (2, 1))], axis=0).astype(f32)
    masks = np.zeros((512, 512), f32)
    for wflag in range(2):
        for mi in range(2):
            m = np.zeros((128, 512), f32)
            q_off = mi * 128 + np.arange(128)[:, None]
            j = np.arange(512)[None, :]
            allowed = (j >= q_off) & (j <= q_off + 256)
            if wflag == 1:
                allowed &= (j >= 256)
            m[~allowed] = -1e30
            masks[(wflag * 2 + mi) * 128:(wflag * 2 + mi + 1) * 128] = m
    onbm = np.zeros((128, 2), f32)
    onbm[0:64, 0] = 1.0; onbm[64:128, 1] = 1.0
    return dict(wqkv5=wqkv5, wo=np.ascontiguousarray(Wo).astype(f32), w1p=w1p, w2p=w2p,
                biasc=biasc, lngb=lngb, qscols=qscols, tabs=tabs, masks=masks,
                onbm=onbm, onbtm=np.ascontiguousarray(onbm.T), idn=np.eye(128, dtype=f32))


def build_fused():
    nc = bass.Bass("TRN2", target_bir_lowering=False)
    xin = nc.dram_tensor("xin", [D, T], F32, kind="ExternalInput").ap()
    xout = nc.dram_tensor("xout", [D, T], F32, kind="ExternalOutput").ap()
    xh1 = nc.dram_tensor("xh1", [D, T], F32, kind="Internal").ap()
    xh2 = nc.dram_tensor("xh2", [D, T], F32, kind="Internal").ap()
    xh3 = nc.dram_tensor("xh3", [D, T], F32, kind="Internal").ap()
    es = ExitStack()
    with es:
        P = Prog(nc, es)
        ctx = {"nc": nc, "P": P, "pref": "a_", "xin": xin, "xout": xh1, "prev": None}
        build_l0_attn(ctx=ctx)
        ctx.update(pref="b_", xin=xh1, xout=xh2)
        build_l0_ffn(ctx=ctx)
        ctx.update(pref="c_", xin=xh2, xout=xh3)
        build_l1_attn(ctx=ctx)
        ctx.update(pref="d_", xin=xh3, xout=xout)
        build_l1_ffn(ctx=ctx)
        P.finalize()
    return nc


_FUSED_CACHE = {}


def kernel(**inputs):
    d = inputs
    x = np.asarray(d["L0_attn_x"], np.float32)
    if "nc" not in _FUSED_CACHE:
        _FUSED_CACHE["nc"] = build_fused()
    nc = _FUSED_CACHE["nc"]
    am = _prep_l0_attn(d["L0_attn_W_in"], d["L0_attn_W_A"], d["L0_attn_W_B"],
                       d["L0_attn_W_C"], d["L0_attn_W_out"], d["L0_attn_norm_w"])
    bm = _prep_l0_ffn(d["L0_ffn_W1"], d["L0_ffn_b1"], d["L0_ffn_W2"], d["L0_ffn_b2"],
                      d["L0_ffn_scale"])
    cm = _prep_l1_attn(d["L1_attn_W_in"], d["L1_attn_W_x"], d["L1_attn_W_dt"],
                       d["L1_attn_b_dt"], d["L1_attn_gamma"], d["L1_attn_beta"])
    dm = _prep_l1_ffn(d["L1_ffn_Wqkv"], d["L1_ffn_q_scale"], d["L1_ffn_k_scale"],
                      d["L1_ffn_Wo"], d["L1_ffn_ln_g"], d["L1_ffn_ln_b"],
                      d["L1_ffn_W1"], d["L1_ffn_b1"], d["L1_ffn_W2"], d["L1_ffn_b2"])
    wm = {}
    for pref, mm in (("a_", am), ("b_", bm), ("c_", cm), ("d_", dm)):
        for k, v in mm.items():
            wm[pref + k] = v
    in_maps = []
    for b in range(N_CORES):
        m = dict(wm)
        m["xin"] = np.ascontiguousarray(x[b].T).astype(np.float32)
        in_maps.append(m)
    res = run_bass_kernel_spmd(nc, in_maps, core_ids=list(range(N_CORES)))
    return np.ascontiguousarray(
        np.stack([np.ascontiguousarray(r["xout"].T) for r in res.results]))
